# version 07
# baseline (speedup 1.0000x reference)
import numpy as np
import ml_dtypes
import concourse.bass as bass
import concourse.mybir as mybir
from concourse.bass_utils import run_bass_kernel_spmd
from contextlib import ExitStack

F32 = mybir.dt.float32
BF16 = mybir.dt.bfloat16
AF = mybir.ActivationFunctionType
ALU = mybir.AluOpType
AX = mybir.AxisListType

D = 1024
DC = 8
P129 = 129


DTSZ = {F32: 4, BF16: 2, mybir.dt.float16: 2, mybir.dt.int32: 4, mybir.dt.uint8: 1}


class _Rec:
    __slots__ = ("idx", "eng", "method", "args", "kw", "mwaits", "reads", "writes",
                 "is_dma", "ticked", "sem", "val", "edges", "clock", "eidx",
                 "_needed", "_waits", "_fence")

    def __init__(self, idx, eng, method, args, kw, mwaits, is_dma):
        self.idx = idx
        self.eng = eng
        self.method = method
        self.args = args
        self.kw = kw
        self.mwaits = mwaits
        self.is_dma = is_dma
        self.ticked = False
        self.sem = None
        self.val = None
        self.edges = []
        self.clock = None
        self.eidx = 0


class _SemPool:
    def __init__(self, nc, es, name, k, step):
        self.sems = [es.enter_context(nc.semaphore(f"{name}{i}")) for i in range(k)]
        self.count = [0] * k
        self.i = 0
        self.step = step

    def take(self):
        i = self.i
        self.i = (self.i + 1) % len(self.sems)
        prev = self.count[i]
        self.count[i] += self.step
        return self.sems[i], self.count[i], prev


class MK:
    ENGS = ("t", "v", "s", "g", "y")

    def __init__(self, nc):
        self.nc = nc
        self.es = ExitStack()
        self.ops = []
        self.aliases = {}
        self.pools = {
            "t": _SemPool(nc, self.es, "pt_", 12, 1),
            "v": _SemPool(nc, self.es, "pv_", 16, 1),
            "s": _SemPool(nc, self.es, "ps_", 12, 1),
            "g": _SemPool(nc, self.es, "pg_", 6, 1),
            "y": _SemPool(nc, self.es, "py_", 4, 1),
            "gd": _SemPool(nc, self.es, "pgd_", 16, 16),
            "yd": _SemPool(nc, self.es, "pyd_", 12, 16),
        }

    def alias(self, name, shape, dtype, backing):
        v = self.nc.virtual_sbuf_tensor(name, list(shape), dtype)
        self.nc.back_virtual_tensor_with(v, backing)
        self.aliases[name] = backing.name
        return v

    def op(self, eng, method, *args, wait=(), tick=False, **kw):
        r = _Rec(len(self.ops), eng, method, args, kw,
                 tuple(t for t in wait if t is not None), False)
        self.ops.append(r)
        return r

    def dma(self, eng, out, in_, wait=()):
        r = _Rec(len(self.ops), eng, "dma_start", (), {"out": out, "in_": in_},
                 tuple(t for t in wait if t is not None), True)
        self.ops.append(r)
        return r

    def tick(self, eng, wait=()):
        return self.op(eng, "nop", wait=wait)

    def mm(self, out, lhsT, rhs, start, stop, wait=(), tick=False, **kw):
        return self.op("t", "matmul", out, lhsT, rhs, start=start, stop=stop, wait=wait, **kw)

    PSUM_NAMES = ("PSA", "PSB", "PST")

    def _rect(self, ap):
        t = ap.tensor
        name = self.aliases.get(t.name, t.name)
        sz = DTSZ[t.dtype]
        dims = list(ap.ap)
        off = ap.offset
        if len(t.shape) >= 2 and t.shape[0] <= 128 and not name.startswith("DR_"):
            pcols = t.shape[1] if len(t.shape) == 2 else int(np.prod(t.shape[1:]))
            p0 = off // pcols
            c0 = off % pcols
            p1 = p0 + (dims[0][1] if dims else 1)
            cspan = sum((cnt - 1) * st for st, cnt in dims[1:]) + 1 if len(dims) > 1 else 1
            b0, b1 = c0 * sz, (c0 + cspan) * sz
            if name in self.PSUM_NAMES:
                p0, p1 = 0, 128
                b0 = (b0 // 2048) * 2048
                b1 = ((b1 + 2047) // 2048) * 2048
            return (name, p0, p1, b0, b1)
        span = sum((cnt - 1) * st for st, cnt in dims) + 1
        return (name, 0, 1, off * sz, (off + span) * sz)

    def _is_ap(self, x):
        return hasattr(x, "tensor") and hasattr(x, "offset")

    def _io(self, r):
        writes, reads = [], []
        outs = []
        if "out" in r.kw:
            outs.append(r.kw["out"])
        if "accum_out" in r.kw and self._is_ap(r.kw.get("accum_out")):
            outs.append(r.kw["accum_out"])
        args = list(r.args)
        if args and self._is_ap(args[0]):
            outs.append(args[0])
            args = args[1:]
        for a in args:
            if self._is_ap(a):
                reads.append(a)
        for k, vv in r.kw.items():
            if k in ("out", "accum_out"):
                continue
            if k == "outs" and isinstance(vv, list):
                outs.extend(x for x in vv if self._is_ap(x))
                continue
            if isinstance(vv, list):
                reads.extend(x for x in vv if self._is_ap(x))
                continue
            if self._is_ap(vv):
                reads.append(vv)
        return [self._rect(a) for a in reads], [self._rect(a) for a in outs]

    @staticmethod
    def _buckets(rect):
        name, p0, p1, b0, b1 = rect
        for pb in range(p0 // 32, (p1 + 31) // 32):
            for cb in range(b0 // 512, (b1 + 511) // 512):
                yield (name, pb, cb)

    @staticmethod
    def _overlap(a, b):
        return a[0] == b[0] and a[1] < b[2] and b[1] < a[2] and a[3] < b[4] and b[3] < a[4]

    def flush(self, verbose=False):
        last_w = {}
        readers = {}
        eidx = {e: 0 for e in self.ENGS}
        for r in self.ops:
            r.eidx = eidx[r.eng]
            eidx[r.eng] += 1
            reads, writes = self._io(r)
            r.reads, r.writes = reads, writes
            edges = {}
            for rect in reads:
                short = (rect[1], rect[3], rect[2], rect[4])
                for bk in self._buckets(rect):
                    w = last_w.get(bk)
                    if w is not None and w[1] is not r:
                        edges[w[1].idx] = w[1]
            for rect in writes:
                for bk in self._buckets(rect):
                    w = last_w.get(bk)
                    if w is not None and w[1] is not r:
                        edges[w[1].idx] = w[1]
                    for rd in readers.get(bk, ()):
                        if rd is not r:
                            edges[rd.idx] = rd
            for rect in reads:
                for bk in self._buckets(rect):
                    readers.setdefault(bk, []).append(r)
            for rect in writes:
                for bk in self._buckets(rect):
                    last_w[bk] = (rect, r)
                    readers[bk] = []
            r.edges = [self.ops[i] for i in sorted(edges)]

        prev_on_engine = {e: None for e in self.ENGS}
        for r in self.ops:
            ck = {}
            pred = prev_on_engine[r.eng]
            if pred is not None:
                ck.update(pred.clock)
                if r.eng == "t":
                    ck["T"] = max(ck.get("T", -1), pred.eidx)
            needed = []
            for p in r.edges + [w if isinstance(w, _Rec) else None for w in r.mwaits]:
                if p is None:
                    continue
                if p.eng == "t" and r.eng == "t" and not p.is_dma:
                    if ck.get("T", -1) < p.eidx:
                        ck["T"] = max(ck.get("T", -1), p.eidx)
                        for k, vv in p.clock.items():
                            if ck.get(k, -1 if k == "T" else 0) < vv:
                                ck[k] = vv
                    continue
                needed.append(p)
            for p in needed:
                p.ticked = True
            r.clock = ck
            r._needed = needed
            prev_on_engine[r.eng] = r

        for r in self.ops:
            if r.ticked or r.is_dma:
                pool = self.pools[r.eng + "d"] if r.is_dma else self.pools[r.eng]
                r.sem, r.val, r._fence = pool.take()
        prev_on_engine = {e: None for e in self.ENGS}
        nwaits = 0
        for r in self.ops:
            ck = {}
            pred = prev_on_engine[r.eng]
            if pred is not None:
                ck.update(pred.clock)
                if r.eng == "t":
                    ck["T"] = max(ck.get("T", -1), pred.eidx)
            waits = []
            for p in r._needed:
                if p.eng == "t" and r.eng == "t" and not p.is_dma:
                    continue
                if p.sem is None:
                    raise RuntimeError(f"producer without sem: {p.method}")
                key = id(p.sem)
                if ck.get(key, 0) >= p.val:
                    continue
                waits.append((p.sem, p.val))
                ck[key] = p.val
                for k, vv in p.clock.items():
                    if k == "T":
                        ck["T"] = max(ck.get("T", -1), vv)
                    elif ck.get(k, 0) < vv:
                        ck[k] = vv
                if p.eng == "t" and not p.is_dma:
                    ck["T"] = max(ck.get("T", -1), p.eidx)
            r._waits = waits
            nwaits += len(waits)
            r.clock = ck
            prev_on_engine[r.eng] = r
        if verbose:
            nt = sum(1 for r in self.ops if r.ticked)
            print(f"[mk] ops={len(self.ops)} ticked={nt} waits={nwaits}")

        eng_ops = {e: [r for r in self.ops if r.eng == e] for e in self.ENGS}

        with self.nc.Block() as block:
            def mk_body(name):
                def body(eng):
                    seen = {}
                    for r in eng_ops[name]:
                        if r.sem is not None and r._fence > 0 and seen.get(id(r.sem), 0) < r._fence:
                            eng.wait_ge(r.sem, r._fence)
                            seen[id(r.sem)] = r._fence
                        for (sem, val) in r._waits:
                            if seen.get(id(sem), 0) < val:
                                eng.wait_ge(sem, val)
                                seen[id(sem)] = val
                        ins = getattr(eng, r.method)(*r.args, **r.kw)
                        if r.sem is not None:
                            ins.then_inc(r.sem, 16 if r.is_dma else 1)
                return body
            block.tensor(mk_body("t"))
            block.vector(mk_body("v"))
            block.scalar(mk_body("s"))
            block.gpsimd(mk_body("g"))
            block.sync(mk_body("y"))


def host_constants(S):
    c = {"c_ident": np.eye(128, dtype=np.float32).astype(ml_dtypes.bfloat16),
         "c_identF": np.eye(128, dtype=np.float32)}
    inv_freq = 1.0 / 10000.0 ** (np.arange(0, 64, 2, dtype=np.float32) / 64.0)
    t = np.arange(S, dtype=np.float32)
    freqs = inv_freq[:, None] * t[None, :]
    emb = np.concatenate([freqs, freqs], 0)
    cos = np.cos(emb)
    sin = np.sin(emb)
    c["c_cos"] = np.concatenate([cos, cos], 0).astype(ml_dtypes.bfloat16)
    c["c_sin"] = np.concatenate([sin, sin], 0).astype(ml_dtypes.bfloat16)
    perm = np.zeros((128, 128), dtype=np.float32)
    for blk in (0, 64):
        for d_ in range(32):
            perm[blk + d_, blk + d_ + 32] = -1.0
            perm[blk + 32 + d_, blk + d_] = 1.0
    c["c_permT"] = perm.T.copy().astype(ml_dtypes.bfloat16)
    msk = np.zeros((128, 8 * 512), dtype=ml_dtypes.bfloat16)
    for ci in range(8):
        kc = ci if ci < 4 else ci + 4
        for p in range(128):
            kj = kc * 128 + p
            qi = np.arange(512)
            msk[p, ci * 512:(ci + 1) * 512] = ((kj >= qi) & (kj <= qi + 1024)).astype(np.float32)
    c["c_l1mask"] = msk
    return c


def wimg(wT):
    Din, O = wT.shape
    return np.ascontiguousarray(wT.reshape(Din // 128, 128, O).transpose(1, 0, 2)
                                .reshape(128, (Din // 128) * O))


def prep_l0a(inputs):
    d = {}
    d["l0a_wq"] = wimg(np.ascontiguousarray(inputs["L0_attn_w_q"].T))
    d["l0a_wk"] = wimg(np.ascontiguousarray(inputs["L0_attn_w_k"].T))
    d["l0a_wv"] = wimg(np.ascontiguousarray(inputs["L0_attn_w_v"].T))
    r32 = np.zeros((1, 1360), dtype=np.float32)
    r32[0, 0:129] = inputs["L0_attn_b_q"]
    r32[0, 160:289] = inputs["L0_attn_b_k"]
    r32[0, 320:1344] = inputs["L0_attn_b_v"]
    d["l0a_rows32"] = r32
    d["l0a_k0x"] = np.ascontiguousarray(inputs["L0_attn_k_0"][128:129][None, :])
    d["l0a_rows64"] = np.ascontiguousarray(inputs["L0_attn_v_0"][None, :])
    d["l0a_k0m"] = np.ascontiguousarray(inputs["L0_attn_k_0"][0:128, None])
    return d




def prep_l0f(inputs):
    d = {}
    d["l0f_wqkv"] = qkv_group_img(np.ascontiguousarray(inputs["L0_ffn_qkv_w"].T))
    d["l0f_wout"] = wimg(np.ascontiguousarray(inputs["L0_ffn_out_w"].T))
    rows = np.zeros((1, 4096), dtype=np.float32)
    rows[0, 0:3072] = inputs["L0_ffn_qkv_b"]
    rows[0, 3072:4096] = inputs["L0_ffn_out_b"]
    d["l0f_rows"] = rows
    d["l0f_n1"] = np.ascontiguousarray(inputs["L0_ffn_norm1_w"].reshape(8, 128).T)
    d["l0f_n2"] = np.ascontiguousarray(inputs["L0_ffn_norm2_w"].reshape(8, 128).T)
    d["l0f_gw"] = wimg(np.ascontiguousarray(inputs["L0_ffn_gate_w"].T))
    w1 = inputs["L0_ffn_w1"]; w2 = inputs["L0_ffn_w2"]; w3 = inputs["L0_ffn_w3"]
    w1r = w1.reshape(8, 16, 128, 8, 128).transpose(0, 4, 1, 3, 2)
    w1r = w1.reshape(8, 16, 128, 8, 128).transpose(0, 4, 1, 3, 2)
    w2r = w2.reshape(8, 16, 128, 8, 128).transpose(0, 4, 1, 3, 2)
    img = np.stack([w1r, w2r], axis=4)
    d["l0f_w12"] = np.ascontiguousarray(img.reshape(8, 128, 16 * 2048))
    d["l0f_w3"] = np.ascontiguousarray(w3.reshape(8, 1024, 16, 128).transpose(0, 3, 2, 1))
    d["l0f_b1"] = np.ascontiguousarray(inputs["L0_ffn_b1"].reshape(8, 16, 128).transpose(0, 2, 1))
    d["l0f_b2"] = np.ascontiguousarray(inputs["L0_ffn_b2"])
    d["l0f_b3"] = np.ascontiguousarray(inputs["L0_ffn_b3"])
    return d

def qkv_group_img(qkvT):
    img = np.empty((128, 8 * 3072), dtype=np.float32)
    for g in range(8):
        for dc in range(8):
            blk = img[:, g * 3072 + dc * 384:g * 3072 + (dc + 1) * 384]
            rows = slice(dc * 128, (dc + 1) * 128)
            for which in range(3):
                blk[:, which * 128:(which + 1) * 128] = \
                    qkvT[rows, which * 1024 + 128 * g:which * 1024 + 128 * (g + 1)]
    return img


def prep_l1a(inputs):
    d = {}
    qkvT = np.concatenate([np.ascontiguousarray(inputs["L1_attn_wq"].T),
                           np.ascontiguousarray(inputs["L1_attn_wk"].T),
                           np.ascontiguousarray(inputs["L1_attn_wv"].T)], axis=1)
    d["l1a_wqkv"] = qkv_group_img(qkvT)
    d["l1a_wout"] = wimg(np.ascontiguousarray(inputs["L1_attn_wo"].T))
    rows = np.zeros((1, 4096), dtype=np.float32)
    rows[0, 0:1024] = inputs["L1_attn_bq"]
    rows[0, 1024:2048] = inputs["L1_attn_bk"]
    rows[0, 2048:3072] = inputs["L1_attn_bv"]
    rows[0, 3072:4096] = inputs["L1_attn_bo"]
    d["l1a_rows"] = rows
    return d


def prep_l1f(inputs):
    d = {}
    phi = inputs["L1_ffn_phi"].reshape(1024, 2048)
    d["l1f_phi"] = wimg(np.ascontiguousarray(phi))
    W = inputs["L1_ffn_W"]
    d["l1f_W"] = np.ascontiguousarray(
        W.reshape(64, 8, 128, 1024).transpose(0, 2, 1, 3).reshape(64, 128, 8192))
    d["l1f_brep"] = np.ascontiguousarray(np.repeat(inputs["L1_ffn_b"], 32, axis=0))
    return d


def build_program(S=2048, parts=("l0a",), n_cores=8):
    nc = bass.Bass("TRN2", target_bir_lowering=False, debug=False)
    mk = MK(nc)
    NT = S // 128
    NQ = S // 512

    ap = {}

    def inp(name, shape, dt=F32):
        ap[name] = nc.dram_tensor(name, list(shape), dt, kind="ExternalInput").ap()
        return ap[name]

    xT = inp("xT", [D, S])
    inp("c_ident", [128, 128], BF16)
    inp("c_identF", [128, 128])
    if "l0a" in parts:
        inp("l0a_wq", [128, 8 * P129]); inp("l0a_wk", [128, 8 * P129])
        inp("l0a_wv", [128, 8 * 1024])
        inp("l0a_rows32", [1, 1360]); inp("l0a_rows64", [1, 1024]); inp("l0a_k0x", [1, 1])
        inp("l0a_k0m", [128, 1])
    if "l0fa" in parts:
        inp("l0f_wqkv", [128, 8 * 3072]); inp("l0f_wout", [128, 8 * 1024])
        inp("l0f_rows", [1, 4096]); inp("l0f_n1", [128, 8]); inp("l0f_n2", [128, 8])
        inp("c_cos", [128, S], BF16); inp("c_sin", [128, S], BF16)
        inp("c_permT", [128, 128], BF16)
    if "l0fm" in parts:
        inp("l0f_gw", [128, 64])
        inp("l0f_w12", [8, 128, 16 * 2048])
        inp("l0f_w3", [8, 128, 16, 1024])
        inp("l0f_b1", [8, 128, 16]); inp("l0f_b2", [8, 2048]); inp("l0f_b3", [8, 1024])
        if "l0fa" not in parts:
            inp("tx1", [D, S])
        if "l0fa" not in parts or True:
            pass
    if "l0fm" in parts and "l0fa" not in parts:
        inp("l0f_n2", [128, 8])
    if "l1a" in parts:
        inp("l1a_wqkv", [128, 8 * 3072]); inp("l1a_wout", [128, 8 * 1024])
        inp("l1a_rows", [1, 4096]); inp("c_l1mask", [128, 8 * 512], BF16)
    if "l1f" in parts:
        inp("l1f_phi", [128, 8 * 2048])
        inp("l1f_W", [64, 128, 8192])
        inp("l1f_brep", [2048, 1024])
    out_x = nc.dram_tensor("outX", [D, S], F32, kind="ExternalOutput").ap()
    if "l1f" in parts:
        hE = nc.dram_tensor("hE", [S, 2048], F32, kind="Internal")
        hS = nc.dram_tensor("hS", [S, 2048], F32, kind="Internal")
    hx1 = nc.dram_tensor("hx1", [D, S], F32, kind="Internal").ap()
    hcomb = nc.dram_tensor("hcomb", [8, S], F32, kind="Internal").ap()
    hattn = nc.dram_tensor("hattn", [D, S], BF16, kind="Internal").ap()

    A_X = nc.alloc_sbuf_tensor("A_X", [128, DC * S], F32)
    A_NB = nc.alloc_sbuf_tensor("A_NB", [128, max(DC * S, DC * 2048)], BF16)
    A_VT = nc.alloc_sbuf_tensor("A_VT", [128, max(DC * S, 2080 + 4 * S, 16384)], BF16)
    PTCOLS = max((S // 128 + 1) * 512 + (S + 128) + 2 * S, DC * S, DC * (S // 2) + 8192, 16384)
    A_PT = nc.alloc_sbuf_tensor("A_PT", [128, PTCOLS], BF16)
    A_WS = nc.alloc_sbuf_tensor("A_WS", [128, 8192], BF16)
    A_SC = nc.alloc_sbuf_tensor("A_SC", [128, 4608], F32)
    rowsB = nc.alloc_sbuf_tensor("rowsB", [128, 4100], BF16)
    onesB = nc.alloc_sbuf_tensor("onesB", [128, 512], BF16)
    onesF = nc.alloc_sbuf_tensor("onesF", [128, 128], F32)
    ident = nc.alloc_sbuf_tensor("ident", [128, 128], BF16)
    identG = nc.alloc_sbuf_tensor("identG", [128, 128], F32)
    maxcG = nc.alloc_sbuf_tensor("maxcG", [128, 1], F32)
    negm = nc.alloc_sbuf_tensor("negm", [128, 1], F32)
    k0m = nc.alloc_sbuf_tensor("k0m", [128, 1], BF16)

    tmpf = A_SC[:, 4096:4608]
    tmpg = nc.alloc_sbuf_tensor("tmpg", [128, 512], F32)
    PSA = nc.alloc_psum_tensor("PSA", [128, 2048], F32)
    PSB = nc.alloc_psum_tensor("PSB", [128, 1536], F32)
    PST = nc.alloc_psum_tensor("PST", [128, 1024], BF16)

    mk.op("v", "memset", onesB[:], 1.0)
    mk.op("v", "memset", onesF[:], 1.0)
    mk.dma("y", identG[:], ap["c_identF"])
    mk.dma("y", ident[:], ap["c_ident"])
    for dc in range(DC):
        mk.dma("y", A_X[:, dc * S:(dc + 1) * S], xT[dc * 128:(dc + 1) * 128, :])

    def rowpack(idx):
        r, b = divmod(idx, 3)
        return PSB[32 * b:32 * b + 1, r * 512:(r + 1) * 512]

    def X(dc, sl=None):
        base = dc * S
        return A_X[:, base:base + S] if sl is None else A_X[:, base + sl.start:base + sl.stop]

    def ln_sum_stats(src_chunks, want_mean=True):
        sq_tile = A_PT[:, 0:S]
        for dc in range(DC):
            mk.op("s", "activation", sq_tile, src_chunks[dc], AF.Square)
            for ts in range(NQ):
                sl = slice(ts * 512, (ts + 1) * 512)
                if want_mean:
                    mk.mm(rowpack(ts), onesB[:, 0:1], src_chunks[dc][:, sl],
                          start=(dc == 0), stop=(dc == DC - 1))
                mk.mm(rowpack(4 + ts), onesB[:, 0:1], sq_tile[:, sl],
                      start=(dc == 0), stop=(dc == DC - 1))

    def part_l0a():
        wq = A_WS[:, 0:8 * P129]
        wk = A_WS[:, 8 * P129:16 * P129]
        wvh = A_WS[:, 16 * P129:16 * P129 + 8 * 512]
        mk.dma("g", wq, ap["l0a_wq"])
        mk.dma("g", wk, ap["l0a_wk"])
        rows32 = rowsB[32:33, 0:1360]
        mk.dma("g", rows32, ap["l0a_rows32"])
        rows64 = rowsB[64:65, 0:1024]
        mk.dma("g", rows64, ap["l0a_rows64"])
        mk.dma("g", k0m[:], ap["l0a_k0m"])
        bq = rowsB[32:33, 0:129]; bk = rowsB[32:33, 160:289]
        bv = rowsB[32:33, 320:1344]
        k0x = rowsB[0:1, 4096:4097]
        mk.dma("g", k0x, ap["l0a_k0x"])
        v0r = rowsB[64:65, 0:1024]
        qxr = rowsB[0:1, 0:S]
        kxr = rowsB[0:1, 2048:2048 + S]
        ptx = rowsB[64:65, 1024:1024 + 512]

        pp0 = 0
        preP0 = pp0 + (NT + 1) * 512
        qf0 = preP0 + S + 128
        kf0 = qf0 + S
        assert kf0 + S <= PTCOLS
        preP = A_PT[:, preP0:preP0 + S + 128]
        qf = A_PT[:, qf0:qf0 + S]
        kf = A_PT[:, kf0:kf0 + S]

        def PTt(kc, a, b):
            return A_PT[:, pp0 + kc * 512 + a:pp0 + kc * 512 + b]

        xb = [A_NB[:, dc * S:(dc + 1) * S] for dc in range(DC)]
        for dc in range(DC):
            mk.op("v", "tensor_copy", xb[dc], X(dc))

        sq_tile = A_PT[:, pp0:pp0 + S]
        for dc in range(DC):
            mk.op("s", "activation", sq_tile, xb[dc], AF.Square)
            for ts in range(NQ):
                sl = slice(ts * 512, (ts + 1) * 512)
                mk.mm(rowpack(ts), onesB[:, 0:1], xb[dc][:, sl],
                      start=(dc == 0), stop=(dc == DC - 1))
                mk.mm(rowpack(4 + ts), onesB[:, 0:1], sq_tile[:, sl],
                      start=(dc == 0), stop=(dc == DC - 1))
        mu = A_SC[0:1, 0:S]; rstd = A_SC[0:1, 2048:2048 + S]; rtmp = A_SC[0:1, 4096:4608]
        for ts in range(NQ):
            sl = slice(ts * 512, (ts + 1) * 512)
            mk.op("v", "tensor_scalar_mul", mu[:, sl], rowpack(ts), 1.0 / D)
            mk.op("v", "tensor_scalar_mul", rstd[:, sl], rowpack(4 + ts), 1.0 / D)
        for ts in range(NQ):
            sl = slice(ts * 512, (ts + 1) * 512)
            mk.op("v", "tensor_mul", rtmp, mu[:, sl], mu[:, sl])
            mk.op("v", "tensor_sub", rstd[:, sl], rstd[:, sl], rtmp)
            mk.op("v", "tensor_scalar_add", rstd[:, sl], rstd[:, sl], 1e-5)
            mk.op("s", "activation", rtmp, rstd[:, sl], AF.Sqrt)
            mk.op("v", "reciprocal", rstd[:, sl], rtmp)
            mk.op("v", "tensor_mul", mu[:, sl], mu[:, sl], rstd[:, sl])

        for ts in range(NQ):
            sl = slice(ts * 512, (ts + 1) * 512)
            mk.mm(PSB[:, 0:512], onesF[0:1, :], rstd[:, sl], start=True, stop=True)
            mk.mm(PSB[:, 512:1024], onesF[0:1, :], mu[:, sl], start=True, stop=True)
            for dc in range(DC):
                mk.op("v", "tensor_mul", tmpf[:], X(dc, sl), PSB[:, 0:512])
                mk.op("v", "tensor_sub", xb[dc][:, sl], tmpf[:], PSB[:, 512:1024])

        for (wT, brow, brow_x, dstf, dstx) in (
                (wq, bq, rowsB[32:33, 128:129], qf, qxr),
                (wk, bk, rowsB[32:33, 288:289], kf, kxr)):
            for ts in range(NQ):
                sl = slice(ts * 512, (ts + 1) * 512)
                for dc in range(DC):
                    mk.mm(PSA[:, sl], wT[:, dc * P129:dc * P129 + 128], xb[dc][:, sl],
                          start=(dc == 0), stop=False)
                mk.mm(PSA[:, sl], brow[:, 0:128], onesB[32:33, :], start=False, stop=True)
                for dc in range(DC):
                    mk.mm(rowpack(ts), wT[:, dc * P129 + 128:dc * P129 + 129],
                          xb[dc][:, sl], start=(dc == 0), stop=False)
                mk.mm(rowpack(ts), brow_x, onesB[32:33, 0:512], start=False, stop=True)
            mk.op("v", "tensor_copy", dstf, PSA[:, 0:S])
            for ts in range(NQ):
                mk.op("v", "tensor_copy", dstx[:, ts * 512:(ts + 1) * 512], rowpack(ts))

        wv_img = ap["l0a_wv"]
        for oh in range(2):
            mk.dma("g", wvh, wv_img.rearrange("p (dc o) -> p dc o", o=1024)[:, :, oh * 512:(oh + 1) * 512])
            for tc in range(NT):
                tsl = slice(tc * 128, (tc + 1) * 128)
                osl = slice(oh * 512, (oh + 1) * 512)
                for dc in range(DC):
                    mk.mm(PSB[:, osl], xb[dc][:, tsl],
                          wvh[:, dc * 512:(dc + 1) * 512],
                          start=(dc == 0), stop=False)
                mk.mm(PSB[:, osl], onesB[32:33, 0:128], bv[:, oh * 512:(oh + 1) * 512],
                      start=False, stop=True)
                mk.op("v", "tensor_copy", A_VT[:, tc * 1024 + oh * 512:tc * 1024 + (oh + 1) * 512],
                      PSB[:, osl])

        for qb in range(NQ):
            qsl = slice(qb * 512, (qb + 1) * 512)
            for qt in range(4):
                qtile = slice(qb * 512 + qt * 128, qb * 512 + (qt + 1) * 128)
                for kt in range(NQ):
                    ksl = slice(kt * 512, (kt + 1) * 512)
                    mk.mm(PSA[:, ksl], qf[:, qtile], kf[:, ksl], start=True, stop=False)
                    mk.mm(PSA[:, ksl], qxr[:, qtile], kxr[:, ksl], start=False, stop=True)
                mk.mm(PSB[:, 1024:1025], qf[:, qtile], k0m[:, 0:1], start=True, stop=False)
                mk.mm(PSB[:, 1024:1025], qxr[:, qtile], k0x[:, 0:1], start=False, stop=True)
                mk.op("v", "reduce_max", negm[:, 0:1], PSA[:, 0:S], axis=AX.X)
                mk.op("v", "tensor_max", negm[:, 0:1], negm[:, 0:1], PSB[:, 1024:1025])
                mk.op("v", "tensor_scalar_mul", negm[:, 0:1], negm[:, 0:1], -1.0)
                mk.op("s", "activation", preP[:, 0:S], PSA[:, 0:S], AF.Exp,
                      bias=negm[:, 0:1])
                mk.op("s", "activation", preP[:, S:S + 1], PSB[:, 1024:1025], AF.Exp,
                      bias=negm[:, 0:1])
                for kc in range(NT):
                    slot = (kc % 2) * 128
                    mk.op("t", "transpose", PST[:, slot:slot + 128],
                          preP[:, kc * 128:(kc + 1) * 128], ident[:])
                    mk.op("v", "tensor_copy", PTt(kc, qt * 128, (qt + 1) * 128),
                          PST[:, slot:slot + 128])
                mk.op("t", "transpose", PST[0:1, 256:384], preP[:, S:S + 1], ident[:])
                mk.op("v", "tensor_copy", ptx[:, qt * 128:(qt + 1) * 128],
                      PST[0:1, 256:384])
            import os
            if os.environ.get("MK_STOP") == "evac":
                return
            for kc in range(NT):
                mk.mm(PSB[0:1, 0:512], onesB[:, 0:1], PTt(kc, 0, 512),
                      start=(kc == 0), stop=False)
            mk.mm(PSB[0:1, 0:512], onesB[64:65, 0:1], ptx[:, 0:512], start=False, stop=True)
            sums_r = A_SC[32:33, 0:512]
            mk.op("v", "tensor_copy", sums_r, PSB[0:1, 0:512])
            mk.op("v", "reciprocal", sums_r, sums_r)
            mk.mm(PSB[:, 0:512], onesF[32:33, :], sums_r, start=True, stop=True)
            mk.op("v", "tensor_copy", tmpg[:], PSB[:, 0:512])
            mk.mm(PSB[:, 512:1024], onesF[0:1, :], rstd[:, qsl], start=True, stop=True)
            mk.mm(PSB[:, 1024:1536], onesF[0:1, :], mu[:, qsl], start=True, stop=True)
            if os.environ.get("MK_STOP") == "sums":
                return
            for half in range(2):
                for dci in range(4):
                    dc = half * 4 + dci
                    csl = slice(dci * 512, (dci + 1) * 512)
                    for kc in range(NT):
                        mk.mm(PSA[:, csl],
                              A_VT[:, kc * 1024 + dc * 128:kc * 1024 + (dc + 1) * 128],
                              PTt(kc, 0, 512),
                              start=(kc == 0), stop=False)
                    mk.mm(PSA[:, csl], v0r[:, dc * 128:(dc + 1) * 128], ptx[:, 0:512],
                          start=False, stop=True)
                for dci in range(4):
                    dc = half * 4 + dci
                    csl = slice(dci * 512, (dci + 1) * 512)
                    mk.op("v", "tensor_mul", tmpf[:], X(dc, qsl), PSB[:, 512:1024])
                    mk.op("v", "tensor_sub", tmpf[:], tmpf[:], PSB[:, 1024:1536])
                    mk.op("v", "tensor_add", X(dc, qsl), X(dc, qsl), tmpf[:])
                    mk.op("v", "tensor_mul", tmpf[:], PSA[:, csl], tmpg[:])
                    mk.op("v", "tensor_add", X(dc, qsl), X(dc, qsl), tmpf[:])


    def part_l0fa():
        NG = 8
        mk.dma("g", rowsB[32:33, 0:4096], ap["l0f_rows"])
        qkvb = rowsB[32:33, 0:3072]
        outb = rowsB[32:33, 3072:4096]
        n1sb = nc.alloc_sbuf_tensor("n1sb", [128, 8], BF16)
        n2sb = nc.alloc_sbuf_tensor("n2sb", [128, 8], BF16)
        mk.dma("g", n1sb[:], ap["l0f_n1"])
        mk.dma("g", n2sb[:], ap["l0f_n2"])
        ap["__n2sb__"] = n2sb
        vt_q = 0
        vt_k = 2 * S
        vt_v0 = 4 * S
        vt_cos = 4 * S + 2080
        vt_sin = 5 * S + 2080
        cosT = A_VT[:, vt_cos:vt_cos + S]
        sinT = A_VT[:, vt_sin:vt_sin + S]
        mk.dma("y", cosT, ap["c_cos"])
        mk.dma("y", sinT, ap["c_sin"])
        perm = nc.alloc_sbuf_tensor("perm", [128, 128], BF16)
        mk.dma("y", perm[:], ap["c_permT"])
        identF2 = identG

        def qh(h):
            return A_VT[0:65, vt_q + h * S:vt_q + (h + 1) * S]

        def kh(h):
            return A_VT[0:65, vt_k + h * S:vt_k + (h + 1) * S]
        bbuf = A_PT[:, 0:S]
        proll0 = S
        attst0 = S + 2048
        attn_st = A_PT[:, attst0:attst0 + S]

        for dc in range(DC):
            mk.dma("y", hx1[dc * 128:(dc + 1) * 128, :], X(dc))

        x1b = [A_NB[:, dc * S:(dc + 1) * S] for dc in range(DC)]
        for dc in range(DC):
            mk.op("v", "tensor_copy", x1b[dc], X(dc))
        ln_sum_stats(x1b)
        mu = A_SC[0:1, 0:S]
        rstd = A_SC[0:1, 2048:2048 + S]
        rtmp = A_SC[0:1, 4096:4608]
        rmss = A_SC[32:33, 512:512 + S]
        for ts in range(NQ):
            sl = slice(ts * 512, (ts + 1) * 512)
            mk.op("v", "tensor_scalar_mul", mu[:, sl], rowpack(ts), 1.0 / D)
            mk.op("v", "tensor_scalar_mul", rstd[:, sl], rowpack(4 + ts), 1.0 / D)
            mk.op("v", "tensor_mul", rtmp, mu[:, sl], mu[:, sl])
            mk.op("v", "tensor_sub", rstd[:, sl], rstd[:, sl], rtmp)
            mk.op("v", "tensor_scalar_add", rstd[:, sl], rstd[:, sl], 1e-5)
            mk.op("s", "activation", rtmp, rstd[:, sl], AF.Sqrt)
            mk.op("v", "reciprocal", rstd[:, sl], rtmp)
            mk.op("v", "tensor_mul", mu[:, sl], mu[:, sl], rstd[:, sl])
            mk.op("v", "tensor_mul", rtmp, rstd[:, sl], rstd[:, sl])
            mk.op("v", "tensor_scalar", rtmp, rtmp, -1e-5, 1.0 + 1e-6,
                  op0=ALU.mult, op1=ALU.add)
            mk.op("s", "activation", rtmp, rtmp, AF.Sqrt)
            mk.op("v", "reciprocal", rmss[:, sl], rtmp)

        for ts in range(NQ):
            sl = slice(ts * 512, (ts + 1) * 512)
            mk.mm(PSB[:, 0:512], onesF[0:1, :], rstd[:, sl], start=True, stop=True)
            mk.mm(PSB[:, 512:1024], onesF[0:1, :], mu[:, sl], start=True, stop=True)
            mk.mm(PSB[:, 1024:1536], onesF[32:33, :], rmss[:, sl], start=True, stop=True)
            for dc in range(DC):
                mk.op("v", "tensor_mul", tmpf, X(dc, sl), PSB[:, 0:512])
                mk.op("v", "tensor_sub", X(dc, sl), tmpf, PSB[:, 512:1024])
                mk.op("v", "scalar_tensor_tensor", x1b[dc][:, sl], X(dc, sl),
                      n1sb[:, dc:dc + 1], PSB[:, 1024:1536],
                      op0=ALU.mult, op1=ALU.mult)
        a = x1b

        for g in range(NG):
            ws0 = (g % 2) * 3072
            wg = A_WS[:, ws0:ws0 + 3072]
            mk.dma("g", wg, ap["l0f_wqkv"][:, g * 3072:(g + 1) * 3072])
            for which in range(2):
                for ts in range(NQ):
                    sl = slice(ts * 512, (ts + 1) * 512)
                    for dc in range(DC):
                        mk.mm(PSA[:, sl], wg[:, dc * 384 + which * 128:dc * 384 + which * 128 + 128],
                              a[dc][:, sl], start=(dc == 0), stop=False)
                    mk.mm(PSA[:, sl], qkvb[:, which * 1024 + g * 128:which * 1024 + (g + 1) * 128],
                          onesB[32:33, :], start=False, stop=True)
                for ts in range(NQ):
                    sl = slice(ts * 512, (ts + 1) * 512)
                    mk.op("v", "tensor_mul", bbuf[:, sl], PSA[:, sl], sinT[:, sl])
                    mk.op("v", "tensor_mul", tmpf, PSA[:, sl], cosT[:, sl])
                    mk.mm(PSB[:, 0:512], perm[:], bbuf[:, sl], start=True, stop=True)
                    for h in range(2):
                        dst = qh(h) if which == 0 else kh(h)
                        mk.op("v", "tensor_add", dst[0:64, sl.start:sl.stop],
                              tmpf[h * 64:(h + 1) * 64, :],
                              PSB[h * 64:(h + 1) * 64, 0:512])
            for h in range(2):
                mk.op("v", "memset", kh(h)[64:65, 0:S], -1.0)
            mk.op("v", "memset", A_VT[:, vt_v0:vt_v0 + 2080], 1.0)
            for tc in range(NT):
                for dc in range(DC):
                    mk.mm(PSB[:, 512:640], a[dc][:, tc * 128:(tc + 1) * 128],
                          wg[:, dc * 384 + 256:dc * 384 + 384],
                          start=(dc == 0), stop=False)
                mk.mm(PSB[:, 512:640], onesB[32:33, 0:128],
                      qkvb[:, 2048 + g * 128:2048 + (g + 1) * 128],
                      start=False, stop=True)
                mk.op("v", "tensor_copy", A_VT[:, vt_v0 + tc * 130:vt_v0 + tc * 130 + 64],
                      PSB[:, 512:576])
                mk.op("v", "tensor_copy", A_VT[:, vt_v0 + tc * 130 + 65:vt_v0 + tc * 130 + 129],
                      PSB[:, 576:640])
            maxc = maxcG
            for h in range(2):
                for qt in range(NT):
                    qtl = slice(qt * 128, (qt + 1) * 128)
                    for kt in range(NQ):
                        mk.mm(PSA[:, kt * 512:(kt + 1) * 512], qh(h)[0:64, qtl],
                              kh(h)[0:64, kt * 512:(kt + 1) * 512], start=True, stop=True)
                    mk.op("v", "reduce_max", maxc[:, 0:1], PSA[:, 0:S], axis=AX.X)
                    mk.op("t", "transpose", PSB[0:1, 512:640], maxc[:, 0:1], identF2[:])
                    mk.op("v", "tensor_copy", qh(h)[64:65, qt * 128:(qt + 1) * 128],
                          PSB[0:1, 512:640])
            for h in range(2):
                for qt in range(NQ):
                    qsl = slice(qt * 512, (qt + 1) * 512)
                    pvslot = PSB[0:65, ((h * NQ + qt) % 2) * 512:((h * NQ + qt) % 2) * 512 + 512]
                    for kc2 in range(0, NT, 2):
                        slot = (kc2 // 2) % 2
                        pa = PSA[:, slot * 1024:slot * 1024 + 1024]
                        for j in range(2):
                            kc = kc2 + j
                            mk.mm(pa[:, j * 512:(j + 1) * 512],
                                  kh(h)[:, kc * 128:(kc + 1) * 128],
                                  qh(h)[:, qsl], start=True, stop=True)
                        roll = A_PT[:, proll0 + slot * 1024:proll0 + slot * 1024 + 1024]
                        mk.op("s", "activation", roll, pa, AF.Exp, scale=0.125)
                        for j in range(2):
                            kc = kc2 + j
                            mk.mm(pvslot,
                                  A_VT[:, vt_v0 + kc * 130 + h * 65:vt_v0 + kc * 130 + (h + 1) * 65],
                                  roll[:, j * 512:(j + 1) * 512],
                                  start=(kc == 0), stop=(kc == NT - 1))
                    mk.op("v", "tensor_copy", attn_st[h * 64:(h + 1) * 64, qsl],
                          pvslot[0:64, :])
                    mk.op("v", "tensor_copy",
                          A_SC[32:33, (h * NQ + qt) * 512:(h * NQ + qt) * 512 + 512],
                          pvslot[64:65, :])
            mk.op("v", "reciprocal", A_SC[32:33, 0:2 * NQ * 512],
                  A_SC[32:33, 0:2 * NQ * 512])
            for h in range(2):
                for qt in range(NQ):
                    qsl = slice(qt * 512, (qt + 1) * 512)
                    mk.mm(PSB[:, 1024:1536], onesF[32:33, :],
                          A_SC[32:33, (h * NQ + qt) * 512:(h * NQ + qt) * 512 + 512],
                          start=True, stop=True)
                    mk.op("v", "tensor_mul", attn_st[h * 64:(h + 1) * 64, qsl],
                          attn_st[h * 64:(h + 1) * 64, qsl], PSB[0:64, 1024:1536])
            mk.dma("y", hattn[g * 128:(g + 1) * 128, :], attn_st)

        wo = A_WS[:, 0:8192]
        mk.dma("g", wo, ap["l0f_wout"])
        for dc in range(DC):
            mk.dma("y", A_PT[:, dc * S:(dc + 1) * S], hattn[dc * 128:(dc + 1) * 128, :])
        for oc in range(DC):
            for ts in range(NQ):
                sl = slice(ts * 512, (ts + 1) * 512)
                for dc in range(DC):
                    mk.mm(PSA[:, sl], wo[:, dc * 1024 + oc * 128:dc * 1024 + (oc + 1) * 128],
                          A_PT[:, dc * S + ts * 512:dc * S + (ts + 1) * 512],
                          start=(dc == 0), stop=False)
                mk.mm(PSA[:, sl], outb[:, oc * 128:(oc + 1) * 128], onesB[32:33, :],
                      start=False, stop=True)
                mk.op("v", "tensor_add", X(oc, sl), X(oc, sl), PSA[:, sl])


    def part_l0fm(have_l0fa):
        NE, NH = 8, 16
        if not have_l0fa:
            n2sb = nc.alloc_sbuf_tensor("n2sb_m", [128, 8], BF16)
            mk.dma("g", n2sb[:], ap["l0f_n2"])
            for dc in range(DC):
                mk.dma("g", hx1[dc * 128:(dc + 1) * 128, :], ap["tx1"][dc * 128:(dc + 1) * 128, :])
        else:
            n2sb = ap["__n2sb__"]
        gwsb = nc.alloc_sbuf_tensor("gwsb", [128, 64], F32)
        mk.dma("g", gwsb[:], ap["l0f_gw"])
        n2f = nc.alloc_sbuf_tensor("n2f", [128, 8], F32)
        mk.dma("g", n2f[:], ap["l0f_n2"])
        rmsscol = nc.alloc_sbuf_tensor("rmsscol", [128, 16], F32)
        glt = nc.alloc_sbuf_tensor("glt", [128, 8], F32)
        v1 = nc.alloc_sbuf_tensor("v1", [128, 1], F32)
        v2 = nc.alloc_sbuf_tensor("v2", [128, 1], F32)
        sg = nc.alloc_sbuf_tensor("sg", [128, 1], F32)
        og = nc.alloc_sbuf_tensor("og", [128, 1], F32)
        tmp8 = nc.alloc_sbuf_tensor("tmp8", [128, 8], F32)
        mask1 = nc.alloc_sbuf_tensor("mask1", [128, 8], F32)
        combtile = nc.alloc_sbuf_tensor("combtile", [128, 8], F32)
        stage8 = nc.alloc_sbuf_tensor("stage8", [8, 128], F32)
        identF = identG
        b1sb = nc.alloc_sbuf_tensor("b1sb", [128, 16], F32)

        m = [A_NB[:, dc * S:(dc + 1) * S] for dc in range(DC)]

        for dc in range(DC):
            mk.op("v", "tensor_copy", m[dc], X(dc))
        ln_sum_stats(m, want_mean=False)
        rmss = A_SC[0:1, 0:S]
        rtmp = A_SC[0:1, 4096:4608]
        for ts in range(NQ):
            sl = slice(ts * 512, (ts + 1) * 512)
            mk.op("v", "tensor_scalar", rtmp, rowpack(4 + ts), 1.0 / D, 1e-6,
                  op0=ALU.mult, op1=ALU.add)
            mk.op("s", "activation", rtmp, rtmp, AF.Sqrt)
            mk.op("v", "reciprocal", rmss[:, sl], rtmp)
        for ts in range(NQ):
            sl = slice(ts * 512, (ts + 1) * 512)
            mk.mm(PSB[:, 0:512], onesF[0:1, :], rmss[:, sl], start=True, stop=True)
            for dc in range(DC):
                mk.op("v", "scalar_tensor_tensor", m[dc][:, sl], X(dc, sl),
                      n2sb[:, dc:dc + 1], PSB[:, 0:512], op0=ALU.mult, op1=ALU.mult)

        for dc in range(DC):
            mk.op("v", "tensor_scalar", gwsb[:, dc * 8:(dc + 1) * 8],
                  gwsb[:, dc * 8:(dc + 1) * 8], n2f[:, dc:dc + 1], None, op0=ALU.mult)
        for tt in range(NT):
            mk.op("t", "transpose", PSB[:, 1032:1033],
                  rmss[:, tt * 128:(tt + 1) * 128], identF[0:1, 0:1])
            mk.op("v", "tensor_copy", rmsscol[:, tt % 16:tt % 16 + 1], PSB[:, 1032:1033])
            tsl = slice(tt * 128, (tt + 1) * 128)
            for dc in range(DC):
                mk.mm(PSB[:, 1024:1032], X(dc)[:, tsl], gwsb[:, dc * 8:(dc + 1) * 8],
                      start=(dc == 0), stop=(dc == DC - 1))
            mk.op("v", "tensor_scalar", glt[:], PSB[:, 1024:1032],
                  rmsscol[:, tt % 16:tt % 16 + 1], None, op0=ALU.mult)
            mk.op("v", "reduce_max", v1[:, 0:1], glt[:], axis=AX.X)
            mk.op("v", "tensor_scalar", mask1[:], glt[:], v1[:, 0:1], None, op0=ALU.is_equal)
            mk.op("v", "scalar_tensor_tensor", tmp8[:], mask1[:], -1e30, glt[:],
                  op0=ALU.mult, op1=ALU.add)
            mk.op("v", "reduce_max", v2[:, 0:1], tmp8[:], axis=AX.X)
            mk.op("v", "tensor_sub", sg[:, 0:1], v2[:, 0:1], v1[:, 0:1])
            mk.op("s", "activation", sg[:, 0:1], sg[:, 0:1], AF.Exp)
            mk.op("v", "tensor_scalar_add", sg[:, 0:1], sg[:, 0:1], 1.0)
            mk.op("v", "reciprocal", sg[:, 0:1], sg[:, 0:1])
            mk.op("v", "tensor_scalar", og[:, 0:1], sg[:, 0:1], -1.0, 1.0,
                  op0=ALU.mult, op1=ALU.add)
            mk.op("v", "tensor_scalar", mask1[:], mask1[:], sg[:, 0:1], None, op0=ALU.mult)
            mk.op("v", "tensor_scalar", tmp8[:], tmp8[:], v2[:, 0:1], None, op0=ALU.is_equal)
            mk.op("v", "tensor_scalar", tmp8[:], tmp8[:], og[:, 0:1], None, op0=ALU.mult)
            mk.op("v", "tensor_add", combtile[:], mask1[:], tmp8[:])
            mk.op("t", "transpose", PSB[0:8, 0:128], combtile[:], identF[:])
            mk.op("v", "tensor_copy", stage8[:], PSB[0:8, 0:128])
            mk.dma("y", hcomb[:, tt * 128:(tt + 1) * 128], stage8[:])

        for dc in range(DC):
            for ts in range(NQ):
                sl = slice(ts * 512, (ts + 1) * 512)
                mk.dma("y", tmpg[:], hx1[dc * 128:(dc + 1) * 128, ts * 512:(ts + 1) * 512])
                mk.op("v", "tensor_add", X(dc, sl), X(dc, sl), tmpg[:])

        combrow = A_SC[0:1, 2048:2048 + S]
        combrow32 = A_SC[32:33, 0:S]
        brow = rowsB[0:1, 2048:4096]
        TH = S // 2
        NTH = max(1, TH // 512)
        THW = min(512, TH)

        me0 = 0
        w12s = (DC * TH, DC * TH + 2048)
        w3s = (DC * TH + 4096, DC * TH)
        for e in range(NE):
            mk.dma("g", combrow, hcomb[e:e + 1, :])
            mk.dma("g", combrow32, hcomb[e:e + 1, :])
            mk.dma("g", b1sb[:], ap["l0f_b1"][e])
            for tbh in range(2):
                mk.dma("g", brow, ap["l0f_b2"][e:e + 1, :])
                for hc in range(NH):
                    wsl = w12s[hc % 2]
                    w12 = A_WS[:, 0:1] if False else A_PT[:, wsl:wsl + 2048]
                    mk.dma("g", w12, ap["l0f_w12"][e, :, hc * 2048:(hc + 1) * 2048])
                    pa = PSA[:, (hc % 2) * TH:(hc % 2) * TH + TH]
                    pb = PSB[:, 0:TH]
                    for th in range(NTH):
                        sl = slice(tbh * TH + th * THW, tbh * TH + (th + 1) * THW)
                        psl = slice(th * THW, (th + 1) * THW)
                        for dc in range(DC):
                            mk.mm(pa[:, psl], w12[:, dc * 256:dc * 256 + 128],
                                  m[dc][:, sl], start=(dc == 0), stop=(dc == DC - 1))
                        for dc in range(DC):
                            mk.mm(pb[:, psl], w12[:, dc * 256 + 128:dc * 256 + 256],
                                  m[dc][:, sl], start=(dc == 0), stop=False)
                        mk.mm(pb[:, psl], brow[:, hc * 128:(hc + 1) * 128],
                              onesB[0:1, 0:THW], start=False, stop=True)
                    hrow = A_VT[:, hc * TH:(hc + 1) * TH]
                    mk.op("s", "activation", hrow, pa, AF.Silu, bias=b1sb[:, hc:hc + 1])
                    mk.op("v", "tensor_mul", hrow, hrow, pb)
                    for th in range(NTH):
                        psl = slice(th * THW, (th + 1) * THW)
                        sl = slice(tbh * TH + th * THW, tbh * TH + (th + 1) * THW)
                        mk.mm(PSB[:, 1024:1024 + THW], onesF[0:1, :],
                              combrow[:, sl.start:sl.stop], start=True, stop=True)
                        mk.op("v", "tensor_mul", hrow[:, psl], hrow[:, psl],
                              PSB[:, 1024:1024 + THW])
                b3row = A_SC[32:33, 2048:2048 + 1024]
                mk.dma("g", b3row, ap["l0f_b3"][e:e + 1, 0:1024])
                for p2 in range(4):
                    wsl = w3s[p2 % 2]
                    w3c = A_PT[:, wsl:wsl + 4096]
                    mk.dma("g", w3c, ap["l0f_w3"][e, :, :, p2 * 256:(p2 + 1) * 256])
                    for dc2 in range(2):
                        dc = p2 * 2 + dc2
                        py = PSA[:, dc2 * TH:(dc2 + 1) * TH]
                        for th in range(NTH):
                            sl = slice(tbh * TH + th * THW, tbh * TH + (th + 1) * THW)
                            psl = slice(th * THW, (th + 1) * THW)
                            for hc in range(NH):
                                mk.mm(py[:, psl], w3c[:, hc * 256 + dc2 * 128:hc * 256 + (dc2 + 1) * 128],
                                      A_VT[:, hc * TH + th * THW:hc * TH + (th + 1) * THW],
                                      start=(hc == 0), stop=False)
                            mk.mm(py[:, psl], b3row[:, dc * 128:(dc + 1) * 128],
                                  combrow32[:, sl.start:sl.stop], start=False, stop=True)
                            mk.op("v", "tensor_add", X(dc, sl), X(dc, sl), py[:, psl])


    def part_l1a():
        NG = 8
        NW = S // 512
        mk.dma("g", rowsB[32:33, 0:4096], ap["l1a_rows"])
        qkvb = rowsB[32:33, 0:3072]
        outb = rowsB[32:33, 3072:4096]
        padc = rowsB[64:65, 2048:2113]
        mk.op("v", "memset", padc[0:1, 0:64], -512.0)
        mk.op("v", "memset", padc[0:1, 64:65], 512.0)
        epad = rowsB[64:65, 2176:2688]

        vt_v0 = 0
        vt_q = 2080
        vt_k = 2080 + 2 * S
        attst0 = S + 2048
        attn_st = A_PT[:, attst0:attst0 + S]
        proll0 = S
        msk = A_PT[:, attst0 + S:attst0 + S + 4096]
        mk.dma("g", msk, ap["c_l1mask"])
        identF3 = identG
        maxc1 = maxcG

        def qh(h):
            return A_VT[0:65, vt_q + h * S:vt_q + (h + 1) * S]

        def kh(h):
            return A_VT[0:65, vt_k + h * S:vt_k + (h + 1) * S]

        zb = [A_NB[:, dc * S:(dc + 1) * S] for dc in range(DC)]
        for dc in range(DC):
            mk.op("v", "tensor_copy", zb[dc], X(dc))
        ln_sum_stats(zb)
        mu = A_SC[0:1, 0:S]
        rstd = A_SC[0:1, 2048:2048 + S]
        rtmp = A_SC[0:1, 4096:4608]
        for ts in range(NQ):
            sl = slice(ts * 512, (ts + 1) * 512)
            mk.op("v", "tensor_scalar_mul", mu[:, sl], rowpack(ts), 1.0 / D)
            mk.op("v", "tensor_scalar_mul", rstd[:, sl], rowpack(4 + ts), 1.0 / D)
            mk.op("v", "tensor_mul", rtmp, mu[:, sl], mu[:, sl])
            mk.op("v", "tensor_sub", rstd[:, sl], rstd[:, sl], rtmp)
            mk.op("v", "tensor_scalar_add", rstd[:, sl], rstd[:, sl], 1e-5)
            mk.op("s", "activation", rtmp, rstd[:, sl], AF.Sqrt)
            mk.op("v", "reciprocal", rstd[:, sl], rtmp)
            mk.op("v", "tensor_mul", mu[:, sl], mu[:, sl], rstd[:, sl])
        for ts in range(NQ):
            sl = slice(ts * 512, (ts + 1) * 512)
            mk.mm(PSB[:, 0:512], onesF[0:1, :], rstd[:, sl], start=True, stop=True)
            mk.mm(PSB[:, 512:1024], onesF[0:1, :], mu[:, sl], start=True, stop=True)
            for dc in range(DC):
                mk.op("v", "tensor_mul", tmpf, X(dc, sl), PSB[:, 0:512])
                mk.op("v", "tensor_sub", zb[dc][:, sl], tmpf, PSB[:, 512:1024])

        for g in range(NG):
            ws0 = (g % 2) * 3072
            wg = A_WS[:, ws0:ws0 + 3072]
            mk.dma("g", wg, ap["l1a_wqkv"][:, g * 3072:(g + 1) * 3072])
            for which in range(2):
                for ts in range(NQ):
                    sl = slice(ts * 512, (ts + 1) * 512)
                    for dc in range(DC):
                        mk.mm(PSA[:, sl], wg[:, dc * 384 + which * 128:dc * 384 + which * 128 + 128],
                              zb[dc][:, sl], start=(dc == 0), stop=False)
                    mk.mm(PSA[:, sl], qkvb[:, which * 1024 + g * 128:which * 1024 + (g + 1) * 128],
                          onesB[32:33, :], start=False, stop=True)
                for ts in range(NQ):
                    sl = slice(ts * 512, (ts + 1) * 512)
                    for h in range(2):
                        dst = qh(h) if which == 0 else kh(h)
                        mk.op("v", "tensor_copy", dst[0:64, sl.start:sl.stop],
                              PSA[h * 64:(h + 1) * 64, sl])
            for h in range(2):
                mk.op("v", "memset", kh(h)[64:65, 0:S], -1.0)
            mk.op("v", "memset", A_VT[:, vt_v0:vt_v0 + 2080], 1.0)
            for tc in range(NT):
                for dc in range(DC):
                    mk.mm(PSB[:, 512:640], zb[dc][:, tc * 128:(tc + 1) * 128],
                          wg[:, dc * 384 + 256:dc * 384 + 384],
                          start=(dc == 0), stop=False)
                mk.mm(PSB[:, 512:640], onesB[32:33, 0:128],
                      qkvb[:, 2048 + g * 128:2048 + (g + 1) * 128],
                      start=False, stop=True)
                mk.op("v", "tensor_copy", A_VT[:, vt_v0 + tc * 130:vt_v0 + tc * 130 + 64],
                      PSB[:, 512:576])
                mk.op("v", "tensor_copy", A_VT[:, vt_v0 + tc * 130 + 65:vt_v0 + tc * 130 + 129],
                      PSB[:, 576:640])
            for h in range(2):
                for w in range(NW):
                    qsl = slice(w * 512, (w + 1) * 512)
                    kcs = [kc for kc in range(12) if 0 <= (w - 1) * 512 + kc * 128 < S]
                    has_lpad = (w == 0)
                    has_rpad = (w == NW - 1)
                    npad = (1 if has_lpad else 0) + (1 if has_rpad else 0)
                    kt0 = (w - 1) * 512 + kcs[0] * 128
                    kt1 = (w - 1) * 512 + (kcs[-1] + 1) * 128
                    if npad:
                        mk.mm(rowpack(6), onesB[0:64, 0:1], qh(h)[0:64, qsl],
                              start=True, stop=True)
                    for qt in range(4):
                        qtl = slice(w * 512 + qt * 128, w * 512 + (qt + 1) * 128)
                        nk = kt1 - kt0
                        for i, c0 in enumerate(range(kt0, kt1, 512)):
                            cw = min(512, kt1 - c0)
                            mk.mm(PSA[:, i * 512:i * 512 + cw], qh(h)[0:64, qtl],
                                  kh(h)[0:64, c0:c0 + cw], start=True, stop=True)
                        mk.op("v", "reduce_max", maxc1[:, 0:1], PSA[:, 0:nk], axis=AX.X)
                        mk.op("t", "transpose", PSB[0:1, 512:640], maxc1[:, 0:1], identF3[:])
                        mk.op("v", "tensor_copy", qh(h)[64:65, qtl.start:qtl.stop],
                              PSB[0:1, 512:640])
                    if npad:
                        mrow = A_SC[0:1, 4096:4608]
                        spadrow = A_SC[0:1, 0:512]
                        mk.op("v", "tensor_scalar_mul", spadrow, rowpack(6), -1.0)
                        mk.op("v", "tensor_copy", mrow[:, 0:512], qh(h)[64:65, qsl])
                        mk.op("v", "tensor_max", mrow[:, 0:512], mrow[:, 0:512], spadrow)
                        mk.op("v", "tensor_copy", qh(h)[64:65, qsl.start:qsl.stop], mrow[:, 0:512])
                        mk.op("v", "tensor_sub", spadrow, spadrow, mrow[:, 0:512])
                        mk.op("s", "activation", epad, spadrow, AF.Exp)
                    pvslot = PSB[0:65, ((h * NW + w) % 2) * 512:((h * NW + w) % 2) * 512 + 512]
                    for idx, kc in enumerate(kcs):
                        kt = (w - 1) * 512 + kc * 128
                        ktc = kt // 128
                        slot = idx % 2
                        pa = PSA[:, slot * 1024:slot * 1024 + 512]
                        mk.mm(pa, kh(h)[:, kt:kt + 128], qh(h)[:, qsl],
                              start=True, stop=True)
                        roll = A_PT[:, proll0 + slot * 1024:proll0 + slot * 1024 + 512]
                        mk.op("s", "activation", roll, pa, AF.Exp)
                        if kc < 4:
                            ci = kc
                        elif kc >= 8:
                            ci = kc - 4
                        else:
                            ci = None
                        if ci is not None:
                            mk.op("v", "tensor_mul", roll, roll, msk[:, ci * 512:(ci + 1) * 512])
                        mk.mm(pvslot,
                              A_VT[:, vt_v0 + ktc * 130 + h * 65:vt_v0 + ktc * 130 + (h + 1) * 65],
                              roll, start=(idx == 0), stop=(npad == 0 and idx == len(kcs) - 1))
                    for t in range(npad):
                        mk.mm(pvslot, padc[0:1, :], epad,
                              start=False, stop=(t == npad - 1))
                    mk.op("v", "tensor_copy", attn_st[h * 64:(h + 1) * 64, qsl],
                          pvslot[0:64, :])
                    mk.op("v", "tensor_copy",
                          A_SC[32:33, (h * NW + w) * 512:(h * NW + w) * 512 + 512],
                          pvslot[64:65, :])
            mk.op("v", "reciprocal", A_SC[32:33, 0:2 * NW * 512],
                  A_SC[32:33, 0:2 * NW * 512])
            for h in range(2):
                for w in range(NW):
                    qsl = slice(w * 512, (w + 1) * 512)
                    mk.mm(PSB[:, 1024:1536], onesF[32:33, :],
                          A_SC[32:33, (h * NW + w) * 512:(h * NW + w) * 512 + 512],
                          start=True, stop=True)
                    mk.op("v", "tensor_mul", attn_st[h * 64:(h + 1) * 64, qsl],
                          attn_st[h * 64:(h + 1) * 64, qsl], PSB[0:64, 1024:1536])
            mk.dma("y", hattn[g * 128:(g + 1) * 128, :], attn_st)

        wo = A_WS[:, 0:8192]
        mk.dma("g", wo, ap["l1a_wout"])
        for dc in range(DC):
            mk.dma("y", A_PT[:, dc * S:(dc + 1) * S], hattn[dc * 128:(dc + 1) * 128, :])
        for oc in range(DC):
            for ts in range(NQ):
                sl = slice(ts * 512, (ts + 1) * 512)
                for dc in range(DC):
                    mk.mm(PSA[:, sl], wo[:, dc * 1024 + oc * 128:dc * 1024 + (oc + 1) * 128],
                          A_PT[:, dc * S + ts * 512:dc * S + (ts + 1) * 512],
                          start=(dc == 0), stop=False)
                mk.mm(PSA[:, sl], outb[:, oc * 128:(oc + 1) * 128], onesB[32:33, :],
                      start=False, stop=True)
                mk.op("v", "tensor_add", X(oc, sl), X(oc, sl), PSA[:, sl])


    def part_l1f():
        NP = 2048
        MT = S // 128
        rg = [list(range(n_cores))]
        ub = [A_NB[:, dc * S:(dc + 1) * S] for dc in range(DC)]
        for dc in range(DC):
            mk.op("v", "tensor_copy", ub[dc], X(dc))
        ln_sum_stats(ub)
        mu = A_SC[0:1, 0:S]
        rstd = A_SC[0:1, 2048:2048 + S]
        rtmp = A_SC[0:1, 4096:4608]
        for ts in range(NQ):
            sl = slice(ts * 512, (ts + 1) * 512)
            mk.op("v", "tensor_scalar_mul", mu[:, sl], rowpack(ts), 1.0 / D)
            mk.op("v", "tensor_scalar_mul", rstd[:, sl], rowpack(4 + ts), 1.0 / D)
            mk.op("v", "tensor_mul", rtmp, mu[:, sl], mu[:, sl])
            mk.op("v", "tensor_sub", rstd[:, sl], rstd[:, sl], rtmp)
            mk.op("v", "tensor_scalar_add", rstd[:, sl], rstd[:, sl], 1e-5)
            mk.op("s", "activation", rtmp, rstd[:, sl], AF.Sqrt)
            mk.op("v", "reciprocal", rstd[:, sl], rtmp)
            mk.op("v", "tensor_mul", mu[:, sl], mu[:, sl], rstd[:, sl])
        for ts in range(NQ):
            sl = slice(ts * 512, (ts + 1) * 512)
            mk.mm(PSB[:, 0:512], onesF[0:1, :], rstd[:, sl], start=True, stop=True)
            mk.mm(PSB[:, 512:1024], onesF[0:1, :], mu[:, sl], start=True, stop=True)
            for dc in range(DC):
                mk.op("v", "tensor_mul", tmpf, X(dc, sl), PSB[:, 0:512])
                mk.op("v", "tensor_sub", ub[dc][:, sl], tmpf, PSB[:, 512:1024])

        for mt in range(MT):
            for dc in range(DC):
                slot = (dc % 2) * 128
                mk.op("t", "transpose", PST[:, slot:slot + 128],
                      ub[dc][:, mt * 128:(mt + 1) * 128], ident[:])
                mk.op("v", "tensor_copy",
                      A_VT[:, mt * 1024 + dc * 128:mt * 1024 + (dc + 1) * 128],
                      PST[:, slot:slot + 128])

        phih = A_WS[:, 0:8192]
        ar_chunk = max(512, S // 4)
        n_chunks = S // ar_chunk
        done_rows = 0
        for nph in range(2):
            mk.dma("g", phih, ap["l1f_phi"].rearrange("p (dc np) -> p dc np", np=2048)[:, :, nph * 1024:(nph + 1) * 1024])
            for mt in range(MT):
                for s5 in range(2):
                    nsl = slice(s5 * 512, (s5 + 1) * 512)
                    for dc in range(DC):
                        mk.mm(PSA[:, nsl], ub[dc][:, mt * 128:(mt + 1) * 128],
                              phih[:, dc * 1024 + s5 * 512:dc * 1024 + (s5 + 1) * 512],
                              start=(dc == 0), stop=(dc == DC - 1))
                    mk.op("s", "activation", tmpg[:], PSA[:, nsl], AF.Exp)
                    mk.dma("y", hE.ap()[mt * 128:(mt + 1) * 128,
                                        nph * 1024 + s5 * 512:nph * 1024 + (s5 + 1) * 512],
                           tmpg[:])
            if nph == 1:
                for c in range(n_chunks):
                    mk.op("g", "collective_compute", "AllReduce", ALU.add,
                          replica_groups=rg,
                          ins=[hE.ap()[c * ar_chunk:(c + 1) * ar_chunk, :].opt()],
                          outs=[hS.ap()[c * ar_chunk:(c + 1) * ar_chunk, :].opt()])

        dcache0 = 0
        for npb in range(4):
            nsl = slice(npb * 512, (npb + 1) * 512)
            for mt in range(MT):
                mk.dma("y", tmpg[:], hE.ap()[mt * 128:(mt + 1) * 128, nsl])
                mk.dma("y", tmpf[:, 0:512] if False else tmpf, hS.ap()[mt * 128:(mt + 1) * 128, nsl])
                mk.op("v", "reciprocal", tmpf, tmpf)
                mk.op("v", "tensor_mul", A_PT[:, dcache0 + mt * 512:dcache0 + (mt + 1) * 512],
                      tmpg[:], tmpf)
            for half in range(2):
                for dci in range(4):
                    dc = half * 4 + dci
                    for mt in range(MT):
                        mk.mm(PSA[:, dci * 512:(dci + 1) * 512],
                              A_VT[:, mt * 1024 + dc * 128:mt * 1024 + (dc + 1) * 128],
                              A_PT[:, dcache0 + mt * 512:dcache0 + (mt + 1) * 512],
                              start=(mt == 0), stop=(mt == MT - 1))
                for dci in range(4):
                    dc = half * 4 + dci
                    mk.op("v", "tensor_copy",
                          A_NB[:, dc * 2048 + npb * 512:dc * 2048 + (npb + 1) * 512],
                          PSA[:, dci * 512:(dci + 1) * 512])

        for npq in range(16):
            pbase = 0 if npq % 2 == 0 else 1024
            for lane in range(4):
                n = npq * 4 + lane
                wslot = A_WS[:, 0:8192] if n % 2 == 0 else A_PT[:, 0:8192]
                mk.dma("g", wslot, ap["l1f_W"][n])
                for eh in range(2):
                    pslot = PSA[:, pbase + eh * 512:pbase + (eh + 1) * 512]
                    for dc in range(DC):
                        mk.mm(pslot[lane * 32:lane * 32 + 32, :],
                              A_NB[:, dc * 2048 + npq * 128 + lane * 32:dc * 2048 + npq * 128 + (lane + 1) * 32],
                              wslot[:, dc * 1024 + eh * 512:dc * 1024 + (eh + 1) * 512],
                              start=(dc == 0), stop=(dc == DC - 1),
                              tile_position=(0, lane * 32), skip_group_check=True)
            for lane in range(4):
                n = npq * 4 + lane
                brep = rowsB[0:32, (lane % 2) * 1024:(lane % 2) * 1024 + 1024]
                mk.dma("g", brep, ap["l1f_brep"][npq * 128 + lane * 32:npq * 128 + (lane + 1) * 32, :])
                for eh in range(2):
                    pslot = PSA[:, pbase + eh * 512:pbase + (eh + 1) * 512]
                    mk.op("v", "tensor_add",
                          A_VT[:, npq * 1024 + eh * 512:npq * 1024 + (eh + 1) * 512][lane * 32:lane * 32 + 32, :],
                          pslot[lane * 32:lane * 32 + 32, :], brep[:, eh * 512:(eh + 1) * 512])

        est0 = 8192
        for mb in range(S // 256):
            for i in range(2):
                mrow = mb * 256 + i * 128
                mk.dma("g", A_PT[:, est0 + i * 2048:est0 + (i + 1) * 2048],
                       hE.ap()[mrow:mrow + 128, :])
                for npc in range(16):
                    slot = (npc % 2) * 128
                    mk.op("t", "transpose", PST[:, slot:slot + 128],
                          A_PT[:, est0 + i * 2048 + npc * 128:est0 + i * 2048 + (npc + 1) * 128],
                          ident[:])
                    mk.op("v", "tensor_copy",
                          A_PT[:, est0 + 4096 + npc * 256 + i * 128:est0 + 4096 + npc * 256 + (i + 1) * 128],
                          PST[:, slot:slot + 128])
            for npc in range(16):
                enp = A_PT[:, est0 + 4096 + npc * 256:est0 + 4096 + (npc + 1) * 256]
                for ec in range(DC):
                    bank, off = divmod(ec, 2)
                    mk.mm(PSA[:, bank * 512 + off * 256:bank * 512 + off * 256 + 256],
                          A_VT[:, npc * 1024 + ec * 128:npc * 1024 + (ec + 1) * 128],
                          enp, start=(npc == 0 and off == 0), stop=(npc == 15 and off == 1),
                          skip_group_check=True)
                mk.mm(rowpack(6)[:, 0:256], onesB[:, 0:1], enp,
                      start=(npc == 0), stop=(npc == 15))
            srow = A_SC[32:33, 0:256]
            mk.op("v", "tensor_copy", srow, rowpack(6)[:, 0:256])
            mk.op("v", "reciprocal", srow, srow)
            mk.mm(PSB[:, 512:768], onesF[32:33, :], srow, start=True, stop=True)
            mk.op("v", "tensor_copy", tmpg[:, 0:256], PSB[:, 512:768])
            for ec in range(DC):
                bank, off = divmod(ec, 2)
                mk.op("v", "tensor_mul", tmpf[:, 0:256],
                      PSA[:, bank * 512 + off * 256:bank * 512 + off * 256 + 256],
                      tmpg[:, 0:256])
                mk.op("v", "tensor_add", X(ec)[:, mb * 256:(mb + 1) * 256],
                      X(ec)[:, mb * 256:(mb + 1) * 256], tmpf[:, 0:256])

    if "l0a" in parts:
        part_l0a()
    if "l0fa" in parts:
        part_l0fa()
    if "l0fm" in parts:
        part_l0fm("l0fa" in parts)
    if "l1a" in parts:
        part_l1a()
    if "l1f" in parts:
        part_l1f()

    for dc in range(DC):
        mk.dma("y", out_x[dc * 128:(dc + 1) * 128, :], A_X[:, dc * S:(dc + 1) * S])
    mk.op("y", "nop")
    mk.flush(verbose=True)
    return nc


_CACHE = {}


def kernel(**inputs):
    S = 2048
    n_cores = 8
    if "nc" not in _CACHE:
        _CACHE["nc"] = build_program(S=S, parts=("l0a", "l0fa", "l0fm", "l1a", "l1f"),
                                     n_cores=n_cores)
    nc = _CACHE["nc"]
    prep = {}
    prep.update(prep_l0a(inputs))
    prep.update(prep_l0f(inputs))
    prep.update(prep_l1a(inputs))
    prep.update(prep_l1f(inputs))
    prep.update(host_constants(S))
    x = np.ascontiguousarray(np.asarray(inputs["L0_attn_x"], dtype=np.float32))
    maps = []
    for c in range(n_cores):
        im = dict(prep)
        im["xT"] = np.ascontiguousarray(x[c].T)
        maps.append(im)
    res = run_bass_kernel_spmd(nc, maps, core_ids=list(range(n_cores)))
    out = np.stack([np.ascontiguousarray(res.results[c]["outX"].T) for c in range(n_cores)], 0)
    return out


# version 11
# speedup vs baseline: 1.1203x; 1.1203x over previous
import numpy as np
import ml_dtypes
import concourse.bass as bass
import concourse.mybir as mybir
from concourse.bass_utils import run_bass_kernel_spmd
from contextlib import ExitStack

F32 = mybir.dt.float32
BF16 = mybir.dt.bfloat16
AF = mybir.ActivationFunctionType
ALU = mybir.AluOpType
AX = mybir.AxisListType

D = 1024
DC = 8
P129 = 129


DTSZ = {F32: 4, BF16: 2, mybir.dt.float16: 2, mybir.dt.int32: 4, mybir.dt.uint8: 1}


class _Rec:
    __slots__ = ("idx", "eng", "method", "args", "kw", "mwaits", "reads", "writes",
                 "is_dma", "ticked", "sem", "val", "edges", "clock", "eidx",
                 "_needed", "_waits", "_fence")

    def __init__(self, idx, eng, method, args, kw, mwaits, is_dma):
        self.idx = idx
        self.eng = eng
        self.method = method
        self.args = args
        self.kw = kw
        self.mwaits = mwaits
        self.is_dma = is_dma
        self.ticked = False
        self.sem = None
        self.val = None
        self.edges = []
        self.clock = None
        self.eidx = 0


class _SemPool:
    def __init__(self, nc, es, name, k, step):
        self.sems = [es.enter_context(nc.semaphore(f"{name}{i}")) for i in range(k)]
        self.count = [0] * k
        self.i = 0
        self.step = step

    def take(self):
        i = self.i
        self.i = (self.i + 1) % len(self.sems)
        prev = self.count[i]
        self.count[i] += self.step
        return self.sems[i], self.count[i], prev


class MK:
    ENGS = ("t", "v", "s", "g", "y")

    def __init__(self, nc):
        self.nc = nc
        self.es = ExitStack()
        self.ops = []
        self.aliases = {}
        self.pools = {
            "t": _SemPool(nc, self.es, "pt_", 12, 1),
            "v": _SemPool(nc, self.es, "pv_", 16, 1),
            "s": _SemPool(nc, self.es, "ps_", 12, 1),
            "g": _SemPool(nc, self.es, "pg_", 6, 1),
            "y": _SemPool(nc, self.es, "py_", 4, 1),
            "gd": _SemPool(nc, self.es, "pgd_", 16, 16),
            "yd": _SemPool(nc, self.es, "pyd_", 12, 16),
        }

    def alias(self, name, shape, dtype, backing):
        v = self.nc.virtual_sbuf_tensor(name, list(shape), dtype)
        self.nc.back_virtual_tensor_with(v, backing)
        self.aliases[name] = backing.name
        return v

    def op(self, eng, method, *args, wait=(), tick=False, **kw):
        r = _Rec(len(self.ops), eng, method, args, kw,
                 tuple(t for t in wait if t is not None), False)
        self.ops.append(r)
        return r

    def dma(self, eng, out, in_, wait=()):
        r = _Rec(len(self.ops), eng, "dma_start", (), {"out": out, "in_": in_},
                 tuple(t for t in wait if t is not None), True)
        self.ops.append(r)
        return r

    def tick(self, eng, wait=()):
        return self.op(eng, "nop", wait=wait)

    def mm(self, out, lhsT, rhs, start, stop, wait=(), tick=False, **kw):
        return self.op("t", "matmul", out, lhsT, rhs, start=start, stop=stop, wait=wait, **kw)

    PSUM_NAMES = ("PSA", "PSB", "PST")

    def _rect(self, ap):
        t = ap.tensor
        name = self.aliases.get(t.name, t.name)
        sz = DTSZ[t.dtype]
        dims = list(ap.ap)
        off = ap.offset
        if len(t.shape) >= 2 and t.shape[0] <= 128 and not name.startswith("DR_"):
            pcols = t.shape[1] if len(t.shape) == 2 else int(np.prod(t.shape[1:]))
            p0 = off // pcols
            c0 = off % pcols
            p1 = p0 + (dims[0][1] if dims else 1)
            cspan = sum((cnt - 1) * st for st, cnt in dims[1:]) + 1 if len(dims) > 1 else 1
            b0, b1 = c0 * sz, (c0 + cspan) * sz
            if name in self.PSUM_NAMES:
                p0, p1 = 0, 128
                b0 = (b0 // 2048) * 2048
                b1 = ((b1 + 2047) // 2048) * 2048
            return (name, p0, p1, b0, b1)
        span = sum((cnt - 1) * st for st, cnt in dims) + 1
        return (name, 0, 1, off * sz, (off + span) * sz)

    def _is_ap(self, x):
        return hasattr(x, "tensor") and hasattr(x, "offset")

    def _io(self, r):
        writes, reads = [], []
        outs = []
        if "out" in r.kw:
            outs.append(r.kw["out"])
        if "accum_out" in r.kw and self._is_ap(r.kw.get("accum_out")):
            outs.append(r.kw["accum_out"])
        args = list(r.args)
        if args and self._is_ap(args[0]):
            outs.append(args[0])
            args = args[1:]
        for a in args:
            if self._is_ap(a):
                reads.append(a)
        for k, vv in r.kw.items():
            if k in ("out", "accum_out"):
                continue
            if k == "outs" and isinstance(vv, list):
                outs.extend(x for x in vv if self._is_ap(x))
                continue
            if isinstance(vv, list):
                reads.extend(x for x in vv if self._is_ap(x))
                continue
            if self._is_ap(vv):
                reads.append(vv)
        return [self._rect(a) for a in reads], [self._rect(a) for a in outs]

    @staticmethod
    def _buckets(rect):
        name, p0, p1, b0, b1 = rect
        for pb in range(p0 // 32, (p1 + 31) // 32):
            for cb in range(b0 // 512, (b1 + 511) // 512):
                yield (name, pb, cb)

    @staticmethod
    def _overlap(a, b):
        return a[0] == b[0] and a[1] < b[2] and b[1] < a[2] and a[3] < b[4] and b[3] < a[4]

    def flush(self, verbose=False):
        last_w = {}
        readers = {}
        eidx = {e: 0 for e in self.ENGS}
        for r in self.ops:
            r.eidx = eidx[r.eng]
            eidx[r.eng] += 1
            reads, writes = self._io(r)
            r.reads, r.writes = reads, writes
            edges = {}
            for rect in reads:
                short = (rect[1], rect[3], rect[2], rect[4])
                for bk in self._buckets(rect):
                    w = last_w.get(bk)
                    if w is not None and w[1] is not r:
                        edges[w[1].idx] = w[1]
            for rect in writes:
                for bk in self._buckets(rect):
                    w = last_w.get(bk)
                    if w is not None and w[1] is not r:
                        edges[w[1].idx] = w[1]
                    for rd in readers.get(bk, ()):
                        if rd is not r:
                            edges[rd.idx] = rd
            for rect in reads:
                for bk in self._buckets(rect):
                    readers.setdefault(bk, []).append(r)
            for rect in writes:
                for bk in self._buckets(rect):
                    last_w[bk] = (rect, r)
                    readers[bk] = []
            r.edges = [self.ops[i] for i in sorted(edges)]

        prev_on_engine = {e: None for e in self.ENGS}
        for r in self.ops:
            ck = {}
            pred = prev_on_engine[r.eng]
            if pred is not None:
                ck.update(pred.clock)
                if r.eng == "t":
                    ck["T"] = max(ck.get("T", -1), pred.eidx)
            needed = []
            for p in r.edges + [w if isinstance(w, _Rec) else None for w in r.mwaits]:
                if p is None:
                    continue
                if p.eng == "t" and r.eng == "t" and not p.is_dma:
                    if ck.get("T", -1) < p.eidx:
                        ck["T"] = max(ck.get("T", -1), p.eidx)
                        for k, vv in p.clock.items():
                            if ck.get(k, -1 if k == "T" else 0) < vv:
                                ck[k] = vv
                    continue
                needed.append(p)
            for p in needed:
                p.ticked = True
            r.clock = ck
            r._needed = needed
            prev_on_engine[r.eng] = r

        for r in self.ops:
            if r.ticked or r.is_dma:
                pool = self.pools[r.eng + "d"] if r.is_dma else self.pools[r.eng]
                r.sem, r.val, r._fence = pool.take()
        prev_on_engine = {e: None for e in self.ENGS}
        nwaits = 0
        for r in self.ops:
            ck = {}
            pred = prev_on_engine[r.eng]
            if pred is not None:
                ck.update(pred.clock)
                if r.eng == "t":
                    ck["T"] = max(ck.get("T", -1), pred.eidx)
            waits = []
            for p in r._needed:
                if p.eng == "t" and r.eng == "t" and not p.is_dma:
                    continue
                if p.sem is None:
                    raise RuntimeError(f"producer without sem: {p.method}")
                key = id(p.sem)
                if ck.get(key, 0) >= p.val:
                    continue
                waits.append((p.sem, p.val))
                ck[key] = p.val
                for k, vv in p.clock.items():
                    if k == "T":
                        ck["T"] = max(ck.get("T", -1), vv)
                    elif ck.get(k, 0) < vv:
                        ck[k] = vv
                if p.eng == "t" and not p.is_dma:
                    ck["T"] = max(ck.get("T", -1), p.eidx)
            r._waits = waits
            nwaits += len(waits)
            r.clock = ck
            prev_on_engine[r.eng] = r
        if verbose:
            nt = sum(1 for r in self.ops if r.ticked)
            print(f"[mk] ops={len(self.ops)} ticked={nt} waits={nwaits}")

        eng_ops = {e: [r for r in self.ops if r.eng == e] for e in self.ENGS}

        with self.nc.Block() as block:
            def mk_body(name):
                def body(eng):
                    seen = {}
                    for r in eng_ops[name]:
                        if r.sem is not None and r._fence > 0 and seen.get(id(r.sem), 0) < r._fence:
                            eng.wait_ge(r.sem, r._fence)
                            seen[id(r.sem)] = r._fence
                        for (sem, val) in r._waits:
                            if seen.get(id(sem), 0) < val:
                                eng.wait_ge(sem, val)
                                seen[id(sem)] = val
                        ins = getattr(eng, r.method)(*r.args, **r.kw)
                        if r.sem is not None:
                            ins.then_inc(r.sem, 16 if r.is_dma else 1)
                return body
            block.tensor(mk_body("t"))
            block.vector(mk_body("v"))
            block.scalar(mk_body("s"))
            block.gpsimd(mk_body("g"))
            block.sync(mk_body("y"))


def host_constants(S):
    c = {"c_ident": np.eye(128, dtype=np.float32).astype(ml_dtypes.bfloat16),
         "c_identF": np.eye(128, dtype=np.float32)}
    inv_freq = 1.0 / 10000.0 ** (np.arange(0, 64, 2, dtype=np.float32) / 64.0)
    t = np.arange(S, dtype=np.float32)
    freqs = inv_freq[:, None] * t[None, :]
    emb = np.concatenate([freqs, freqs], 0)
    cos = np.cos(emb)
    sin = np.sin(emb)
    c["c_cos"] = np.concatenate([cos, cos], 0).astype(ml_dtypes.bfloat16)
    c["c_sin"] = np.concatenate([sin, sin], 0).astype(ml_dtypes.bfloat16)
    perm = np.zeros((128, 128), dtype=np.float32)
    for blk in (0, 64):
        for d_ in range(32):
            perm[blk + d_, blk + d_ + 32] = -1.0
            perm[blk + 32 + d_, blk + d_] = 1.0
    c["c_permT"] = perm.T.copy().astype(ml_dtypes.bfloat16)
    msk = np.zeros((128, 8 * 512), dtype=ml_dtypes.bfloat16)
    for ci in range(8):
        kc = ci if ci < 4 else ci + 4
        for p in range(128):
            kj = kc * 128 + p
            qi = np.arange(512)
            msk[p, ci * 512:(ci + 1) * 512] = ((kj >= qi) & (kj <= qi + 1024)).astype(np.float32)
    c["c_l1mask"] = msk
    return c


def wimg(wT):
    Din, O = wT.shape
    return np.ascontiguousarray(wT.reshape(Din // 128, 128, O).transpose(1, 0, 2)
                                .reshape(128, (Din // 128) * O))


def prep_l0a(inputs):
    d = {}
    d["l0a_wq"] = wimg(np.ascontiguousarray(inputs["L0_attn_w_q"].T))
    d["l0a_wk"] = wimg(np.ascontiguousarray(inputs["L0_attn_w_k"].T))
    d["l0a_wv"] = wimg(np.ascontiguousarray(inputs["L0_attn_w_v"].T))
    r32 = np.zeros((1, 1360), dtype=np.float32)
    r32[0, 0:129] = inputs["L0_attn_b_q"]
    r32[0, 160:289] = inputs["L0_attn_b_k"]
    r32[0, 320:1344] = inputs["L0_attn_b_v"]
    d["l0a_rows32"] = r32
    d["l0a_k0x"] = np.ascontiguousarray(inputs["L0_attn_k_0"][128:129][None, :])
    d["l0a_rows64"] = np.ascontiguousarray(inputs["L0_attn_v_0"][None, :])
    d["l0a_k0m"] = np.ascontiguousarray(inputs["L0_attn_k_0"][0:128, None])
    d["l0a_bcols"] = np.stack([inputs["L0_attn_b_q"][0:128], inputs["L0_attn_b_k"][0:128]], 1)
    return d




def prep_l0f(inputs):
    d = {}
    d["l0f_wqkv"] = qkv_group_img(np.ascontiguousarray(inputs["L0_ffn_qkv_w"].T))
    d["l0f_wout"] = wimg(np.ascontiguousarray(inputs["L0_ffn_out_w"].T))
    rows = np.zeros((1, 4096), dtype=np.float32)
    rows[0, 0:3072] = inputs["L0_ffn_qkv_b"]
    rows[0, 3072:4096] = inputs["L0_ffn_out_b"]
    d["l0f_rows"] = rows
    d["l0f_n1"] = np.ascontiguousarray(inputs["L0_ffn_norm1_w"].reshape(8, 128).T)
    d["l0f_n2"] = np.ascontiguousarray(inputs["L0_ffn_norm2_w"].reshape(8, 128).T)
    d["l0f_gw"] = wimg(np.ascontiguousarray(inputs["L0_ffn_gate_w"].T))
    w1 = inputs["L0_ffn_w1"]; w2 = inputs["L0_ffn_w2"]; w3 = inputs["L0_ffn_w3"]
    w1r = w1.reshape(8, 16, 128, 8, 128).transpose(0, 4, 1, 3, 2)
    w1r = w1.reshape(8, 16, 128, 8, 128).transpose(0, 4, 1, 3, 2)
    w2r = w2.reshape(8, 16, 128, 8, 128).transpose(0, 4, 1, 3, 2)
    img = np.stack([w1r, w2r], axis=4)
    d["l0f_w12"] = np.ascontiguousarray(img.reshape(8, 128, 16 * 2048))
    d["l0f_w3"] = np.ascontiguousarray(w3.reshape(8, 1024, 16, 128).transpose(0, 3, 2, 1))
    d["l0f_b1"] = np.ascontiguousarray(inputs["L0_ffn_b1"].reshape(8, 16, 128).transpose(0, 2, 1))
    d["l0f_b2"] = np.ascontiguousarray(inputs["L0_ffn_b2"])
    d["l0f_b3"] = np.ascontiguousarray(inputs["L0_ffn_b3"])
    d["l0f_b2c"] = np.ascontiguousarray(inputs["L0_ffn_b2"].reshape(8, 16, 128).transpose(0, 2, 1))
    qb = inputs["L0_ffn_qkv_b"]
    d["l0f_bqk"] = np.ascontiguousarray(np.stack(
        [qb[g * 128:(g + 1) * 128] for g in range(8)] +
        [qb[1024 + g * 128:1024 + (g + 1) * 128] for g in range(8)], 1))
    d["l0f_boc"] = np.ascontiguousarray(inputs["L0_ffn_out_b"].reshape(8, 128).T)
    return d

def qkv_group_img(qkvT):
    img = np.empty((128, 8 * 3072), dtype=np.float32)
    for g in range(8):
        for dc in range(8):
            blk = img[:, g * 3072 + dc * 384:g * 3072 + (dc + 1) * 384]
            rows = slice(dc * 128, (dc + 1) * 128)
            for which in range(3):
                blk[:, which * 128:(which + 1) * 128] = \
                    qkvT[rows, which * 1024 + 128 * g:which * 1024 + 128 * (g + 1)]
    return img


def prep_l1a(inputs):
    d = {}
    qkvT = np.concatenate([np.ascontiguousarray(inputs["L1_attn_wq"].T),
                           np.ascontiguousarray(inputs["L1_attn_wk"].T),
                           np.ascontiguousarray(inputs["L1_attn_wv"].T)], axis=1)
    d["l1a_wqkv"] = qkv_group_img(qkvT)
    d["l1a_wout"] = wimg(np.ascontiguousarray(inputs["L1_attn_wo"].T))
    rows = np.zeros((1, 4096), dtype=np.float32)
    rows[0, 0:1024] = inputs["L1_attn_bq"]
    rows[0, 1024:2048] = inputs["L1_attn_bk"]
    rows[0, 2048:3072] = inputs["L1_attn_bv"]
    rows[0, 3072:4096] = inputs["L1_attn_bo"]
    d["l1a_rows"] = rows
    d["l1a_bqk"] = np.ascontiguousarray(np.stack(
        [inputs["L1_attn_bq"][g * 128:(g + 1) * 128] for g in range(8)] +
        [inputs["L1_attn_bk"][g * 128:(g + 1) * 128] for g in range(8)], 1))
    d["l1a_boc"] = np.ascontiguousarray(inputs["L1_attn_bo"].reshape(8, 128).T)
    return d


def prep_l1f(inputs):
    d = {}
    phi = inputs["L1_ffn_phi"].reshape(1024, 2048)
    d["l1f_phi"] = wimg(np.ascontiguousarray(phi))
    W = inputs["L1_ffn_W"]
    d["l1f_W"] = np.ascontiguousarray(
        W.reshape(64, 8, 128, 1024).transpose(0, 2, 1, 3).reshape(64, 128, 8192))
    d["l1f_brep"] = np.ascontiguousarray(np.repeat(inputs["L1_ffn_b"], 32, axis=0))
    return d


def build_program(S=2048, parts=("l0a",), n_cores=8):
    nc = bass.Bass("TRN2", target_bir_lowering=False, debug=False)
    mk = MK(nc)
    NT = S // 128
    NQ = S // 512

    ap = {}

    def inp(name, shape, dt=F32):
        ap[name] = nc.dram_tensor(name, list(shape), dt, kind="ExternalInput").ap()
        return ap[name]

    xT = inp("xT", [D, S])
    inp("c_ident", [128, 128], BF16)
    inp("c_identF", [128, 128])
    if "l0a" in parts:
        inp("l0a_wq", [128, 8 * P129]); inp("l0a_wk", [128, 8 * P129])
        inp("l0a_wv", [128, 8 * 1024])
        inp("l0a_rows32", [1, 1360]); inp("l0a_rows64", [1, 1024]); inp("l0a_k0x", [1, 1])
        inp("l0a_k0m", [128, 1]); inp("l0a_bcols", [128, 2])
    if "l0fa" in parts:
        inp("l0f_wqkv", [128, 8 * 3072]); inp("l0f_wout", [128, 8 * 1024])
        inp("l0f_rows", [1, 4096]); inp("l0f_n1", [128, 8]); inp("l0f_n2", [128, 8])
        inp("l0f_bqk", [128, 16]); inp("l0f_boc", [128, 8])
        inp("c_cos", [128, S], BF16); inp("c_sin", [128, S], BF16)
        inp("c_permT", [128, 128], BF16)
    if "l0fm" in parts:
        inp("l0f_gw", [128, 64])
        inp("l0f_w12", [8, 128, 16 * 2048])
        inp("l0f_w3", [8, 128, 16, 1024])
        inp("l0f_b1", [8, 128, 16]); inp("l0f_b2c", [8, 128, 16]); inp("l0f_b3", [8, 1024])
        if "l0fa" not in parts:
            inp("tx1", [D, S])
        if "l0fa" not in parts or True:
            pass
    if "l0fm" in parts and "l0fa" not in parts:
        inp("l0f_n2", [128, 8])
    if "l1a" in parts:
        inp("l1a_wqkv", [128, 8 * 3072]); inp("l1a_wout", [128, 8 * 1024])
        inp("l1a_rows", [1, 4096]); inp("c_l1mask", [128, 8 * 512], BF16)
        inp("l1a_bqk", [128, 16]); inp("l1a_boc", [128, 8])
    if "l1f" in parts:
        inp("l1f_phi", [128, 8 * 2048])
        inp("l1f_W", [64, 128, 8192])
        inp("l1f_brep", [2048, 1024])
    out_x = nc.dram_tensor("outX", [D, S], F32, kind="ExternalOutput").ap()
    if "l1f" in parts:
        hE = nc.dram_tensor("hE", [S, 2048], F32, kind="Internal")
        hS = nc.dram_tensor("hS", [S, 2048], F32, kind="Internal")
    hx1 = nc.dram_tensor("hx1", [D, S], F32, kind="Internal").ap()
    hcomb = nc.dram_tensor("hcomb", [8, S], F32, kind="Internal").ap()
    hattn = nc.dram_tensor("hattn", [D, S], BF16, kind="Internal").ap()

    A_X = nc.alloc_sbuf_tensor("A_X", [128, DC * S], F32)
    A_NB = nc.alloc_sbuf_tensor("A_NB", [128, max(DC * S, DC * 2048)], BF16)
    A_VT = nc.alloc_sbuf_tensor("A_VT", [128, max(DC * S, 2080 + 4 * S, 16384)], BF16)
    PTCOLS = max((S // 128 + 1) * 512 + (S + 128) + 2 * S, DC * S, DC * (S // 2) + 8192, 16384)
    A_PT = nc.alloc_sbuf_tensor("A_PT", [128, PTCOLS], BF16)
    A_WS = nc.alloc_sbuf_tensor("A_WS", [128, 8192], BF16)
    A_SC = nc.alloc_sbuf_tensor("A_SC", [128, 4608], F32)
    rowsB = nc.alloc_sbuf_tensor("rowsB", [128, 4100], BF16)
    onesB = nc.alloc_sbuf_tensor("onesB", [128, 512], BF16)
    onesF = nc.alloc_sbuf_tensor("onesF", [128, 128], F32)
    ident = nc.alloc_sbuf_tensor("ident", [128, 128], BF16)
    identG = nc.alloc_sbuf_tensor("identG", [128, 128], F32)
    ap["__bqkL__"] = nc.alloc_sbuf_tensor("bqkL", [128, 16], F32)
    maxcG = nc.alloc_sbuf_tensor("maxcG", [128, 1], F32)
    k0m = nc.alloc_sbuf_tensor("k0m", [128, 1], BF16)

    tmpf = A_SC[:, 4096:4608]
    negm = maxcG
    tmpg = nc.alloc_sbuf_tensor("tmpg", [128, 512], F32)
    PSA = nc.alloc_psum_tensor("PSA", [128, 2048], F32)
    PSB = nc.alloc_psum_tensor("PSB", [128, 1536], F32)
    PST = nc.alloc_psum_tensor("PST", [128, 1024], BF16)

    mk.op("v", "memset", onesB[:], 1.0)
    mk.op("v", "memset", onesF[:], 1.0)
    mk.dma("y", identG[:], ap["c_identF"])
    mk.dma("y", ident[:], ap["c_ident"])
    for dc in range(DC):
        mk.dma("y", A_X[:, dc * S:(dc + 1) * S], xT[dc * 128:(dc + 1) * 128, :])

    def rowpack(idx):
        r, b = divmod(idx, 3)
        return PSB[32 * b:32 * b + 1, r * 512:(r + 1) * 512]

    def X(dc, sl=None):
        base = dc * S
        return A_X[:, base:base + S] if sl is None else A_X[:, base + sl.start:base + sl.stop]

    def ln_sum_stats(src_chunks, want_mean=True):
        sq_tile = A_PT[:, 0:S]
        for dc in range(DC):
            mk.op("s", "activation", sq_tile, src_chunks[dc], AF.Square)
            for ts in range(NQ):
                sl = slice(ts * 512, (ts + 1) * 512)
                if want_mean:
                    mk.mm(rowpack(ts), onesB[:, 0:1], src_chunks[dc][:, sl],
                          start=(dc == 0), stop=(dc == DC - 1))
                mk.mm(rowpack(4 + ts), onesB[:, 0:1], sq_tile[:, sl],
                      start=(dc == 0), stop=(dc == DC - 1))

    def part_l0a():
        wq = A_WS[:, 0:8 * P129]
        wk = A_WS[:, 8 * P129:16 * P129]
        wvh = A_WS[:, 16 * P129:16 * P129 + 8 * 512]
        mk.dma("g", wq, ap["l0a_wq"])
        mk.dma("g", wk, ap["l0a_wk"])
        rows32 = rowsB[32:33, 0:1360]
        mk.dma("g", rows32, ap["l0a_rows32"])
        rows64 = rowsB[64:65, 0:1024]
        mk.dma("g", rows64, ap["l0a_rows64"])
        mk.dma("g", k0m[:], ap["l0a_k0m"])
        bq = rowsB[32:33, 0:129]; bk = rowsB[32:33, 160:289]
        bv = rowsB[32:33, 320:1344]
        k0x = rowsB[0:1, 4096:4097]
        mk.dma("g", k0x, ap["l0a_k0x"])
        bcolA = nc.alloc_sbuf_tensor("bcolA", [128, 2], F32)
        mk.dma("g", bcolA[:], ap["l0a_bcols"])
        v0r = rowsB[64:65, 0:1024]
        qxr = rowsB[0:1, 0:S]
        kxr = rowsB[0:1, 2048:2048 + S]
        ptx = rowsB[64:65, 1024:1024 + 512]

        pp0 = 0
        preP0 = pp0 + (NT + 1) * 512
        qf0 = preP0 + S + 128
        kf0 = qf0 + S
        assert kf0 + S <= PTCOLS
        preP = A_PT[:, preP0:preP0 + S + 128]
        qf = A_PT[:, qf0:qf0 + S]
        kf = A_PT[:, kf0:kf0 + S]

        def PTt(kc, a, b):
            return A_PT[:, pp0 + kc * 512 + a:pp0 + kc * 512 + b]

        xb = [A_NB[:, dc * S:(dc + 1) * S] for dc in range(DC)]
        for dc in range(DC):
            mk.op("v", "tensor_copy", xb[dc], X(dc))

        sq_tile = A_PT[:, pp0:pp0 + S]
        for dc in range(DC):
            mk.op("s", "activation", sq_tile, xb[dc], AF.Square)
            for ts in range(NQ):
                sl = slice(ts * 512, (ts + 1) * 512)
                mk.mm(rowpack(ts), onesB[:, 0:1], xb[dc][:, sl],
                      start=(dc == 0), stop=(dc == DC - 1))
                mk.mm(rowpack(4 + ts), onesB[:, 0:1], sq_tile[:, sl],
                      start=(dc == 0), stop=(dc == DC - 1))
        mu = A_SC[0:1, 0:S]; rstd = A_SC[0:1, 2048:2048 + S]; rtmp = A_SC[0:1, 4096:4608]
        for ts in range(NQ):
            sl = slice(ts * 512, (ts + 1) * 512)
            mk.op("v", "tensor_scalar_mul", mu[:, sl], rowpack(ts), 1.0 / D)
            mk.op("v", "tensor_scalar_mul", rstd[:, sl], rowpack(4 + ts), 1.0 / D)
        for ts in range(NQ):
            sl = slice(ts * 512, (ts + 1) * 512)
            mk.op("v", "tensor_mul", rtmp, mu[:, sl], mu[:, sl])
            mk.op("v", "tensor_sub", rstd[:, sl], rstd[:, sl], rtmp)
            mk.op("v", "tensor_scalar_add", rstd[:, sl], rstd[:, sl], 1e-5)
            mk.op("s", "activation", rtmp, rstd[:, sl], AF.Sqrt)
            mk.op("v", "reciprocal", rstd[:, sl], rtmp)
            mk.op("v", "tensor_mul", mu[:, sl], mu[:, sl], rstd[:, sl])

        for ts in range(NQ):
            sl = slice(ts * 512, (ts + 1) * 512)
            mk.mm(PSB[:, 0:512], onesF[0:1, :], rstd[:, sl], start=True, stop=True)
            mk.mm(PSB[:, 512:1024], onesF[0:1, :], mu[:, sl], start=True, stop=True)
            for dc in range(DC):
                mk.op("v", "tensor_mul", tmpf[:], X(dc, sl), PSB[:, 0:512])
                mk.op("v", "tensor_sub", xb[dc][:, sl], tmpf[:], PSB[:, 512:1024])

        for wi, (wT, brow_x, dstf, dstx) in enumerate((
                (wq, rowsB[32:33, 128:129], qf, qxr),
                (wk, rowsB[32:33, 288:289], kf, kxr))):
            for ts in range(NQ):
                sl = slice(ts * 512, (ts + 1) * 512)
                for dc in range(DC):
                    mk.mm(PSA[:, sl], wT[:, dc * P129:dc * P129 + 128], xb[dc][:, sl],
                          start=(dc == 0), stop=(dc == DC - 1))
                for dc in range(DC):
                    mk.mm(rowpack(ts), wT[:, dc * P129 + 128:dc * P129 + 129],
                          xb[dc][:, sl], start=(dc == 0), stop=False)
                mk.mm(rowpack(ts), brow_x, onesB[32:33, 0:512], start=False, stop=True)
            mk.op("v", "tensor_scalar", dstf, PSA[:, 0:S], bcolA[:, wi:wi + 1], None,
                  op0=ALU.add)
            for ts in range(NQ):
                mk.op("v", "tensor_copy", dstx[:, ts * 512:(ts + 1) * 512], rowpack(ts))

        wv_img = ap["l0a_wv"]
        for oh in range(2):
            mk.dma("g", wvh, wv_img.rearrange("p (dc o) -> p dc o", o=1024)[:, :, oh * 512:(oh + 1) * 512])
            for tc in range(NT):
                tsl = slice(tc * 128, (tc + 1) * 128)
                osl = slice(oh * 512, (oh + 1) * 512)
                for dc in range(DC):
                    mk.mm(PSB[:, osl], xb[dc][:, tsl],
                          wvh[:, dc * 512:(dc + 1) * 512],
                          start=(dc == 0), stop=False)
                mk.mm(PSB[:, osl], onesB[32:33, 0:128], bv[:, oh * 512:(oh + 1) * 512],
                      start=False, stop=True)
                mk.op("v", "tensor_copy", A_VT[:, tc * 1024 + oh * 512:tc * 1024 + (oh + 1) * 512],
                      PSB[:, osl])

        for qb in range(NQ):
            qsl = slice(qb * 512, (qb + 1) * 512)
            for qt in range(4):
                qtile = slice(qb * 512 + qt * 128, qb * 512 + (qt + 1) * 128)
                for kt in range(NQ):
                    ksl = slice(kt * 512, (kt + 1) * 512)
                    mk.mm(PSA[:, ksl], qf[:, qtile], kf[:, ksl], start=True, stop=False)
                    mk.mm(PSA[:, ksl], qxr[:, qtile], kxr[:, ksl], start=False, stop=True)
                mk.mm(PSB[:, 1024:1025], qf[:, qtile], k0m[:, 0:1], start=True, stop=False)
                mk.mm(PSB[:, 1024:1025], qxr[:, qtile], k0x[:, 0:1], start=False, stop=True)
                mk.op("v", "reduce_max", negm[:, 0:1], PSA[:, 0:S], axis=AX.X)
                mk.op("v", "tensor_max", negm[:, 0:1], negm[:, 0:1], PSB[:, 1024:1025])
                mk.op("v", "tensor_scalar_mul", negm[:, 0:1], negm[:, 0:1], -1.0)
                mk.op("s", "activation", preP[:, 0:S], PSA[:, 0:S], AF.Exp,
                      bias=negm[:, 0:1])
                mk.op("s", "activation", preP[:, S:S + 1], PSB[:, 1024:1025], AF.Exp,
                      bias=negm[:, 0:1])
                for kc in range(NT):
                    slot = (kc % 2) * 128
                    mk.op("t", "transpose", PST[:, slot:slot + 128],
                          preP[:, kc * 128:(kc + 1) * 128], ident[:])
                    mk.op("v", "tensor_copy", PTt(kc, qt * 128, (qt + 1) * 128),
                          PST[:, slot:slot + 128])
                mk.op("t", "transpose", PST[0:1, 256:384], preP[:, S:S + 1], ident[:])
                mk.op("v", "tensor_copy", ptx[:, qt * 128:(qt + 1) * 128],
                      PST[0:1, 256:384])
            import os
            if os.environ.get("MK_STOP") == "evac":
                return
            for kc in range(NT):
                mk.mm(PSB[0:1, 0:512], onesB[:, 0:1], PTt(kc, 0, 512),
                      start=(kc == 0), stop=False)
            mk.mm(PSB[0:1, 0:512], onesB[64:65, 0:1], ptx[:, 0:512], start=False, stop=True)
            sums_r = A_SC[32:33, 0:512]
            mk.op("v", "tensor_copy", sums_r, PSB[0:1, 0:512])
            mk.op("v", "reciprocal", sums_r, sums_r)
            mk.mm(PSB[:, 0:512], onesF[32:33, :], sums_r, start=True, stop=True)
            mk.op("v", "tensor_copy", tmpg[:], PSB[:, 0:512])
            mk.mm(PSB[:, 512:1024], onesF[0:1, :], rstd[:, qsl], start=True, stop=True)
            mk.mm(PSB[:, 1024:1536], onesF[0:1, :], mu[:, qsl], start=True, stop=True)
            if os.environ.get("MK_STOP") == "sums":
                return
            for half in range(2):
                for dci in range(4):
                    dc = half * 4 + dci
                    csl = slice(dci * 512, (dci + 1) * 512)
                    for kc in range(NT):
                        mk.mm(PSA[:, csl],
                              A_VT[:, kc * 1024 + dc * 128:kc * 1024 + (dc + 1) * 128],
                              PTt(kc, 0, 512),
                              start=(kc == 0), stop=False)
                    mk.mm(PSA[:, csl], v0r[:, dc * 128:(dc + 1) * 128], ptx[:, 0:512],
                          start=False, stop=True)
                for dci in range(4):
                    dc = half * 4 + dci
                    csl = slice(dci * 512, (dci + 1) * 512)
                    mk.op("v", "tensor_mul", tmpf[:], X(dc, qsl), PSB[:, 512:1024])
                    mk.op("v", "tensor_sub", tmpf[:], tmpf[:], PSB[:, 1024:1536])
                    mk.op("v", "tensor_add", X(dc, qsl), X(dc, qsl), tmpf[:])
                    mk.op("v", "tensor_mul", tmpf[:], PSA[:, csl], tmpg[:])
                    mk.op("v", "tensor_add", X(dc, qsl), X(dc, qsl), tmpf[:])


    def part_l0fa():
        NG = 8
        mk.dma("g", rowsB[32:33, 0:4096], ap["l0f_rows"])
        qkvb = rowsB[32:33, 0:3072]
        outb = rowsB[32:33, 3072:4096]
        n1sb = nc.alloc_sbuf_tensor("n1sb", [128, 8], BF16)
        n2sb = nc.alloc_sbuf_tensor("n2sb", [128, 8], BF16)
        mk.dma("g", n1sb[:], ap["l0f_n1"])
        mk.dma("g", n2sb[:], ap["l0f_n2"])
        ap["__n2sb__"] = n2sb
        bqk = nc.alloc_sbuf_tensor("bqk", [128, 16], F32)
        boc = nc.alloc_sbuf_tensor("boc", [128, 8], F32)
        ap["__bqk__"] = bqk
        mk.dma("g", bqk[:], ap["l0f_bqk"])
        mk.dma("g", boc[:], ap["l0f_boc"])
        vt_q = 0
        vt_k = 2 * S
        vt_v0 = 4 * S
        vt_cos = 4 * S + 2080
        vt_sin = 5 * S + 2080
        cosT = A_VT[:, vt_cos:vt_cos + S]
        sinT = A_VT[:, vt_sin:vt_sin + S]
        mk.dma("y", cosT, ap["c_cos"])
        mk.dma("y", sinT, ap["c_sin"])
        perm = nc.alloc_sbuf_tensor("perm", [128, 128], BF16)
        mk.dma("y", perm[:], ap["c_permT"])
        identF2 = identG

        def qh(h):
            return A_VT[0:65, vt_q + h * S:vt_q + (h + 1) * S]

        def kh(h):
            return A_VT[0:65, vt_k + h * S:vt_k + (h + 1) * S]
        bbuf = A_PT[:, 0:S]
        proll0 = S
        attst0 = S + 2048
        attn_st = A_PT[:, attst0:attst0 + S]

        for dc in range(DC):
            mk.dma("y", hx1[dc * 128:(dc + 1) * 128, :], X(dc))

        x1b = [A_NB[:, dc * S:(dc + 1) * S] for dc in range(DC)]
        for dc in range(DC):
            mk.op("v", "tensor_copy", x1b[dc], X(dc))
        ln_sum_stats(x1b)
        mu = A_SC[0:1, 0:S]
        rstd = A_SC[0:1, 2048:2048 + S]
        rtmp = A_SC[0:1, 4096:4608]
        rmss = A_SC[32:33, 512:512 + S]
        for ts in range(NQ):
            sl = slice(ts * 512, (ts + 1) * 512)
            mk.op("v", "tensor_scalar_mul", mu[:, sl], rowpack(ts), 1.0 / D)
            mk.op("v", "tensor_scalar_mul", rstd[:, sl], rowpack(4 + ts), 1.0 / D)
            mk.op("v", "tensor_mul", rtmp, mu[:, sl], mu[:, sl])
            mk.op("v", "tensor_sub", rstd[:, sl], rstd[:, sl], rtmp)
            mk.op("v", "tensor_scalar_add", rstd[:, sl], rstd[:, sl], 1e-5)
            mk.op("s", "activation", rtmp, rstd[:, sl], AF.Sqrt)
            mk.op("v", "reciprocal", rstd[:, sl], rtmp)
            mk.op("v", "tensor_mul", mu[:, sl], mu[:, sl], rstd[:, sl])
            mk.op("v", "tensor_mul", rtmp, rstd[:, sl], rstd[:, sl])
            mk.op("v", "tensor_scalar", rtmp, rtmp, -1e-5, 1.0 + 1e-6,
                  op0=ALU.mult, op1=ALU.add)
            mk.op("s", "activation", rtmp, rtmp, AF.Sqrt)
            mk.op("v", "reciprocal", rmss[:, sl], rtmp)

        for ts in range(NQ):
            sl = slice(ts * 512, (ts + 1) * 512)
            mk.mm(PSB[:, 0:512], onesF[0:1, :], rstd[:, sl], start=True, stop=True)
            mk.mm(PSB[:, 512:1024], onesF[0:1, :], mu[:, sl], start=True, stop=True)
            mk.mm(PSB[:, 1024:1536], onesF[32:33, :], rmss[:, sl], start=True, stop=True)
            for dc in range(DC):
                mk.op("v", "tensor_mul", tmpf, X(dc, sl), PSB[:, 0:512])
                mk.op("v", "tensor_sub", X(dc, sl), tmpf, PSB[:, 512:1024])
                mk.op("v", "scalar_tensor_tensor", x1b[dc][:, sl], X(dc, sl),
                      n1sb[:, dc:dc + 1], PSB[:, 1024:1536],
                      op0=ALU.mult, op1=ALU.mult)
        a = x1b

        for g in range(NG):
            ws0 = (g % 2) * 3072
            wg = A_WS[:, ws0:ws0 + 3072]
            mk.dma("g", wg, ap["l0f_wqkv"][:, g * 3072:(g + 1) * 3072])
            for which in range(2):
                bcol = bqk[:, which * 8 + g:which * 8 + g + 1]
                for ts in range(NQ):
                    sl = slice(ts * 512, (ts + 1) * 512)
                    for dc in range(DC):
                        mk.mm(PSA[:, sl], wg[:, dc * 384 + which * 128:dc * 384 + which * 128 + 128],
                              a[dc][:, sl], start=(dc == 0), stop=(dc == DC - 1))
                for ts in range(NQ):
                    sl = slice(ts * 512, (ts + 1) * 512)
                    mk.op("v", "scalar_tensor_tensor", bbuf[:, sl], PSA[:, sl], bcol,
                          sinT[:, sl], op0=ALU.add, op1=ALU.mult)
                    mk.op("v", "scalar_tensor_tensor", tmpf, PSA[:, sl], bcol,
                          cosT[:, sl], op0=ALU.add, op1=ALU.mult)
                    mk.mm(PSB[:, 0:512], perm[:], bbuf[:, sl], start=True, stop=True)
                    for h in range(2):
                        dst = qh(h) if which == 0 else kh(h)
                        mk.op("v", "tensor_add", dst[0:64, sl.start:sl.stop],
                              tmpf[h * 64:(h + 1) * 64, :],
                              PSB[h * 64:(h + 1) * 64, 0:512])
            for h in range(2):
                mk.op("v", "memset", kh(h)[64:65, 0:S], -1.0)
            mk.op("v", "memset", A_VT[:, vt_v0:vt_v0 + 2080], 1.0)
            for tc in range(NT):
                for dc in range(DC):
                    mk.mm(PSB[:, 512:640], a[dc][:, tc * 128:(tc + 1) * 128],
                          wg[:, dc * 384 + 256:dc * 384 + 384],
                          start=(dc == 0), stop=False)
                mk.mm(PSB[:, 512:640], onesB[32:33, 0:128],
                      qkvb[:, 2048 + g * 128:2048 + (g + 1) * 128],
                      start=False, stop=True)
                mk.op("v", "tensor_copy", A_VT[:, vt_v0 + tc * 130:vt_v0 + tc * 130 + 64],
                      PSB[:, 512:576])
                mk.op("v", "tensor_copy", A_VT[:, vt_v0 + tc * 130 + 65:vt_v0 + tc * 130 + 129],
                      PSB[:, 576:640])
            maxc = maxcG
            for h in range(2):
                for qt in range(NT):
                    qtl = slice(qt * 128, (qt + 1) * 128)
                    for kt in range(NQ):
                        mk.mm(PSA[:, kt * 512:(kt + 1) * 512], qh(h)[0:64, qtl],
                              kh(h)[0:64, kt * 512:(kt + 1) * 512], start=True, stop=True)
                    mk.op("v", "reduce_max", maxc[:, 0:1], PSA[:, 0:S], axis=AX.X)
                    mk.op("t", "transpose", PSB[0:1, 512:640], maxc[:, 0:1], identF2[:])
                    mk.op("v", "tensor_copy", qh(h)[64:65, qt * 128:(qt + 1) * 128],
                          PSB[0:1, 512:640])
            for h in range(2):
                for qt in range(NQ):
                    qsl = slice(qt * 512, (qt + 1) * 512)
                    pvslot = PSB[0:65, ((h * NQ + qt) % 2) * 512:((h * NQ + qt) % 2) * 512 + 512]
                    for kc2 in range(0, NT, 2):
                        slot = (kc2 // 2) % 2
                        pa = PSA[:, slot * 1024:slot * 1024 + 1024]
                        for j in range(2):
                            kc = kc2 + j
                            mk.mm(pa[:, j * 512:(j + 1) * 512],
                                  kh(h)[:, kc * 128:(kc + 1) * 128],
                                  qh(h)[:, qsl], start=True, stop=True)
                        roll = A_PT[:, proll0 + slot * 1024:proll0 + slot * 1024 + 1024]
                        mk.op("s", "activation", roll, pa, AF.Exp, scale=0.125)
                        for j in range(2):
                            kc = kc2 + j
                            mk.mm(pvslot,
                                  A_VT[:, vt_v0 + kc * 130 + h * 65:vt_v0 + kc * 130 + (h + 1) * 65],
                                  roll[:, j * 512:(j + 1) * 512],
                                  start=(kc == 0), stop=(kc == NT - 1))
                    mk.op("v", "tensor_copy", attn_st[h * 64:(h + 1) * 64, qsl],
                          pvslot[0:64, :])
                    mk.op("v", "tensor_copy",
                          A_SC[32:33, (h * NQ + qt) * 512:(h * NQ + qt) * 512 + 512],
                          pvslot[64:65, :])
            mk.op("v", "reciprocal", A_SC[32:33, 0:2 * NQ * 512],
                  A_SC[32:33, 0:2 * NQ * 512])
            for h in range(2):
                for qt in range(NQ):
                    qsl = slice(qt * 512, (qt + 1) * 512)
                    mk.mm(PSB[:, 1024:1536], onesF[32:33, :],
                          A_SC[32:33, (h * NQ + qt) * 512:(h * NQ + qt) * 512 + 512],
                          start=True, stop=True)
                    mk.op("v", "tensor_mul", attn_st[h * 64:(h + 1) * 64, qsl],
                          attn_st[h * 64:(h + 1) * 64, qsl], PSB[0:64, 1024:1536])
            mk.dma("y", hattn[g * 128:(g + 1) * 128, :], attn_st)

        wo = A_WS[:, 0:8192]
        mk.dma("g", wo, ap["l0f_wout"])
        for dc in range(DC):
            mk.dma("y", A_PT[:, dc * S:(dc + 1) * S], hattn[dc * 128:(dc + 1) * 128, :])
        for oc in range(DC):
            for ts in range(NQ):
                sl = slice(ts * 512, (ts + 1) * 512)
                for dc in range(DC):
                    mk.mm(PSA[:, sl], wo[:, dc * 1024 + oc * 128:dc * 1024 + (oc + 1) * 128],
                          A_PT[:, dc * S + ts * 512:dc * S + (ts + 1) * 512],
                          start=(dc == 0), stop=(dc == DC - 1))
                mk.op("v", "scalar_tensor_tensor", X(oc, sl), PSA[:, sl],
                      boc[:, oc:oc + 1], X(oc, sl), op0=ALU.add, op1=ALU.add)


    def part_l0fm(have_l0fa):
        NE, NH = 8, 16
        if not have_l0fa:
            n2sb = nc.alloc_sbuf_tensor("n2sb_m", [128, 8], BF16)
            mk.dma("g", n2sb[:], ap["l0f_n2"])
            for dc in range(DC):
                mk.dma("g", hx1[dc * 128:(dc + 1) * 128, :], ap["tx1"][dc * 128:(dc + 1) * 128, :])
        else:
            n2sb = ap["__n2sb__"]
        gwsb = nc.alloc_sbuf_tensor("gwsb", [128, 64], F32)
        mk.dma("g", gwsb[:], ap["l0f_gw"])
        n2f = nc.alloc_sbuf_tensor("n2f", [128, 8], F32)
        mk.dma("g", n2f[:], ap["l0f_n2"])
        rmsscol = nc.alloc_sbuf_tensor("rmsscol", [128, 16], F32)
        glt = nc.alloc_sbuf_tensor("glt", [128, 8], F32)
        v1 = nc.alloc_sbuf_tensor("v1", [128, 1], F32)
        v2 = nc.alloc_sbuf_tensor("v2", [128, 1], F32)
        sg = nc.alloc_sbuf_tensor("sg", [128, 1], F32)
        og = nc.alloc_sbuf_tensor("og", [128, 1], F32)
        tmp8 = nc.alloc_sbuf_tensor("tmp8", [128, 8], F32)
        mask1 = nc.alloc_sbuf_tensor("mask1", [128, 8], F32)
        identF = identG
        stage8 = tmpg[0:8, 0:128]
        b1sb = ap["__bqk__"]
        b2sb = ap["__bqkL__"]

        m = [A_NB[:, dc * S:(dc + 1) * S] for dc in range(DC)]

        for dc in range(DC):
            mk.op("v", "tensor_copy", m[dc], X(dc))
        ln_sum_stats(m, want_mean=False)
        rmss = A_SC[0:1, 0:S]
        rtmp = A_SC[0:1, 4096:4608]
        for ts in range(NQ):
            sl = slice(ts * 512, (ts + 1) * 512)
            mk.op("v", "tensor_scalar", rtmp, rowpack(4 + ts), 1.0 / D, 1e-6,
                  op0=ALU.mult, op1=ALU.add)
            mk.op("s", "activation", rtmp, rtmp, AF.Sqrt)
            mk.op("v", "reciprocal", rmss[:, sl], rtmp)
        for ts in range(NQ):
            sl = slice(ts * 512, (ts + 1) * 512)
            mk.mm(PSB[:, 0:512], onesF[0:1, :], rmss[:, sl], start=True, stop=True)
            for dc in range(DC):
                mk.op("v", "scalar_tensor_tensor", m[dc][:, sl], X(dc, sl),
                      n2sb[:, dc:dc + 1], PSB[:, 0:512], op0=ALU.mult, op1=ALU.mult)

        for dc in range(DC):
            mk.op("v", "tensor_scalar", gwsb[:, dc * 8:(dc + 1) * 8],
                  gwsb[:, dc * 8:(dc + 1) * 8], n2f[:, dc:dc + 1], None, op0=ALU.mult)
        for tt in range(NT):
            mk.op("t", "transpose", PSB[:, 1032:1033],
                  rmss[:, tt * 128:(tt + 1) * 128], identF[0:1, 0:1])
            mk.op("v", "tensor_copy", rmsscol[:, tt % 16:tt % 16 + 1], PSB[:, 1032:1033])
            tsl = slice(tt * 128, (tt + 1) * 128)
            for dc in range(DC):
                mk.mm(PSB[:, 1024:1032], X(dc)[:, tsl], gwsb[:, dc * 8:(dc + 1) * 8],
                      start=(dc == 0), stop=(dc == DC - 1))
            mk.op("v", "tensor_scalar", glt[:], PSB[:, 1024:1032],
                  rmsscol[:, tt % 16:tt % 16 + 1], None, op0=ALU.mult)
            mk.op("v", "reduce_max", v1[:, 0:1], glt[:], axis=AX.X)
            mk.op("v", "tensor_scalar", mask1[:], glt[:], v1[:, 0:1], None, op0=ALU.is_equal)
            mk.op("v", "scalar_tensor_tensor", tmp8[:], mask1[:], -1e30, glt[:],
                  op0=ALU.mult, op1=ALU.add)
            mk.op("v", "reduce_max", v2[:, 0:1], tmp8[:], axis=AX.X)
            mk.op("v", "tensor_sub", sg[:, 0:1], v2[:, 0:1], v1[:, 0:1])
            mk.op("s", "activation", sg[:, 0:1], sg[:, 0:1], AF.Exp)
            mk.op("v", "tensor_scalar_add", sg[:, 0:1], sg[:, 0:1], 1.0)
            mk.op("v", "reciprocal", sg[:, 0:1], sg[:, 0:1])
            mk.op("v", "tensor_scalar", og[:, 0:1], sg[:, 0:1], -1.0, 1.0,
                  op0=ALU.mult, op1=ALU.add)
            mk.op("v", "tensor_scalar", mask1[:], mask1[:], sg[:, 0:1], None, op0=ALU.mult)
            mk.op("v", "tensor_scalar", tmp8[:], tmp8[:], v2[:, 0:1], None, op0=ALU.is_equal)
            mk.op("v", "tensor_scalar", tmp8[:], tmp8[:], og[:, 0:1], None, op0=ALU.mult)
            combtile = glt
            mk.op("v", "tensor_add", combtile[:], mask1[:], tmp8[:])
            mk.op("t", "transpose", PSB[0:8, 0:128], combtile[:], identF[:])
            mk.op("v", "tensor_copy", stage8[:], PSB[0:8, 0:128])
            mk.dma("y", hcomb[:, tt * 128:(tt + 1) * 128], stage8[:])

        for dc in range(DC):
            for ts in range(NQ):
                sl = slice(ts * 512, (ts + 1) * 512)
                mk.dma("y", tmpg[:], hx1[dc * 128:(dc + 1) * 128, ts * 512:(ts + 1) * 512])
                mk.op("v", "tensor_add", X(dc, sl), X(dc, sl), tmpg[:])

        combrow = A_SC[0:1, 2048:2048 + S]
        combrow32 = A_SC[32:33, 0:S]
        brow = rowsB[0:1, 2048:4096]
        TH = S // 2
        NTH = max(1, TH // 512)
        THW = min(512, TH)

        me0 = 0
        w12s = (DC * TH, DC * TH + 2048)
        w3s = (DC * TH + 4096, DC * TH)
        for e in range(NE):
            mk.dma("g", combrow, hcomb[e:e + 1, :])
            mk.dma("g", combrow32, hcomb[e:e + 1, :])
            mk.dma("g", b1sb[:], ap["l0f_b1"][e])
            mk.dma("g", b2sb[:], ap["l0f_b2c"][e])
            for tbh in range(2):
                for th in range(NTH):
                    sl = slice(tbh * TH + th * THW, tbh * TH + (th + 1) * THW)
                    mk.mm(PSB[:, 1024:1024 + THW], onesF[0:1, :],
                          combrow[:, sl.start:sl.stop], start=True, stop=True)
                    mk.op("v", "tensor_copy",
                          rowsB[:, 2048 + th * THW:2048 + (th + 1) * THW],
                          PSB[:, 1024:1024 + THW])
                for hc in range(NH):
                    wsl = w12s[hc % 2]
                    w12 = A_WS[:, 0:1] if False else A_PT[:, wsl:wsl + 2048]
                    mk.dma("g", w12, ap["l0f_w12"][e, :, hc * 2048:(hc + 1) * 2048])
                    pa = PSA[:, (hc % 2) * TH:(hc % 2) * TH + TH]
                    pb = PSB[:, 0:TH]
                    for th in range(NTH):
                        sl = slice(tbh * TH + th * THW, tbh * TH + (th + 1) * THW)
                        psl = slice(th * THW, (th + 1) * THW)
                        for dc in range(DC):
                            mk.mm(pa[:, psl], w12[:, dc * 256:dc * 256 + 128],
                                  m[dc][:, sl], start=(dc == 0), stop=(dc == DC - 1))
                        for dc in range(DC):
                            mk.mm(pb[:, psl], w12[:, dc * 256 + 128:dc * 256 + 256],
                                  m[dc][:, sl], start=(dc == 0), stop=(dc == DC - 1))
                    hrow = A_VT[:, hc * TH:(hc + 1) * TH]
                    mk.op("s", "activation", hrow, pa, AF.Silu, bias=b1sb[:, hc:hc + 1])
                    mk.op("v", "scalar_tensor_tensor", hrow, pb, b2sb[:, hc:hc + 1],
                          hrow, op0=ALU.add, op1=ALU.mult)
                    for th in range(NTH):
                        psl = slice(th * THW, (th + 1) * THW)
                        mk.op("v", "tensor_mul", hrow[:, psl], hrow[:, psl],
                              rowsB[:, 2048 + th * THW:2048 + (th + 1) * THW])
                b3row = A_SC[32:33, 2048:2048 + 1024]
                mk.dma("g", b3row, ap["l0f_b3"][e:e + 1, 0:1024])
                for p2 in range(4):
                    wsl = w3s[p2 % 2]
                    w3c = A_PT[:, wsl:wsl + 4096]
                    mk.dma("g", w3c, ap["l0f_w3"][e, :, :, p2 * 256:(p2 + 1) * 256])
                    for dc2 in range(2):
                        dc = p2 * 2 + dc2
                        py = PSA[:, dc2 * TH:(dc2 + 1) * TH]
                        for th in range(NTH):
                            sl = slice(tbh * TH + th * THW, tbh * TH + (th + 1) * THW)
                            psl = slice(th * THW, (th + 1) * THW)
                            for hc in range(NH):
                                mk.mm(py[:, psl], w3c[:, hc * 256 + dc2 * 128:hc * 256 + (dc2 + 1) * 128],
                                      A_VT[:, hc * TH + th * THW:hc * TH + (th + 1) * THW],
                                      start=(hc == 0), stop=False)
                            mk.mm(py[:, psl], b3row[:, dc * 128:(dc + 1) * 128],
                                  combrow32[:, sl.start:sl.stop], start=False, stop=True)
                            mk.op("v", "tensor_add", X(dc, sl), X(dc, sl), py[:, psl])


    def part_l1a():
        NG = 8
        NW = S // 512
        mk.dma("g", rowsB[32:33, 0:4096], ap["l1a_rows"])
        qkvb = rowsB[32:33, 0:3072]
        outb = rowsB[32:33, 3072:4096]
        bqkL = ap["__bqkL__"]
        bocL = nc.alloc_sbuf_tensor("bocL", [128, 8], F32)
        mk.dma("g", bqkL[:], ap["l1a_bqk"])
        mk.dma("g", bocL[:], ap["l1a_boc"])
        padc = rowsB[64:65, 2048:2113]
        mk.op("v", "memset", padc[0:1, 0:64], -512.0)
        mk.op("v", "memset", padc[0:1, 64:65], 512.0)
        epad = rowsB[64:65, 2176:2688]

        vt_v0 = 0
        vt_q = 2080
        vt_k = 2080 + 2 * S
        attst0 = S + 2048
        attn_st = A_PT[:, attst0:attst0 + S]
        proll0 = S
        msk = A_PT[:, attst0 + S:attst0 + S + 4096]
        mk.dma("g", msk, ap["c_l1mask"])
        identF3 = identG
        maxc1 = maxcG

        def qh(h):
            return A_VT[0:65, vt_q + h * S:vt_q + (h + 1) * S]

        def kh(h):
            return A_VT[0:65, vt_k + h * S:vt_k + (h + 1) * S]

        zb = [A_NB[:, dc * S:(dc + 1) * S] for dc in range(DC)]
        for dc in range(DC):
            mk.op("v", "tensor_copy", zb[dc], X(dc))
        ln_sum_stats(zb)
        mu = A_SC[0:1, 0:S]
        rstd = A_SC[0:1, 2048:2048 + S]
        rtmp = A_SC[0:1, 4096:4608]
        for ts in range(NQ):
            sl = slice(ts * 512, (ts + 1) * 512)
            mk.op("v", "tensor_scalar_mul", mu[:, sl], rowpack(ts), 1.0 / D)
            mk.op("v", "tensor_scalar_mul", rstd[:, sl], rowpack(4 + ts), 1.0 / D)
            mk.op("v", "tensor_mul", rtmp, mu[:, sl], mu[:, sl])
            mk.op("v", "tensor_sub", rstd[:, sl], rstd[:, sl], rtmp)
            mk.op("v", "tensor_scalar_add", rstd[:, sl], rstd[:, sl], 1e-5)
            mk.op("s", "activation", rtmp, rstd[:, sl], AF.Sqrt)
            mk.op("v", "reciprocal", rstd[:, sl], rtmp)
            mk.op("v", "tensor_mul", mu[:, sl], mu[:, sl], rstd[:, sl])
        for ts in range(NQ):
            sl = slice(ts * 512, (ts + 1) * 512)
            mk.mm(PSB[:, 0:512], onesF[0:1, :], rstd[:, sl], start=True, stop=True)
            mk.mm(PSB[:, 512:1024], onesF[0:1, :], mu[:, sl], start=True, stop=True)
            for dc in range(DC):
                mk.op("v", "tensor_mul", tmpf, X(dc, sl), PSB[:, 0:512])
                mk.op("v", "tensor_sub", zb[dc][:, sl], tmpf, PSB[:, 512:1024])

        for g in range(NG):
            ws0 = (g % 2) * 3072
            wg = A_WS[:, ws0:ws0 + 3072]
            mk.dma("g", wg, ap["l1a_wqkv"][:, g * 3072:(g + 1) * 3072])
            for which in range(2):
                bcol = bqkL[:, which * 8 + g:which * 8 + g + 1]
                for ts in range(NQ):
                    sl = slice(ts * 512, (ts + 1) * 512)
                    for dc in range(DC):
                        mk.mm(PSA[:, sl], wg[:, dc * 384 + which * 128:dc * 384 + which * 128 + 128],
                              zb[dc][:, sl], start=(dc == 0), stop=(dc == DC - 1))
                for ts in range(NQ):
                    sl = slice(ts * 512, (ts + 1) * 512)
                    for h in range(2):
                        dst = qh(h) if which == 0 else kh(h)
                        mk.op("v", "tensor_scalar", dst[0:64, sl.start:sl.stop],
                              PSA[h * 64:(h + 1) * 64, sl],
                              bcol[h * 64:(h + 1) * 64, 0:1], None, op0=ALU.add)
            for h in range(2):
                mk.op("v", "memset", kh(h)[64:65, 0:S], -1.0)
            mk.op("v", "memset", A_VT[:, vt_v0:vt_v0 + 2080], 1.0)
            for tc in range(NT):
                for dc in range(DC):
                    mk.mm(PSB[:, 512:640], zb[dc][:, tc * 128:(tc + 1) * 128],
                          wg[:, dc * 384 + 256:dc * 384 + 384],
                          start=(dc == 0), stop=False)
                mk.mm(PSB[:, 512:640], onesB[32:33, 0:128],
                      qkvb[:, 2048 + g * 128:2048 + (g + 1) * 128],
                      start=False, stop=True)
                mk.op("v", "tensor_copy", A_VT[:, vt_v0 + tc * 130:vt_v0 + tc * 130 + 64],
                      PSB[:, 512:576])
                mk.op("v", "tensor_copy", A_VT[:, vt_v0 + tc * 130 + 65:vt_v0 + tc * 130 + 129],
                      PSB[:, 576:640])
            for h in range(2):
                for w in range(NW):
                    qsl = slice(w * 512, (w + 1) * 512)
                    kcs = [kc for kc in range(12) if 0 <= (w - 1) * 512 + kc * 128 < S]
                    has_lpad = (w == 0)
                    has_rpad = (w == NW - 1)
                    npad = (1 if has_lpad else 0) + (1 if has_rpad else 0)
                    kt0 = (w - 1) * 512 + kcs[0] * 128
                    kt1 = (w - 1) * 512 + (kcs[-1] + 1) * 128
                    if npad:
                        mk.mm(rowpack(6), onesB[0:64, 0:1], qh(h)[0:64, qsl],
                              start=True, stop=True)
                    for qt in range(4):
                        qtl = slice(w * 512 + qt * 128, w * 512 + (qt + 1) * 128)
                        nk = kt1 - kt0
                        for i, c0 in enumerate(range(kt0, kt1, 512)):
                            cw = min(512, kt1 - c0)
                            mk.mm(PSA[:, i * 512:i * 512 + cw], qh(h)[0:64, qtl],
                                  kh(h)[0:64, c0:c0 + cw], start=True, stop=True)
                        mk.op("v", "reduce_max", maxc1[:, 0:1], PSA[:, 0:nk], axis=AX.X)
                        mk.op("t", "transpose", PSB[0:1, 512:640], maxc1[:, 0:1], identF3[:])
                        mk.op("v", "tensor_copy", qh(h)[64:65, qtl.start:qtl.stop],
                              PSB[0:1, 512:640])
                    if npad:
                        mrow = A_SC[0:1, 4096:4608]
                        spadrow = A_SC[0:1, 0:512]
                        mk.op("v", "tensor_scalar_mul", spadrow, rowpack(6), -1.0)
                        mk.op("v", "tensor_copy", mrow[:, 0:512], qh(h)[64:65, qsl])
                        mk.op("v", "tensor_max", mrow[:, 0:512], mrow[:, 0:512], spadrow)
                        mk.op("v", "tensor_copy", qh(h)[64:65, qsl.start:qsl.stop], mrow[:, 0:512])
                        mk.op("v", "tensor_sub", spadrow, spadrow, mrow[:, 0:512])
                        mk.op("s", "activation", epad, spadrow, AF.Exp)
                    pvslot = PSB[0:65, ((h * NW + w) % 2) * 512:((h * NW + w) % 2) * 512 + 512]
                    for idx, kc in enumerate(kcs):
                        kt = (w - 1) * 512 + kc * 128
                        ktc = kt // 128
                        slot = idx % 2
                        pa = PSA[:, slot * 1024:slot * 1024 + 512]
                        mk.mm(pa, kh(h)[:, kt:kt + 128], qh(h)[:, qsl],
                              start=True, stop=True)
                        roll = A_PT[:, proll0 + slot * 1024:proll0 + slot * 1024 + 512]
                        mk.op("s", "activation", roll, pa, AF.Exp)
                        if kc < 4:
                            ci = kc
                        elif kc >= 8:
                            ci = kc - 4
                        else:
                            ci = None
                        if ci is not None:
                            mk.op("v", "tensor_mul", roll, roll, msk[:, ci * 512:(ci + 1) * 512])
                        mk.mm(pvslot,
                              A_VT[:, vt_v0 + ktc * 130 + h * 65:vt_v0 + ktc * 130 + (h + 1) * 65],
                              roll, start=(idx == 0), stop=(npad == 0 and idx == len(kcs) - 1))
                    for t in range(npad):
                        mk.mm(pvslot, padc[0:1, :], epad,
                              start=False, stop=(t == npad - 1))
                    mk.op("v", "tensor_copy", attn_st[h * 64:(h + 1) * 64, qsl],
                          pvslot[0:64, :])
                    mk.op("v", "tensor_copy",
                          A_SC[32:33, (h * NW + w) * 512:(h * NW + w) * 512 + 512],
                          pvslot[64:65, :])
            mk.op("v", "reciprocal", A_SC[32:33, 0:2 * NW * 512],
                  A_SC[32:33, 0:2 * NW * 512])
            for h in range(2):
                for w in range(NW):
                    qsl = slice(w * 512, (w + 1) * 512)
                    mk.mm(PSB[:, 1024:1536], onesF[32:33, :],
                          A_SC[32:33, (h * NW + w) * 512:(h * NW + w) * 512 + 512],
                          start=True, stop=True)
                    mk.op("v", "tensor_mul", attn_st[h * 64:(h + 1) * 64, qsl],
                          attn_st[h * 64:(h + 1) * 64, qsl], PSB[0:64, 1024:1536])
            mk.dma("y", hattn[g * 128:(g + 1) * 128, :], attn_st)

        wo = A_WS[:, 0:8192]
        mk.dma("g", wo, ap["l1a_wout"])
        for dc in range(DC):
            mk.dma("y", A_PT[:, dc * S:(dc + 1) * S], hattn[dc * 128:(dc + 1) * 128, :])
        for oc in range(DC):
            for ts in range(NQ):
                sl = slice(ts * 512, (ts + 1) * 512)
                for dc in range(DC):
                    mk.mm(PSA[:, sl], wo[:, dc * 1024 + oc * 128:dc * 1024 + (oc + 1) * 128],
                          A_PT[:, dc * S + ts * 512:dc * S + (ts + 1) * 512],
                          start=(dc == 0), stop=(dc == DC - 1))
                mk.op("v", "scalar_tensor_tensor", X(oc, sl), PSA[:, sl],
                      bocL[:, oc:oc + 1], X(oc, sl), op0=ALU.add, op1=ALU.add)


    def part_l1f():
        NP = 2048
        MT = S // 128
        rg = [list(range(n_cores))]
        ub = [A_NB[:, dc * S:(dc + 1) * S] for dc in range(DC)]
        for dc in range(DC):
            mk.op("v", "tensor_copy", ub[dc], X(dc))
        ln_sum_stats(ub)
        mu = A_SC[0:1, 0:S]
        rstd = A_SC[0:1, 2048:2048 + S]
        rtmp = A_SC[0:1, 4096:4608]
        for ts in range(NQ):
            sl = slice(ts * 512, (ts + 1) * 512)
            mk.op("v", "tensor_scalar_mul", mu[:, sl], rowpack(ts), 1.0 / D)
            mk.op("v", "tensor_scalar_mul", rstd[:, sl], rowpack(4 + ts), 1.0 / D)
            mk.op("v", "tensor_mul", rtmp, mu[:, sl], mu[:, sl])
            mk.op("v", "tensor_sub", rstd[:, sl], rstd[:, sl], rtmp)
            mk.op("v", "tensor_scalar_add", rstd[:, sl], rstd[:, sl], 1e-5)
            mk.op("s", "activation", rtmp, rstd[:, sl], AF.Sqrt)
            mk.op("v", "reciprocal", rstd[:, sl], rtmp)
            mk.op("v", "tensor_mul", mu[:, sl], mu[:, sl], rstd[:, sl])
        for ts in range(NQ):
            sl = slice(ts * 512, (ts + 1) * 512)
            mk.mm(PSB[:, 0:512], onesF[0:1, :], rstd[:, sl], start=True, stop=True)
            mk.mm(PSB[:, 512:1024], onesF[0:1, :], mu[:, sl], start=True, stop=True)
            for dc in range(DC):
                mk.op("v", "tensor_mul", tmpf, X(dc, sl), PSB[:, 0:512])
                mk.op("v", "tensor_sub", ub[dc][:, sl], tmpf, PSB[:, 512:1024])

        for mt in range(MT):
            for dc in range(DC):
                slot = (dc % 2) * 128
                mk.op("t", "transpose", PST[:, slot:slot + 128],
                      ub[dc][:, mt * 128:(mt + 1) * 128], ident[:])
                mk.op("v", "tensor_copy",
                      A_VT[:, mt * 1024 + dc * 128:mt * 1024 + (dc + 1) * 128],
                      PST[:, slot:slot + 128])

        phih = A_WS[:, 0:8192]
        ar_chunk = max(512, S // 4)
        n_chunks = S // ar_chunk
        done_rows = 0
        for nph in range(2):
            mk.dma("g", phih, ap["l1f_phi"].rearrange("p (dc np) -> p dc np", np=2048)[:, :, nph * 1024:(nph + 1) * 1024])
            for mt in range(MT):
                for s5 in range(2):
                    nsl = slice(s5 * 512, (s5 + 1) * 512)
                    for dc in range(DC):
                        mk.mm(PSA[:, nsl], ub[dc][:, mt * 128:(mt + 1) * 128],
                              phih[:, dc * 1024 + s5 * 512:dc * 1024 + (s5 + 1) * 512],
                              start=(dc == 0), stop=(dc == DC - 1))
                    mk.op("s", "activation", tmpg[:], PSA[:, nsl], AF.Exp)
                    mk.dma("y", hE.ap()[mt * 128:(mt + 1) * 128,
                                        nph * 1024 + s5 * 512:nph * 1024 + (s5 + 1) * 512],
                           tmpg[:])
            if nph == 1:
                for c in range(n_chunks):
                    mk.op("g", "collective_compute", "AllReduce", ALU.add,
                          replica_groups=rg,
                          ins=[hE.ap()[c * ar_chunk:(c + 1) * ar_chunk, :].opt()],
                          outs=[hS.ap()[c * ar_chunk:(c + 1) * ar_chunk, :].opt()])

        dcache0 = 0
        for npb in range(4):
            nsl = slice(npb * 512, (npb + 1) * 512)
            for mt in range(MT):
                mk.dma("y", tmpg[:], hE.ap()[mt * 128:(mt + 1) * 128, nsl])
                mk.dma("y", tmpf[:, 0:512] if False else tmpf, hS.ap()[mt * 128:(mt + 1) * 128, nsl])
                mk.op("v", "reciprocal", tmpf, tmpf)
                mk.op("v", "tensor_mul", A_PT[:, dcache0 + mt * 512:dcache0 + (mt + 1) * 512],
                      tmpg[:], tmpf)
            for half in range(2):
                for dci in range(4):
                    dc = half * 4 + dci
                    for mt in range(MT):
                        mk.mm(PSA[:, dci * 512:(dci + 1) * 512],
                              A_VT[:, mt * 1024 + dc * 128:mt * 1024 + (dc + 1) * 128],
                              A_PT[:, dcache0 + mt * 512:dcache0 + (mt + 1) * 512],
                              start=(mt == 0), stop=(mt == MT - 1))
                for dci in range(4):
                    dc = half * 4 + dci
                    mk.op("v", "tensor_copy",
                          A_NB[:, dc * 2048 + npb * 512:dc * 2048 + (npb + 1) * 512],
                          PSA[:, dci * 512:(dci + 1) * 512])

        for npq in range(16):
            pbase = 0 if npq % 2 == 0 else 1024
            for lane in range(4):
                n = npq * 4 + lane
                wslot = A_WS[:, 0:8192] if n % 2 == 0 else A_PT[:, 0:8192]
                mk.dma("g", wslot, ap["l1f_W"][n])
                for eh in range(2):
                    pslot = PSA[:, pbase + eh * 512:pbase + (eh + 1) * 512]
                    for dc in range(DC):
                        mk.mm(pslot[lane * 32:lane * 32 + 32, :],
                              A_NB[:, dc * 2048 + npq * 128 + lane * 32:dc * 2048 + npq * 128 + (lane + 1) * 32],
                              wslot[:, dc * 1024 + eh * 512:dc * 1024 + (eh + 1) * 512],
                              start=(dc == 0), stop=(dc == DC - 1),
                              tile_position=(0, lane * 32), skip_group_check=True)
            for lane in range(4):
                n = npq * 4 + lane
                brep = rowsB[0:32, (lane % 2) * 1024:(lane % 2) * 1024 + 1024]
                mk.dma("g", brep, ap["l1f_brep"][npq * 128 + lane * 32:npq * 128 + (lane + 1) * 32, :])
                for eh in range(2):
                    pslot = PSA[:, pbase + eh * 512:pbase + (eh + 1) * 512]
                    mk.op("v", "tensor_add",
                          A_VT[:, npq * 1024 + eh * 512:npq * 1024 + (eh + 1) * 512][lane * 32:lane * 32 + 32, :],
                          pslot[lane * 32:lane * 32 + 32, :], brep[:, eh * 512:(eh + 1) * 512])

        est0 = 8192
        for mb in range(S // 256):
            for i in range(2):
                mrow = mb * 256 + i * 128
                mk.dma("g", A_PT[:, est0 + i * 2048:est0 + (i + 1) * 2048],
                       hE.ap()[mrow:mrow + 128, :])
                for npc in range(16):
                    slot = (npc % 2) * 128
                    mk.op("t", "transpose", PST[:, slot:slot + 128],
                          A_PT[:, est0 + i * 2048 + npc * 128:est0 + i * 2048 + (npc + 1) * 128],
                          ident[:])
                    mk.op("v", "tensor_copy",
                          A_PT[:, est0 + 4096 + npc * 256 + i * 128:est0 + 4096 + npc * 256 + (i + 1) * 128],
                          PST[:, slot:slot + 128])
            for npc in range(16):
                enp = A_PT[:, est0 + 4096 + npc * 256:est0 + 4096 + (npc + 1) * 256]
                for ec in range(DC):
                    bank, off = divmod(ec, 2)
                    mk.mm(PSA[:, bank * 512 + off * 256:bank * 512 + off * 256 + 256],
                          A_VT[:, npc * 1024 + ec * 128:npc * 1024 + (ec + 1) * 128],
                          enp, start=(npc == 0 and off == 0), stop=(npc == 15 and off == 1),
                          skip_group_check=True)
                mk.mm(rowpack(6)[:, 0:256], onesB[:, 0:1], enp,
                      start=(npc == 0), stop=(npc == 15))
            srow = A_SC[32:33, 0:256]
            mk.op("v", "tensor_copy", srow, rowpack(6)[:, 0:256])
            mk.op("v", "reciprocal", srow, srow)
            mk.mm(PSB[:, 512:768], onesF[32:33, :], srow, start=True, stop=True)
            mk.op("v", "tensor_copy", tmpg[:, 0:256], PSB[:, 512:768])
            for ec in range(DC):
                bank, off = divmod(ec, 2)
                mk.op("v", "tensor_mul", tmpf[:, 0:256],
                      PSA[:, bank * 512 + off * 256:bank * 512 + off * 256 + 256],
                      tmpg[:, 0:256])
                mk.op("v", "tensor_add", X(ec)[:, mb * 256:(mb + 1) * 256],
                      X(ec)[:, mb * 256:(mb + 1) * 256], tmpf[:, 0:256])

    if "l0a" in parts:
        part_l0a()
    if "l0fa" in parts:
        part_l0fa()
    if "l0fm" in parts:
        part_l0fm("l0fa" in parts)
    if "l1a" in parts:
        part_l1a()
    if "l1f" in parts:
        part_l1f()

    for dc in range(DC):
        mk.dma("y", out_x[dc * 128:(dc + 1) * 128, :], A_X[:, dc * S:(dc + 1) * S])
    mk.op("y", "nop")
    mk.flush(verbose=True)
    return nc


_CACHE = {}


def kernel(**inputs):
    S = 2048
    n_cores = 8
    if "nc" not in _CACHE:
        _CACHE["nc"] = build_program(S=S, parts=("l0a", "l0fa", "l0fm", "l1a", "l1f"),
                                     n_cores=n_cores)
    nc = _CACHE["nc"]
    prep = {}
    prep.update(prep_l0a(inputs))
    prep.update(prep_l0f(inputs))
    prep.update(prep_l1a(inputs))
    prep.update(prep_l1f(inputs))
    prep.update(host_constants(S))
    x = np.ascontiguousarray(np.asarray(inputs["L0_attn_x"], dtype=np.float32))
    maps = []
    for c in range(n_cores):
        im = dict(prep)
        im["xT"] = np.ascontiguousarray(x[c].T)
        maps.append(im)
    res = run_bass_kernel_spmd(nc, maps, core_ids=list(range(n_cores)))
    out = np.stack([np.ascontiguousarray(res.results[c]["outX"].T) for c in range(n_cores)], 0)
    return out


# version 12
# speedup vs baseline: 1.2712x; 1.1347x over previous
import numpy as np
import ml_dtypes
import concourse.bass as bass
import concourse.mybir as mybir
from concourse.bass_utils import run_bass_kernel_spmd
from contextlib import ExitStack

F32 = mybir.dt.float32
BF16 = mybir.dt.bfloat16
AF = mybir.ActivationFunctionType
ALU = mybir.AluOpType
AX = mybir.AxisListType

D = 1024
DC = 8
P129 = 129


DTSZ = {F32: 4, BF16: 2, mybir.dt.float16: 2, mybir.dt.int32: 4, mybir.dt.uint8: 1}


class _Rec:
    __slots__ = ("idx", "eng", "method", "args", "kw", "mwaits", "reads", "writes",
                 "is_dma", "ticked", "sem", "val", "edges", "clock", "eidx",
                 "_needed", "_waits", "_fence")

    def __init__(self, idx, eng, method, args, kw, mwaits, is_dma):
        self.idx = idx
        self.eng = eng
        self.method = method
        self.args = args
        self.kw = kw
        self.mwaits = mwaits
        self.is_dma = is_dma
        self.ticked = False
        self.sem = None
        self.val = None
        self.edges = []
        self.clock = None
        self.eidx = 0


class _SemPool:
    def __init__(self, nc, es, name, k, step):
        self.sems = [es.enter_context(nc.semaphore(f"{name}{i}")) for i in range(k)]
        self.count = [0] * k
        self.i = 0
        self.step = step

    def take(self):
        i = self.i
        self.i = (self.i + 1) % len(self.sems)
        prev = self.count[i]
        self.count[i] += self.step
        return self.sems[i], self.count[i], prev


class MK:
    ENGS = ("t", "v", "s", "g", "y")

    def __init__(self, nc):
        self.nc = nc
        self.es = ExitStack()
        self.ops = []
        self.aliases = {}
        self.pools = {
            "t": _SemPool(nc, self.es, "pt_", 12, 1),
            "v": _SemPool(nc, self.es, "pv_", 16, 1),
            "s": _SemPool(nc, self.es, "ps_", 12, 1),
            "g": _SemPool(nc, self.es, "pg_", 6, 1),
            "y": _SemPool(nc, self.es, "py_", 4, 1),
            "gd": _SemPool(nc, self.es, "pgd_", 16, 16),
            "yd": _SemPool(nc, self.es, "pyd_", 12, 16),
        }

    def alias(self, name, shape, dtype, backing):
        v = self.nc.virtual_sbuf_tensor(name, list(shape), dtype)
        self.nc.back_virtual_tensor_with(v, backing)
        self.aliases[name] = backing.name
        return v

    def op(self, eng, method, *args, wait=(), tick=False, **kw):
        r = _Rec(len(self.ops), eng, method, args, kw,
                 tuple(t for t in wait if t is not None), False)
        self.ops.append(r)
        return r

    def dma(self, eng, out, in_, wait=()):
        r = _Rec(len(self.ops), eng, "dma_start", (), {"out": out, "in_": in_},
                 tuple(t for t in wait if t is not None), True)
        self.ops.append(r)
        return r

    def tick(self, eng, wait=()):
        return self.op(eng, "nop", wait=wait)

    def mm(self, out, lhsT, rhs, start, stop, wait=(), tick=False, **kw):
        return self.op("t", "matmul", out, lhsT, rhs, start=start, stop=stop, wait=wait, **kw)

    PSUM_NAMES = ("PSA", "PSB", "PST")

    def _rect(self, ap):
        t = ap.tensor
        name = self.aliases.get(t.name, t.name)
        sz = DTSZ[t.dtype]
        dims = list(ap.ap)
        off = ap.offset
        if len(t.shape) >= 2 and t.shape[0] <= 128 and not name.startswith("DR_"):
            pcols = t.shape[1] if len(t.shape) == 2 else int(np.prod(t.shape[1:]))
            p0 = off // pcols
            c0 = off % pcols
            p1 = p0 + (dims[0][1] if dims else 1)
            cspan = sum((cnt - 1) * st for st, cnt in dims[1:]) + 1 if len(dims) > 1 else 1
            b0, b1 = c0 * sz, (c0 + cspan) * sz
            if name in self.PSUM_NAMES:
                p0, p1 = 0, 128
                b0 = (b0 // 2048) * 2048
                b1 = ((b1 + 2047) // 2048) * 2048
            return (name, p0, p1, b0, b1)
        span = sum((cnt - 1) * st for st, cnt in dims) + 1
        return (name, 0, 1, off * sz, (off + span) * sz)

    def _is_ap(self, x):
        return hasattr(x, "tensor") and hasattr(x, "offset")

    def _io(self, r):
        writes, reads = [], []
        outs = []
        if "out" in r.kw:
            outs.append(r.kw["out"])
        if "accum_out" in r.kw and self._is_ap(r.kw.get("accum_out")):
            outs.append(r.kw["accum_out"])
        args = list(r.args)
        if args and self._is_ap(args[0]):
            outs.append(args[0])
            args = args[1:]
        for a in args:
            if self._is_ap(a):
                reads.append(a)
        for k, vv in r.kw.items():
            if k in ("out", "accum_out"):
                continue
            if k == "outs" and isinstance(vv, list):
                outs.extend(x for x in vv if self._is_ap(x))
                continue
            if isinstance(vv, list):
                reads.extend(x for x in vv if self._is_ap(x))
                continue
            if self._is_ap(vv):
                reads.append(vv)
        return [self._rect(a) for a in reads], [self._rect(a) for a in outs]

    @staticmethod
    def _buckets(rect):
        name, p0, p1, b0, b1 = rect
        for pb in range(p0 // 32, (p1 + 31) // 32):
            for cb in range(b0 // 512, (b1 + 511) // 512):
                yield (name, pb, cb)

    @staticmethod
    def _overlap(a, b):
        return a[0] == b[0] and a[1] < b[2] and b[1] < a[2] and a[3] < b[4] and b[3] < a[4]

    def flush(self, verbose=False):
        last_w = {}
        readers = {}
        eidx = {e: 0 for e in self.ENGS}
        for r in self.ops:
            r.eidx = eidx[r.eng]
            eidx[r.eng] += 1
            reads, writes = self._io(r)
            r.reads, r.writes = reads, writes
            edges = {}
            for rect in reads:
                short = (rect[1], rect[3], rect[2], rect[4])
                for bk in self._buckets(rect):
                    w = last_w.get(bk)
                    if w is not None and w[1] is not r:
                        edges[w[1].idx] = w[1]
            for rect in writes:
                for bk in self._buckets(rect):
                    w = last_w.get(bk)
                    if w is not None and w[1] is not r:
                        edges[w[1].idx] = w[1]
                    for rd in readers.get(bk, ()):
                        if rd is not r:
                            edges[rd.idx] = rd
            for rect in reads:
                for bk in self._buckets(rect):
                    readers.setdefault(bk, []).append(r)
            for rect in writes:
                for bk in self._buckets(rect):
                    last_w[bk] = (rect, r)
                    readers[bk] = []
            r.edges = [self.ops[i] for i in sorted(edges)]

        prev_on_engine = {e: None for e in self.ENGS}
        for r in self.ops:
            ck = {}
            pred = prev_on_engine[r.eng]
            if pred is not None:
                ck.update(pred.clock)
                if r.eng == "t":
                    ck["T"] = max(ck.get("T", -1), pred.eidx)
            needed = []
            for p in r.edges + [w if isinstance(w, _Rec) else None for w in r.mwaits]:
                if p is None:
                    continue
                if p.eng == "t" and r.eng == "t" and not p.is_dma:
                    if ck.get("T", -1) < p.eidx:
                        ck["T"] = max(ck.get("T", -1), p.eidx)
                        for k, vv in p.clock.items():
                            if ck.get(k, -1 if k == "T" else 0) < vv:
                                ck[k] = vv
                    continue
                needed.append(p)
            for p in needed:
                p.ticked = True
            r.clock = ck
            r._needed = needed
            prev_on_engine[r.eng] = r

        for r in self.ops:
            if r.ticked or r.is_dma:
                pool = self.pools[r.eng + "d"] if r.is_dma else self.pools[r.eng]
                r.sem, r.val, r._fence = pool.take()
        prev_on_engine = {e: None for e in self.ENGS}
        nwaits = 0
        for r in self.ops:
            ck = {}
            pred = prev_on_engine[r.eng]
            if pred is not None:
                ck.update(pred.clock)
                if r.eng == "t":
                    ck["T"] = max(ck.get("T", -1), pred.eidx)
            waits = []
            for p in r._needed:
                if p.eng == "t" and r.eng == "t" and not p.is_dma:
                    continue
                if p.sem is None:
                    raise RuntimeError(f"producer without sem: {p.method}")
                key = id(p.sem)
                if ck.get(key, 0) >= p.val:
                    continue
                waits.append((p.sem, p.val))
                ck[key] = p.val
                for k, vv in p.clock.items():
                    if k == "T":
                        ck["T"] = max(ck.get("T", -1), vv)
                    elif ck.get(k, 0) < vv:
                        ck[k] = vv
                if p.eng == "t" and not p.is_dma:
                    ck["T"] = max(ck.get("T", -1), p.eidx)
            r._waits = waits
            nwaits += len(waits)
            r.clock = ck
            prev_on_engine[r.eng] = r
        if verbose:
            nt = sum(1 for r in self.ops if r.ticked)
            print(f"[mk] ops={len(self.ops)} ticked={nt} waits={nwaits}")

        eng_ops = {e: [r for r in self.ops if r.eng == e] for e in self.ENGS}

        with self.nc.Block() as block:
            def mk_body(name):
                def body(eng):
                    seen = {}
                    for r in eng_ops[name]:
                        if r.sem is not None and r._fence > 0 and seen.get(id(r.sem), 0) < r._fence:
                            eng.wait_ge(r.sem, r._fence)
                            seen[id(r.sem)] = r._fence
                        for (sem, val) in r._waits:
                            if seen.get(id(sem), 0) < val:
                                eng.wait_ge(sem, val)
                                seen[id(sem)] = val
                        ins = getattr(eng, r.method)(*r.args, **r.kw)
                        if r.sem is not None:
                            ins.then_inc(r.sem, 16 if r.is_dma else 1)
                return body
            block.tensor(mk_body("t"))
            block.vector(mk_body("v"))
            block.scalar(mk_body("s"))
            block.gpsimd(mk_body("g"))
            block.sync(mk_body("y"))


def host_constants(S):
    c = {"c_ident": np.eye(128, dtype=np.float32).astype(ml_dtypes.bfloat16),
         "c_identF": np.eye(128, dtype=np.float32)}
    inv_freq = 1.0 / 10000.0 ** (np.arange(0, 64, 2, dtype=np.float32) / 64.0)
    t = np.arange(S, dtype=np.float32)
    freqs = inv_freq[:, None] * t[None, :]
    emb = np.concatenate([freqs, freqs], 0)
    cos = np.cos(emb)
    sin = np.sin(emb)
    c["c_cos"] = np.concatenate([cos, cos], 0).astype(ml_dtypes.bfloat16)
    c["c_sin"] = np.concatenate([sin, sin], 0).astype(ml_dtypes.bfloat16)
    perm = np.zeros((128, 128), dtype=np.float32)
    for blk in (0, 64):
        for d_ in range(32):
            perm[blk + d_, blk + d_ + 32] = -1.0
            perm[blk + 32 + d_, blk + d_] = 1.0
    c["c_permT"] = perm.T.copy().astype(ml_dtypes.bfloat16)
    msk = np.zeros((128, 8 * 512), dtype=ml_dtypes.bfloat16)
    for ci in range(8):
        kc = ci if ci < 4 else ci + 4
        for p in range(128):
            kj = kc * 128 + p
            qi = np.arange(512)
            msk[p, ci * 512:(ci + 1) * 512] = ((kj >= qi) & (kj <= qi + 1024)).astype(np.float32)
    c["c_l1mask"] = msk
    return c


def wimg(wT):
    Din, O = wT.shape
    return np.ascontiguousarray(wT.reshape(Din // 128, 128, O).transpose(1, 0, 2)
                                .reshape(128, (Din // 128) * O))


def prep_l0a(inputs):
    d = {}
    d["l0a_wq"] = wimg(np.ascontiguousarray(inputs["L0_attn_w_q"].T))
    d["l0a_wk"] = wimg(np.ascontiguousarray(inputs["L0_attn_w_k"].T))
    d["l0a_wv"] = wimg(np.ascontiguousarray(inputs["L0_attn_w_v"].T))
    r32 = np.zeros((1, 1360), dtype=np.float32)
    r32[0, 0:129] = inputs["L0_attn_b_q"]
    r32[0, 160:289] = inputs["L0_attn_b_k"]
    r32[0, 320:1344] = inputs["L0_attn_b_v"]
    d["l0a_rows32"] = r32
    d["l0a_k0x"] = np.ascontiguousarray(inputs["L0_attn_k_0"][128:129][None, :])
    d["l0a_rows64"] = np.ascontiguousarray(inputs["L0_attn_v_0"][None, :])
    d["l0a_k0m"] = np.ascontiguousarray(inputs["L0_attn_k_0"][0:128, None])
    d["l0a_bcols"] = np.stack([inputs["L0_attn_b_q"][0:128], inputs["L0_attn_b_k"][0:128]], 1)
    return d




def prep_l0f(inputs):
    d = {}
    d["l0f_wqkv"] = qkv_group_img(np.ascontiguousarray(inputs["L0_ffn_qkv_w"].T))
    d["l0f_wout"] = wimg(np.ascontiguousarray(inputs["L0_ffn_out_w"].T))
    rows = np.zeros((1, 4096), dtype=np.float32)
    rows[0, 0:3072] = inputs["L0_ffn_qkv_b"]
    rows[0, 3072:4096] = inputs["L0_ffn_out_b"]
    d["l0f_rows"] = rows
    d["l0f_n1"] = np.ascontiguousarray(inputs["L0_ffn_norm1_w"].reshape(8, 128).T)
    d["l0f_n2"] = np.ascontiguousarray(inputs["L0_ffn_norm2_w"].reshape(8, 128).T)
    d["l0f_gw"] = wimg(np.ascontiguousarray(inputs["L0_ffn_gate_w"].T))
    w1 = inputs["L0_ffn_w1"]; w2 = inputs["L0_ffn_w2"]; w3 = inputs["L0_ffn_w3"]
    w1r = w1.reshape(8, 16, 128, 8, 128).transpose(0, 4, 1, 3, 2)
    w1r = w1.reshape(8, 16, 128, 8, 128).transpose(0, 4, 1, 3, 2)
    w2r = w2.reshape(8, 16, 128, 8, 128).transpose(0, 4, 1, 3, 2)
    img = np.stack([w1r, w2r], axis=4)
    d["l0f_w12"] = np.ascontiguousarray(img.reshape(8, 128, 16 * 2048))
    d["l0f_w3"] = np.ascontiguousarray(w3.reshape(8, 1024, 16, 128).transpose(0, 3, 2, 1))
    d["l0f_b1"] = np.ascontiguousarray(inputs["L0_ffn_b1"].reshape(8, 16, 128).transpose(0, 2, 1))
    d["l0f_b2"] = np.ascontiguousarray(inputs["L0_ffn_b2"])
    d["l0f_b3"] = np.ascontiguousarray(inputs["L0_ffn_b3"])
    d["l0f_b2c"] = np.ascontiguousarray(inputs["L0_ffn_b2"].reshape(8, 16, 128).transpose(0, 2, 1))
    qb = inputs["L0_ffn_qkv_b"]
    d["l0f_bqk"] = np.ascontiguousarray(np.stack(
        [qb[g * 128:(g + 1) * 128] for g in range(8)] +
        [qb[1024 + g * 128:1024 + (g + 1) * 128] for g in range(8)], 1))
    d["l0f_boc"] = np.ascontiguousarray(inputs["L0_ffn_out_b"].reshape(8, 128).T)
    return d

def qkv_group_img(qkvT):
    img = np.empty((128, 8 * 3072), dtype=np.float32)
    for g in range(8):
        for dc in range(8):
            blk = img[:, g * 3072 + dc * 384:g * 3072 + (dc + 1) * 384]
            rows = slice(dc * 128, (dc + 1) * 128)
            for which in range(3):
                blk[:, which * 128:(which + 1) * 128] = \
                    qkvT[rows, which * 1024 + 128 * g:which * 1024 + 128 * (g + 1)]
    return img


def prep_l1a(inputs):
    d = {}
    qkvT = np.concatenate([np.ascontiguousarray(inputs["L1_attn_wq"].T),
                           np.ascontiguousarray(inputs["L1_attn_wk"].T),
                           np.ascontiguousarray(inputs["L1_attn_wv"].T)], axis=1)
    d["l1a_wqkv"] = qkv_group_img(qkvT)
    d["l1a_wout"] = wimg(np.ascontiguousarray(inputs["L1_attn_wo"].T))
    rows = np.zeros((1, 4096), dtype=np.float32)
    rows[0, 0:1024] = inputs["L1_attn_bq"]
    rows[0, 1024:2048] = inputs["L1_attn_bk"]
    rows[0, 2048:3072] = inputs["L1_attn_bv"]
    rows[0, 3072:4096] = inputs["L1_attn_bo"]
    d["l1a_rows"] = rows
    d["l1a_bqk"] = np.ascontiguousarray(np.stack(
        [inputs["L1_attn_bq"][g * 128:(g + 1) * 128] for g in range(8)] +
        [inputs["L1_attn_bk"][g * 128:(g + 1) * 128] for g in range(8)], 1))
    d["l1a_boc"] = np.ascontiguousarray(inputs["L1_attn_bo"].reshape(8, 128).T)
    return d


def prep_l1f(inputs):
    d = {}
    phi = inputs["L1_ffn_phi"].reshape(1024, 2048)
    d["l1f_phi"] = wimg(np.ascontiguousarray(phi))
    W = inputs["L1_ffn_W"]
    d["l1f_W"] = np.ascontiguousarray(
        W.reshape(64, 8, 128, 1024).transpose(0, 2, 1, 3).reshape(64, 128, 8192))
    d["l1f_brep"] = np.ascontiguousarray(np.repeat(inputs["L1_ffn_b"], 32, axis=0))
    return d


def build_program(S=2048, parts=("l0a",), n_cores=8):
    nc = bass.Bass("TRN2", target_bir_lowering=False, debug=False)
    mk = MK(nc)
    NT = S // 128
    NQ = S // 512

    ap = {}

    def inp(name, shape, dt=F32):
        ap[name] = nc.dram_tensor(name, list(shape), dt, kind="ExternalInput").ap()
        return ap[name]

    xT = inp("xT", [D, S])
    inp("c_ident", [128, 128], BF16)
    inp("c_identF", [128, 128])
    if "l0a" in parts:
        inp("l0a_wq", [128, 8 * P129]); inp("l0a_wk", [128, 8 * P129])
        inp("l0a_wv", [128, 8 * 1024])
        inp("l0a_rows32", [1, 1360]); inp("l0a_rows64", [1, 1024]); inp("l0a_k0x", [1, 1])
        inp("l0a_k0m", [128, 1]); inp("l0a_bcols", [128, 2])
    if "l0fa" in parts:
        inp("l0f_wqkv", [128, 8 * 3072]); inp("l0f_wout", [128, 8 * 1024])
        inp("l0f_rows", [1, 4096]); inp("l0f_n1", [128, 8]); inp("l0f_n2", [128, 8])
        inp("l0f_bqk", [128, 16]); inp("l0f_boc", [128, 8])
        inp("c_cos", [128, S], BF16); inp("c_sin", [128, S], BF16)
        inp("c_permT", [128, 128], BF16)
    if "l0fm" in parts:
        inp("l0f_gw", [128, 64])
        inp("l0f_w12", [8, 128, 16 * 2048])
        inp("l0f_w3", [8, 128, 16, 1024])
        inp("l0f_b1", [8, 128, 16]); inp("l0f_b2c", [8, 128, 16]); inp("l0f_b3", [8, 1024])
        if "l0fa" not in parts:
            inp("tx1", [D, S])
        if "l0fa" not in parts or True:
            pass
    if "l0fm" in parts and "l0fa" not in parts:
        inp("l0f_n2", [128, 8])
    if "l1a" in parts:
        inp("l1a_wqkv", [128, 8 * 3072]); inp("l1a_wout", [128, 8 * 1024])
        inp("l1a_rows", [1, 4096]); inp("c_l1mask", [128, 8 * 512], BF16)
        inp("l1a_bqk", [128, 16]); inp("l1a_boc", [128, 8])
    if "l1f" in parts:
        inp("l1f_phi", [128, 8 * 2048])
        inp("l1f_W", [64, 128, 8192])
        inp("l1f_brep", [2048, 1024])
    out_x = nc.dram_tensor("outX", [D, S], F32, kind="ExternalOutput").ap()
    if "l1f" in parts:
        hE = nc.dram_tensor("hE", [S, 2048], F32, kind="Internal")
        hS = nc.dram_tensor("hS", [S, 2048], F32, kind="Internal")
        hWbf = nc.dram_tensor("hWbf", [64, 128, 8192], BF16, kind="Internal")
    hx1 = nc.dram_tensor("hx1", [D, S], F32, kind="Internal").ap()
    hcomb = nc.dram_tensor("hcomb", [8, S], F32, kind="Internal").ap()
    hattn = nc.dram_tensor("hattn", [D, S], BF16, kind="Internal").ap()

    A_X = nc.alloc_sbuf_tensor("A_X", [128, DC * S], F32)
    A_NB = nc.alloc_sbuf_tensor("A_NB", [128, max(DC * S, DC * 2048)], BF16)
    A_VT = nc.alloc_sbuf_tensor("A_VT", [128, max(DC * S, 2080 + 4 * S, 16384)], BF16)
    PTCOLS = max((S // 128 + 1) * 512 + (S + 128) + 2 * S, DC * S, DC * (S // 2) + 8192, 16384)
    A_PT = nc.alloc_sbuf_tensor("A_PT", [128, PTCOLS], BF16)
    A_WS = nc.alloc_sbuf_tensor("A_WS", [128, 8192], BF16)
    A_SC = nc.alloc_sbuf_tensor("A_SC", [128, 4608], F32)
    rowsB = nc.alloc_sbuf_tensor("rowsB", [128, 4100], BF16)
    onesB = nc.alloc_sbuf_tensor("onesB", [128, 512], BF16)
    onesF = nc.alloc_sbuf_tensor("onesF", [128, 128], F32)
    ident = nc.alloc_sbuf_tensor("ident", [128, 128], BF16)
    identG = nc.alloc_sbuf_tensor("identG", [128, 128], F32)
    ap["__bqkL__"] = nc.alloc_sbuf_tensor("bqkL", [128, 16], F32)
    maxcG = nc.alloc_sbuf_tensor("maxcG", [128, 1], F32)
    k0m = nc.alloc_sbuf_tensor("k0m", [128, 1], BF16)

    tmpf = A_SC[:, 4096:4608]
    negm = maxcG
    tmpg = nc.alloc_sbuf_tensor("tmpg", [128, 512], F32)
    PSA = nc.alloc_psum_tensor("PSA", [128, 2048], F32)
    PSB = nc.alloc_psum_tensor("PSB", [128, 1536], F32)
    PST = nc.alloc_psum_tensor("PST", [128, 1024], BF16)

    mk.op("v", "memset", onesB[:], 1.0)
    mk.op("v", "memset", onesF[:], 1.0)
    mk.dma("y", identG[:], ap["c_identF"])
    mk.dma("y", ident[:], ap["c_ident"])
    for dc in range(DC):
        mk.dma("y", A_X[:, dc * S:(dc + 1) * S], xT[dc * 128:(dc + 1) * 128, :])

    _precast = []
    if "l1f" in parts:
        _precast = [n for n in range(64)]

    def drain_precast(k):
        for _ in range(k):
            if _precast:
                n = _precast.pop(0)
                mk.dma("g", hWbf.ap()[n], ap["l1f_W"][n])

    def rowpack(idx):
        r, b = divmod(idx, 3)
        return PSB[32 * b:32 * b + 1, r * 512:(r + 1) * 512]

    def X(dc, sl=None):
        base = dc * S
        return A_X[:, base:base + S] if sl is None else A_X[:, base + sl.start:base + sl.stop]

    def ln_sum_stats(src_chunks, want_mean=True):
        sq_tile = A_PT[:, 0:S]
        for dc in range(DC):
            mk.op("s", "activation", sq_tile, src_chunks[dc], AF.Square)
            for ts in range(NQ):
                sl = slice(ts * 512, (ts + 1) * 512)
                if want_mean:
                    mk.mm(rowpack(ts), onesB[:, 0:1], src_chunks[dc][:, sl],
                          start=(dc == 0), stop=(dc == DC - 1))
                mk.mm(rowpack(4 + ts), onesB[:, 0:1], sq_tile[:, sl],
                      start=(dc == 0), stop=(dc == DC - 1))

    def part_l0a():
        wq = A_WS[:, 0:8 * P129]
        wk = A_WS[:, 8 * P129:16 * P129]
        wvh = A_WS[:, 16 * P129:16 * P129 + 8 * 512]
        mk.dma("g", wq, ap["l0a_wq"])
        mk.dma("g", wk, ap["l0a_wk"])
        rows32 = rowsB[32:33, 0:1360]
        mk.dma("g", rows32, ap["l0a_rows32"])
        rows64 = rowsB[64:65, 0:1024]
        mk.dma("g", rows64, ap["l0a_rows64"])
        mk.dma("g", k0m[:], ap["l0a_k0m"])
        bq = rowsB[32:33, 0:129]; bk = rowsB[32:33, 160:289]
        bv = rowsB[32:33, 320:1344]
        k0x = rowsB[0:1, 4096:4097]
        mk.dma("g", k0x, ap["l0a_k0x"])
        bcolA = nc.alloc_sbuf_tensor("bcolA", [128, 2], F32)
        mk.dma("g", bcolA[:], ap["l0a_bcols"])
        v0r = rowsB[64:65, 0:1024]
        qxr = rowsB[0:1, 0:S]
        kxr = rowsB[0:1, 2048:2048 + S]
        ptx = rowsB[64:65, 1024:1024 + 512]

        pp0 = 0
        preP0 = pp0 + (NT + 1) * 512
        qf0 = preP0 + S + 128
        kf0 = qf0 + S
        assert kf0 + S <= PTCOLS
        preP = A_PT[:, preP0:preP0 + S + 128]
        qf = A_PT[:, qf0:qf0 + S]
        kf = A_PT[:, kf0:kf0 + S]

        def PTt(kc, a, b):
            return A_PT[:, pp0 + kc * 512 + a:pp0 + kc * 512 + b]

        xb = [A_NB[:, dc * S:(dc + 1) * S] for dc in range(DC)]
        for dc in range(DC):
            mk.op("v", "tensor_copy", xb[dc], X(dc))

        sq_tile = A_PT[:, pp0:pp0 + S]
        for dc in range(DC):
            mk.op("s", "activation", sq_tile, xb[dc], AF.Square)
            for ts in range(NQ):
                sl = slice(ts * 512, (ts + 1) * 512)
                mk.mm(rowpack(ts), onesB[:, 0:1], xb[dc][:, sl],
                      start=(dc == 0), stop=(dc == DC - 1))
                mk.mm(rowpack(4 + ts), onesB[:, 0:1], sq_tile[:, sl],
                      start=(dc == 0), stop=(dc == DC - 1))
        mu = A_SC[0:1, 0:S]; rstd = A_SC[0:1, 2048:2048 + S]; rtmp = A_SC[0:1, 4096:4608]
        for ts in range(NQ):
            sl = slice(ts * 512, (ts + 1) * 512)
            mk.op("v", "tensor_scalar_mul", mu[:, sl], rowpack(ts), 1.0 / D)
            mk.op("v", "tensor_scalar_mul", rstd[:, sl], rowpack(4 + ts), 1.0 / D)
        for ts in range(NQ):
            sl = slice(ts * 512, (ts + 1) * 512)
            mk.op("v", "tensor_mul", rtmp, mu[:, sl], mu[:, sl])
            mk.op("v", "tensor_sub", rstd[:, sl], rstd[:, sl], rtmp)
            mk.op("v", "tensor_scalar_add", rstd[:, sl], rstd[:, sl], 1e-5)
            mk.op("s", "activation", rtmp, rstd[:, sl], AF.Sqrt)
            mk.op("v", "reciprocal", rstd[:, sl], rtmp)
            mk.op("v", "tensor_mul", mu[:, sl], mu[:, sl], rstd[:, sl])

        for ts in range(NQ):
            sl = slice(ts * 512, (ts + 1) * 512)
            mk.mm(PSB[:, 0:512], onesF[0:1, :], rstd[:, sl], start=True, stop=True)
            mk.mm(PSB[:, 512:1024], onesF[0:1, :], mu[:, sl], start=True, stop=True)
            for dc in range(DC):
                mk.op("v", "tensor_mul", tmpf[:], X(dc, sl), PSB[:, 0:512])
                mk.op("v", "tensor_sub", xb[dc][:, sl], tmpf[:], PSB[:, 512:1024])

        for wi, (wT, brow_x, dstf, dstx) in enumerate((
                (wq, rowsB[32:33, 128:129], qf, qxr),
                (wk, rowsB[32:33, 288:289], kf, kxr))):
            for ts in range(NQ):
                sl = slice(ts * 512, (ts + 1) * 512)
                for dc in range(DC):
                    mk.mm(PSA[:, sl], wT[:, dc * P129:dc * P129 + 128], xb[dc][:, sl],
                          start=(dc == 0), stop=(dc == DC - 1))
                for dc in range(DC):
                    mk.mm(rowpack(ts), wT[:, dc * P129 + 128:dc * P129 + 129],
                          xb[dc][:, sl], start=(dc == 0), stop=False)
                mk.mm(rowpack(ts), brow_x, onesB[32:33, 0:512], start=False, stop=True)
            mk.op("v", "tensor_scalar", dstf, PSA[:, 0:S], bcolA[:, wi:wi + 1], None,
                  op0=ALU.add)
            for ts in range(NQ):
                mk.op("v", "tensor_copy", dstx[:, ts * 512:(ts + 1) * 512], rowpack(ts))

        wv_img = ap["l0a_wv"]
        for oh in range(2):
            mk.dma("g", wvh, wv_img.rearrange("p (dc o) -> p dc o", o=1024)[:, :, oh * 512:(oh + 1) * 512])
            for tc in range(NT):
                tsl = slice(tc * 128, (tc + 1) * 128)
                osl = slice(oh * 512, (oh + 1) * 512)
                for dc in range(DC):
                    mk.mm(PSB[:, osl], xb[dc][:, tsl],
                          wvh[:, dc * 512:(dc + 1) * 512],
                          start=(dc == 0), stop=False)
                mk.mm(PSB[:, osl], onesB[32:33, 0:128], bv[:, oh * 512:(oh + 1) * 512],
                      start=False, stop=True)
                mk.op("v", "tensor_copy", A_VT[:, tc * 1024 + oh * 512:tc * 1024 + (oh + 1) * 512],
                      PSB[:, osl])

        for qb in range(NQ):
            qsl = slice(qb * 512, (qb + 1) * 512)
            for qt in range(4):
                qtile = slice(qb * 512 + qt * 128, qb * 512 + (qt + 1) * 128)
                for kt in range(NQ):
                    ksl = slice(kt * 512, (kt + 1) * 512)
                    mk.mm(PSA[:, ksl], qf[:, qtile], kf[:, ksl], start=True, stop=False)
                    mk.mm(PSA[:, ksl], qxr[:, qtile], kxr[:, ksl], start=False, stop=True)
                mk.mm(PSB[:, 1024:1025], qf[:, qtile], k0m[:, 0:1], start=True, stop=False)
                mk.mm(PSB[:, 1024:1025], qxr[:, qtile], k0x[:, 0:1], start=False, stop=True)
                mk.op("v", "reduce_max", negm[:, 0:1], PSA[:, 0:S], axis=AX.X)
                mk.op("v", "tensor_max", negm[:, 0:1], negm[:, 0:1], PSB[:, 1024:1025])
                mk.op("v", "tensor_scalar_mul", negm[:, 0:1], negm[:, 0:1], -1.0)
                mk.op("s", "activation", preP[:, 0:S], PSA[:, 0:S], AF.Exp,
                      bias=negm[:, 0:1])
                mk.op("s", "activation", preP[:, S:S + 1], PSB[:, 1024:1025], AF.Exp,
                      bias=negm[:, 0:1])
                for kc in range(NT):
                    slot = (kc % 2) * 128
                    mk.op("t", "transpose", PST[:, slot:slot + 128],
                          preP[:, kc * 128:(kc + 1) * 128], ident[:])
                    mk.op("v", "tensor_copy", PTt(kc, qt * 128, (qt + 1) * 128),
                          PST[:, slot:slot + 128])
                mk.op("t", "transpose", PST[0:1, 256:384], preP[:, S:S + 1], ident[:])
                mk.op("v", "tensor_copy", ptx[:, qt * 128:(qt + 1) * 128],
                      PST[0:1, 256:384])
            import os
            if os.environ.get("MK_STOP") == "evac":
                return
            for kc in range(NT):
                mk.mm(PSB[0:1, 0:512], onesB[:, 0:1], PTt(kc, 0, 512),
                      start=(kc == 0), stop=False)
            mk.mm(PSB[0:1, 0:512], onesB[64:65, 0:1], ptx[:, 0:512], start=False, stop=True)
            sums_r = A_SC[32:33, 0:512]
            mk.op("v", "tensor_copy", sums_r, PSB[0:1, 0:512])
            mk.op("v", "reciprocal", sums_r, sums_r)
            mk.mm(PSB[:, 0:512], onesF[32:33, :], sums_r, start=True, stop=True)
            mk.op("v", "tensor_copy", tmpg[:], PSB[:, 0:512])
            mk.mm(PSB[:, 512:1024], onesF[0:1, :], rstd[:, qsl], start=True, stop=True)
            mk.mm(PSB[:, 1024:1536], onesF[0:1, :], mu[:, qsl], start=True, stop=True)
            if os.environ.get("MK_STOP") == "sums":
                return
            for half in range(2):
                for dci in range(4):
                    dc = half * 4 + dci
                    csl = slice(dci * 512, (dci + 1) * 512)
                    for kc in range(NT):
                        mk.mm(PSA[:, csl],
                              A_VT[:, kc * 1024 + dc * 128:kc * 1024 + (dc + 1) * 128],
                              PTt(kc, 0, 512),
                              start=(kc == 0), stop=False)
                    mk.mm(PSA[:, csl], v0r[:, dc * 128:(dc + 1) * 128], ptx[:, 0:512],
                          start=False, stop=True)
                for dci in range(4):
                    dc = half * 4 + dci
                    csl = slice(dci * 512, (dci + 1) * 512)
                    mk.op("v", "tensor_mul", tmpf[:], X(dc, qsl), PSB[:, 512:1024])
                    mk.op("v", "tensor_sub", tmpf[:], tmpf[:], PSB[:, 1024:1536])
                    mk.op("v", "tensor_add", X(dc, qsl), X(dc, qsl), tmpf[:])
                    mk.op("v", "tensor_mul", tmpf[:], PSA[:, csl], tmpg[:])
                    mk.op("v", "tensor_add", X(dc, qsl), X(dc, qsl), tmpf[:])


    def part_l0fa():
        NG = 8
        mk.dma("g", rowsB[32:33, 0:4096], ap["l0f_rows"])
        qkvb = rowsB[32:33, 0:3072]
        outb = rowsB[32:33, 3072:4096]
        n1sb = nc.alloc_sbuf_tensor("n1sb", [128, 8], BF16)
        n2sb = nc.alloc_sbuf_tensor("n2sb", [128, 8], BF16)
        mk.dma("g", n1sb[:], ap["l0f_n1"])
        mk.dma("g", n2sb[:], ap["l0f_n2"])
        ap["__n2sb__"] = n2sb
        bqk = nc.alloc_sbuf_tensor("bqk", [128, 16], F32)
        boc = nc.alloc_sbuf_tensor("boc", [128, 8], F32)
        ap["__bqk__"] = bqk
        mk.dma("g", bqk[:], ap["l0f_bqk"])
        mk.dma("g", boc[:], ap["l0f_boc"])
        vt_q = 0
        vt_k = 2 * S
        vt_v0 = 4 * S
        vt_cos = 4 * S + 2080
        vt_sin = 5 * S + 2080
        cosT = A_VT[:, vt_cos:vt_cos + S]
        sinT = A_VT[:, vt_sin:vt_sin + S]
        mk.dma("y", cosT, ap["c_cos"])
        mk.dma("y", sinT, ap["c_sin"])
        perm = nc.alloc_sbuf_tensor("perm", [128, 128], BF16)
        mk.dma("y", perm[:], ap["c_permT"])
        identF2 = identG

        def qh(h):
            return A_VT[0:65, vt_q + h * S:vt_q + (h + 1) * S]

        def kh(h):
            return A_VT[0:65, vt_k + h * S:vt_k + (h + 1) * S]
        bbuf = A_PT[:, 0:S]
        proll0 = S
        attst0 = S + 3072
        attn_st = A_PT[:, attst0:attst0 + S]

        for dc in range(DC):
            mk.dma("y", hx1[dc * 128:(dc + 1) * 128, :], X(dc))

        x1b = [A_NB[:, dc * S:(dc + 1) * S] for dc in range(DC)]
        for dc in range(DC):
            mk.op("v", "tensor_copy", x1b[dc], X(dc))
        ln_sum_stats(x1b)
        mu = A_SC[0:1, 0:S]
        rstd = A_SC[0:1, 2048:2048 + S]
        rtmp = A_SC[0:1, 4096:4608]
        rmss = A_SC[32:33, 512:512 + S]
        for ts in range(NQ):
            sl = slice(ts * 512, (ts + 1) * 512)
            mk.op("v", "tensor_scalar_mul", mu[:, sl], rowpack(ts), 1.0 / D)
            mk.op("v", "tensor_scalar_mul", rstd[:, sl], rowpack(4 + ts), 1.0 / D)
            mk.op("v", "tensor_mul", rtmp, mu[:, sl], mu[:, sl])
            mk.op("v", "tensor_sub", rstd[:, sl], rstd[:, sl], rtmp)
            mk.op("v", "tensor_scalar_add", rstd[:, sl], rstd[:, sl], 1e-5)
            mk.op("s", "activation", rtmp, rstd[:, sl], AF.Sqrt)
            mk.op("v", "reciprocal", rstd[:, sl], rtmp)
            mk.op("v", "tensor_mul", mu[:, sl], mu[:, sl], rstd[:, sl])
            mk.op("v", "tensor_mul", rtmp, rstd[:, sl], rstd[:, sl])
            mk.op("v", "tensor_scalar", rtmp, rtmp, -1e-5, 1.0 + 1e-6,
                  op0=ALU.mult, op1=ALU.add)
            mk.op("s", "activation", rtmp, rtmp, AF.Sqrt)
            mk.op("v", "reciprocal", rmss[:, sl], rtmp)

        for ts in range(NQ):
            sl = slice(ts * 512, (ts + 1) * 512)
            mk.mm(PSB[:, 0:512], onesF[0:1, :], rstd[:, sl], start=True, stop=True)
            mk.mm(PSB[:, 512:1024], onesF[0:1, :], mu[:, sl], start=True, stop=True)
            mk.mm(PSB[:, 1024:1536], onesF[32:33, :], rmss[:, sl], start=True, stop=True)
            for dc in range(DC):
                mk.op("v", "tensor_mul", tmpf, X(dc, sl), PSB[:, 0:512])
                mk.op("v", "tensor_sub", X(dc, sl), tmpf, PSB[:, 512:1024])
                mk.op("v", "scalar_tensor_tensor", x1b[dc][:, sl], X(dc, sl),
                      n1sb[:, dc:dc + 1], PSB[:, 1024:1536],
                      op0=ALU.mult, op1=ALU.mult)
        a = x1b

        for g in range(NG):
            ws0 = (g % 2) * 3072
            wg = A_WS[:, ws0:ws0 + 3072]
            mk.dma("g", wg, ap["l0f_wqkv"][:, g * 3072:(g + 1) * 3072])
            for which in range(2):
                bcol = bqk[:, which * 8 + g:which * 8 + g + 1]
                for ts in range(NQ):
                    sl = slice(ts * 512, (ts + 1) * 512)
                    for dc in range(DC):
                        mk.mm(PSA[:, sl], wg[:, dc * 384 + which * 128:dc * 384 + which * 128 + 128],
                              a[dc][:, sl], start=(dc == 0), stop=(dc == DC - 1))
                for ts in range(NQ):
                    sl = slice(ts * 512, (ts + 1) * 512)
                    mk.op("v", "scalar_tensor_tensor", bbuf[:, sl], PSA[:, sl], bcol,
                          sinT[:, sl], op0=ALU.add, op1=ALU.mult)
                    mk.op("v", "scalar_tensor_tensor", tmpf, PSA[:, sl], bcol,
                          cosT[:, sl], op0=ALU.add, op1=ALU.mult)
                    mk.mm(PSB[:, 0:512], perm[:], bbuf[:, sl], start=True, stop=True)
                    for h in range(2):
                        dst = qh(h) if which == 0 else kh(h)
                        mk.op("v", "tensor_add", dst[0:64, sl.start:sl.stop],
                              tmpf[h * 64:(h + 1) * 64, :],
                              PSB[h * 64:(h + 1) * 64, 0:512])
            for h in range(2):
                mk.op("v", "memset", kh(h)[64:65, 0:S], -1.0)
            mk.op("v", "memset", A_VT[:, vt_v0:vt_v0 + 2080], 1.0)
            for tc in range(NT):
                for dc in range(DC):
                    mk.mm(PSB[:, 512:640], a[dc][:, tc * 128:(tc + 1) * 128],
                          wg[:, dc * 384 + 256:dc * 384 + 384],
                          start=(dc == 0), stop=False)
                mk.mm(PSB[:, 512:640], onesB[32:33, 0:128],
                      qkvb[:, 2048 + g * 128:2048 + (g + 1) * 128],
                      start=False, stop=True)
                mk.op("v", "tensor_copy", A_VT[:, vt_v0 + tc * 130:vt_v0 + tc * 130 + 64],
                      PSB[:, 512:576])
                mk.op("v", "tensor_copy", A_VT[:, vt_v0 + tc * 130 + 65:vt_v0 + tc * 130 + 129],
                      PSB[:, 576:640])
            maxc = maxcG
            for h in range(2):
                for qt in range(NT):
                    qtl = slice(qt * 128, (qt + 1) * 128)
                    for kt in range(NQ):
                        mk.mm(PSA[:, kt * 512:(kt + 1) * 512], qh(h)[0:64, qtl],
                              kh(h)[0:64, kt * 512:(kt + 1) * 512], start=True, stop=True)
                    mk.op("v", "reduce_max", maxc[:, 0:1], PSA[:, 0:S], axis=AX.X)
                    mk.op("t", "transpose", PSB[0:1, 512:640], maxc[:, 0:1], identF2[:])
                    mk.op("v", "tensor_copy", qh(h)[64:65, qt * 128:(qt + 1) * 128],
                          PSB[0:1, 512:640])
            for h in range(2):
                for qt in range(NQ):
                    qsl = slice(qt * 512, (qt + 1) * 512)
                    pvslot = PSB[0:65, 1024:1536]
                    npair = NT // 2
                    slots = (PSA[:, 0:1024], PSA[:, 1024:2048], PSB[:, 0:1024])

                    def sco(j, h=h, qsl=qsl):
                        pa = slots[j % 3]
                        for jj in range(2):
                            kc = 2 * j + jj
                            mk.mm(pa[:, jj * 512:(jj + 1) * 512],
                                  kh(h)[:, kc * 128:(kc + 1) * 128],
                                  qh(h)[:, qsl], start=True, stop=True)
                        roll = A_PT[:, proll0 + (j % 3) * 1024:proll0 + (j % 3) * 1024 + 1024]
                        mk.op("s", "activation", roll, pa, AF.Exp, scale=0.125)
                        return roll

                    def pv(j, roll, h=h):
                        for jj in range(2):
                            kc = 2 * j + jj
                            mk.mm(pvslot,
                                  A_VT[:, vt_v0 + kc * 130 + h * 65:vt_v0 + kc * 130 + (h + 1) * 65],
                                  roll[:, jj * 512:(jj + 1) * 512],
                                  start=(kc == 0), stop=(kc == NT - 1))

                    rolls = {0: sco(0), 1: sco(1)}
                    for j in range(2, npair):
                        rolls[j] = sco(j)
                        pv(j - 2, rolls[j - 2])
                    pv(npair - 2, rolls[npair - 2])
                    pv(npair - 1, rolls[npair - 1])
                    mk.op("v", "tensor_copy", attn_st[h * 64:(h + 1) * 64, qsl],
                          pvslot[0:64, :])
                    mk.op("v", "tensor_copy",
                          A_SC[32:33, (h * NQ + qt) * 512:(h * NQ + qt) * 512 + 512],
                          pvslot[64:65, :])
            mk.op("v", "reciprocal", A_SC[32:33, 0:2 * NQ * 512],
                  A_SC[32:33, 0:2 * NQ * 512])
            for h in range(2):
                for qt in range(NQ):
                    qsl = slice(qt * 512, (qt + 1) * 512)
                    mk.mm(PSB[:, 1024:1536], onesF[32:33, :],
                          A_SC[32:33, (h * NQ + qt) * 512:(h * NQ + qt) * 512 + 512],
                          start=True, stop=True)
                    mk.op("v", "tensor_mul", attn_st[h * 64:(h + 1) * 64, qsl],
                          attn_st[h * 64:(h + 1) * 64, qsl], PSB[0:64, 1024:1536])
            mk.dma("y", hattn[g * 128:(g + 1) * 128, :], attn_st)
            drain_precast(4)

        wo = A_WS[:, 0:8192]
        mk.dma("g", wo, ap["l0f_wout"])
        for dc in range(DC):
            mk.dma("y", A_PT[:, dc * S:(dc + 1) * S], hattn[dc * 128:(dc + 1) * 128, :])
        for oc in range(DC):
            for ts in range(NQ):
                sl = slice(ts * 512, (ts + 1) * 512)
                for dc in range(DC):
                    mk.mm(PSA[:, sl], wo[:, dc * 1024 + oc * 128:dc * 1024 + (oc + 1) * 128],
                          A_PT[:, dc * S + ts * 512:dc * S + (ts + 1) * 512],
                          start=(dc == 0), stop=(dc == DC - 1))
                mk.op("v", "scalar_tensor_tensor", X(oc, sl), PSA[:, sl],
                      boc[:, oc:oc + 1], X(oc, sl), op0=ALU.add, op1=ALU.add)


    def part_l0fm(have_l0fa):
        NE, NH = 8, 16
        if not have_l0fa:
            n2sb = nc.alloc_sbuf_tensor("n2sb_m", [128, 8], BF16)
            mk.dma("g", n2sb[:], ap["l0f_n2"])
            for dc in range(DC):
                mk.dma("g", hx1[dc * 128:(dc + 1) * 128, :], ap["tx1"][dc * 128:(dc + 1) * 128, :])
        else:
            n2sb = ap["__n2sb__"]
        gwsb = nc.alloc_sbuf_tensor("gwsb", [128, 64], F32)
        mk.dma("g", gwsb[:], ap["l0f_gw"])
        n2f = nc.alloc_sbuf_tensor("n2f", [128, 8], F32)
        mk.dma("g", n2f[:], ap["l0f_n2"])
        rmsscol = nc.alloc_sbuf_tensor("rmsscol", [128, 16], F32)
        glt = nc.alloc_sbuf_tensor("glt", [128, 8], F32)
        v1 = nc.alloc_sbuf_tensor("v1", [128, 1], F32)
        v2 = nc.alloc_sbuf_tensor("v2", [128, 1], F32)
        sg = nc.alloc_sbuf_tensor("sg", [128, 1], F32)
        og = nc.alloc_sbuf_tensor("og", [128, 1], F32)
        tmp8 = nc.alloc_sbuf_tensor("tmp8", [128, 8], F32)
        mask1 = nc.alloc_sbuf_tensor("mask1", [128, 8], F32)
        identF = identG
        stage8 = tmpg[0:8, 0:128]
        b1sb = ap["__bqk__"]
        b2sb = ap["__bqkL__"]

        m = [A_NB[:, dc * S:(dc + 1) * S] for dc in range(DC)]

        for dc in range(DC):
            mk.op("v", "tensor_copy", m[dc], X(dc))
        ln_sum_stats(m, want_mean=False)
        rmss = A_SC[0:1, 0:S]
        rtmp = A_SC[0:1, 4096:4608]
        for ts in range(NQ):
            sl = slice(ts * 512, (ts + 1) * 512)
            mk.op("v", "tensor_scalar", rtmp, rowpack(4 + ts), 1.0 / D, 1e-6,
                  op0=ALU.mult, op1=ALU.add)
            mk.op("s", "activation", rtmp, rtmp, AF.Sqrt)
            mk.op("v", "reciprocal", rmss[:, sl], rtmp)
        for ts in range(NQ):
            sl = slice(ts * 512, (ts + 1) * 512)
            mk.mm(PSB[:, 0:512], onesF[0:1, :], rmss[:, sl], start=True, stop=True)
            for dc in range(DC):
                mk.op("v", "scalar_tensor_tensor", m[dc][:, sl], X(dc, sl),
                      n2sb[:, dc:dc + 1], PSB[:, 0:512], op0=ALU.mult, op1=ALU.mult)

        for dc in range(DC):
            mk.op("v", "tensor_scalar", gwsb[:, dc * 8:(dc + 1) * 8],
                  gwsb[:, dc * 8:(dc + 1) * 8], n2f[:, dc:dc + 1], None, op0=ALU.mult)
        for tt in range(NT):
            mk.op("t", "transpose", PSB[:, 1032:1033],
                  rmss[:, tt * 128:(tt + 1) * 128], identF[0:1, 0:1])
            mk.op("v", "tensor_copy", rmsscol[:, tt % 16:tt % 16 + 1], PSB[:, 1032:1033])
            tsl = slice(tt * 128, (tt + 1) * 128)
            for dc in range(DC):
                mk.mm(PSB[:, 1024:1032], X(dc)[:, tsl], gwsb[:, dc * 8:(dc + 1) * 8],
                      start=(dc == 0), stop=(dc == DC - 1))
            mk.op("v", "tensor_scalar", glt[:], PSB[:, 1024:1032],
                  rmsscol[:, tt % 16:tt % 16 + 1], None, op0=ALU.mult)
            mk.op("v", "reduce_max", v1[:, 0:1], glt[:], axis=AX.X)
            mk.op("v", "tensor_scalar", mask1[:], glt[:], v1[:, 0:1], None, op0=ALU.is_equal)
            mk.op("v", "scalar_tensor_tensor", tmp8[:], mask1[:], -1e30, glt[:],
                  op0=ALU.mult, op1=ALU.add)
            mk.op("v", "reduce_max", v2[:, 0:1], tmp8[:], axis=AX.X)
            mk.op("v", "tensor_sub", sg[:, 0:1], v2[:, 0:1], v1[:, 0:1])
            mk.op("s", "activation", sg[:, 0:1], sg[:, 0:1], AF.Exp)
            mk.op("v", "tensor_scalar_add", sg[:, 0:1], sg[:, 0:1], 1.0)
            mk.op("v", "reciprocal", sg[:, 0:1], sg[:, 0:1])
            mk.op("v", "tensor_scalar", og[:, 0:1], sg[:, 0:1], -1.0, 1.0,
                  op0=ALU.mult, op1=ALU.add)
            mk.op("v", "tensor_scalar", mask1[:], mask1[:], sg[:, 0:1], None, op0=ALU.mult)
            mk.op("v", "tensor_scalar", tmp8[:], tmp8[:], v2[:, 0:1], None, op0=ALU.is_equal)
            mk.op("v", "tensor_scalar", tmp8[:], tmp8[:], og[:, 0:1], None, op0=ALU.mult)
            combtile = glt
            mk.op("v", "tensor_add", combtile[:], mask1[:], tmp8[:])
            mk.op("t", "transpose", PSB[0:8, 0:128], combtile[:], identF[:])
            mk.op("v", "tensor_copy", stage8[:], PSB[0:8, 0:128])
            mk.dma("y", hcomb[:, tt * 128:(tt + 1) * 128], stage8[:])

        for dc in range(DC):
            for ts in range(NQ):
                sl = slice(ts * 512, (ts + 1) * 512)
                mk.dma("y", tmpg[:], hx1[dc * 128:(dc + 1) * 128, ts * 512:(ts + 1) * 512])
                mk.op("v", "tensor_add", X(dc, sl), X(dc, sl), tmpg[:])

        combrow = A_SC[0:1, 2048:2048 + S]
        combrow32 = A_SC[32:33, 0:S]
        brow = rowsB[0:1, 2048:4096]
        TH = S // 2
        NTH = max(1, TH // 512)
        THW = min(512, TH)

        me0 = 0
        w12s = (DC * TH, DC * TH + 2048)
        w3s = (DC * TH + 4096, DC * TH)
        for e in range(NE):
            mk.dma("g", combrow, hcomb[e:e + 1, :])
            mk.dma("g", combrow32, hcomb[e:e + 1, :])
            mk.dma("g", b1sb[:], ap["l0f_b1"][e])
            mk.dma("g", b2sb[:], ap["l0f_b2c"][e])
            drain_precast(4)
            for tbh in range(2):
                for th in range(NTH):
                    sl = slice(tbh * TH + th * THW, tbh * TH + (th + 1) * THW)
                    mk.mm(PSB[:, 1024:1024 + THW], onesF[0:1, :],
                          combrow[:, sl.start:sl.stop], start=True, stop=True)
                    mk.op("v", "tensor_copy",
                          rowsB[:, 2048 + th * THW:2048 + (th + 1) * THW],
                          PSB[:, 1024:1024 + THW])
                for hc in range(NH):
                    wsl = w12s[hc % 2]
                    w12 = A_WS[:, 0:1] if False else A_PT[:, wsl:wsl + 2048]
                    mk.dma("g", w12, ap["l0f_w12"][e, :, hc * 2048:(hc + 1) * 2048])
                    pa = PSA[:, (hc % 2) * TH:(hc % 2) * TH + TH]
                    pb = PSB[:, 0:TH]
                    for th in range(NTH):
                        sl = slice(tbh * TH + th * THW, tbh * TH + (th + 1) * THW)
                        psl = slice(th * THW, (th + 1) * THW)
                        for dc in range(DC):
                            mk.mm(pa[:, psl], w12[:, dc * 256:dc * 256 + 128],
                                  m[dc][:, sl], start=(dc == 0), stop=(dc == DC - 1))
                        for dc in range(DC):
                            mk.mm(pb[:, psl], w12[:, dc * 256 + 128:dc * 256 + 256],
                                  m[dc][:, sl], start=(dc == 0), stop=(dc == DC - 1))
                    hrow = A_VT[:, hc * TH:(hc + 1) * TH]
                    mk.op("s", "activation", hrow, pa, AF.Silu, bias=b1sb[:, hc:hc + 1])
                    mk.op("v", "scalar_tensor_tensor", hrow, pb, b2sb[:, hc:hc + 1],
                          hrow, op0=ALU.add, op1=ALU.mult)
                    for th in range(NTH):
                        psl = slice(th * THW, (th + 1) * THW)
                        mk.op("v", "tensor_mul", hrow[:, psl], hrow[:, psl],
                              rowsB[:, 2048 + th * THW:2048 + (th + 1) * THW])
                b3row = A_SC[32:33, 2048:2048 + 1024]
                mk.dma("g", b3row, ap["l0f_b3"][e:e + 1, 0:1024])
                for p2 in range(4):
                    wsl = w3s[p2 % 2]
                    w3c = A_PT[:, wsl:wsl + 4096]
                    mk.dma("g", w3c, ap["l0f_w3"][e, :, :, p2 * 256:(p2 + 1) * 256])
                    for dc2 in range(2):
                        dc = p2 * 2 + dc2
                        py = PSA[:, dc2 * TH:(dc2 + 1) * TH]
                        for th in range(NTH):
                            sl = slice(tbh * TH + th * THW, tbh * TH + (th + 1) * THW)
                            psl = slice(th * THW, (th + 1) * THW)
                            for hc in range(NH):
                                mk.mm(py[:, psl], w3c[:, hc * 256 + dc2 * 128:hc * 256 + (dc2 + 1) * 128],
                                      A_VT[:, hc * TH + th * THW:hc * TH + (th + 1) * THW],
                                      start=(hc == 0), stop=False)
                            mk.mm(py[:, psl], b3row[:, dc * 128:(dc + 1) * 128],
                                  combrow32[:, sl.start:sl.stop], start=False, stop=True)
                            mk.op("v", "tensor_add", X(dc, sl), X(dc, sl), py[:, psl])


    def part_l1a():
        NG = 8
        NW = S // 512
        mk.dma("g", rowsB[32:33, 0:4096], ap["l1a_rows"])
        qkvb = rowsB[32:33, 0:3072]
        outb = rowsB[32:33, 3072:4096]
        bqkL = ap["__bqkL__"]
        bocL = nc.alloc_sbuf_tensor("bocL", [128, 8], F32)
        mk.dma("g", bqkL[:], ap["l1a_bqk"])
        mk.dma("g", bocL[:], ap["l1a_boc"])
        padc = rowsB[64:65, 2048:2113]
        mk.op("v", "memset", padc[0:1, 0:64], -512.0)
        mk.op("v", "memset", padc[0:1, 64:65], 512.0)
        epad = rowsB[64:65, 2176:2688]

        vt_v0 = 0
        vt_q = 2080
        vt_k = 2080 + 2 * S
        attst0 = S + 3072
        attn_st = A_PT[:, attst0:attst0 + S]
        proll0 = S
        msk = A_PT[:, attst0 + S:attst0 + S + 4096]
        mk.dma("g", msk, ap["c_l1mask"])
        identF3 = identG
        maxc1 = maxcG

        def qh(h):
            return A_VT[0:65, vt_q + h * S:vt_q + (h + 1) * S]

        def kh(h):
            return A_VT[0:65, vt_k + h * S:vt_k + (h + 1) * S]

        zb = [A_NB[:, dc * S:(dc + 1) * S] for dc in range(DC)]
        for dc in range(DC):
            mk.op("v", "tensor_copy", zb[dc], X(dc))
        ln_sum_stats(zb)
        mu = A_SC[0:1, 0:S]
        rstd = A_SC[0:1, 2048:2048 + S]
        rtmp = A_SC[0:1, 4096:4608]
        for ts in range(NQ):
            sl = slice(ts * 512, (ts + 1) * 512)
            mk.op("v", "tensor_scalar_mul", mu[:, sl], rowpack(ts), 1.0 / D)
            mk.op("v", "tensor_scalar_mul", rstd[:, sl], rowpack(4 + ts), 1.0 / D)
            mk.op("v", "tensor_mul", rtmp, mu[:, sl], mu[:, sl])
            mk.op("v", "tensor_sub", rstd[:, sl], rstd[:, sl], rtmp)
            mk.op("v", "tensor_scalar_add", rstd[:, sl], rstd[:, sl], 1e-5)
            mk.op("s", "activation", rtmp, rstd[:, sl], AF.Sqrt)
            mk.op("v", "reciprocal", rstd[:, sl], rtmp)
            mk.op("v", "tensor_mul", mu[:, sl], mu[:, sl], rstd[:, sl])
        for ts in range(NQ):
            sl = slice(ts * 512, (ts + 1) * 512)
            mk.mm(PSB[:, 0:512], onesF[0:1, :], rstd[:, sl], start=True, stop=True)
            mk.mm(PSB[:, 512:1024], onesF[0:1, :], mu[:, sl], start=True, stop=True)
            for dc in range(DC):
                mk.op("v", "tensor_mul", tmpf, X(dc, sl), PSB[:, 0:512])
                mk.op("v", "tensor_sub", zb[dc][:, sl], tmpf, PSB[:, 512:1024])

        for g in range(NG):
            ws0 = (g % 2) * 3072
            wg = A_WS[:, ws0:ws0 + 3072]
            mk.dma("g", wg, ap["l1a_wqkv"][:, g * 3072:(g + 1) * 3072])
            for which in range(2):
                bcol = bqkL[:, which * 8 + g:which * 8 + g + 1]
                for ts in range(NQ):
                    sl = slice(ts * 512, (ts + 1) * 512)
                    for dc in range(DC):
                        mk.mm(PSA[:, sl], wg[:, dc * 384 + which * 128:dc * 384 + which * 128 + 128],
                              zb[dc][:, sl], start=(dc == 0), stop=(dc == DC - 1))
                for ts in range(NQ):
                    sl = slice(ts * 512, (ts + 1) * 512)
                    for h in range(2):
                        dst = qh(h) if which == 0 else kh(h)
                        mk.op("v", "tensor_scalar", dst[0:64, sl.start:sl.stop],
                              PSA[h * 64:(h + 1) * 64, sl],
                              bcol[h * 64:(h + 1) * 64, 0:1], None, op0=ALU.add)
            for h in range(2):
                mk.op("v", "memset", kh(h)[64:65, 0:S], -1.0)
            mk.op("v", "memset", A_VT[:, vt_v0:vt_v0 + 2080], 1.0)
            for tc in range(NT):
                for dc in range(DC):
                    mk.mm(PSB[:, 512:640], zb[dc][:, tc * 128:(tc + 1) * 128],
                          wg[:, dc * 384 + 256:dc * 384 + 384],
                          start=(dc == 0), stop=False)
                mk.mm(PSB[:, 512:640], onesB[32:33, 0:128],
                      qkvb[:, 2048 + g * 128:2048 + (g + 1) * 128],
                      start=False, stop=True)
                mk.op("v", "tensor_copy", A_VT[:, vt_v0 + tc * 130:vt_v0 + tc * 130 + 64],
                      PSB[:, 512:576])
                mk.op("v", "tensor_copy", A_VT[:, vt_v0 + tc * 130 + 65:vt_v0 + tc * 130 + 129],
                      PSB[:, 576:640])
            for h in range(2):
                for w in range(NW):
                    qsl = slice(w * 512, (w + 1) * 512)
                    kcs = [kc for kc in range(12) if 0 <= (w - 1) * 512 + kc * 128 < S]
                    has_lpad = (w == 0)
                    has_rpad = (w == NW - 1)
                    npad = (1 if has_lpad else 0) + (1 if has_rpad else 0)
                    kt0 = (w - 1) * 512 + kcs[0] * 128
                    kt1 = (w - 1) * 512 + (kcs[-1] + 1) * 128
                    if npad:
                        mk.mm(rowpack(6), onesB[0:64, 0:1], qh(h)[0:64, qsl],
                              start=True, stop=True)
                    for qt in range(4):
                        qtl = slice(w * 512 + qt * 128, w * 512 + (qt + 1) * 128)
                        nk = kt1 - kt0
                        for i, c0 in enumerate(range(kt0, kt1, 512)):
                            cw = min(512, kt1 - c0)
                            mk.mm(PSA[:, i * 512:i * 512 + cw], qh(h)[0:64, qtl],
                                  kh(h)[0:64, c0:c0 + cw], start=True, stop=True)
                        mk.op("v", "reduce_max", maxc1[:, 0:1], PSA[:, 0:nk], axis=AX.X)
                        mk.op("t", "transpose", PSB[0:1, 512:640], maxc1[:, 0:1], identF3[:])
                        mk.op("v", "tensor_copy", qh(h)[64:65, qtl.start:qtl.stop],
                              PSB[0:1, 512:640])
                    if npad:
                        mrow = A_SC[0:1, 4096:4608]
                        spadrow = A_SC[0:1, 0:512]
                        mk.op("v", "tensor_scalar_mul", spadrow, rowpack(6), -1.0)
                        mk.op("v", "tensor_copy", mrow[:, 0:512], qh(h)[64:65, qsl])
                        mk.op("v", "tensor_max", mrow[:, 0:512], mrow[:, 0:512], spadrow)
                        mk.op("v", "tensor_copy", qh(h)[64:65, qsl.start:qsl.stop], mrow[:, 0:512])
                        mk.op("v", "tensor_sub", spadrow, spadrow, mrow[:, 0:512])
                        mk.op("s", "activation", epad, spadrow, AF.Exp)
                    pvslot = PSB[0:65, 1024:1536]
                    slots = (PSA[:, 0:1024], PSA[:, 1024:2048], PSB[:, 0:1024])
                    pairs = [kcs[i:i + 2] for i in range(0, len(kcs), 2)]

                    def sco(j, h=h, w=w, qsl=qsl):
                        pa = slots[j % 3]
                        for jj, kc in enumerate(pairs[j]):
                            kt = (w - 1) * 512 + kc * 128
                            mk.mm(pa[:, jj * 512:(jj + 1) * 512],
                                  kh(h)[:, kt:kt + 128], qh(h)[:, qsl],
                                  start=True, stop=True)
                        roll = A_PT[:, proll0 + (j % 3) * 1024:proll0 + (j % 3) * 1024 + 1024]
                        wid = len(pairs[j]) * 512
                        mk.op("s", "activation", roll[:, 0:wid], pa[:, 0:wid], AF.Exp)
                        for jj, kc in enumerate(pairs[j]):
                            ci = kc if kc < 4 else (kc - 4 if kc >= 8 else None)
                            if ci is not None:
                                mk.op("v", "tensor_mul", roll[:, jj * 512:(jj + 1) * 512],
                                      roll[:, jj * 512:(jj + 1) * 512],
                                      msk[:, ci * 512:(ci + 1) * 512])
                        return roll

                    def pv(j, roll, h=h, w=w):
                        for jj, kc in enumerate(pairs[j]):
                            kt = (w - 1) * 512 + kc * 128
                            ktc = kt // 128
                            idx = kcs.index(kc)
                            mk.mm(pvslot,
                                  A_VT[:, vt_v0 + ktc * 130 + h * 65:vt_v0 + ktc * 130 + (h + 1) * 65],
                                  roll[:, jj * 512:(jj + 1) * 512],
                                  start=(idx == 0),
                                  stop=(npad == 0 and idx == len(kcs) - 1))

                    rolls = {0: sco(0)}
                    if len(pairs) > 1:
                        rolls[1] = sco(1)
                    for j in range(2, len(pairs)):
                        rolls[j] = sco(j)
                        pv(j - 2, rolls[j - 2])
                    if len(pairs) > 1:
                        pv(len(pairs) - 2, rolls[len(pairs) - 2])
                    pv(len(pairs) - 1, rolls[len(pairs) - 1])
                    for t in range(npad):
                        mk.mm(pvslot, padc[0:1, :], epad,
                              start=False, stop=(t == npad - 1))
                    mk.op("v", "tensor_copy", attn_st[h * 64:(h + 1) * 64, qsl],
                          pvslot[0:64, :])
                    mk.op("v", "tensor_copy",
                          A_SC[32:33, (h * NW + w) * 512:(h * NW + w) * 512 + 512],
                          pvslot[64:65, :])
            mk.op("v", "reciprocal", A_SC[32:33, 0:2 * NW * 512],
                  A_SC[32:33, 0:2 * NW * 512])
            for h in range(2):
                for w in range(NW):
                    qsl = slice(w * 512, (w + 1) * 512)
                    mk.mm(PSB[:, 1024:1536], onesF[32:33, :],
                          A_SC[32:33, (h * NW + w) * 512:(h * NW + w) * 512 + 512],
                          start=True, stop=True)
                    mk.op("v", "tensor_mul", attn_st[h * 64:(h + 1) * 64, qsl],
                          attn_st[h * 64:(h + 1) * 64, qsl], PSB[0:64, 1024:1536])
            mk.dma("y", hattn[g * 128:(g + 1) * 128, :], attn_st)

        wo = A_WS[:, 0:8192]
        mk.dma("g", wo, ap["l1a_wout"])
        for dc in range(DC):
            mk.dma("y", A_PT[:, dc * S:(dc + 1) * S], hattn[dc * 128:(dc + 1) * 128, :])
        for oc in range(DC):
            for ts in range(NQ):
                sl = slice(ts * 512, (ts + 1) * 512)
                for dc in range(DC):
                    mk.mm(PSA[:, sl], wo[:, dc * 1024 + oc * 128:dc * 1024 + (oc + 1) * 128],
                          A_PT[:, dc * S + ts * 512:dc * S + (ts + 1) * 512],
                          start=(dc == 0), stop=(dc == DC - 1))
                mk.op("v", "scalar_tensor_tensor", X(oc, sl), PSA[:, sl],
                      bocL[:, oc:oc + 1], X(oc, sl), op0=ALU.add, op1=ALU.add)


    def part_l1f():
        drain_precast(64)
        NP = 2048
        MT = S // 128
        rg = [list(range(n_cores))]
        ub = [A_NB[:, dc * S:(dc + 1) * S] for dc in range(DC)]
        for dc in range(DC):
            mk.op("v", "tensor_copy", ub[dc], X(dc))
        ln_sum_stats(ub)
        mu = A_SC[0:1, 0:S]
        rstd = A_SC[0:1, 2048:2048 + S]
        rtmp = A_SC[0:1, 4096:4608]
        for ts in range(NQ):
            sl = slice(ts * 512, (ts + 1) * 512)
            mk.op("v", "tensor_scalar_mul", mu[:, sl], rowpack(ts), 1.0 / D)
            mk.op("v", "tensor_scalar_mul", rstd[:, sl], rowpack(4 + ts), 1.0 / D)
            mk.op("v", "tensor_mul", rtmp, mu[:, sl], mu[:, sl])
            mk.op("v", "tensor_sub", rstd[:, sl], rstd[:, sl], rtmp)
            mk.op("v", "tensor_scalar_add", rstd[:, sl], rstd[:, sl], 1e-5)
            mk.op("s", "activation", rtmp, rstd[:, sl], AF.Sqrt)
            mk.op("v", "reciprocal", rstd[:, sl], rtmp)
            mk.op("v", "tensor_mul", mu[:, sl], mu[:, sl], rstd[:, sl])
        for ts in range(NQ):
            sl = slice(ts * 512, (ts + 1) * 512)
            mk.mm(PSB[:, 0:512], onesF[0:1, :], rstd[:, sl], start=True, stop=True)
            mk.mm(PSB[:, 512:1024], onesF[0:1, :], mu[:, sl], start=True, stop=True)
            for dc in range(DC):
                mk.op("v", "tensor_mul", tmpf, X(dc, sl), PSB[:, 0:512])
                mk.op("v", "tensor_sub", ub[dc][:, sl], tmpf, PSB[:, 512:1024])

        for mt in range(MT):
            for dc in range(DC):
                slot = (dc % 2) * 128
                mk.op("t", "transpose", PST[:, slot:slot + 128],
                      ub[dc][:, mt * 128:(mt + 1) * 128], ident[:])
                mk.op("v", "tensor_copy",
                      A_VT[:, mt * 1024 + dc * 128:mt * 1024 + (dc + 1) * 128],
                      PST[:, slot:slot + 128])

        phih = A_WS[:, 0:8192]
        ar_chunk = max(512, S // 4)
        n_chunks = S // ar_chunk
        done_rows = 0
        for nph in range(2):
            mk.dma("g", phih, ap["l1f_phi"].rearrange("p (dc np) -> p dc np", np=2048)[:, :, nph * 1024:(nph + 1) * 1024])
            for mt in range(MT):
                for s5 in range(2):
                    nsl = slice(s5 * 512, (s5 + 1) * 512)
                    for dc in range(DC):
                        mk.mm(PSA[:, nsl], ub[dc][:, mt * 128:(mt + 1) * 128],
                              phih[:, dc * 1024 + s5 * 512:dc * 1024 + (s5 + 1) * 512],
                              start=(dc == 0), stop=(dc == DC - 1))
                    mk.op("s", "activation", tmpg[:], PSA[:, nsl], AF.Exp)
                    mk.dma("y", hE.ap()[mt * 128:(mt + 1) * 128,
                                        nph * 1024 + s5 * 512:nph * 1024 + (s5 + 1) * 512],
                           tmpg[:])
            if nph == 1:
                for c in range(n_chunks):
                    mk.op("g", "collective_compute", "AllReduce", ALU.add,
                          replica_groups=rg,
                          ins=[hE.ap()[c * ar_chunk:(c + 1) * ar_chunk, :].opt()],
                          outs=[hS.ap()[c * ar_chunk:(c + 1) * ar_chunk, :].opt()])

        dcache0 = 0
        for npb in range(4):
            nsl = slice(npb * 512, (npb + 1) * 512)
            for mt in range(MT):
                mk.dma("y", tmpg[:], hE.ap()[mt * 128:(mt + 1) * 128, nsl])
                mk.dma("y", tmpf[:, 0:512] if False else tmpf, hS.ap()[mt * 128:(mt + 1) * 128, nsl])
                mk.op("v", "reciprocal", tmpf, tmpf)
                mk.op("v", "tensor_mul", A_PT[:, dcache0 + mt * 512:dcache0 + (mt + 1) * 512],
                      tmpg[:], tmpf)
            for half in range(2):
                for dci in range(4):
                    dc = half * 4 + dci
                    for mt in range(MT):
                        mk.mm(PSA[:, dci * 512:(dci + 1) * 512],
                              A_VT[:, mt * 1024 + dc * 128:mt * 1024 + (dc + 1) * 128],
                              A_PT[:, dcache0 + mt * 512:dcache0 + (mt + 1) * 512],
                              start=(mt == 0), stop=(mt == MT - 1))
                for dci in range(4):
                    dc = half * 4 + dci
                    mk.op("v", "tensor_copy",
                          A_NB[:, dc * 2048 + npb * 512:dc * 2048 + (npb + 1) * 512],
                          PSA[:, dci * 512:(dci + 1) * 512])

        wslots = (A_WS[:, 0:8192], A_PT[:, 0:8192], A_PT[:, 8192:16384])
        for npq in range(16):
            pbase = 0 if npq % 2 == 0 else 1024
            for lane in range(4):
                n = npq * 4 + lane
                wslot = wslots[n % 3]
                mk.dma("y", wslot, hWbf.ap()[n])
                for eh in range(2):
                    pslot = PSA[:, pbase + eh * 512:pbase + (eh + 1) * 512]
                    for dc in range(DC):
                        mk.mm(pslot[lane * 32:lane * 32 + 32, :],
                              A_NB[:, dc * 2048 + npq * 128 + lane * 32:dc * 2048 + npq * 128 + (lane + 1) * 32],
                              wslot[:, dc * 1024 + eh * 512:dc * 1024 + (eh + 1) * 512],
                              start=(dc == 0), stop=(dc == DC - 1),
                              tile_position=(0, lane * 32), skip_group_check=True)
            for lane in range(4):
                n = npq * 4 + lane
                brep = rowsB[0:32, (lane % 2) * 1024:(lane % 2) * 1024 + 1024]
                mk.dma("g", brep, ap["l1f_brep"][npq * 128 + lane * 32:npq * 128 + (lane + 1) * 32, :])
                for eh in range(2):
                    pslot = PSA[:, pbase + eh * 512:pbase + (eh + 1) * 512]
                    mk.op("v", "tensor_add",
                          A_VT[:, npq * 1024 + eh * 512:npq * 1024 + (eh + 1) * 512][lane * 32:lane * 32 + 32, :],
                          pslot[lane * 32:lane * 32 + 32, :], brep[:, eh * 512:(eh + 1) * 512])

        est0 = 8192
        for mb in range(S // 256):
            for i in range(2):
                mrow = mb * 256 + i * 128
                mk.dma("g", A_PT[:, est0 + i * 2048:est0 + (i + 1) * 2048],
                       hE.ap()[mrow:mrow + 128, :])
                for npc in range(16):
                    slot = (npc % 2) * 128
                    mk.op("t", "transpose", PST[:, slot:slot + 128],
                          A_PT[:, est0 + i * 2048 + npc * 128:est0 + i * 2048 + (npc + 1) * 128],
                          ident[:])
                    mk.op("v", "tensor_copy",
                          A_PT[:, est0 + 4096 + npc * 256 + i * 128:est0 + 4096 + npc * 256 + (i + 1) * 128],
                          PST[:, slot:slot + 128])
            for npc in range(16):
                enp = A_PT[:, est0 + 4096 + npc * 256:est0 + 4096 + (npc + 1) * 256]
                for ec in range(DC):
                    bank, off = divmod(ec, 2)
                    mk.mm(PSA[:, bank * 512 + off * 256:bank * 512 + off * 256 + 256],
                          A_VT[:, npc * 1024 + ec * 128:npc * 1024 + (ec + 1) * 128],
                          enp, start=(npc == 0 and off == 0), stop=(npc == 15 and off == 1),
                          skip_group_check=True)
                mk.mm(rowpack(6)[:, 0:256], onesB[:, 0:1], enp,
                      start=(npc == 0), stop=(npc == 15))
            srow = A_SC[32:33, 0:256]
            mk.op("v", "tensor_copy", srow, rowpack(6)[:, 0:256])
            mk.op("v", "reciprocal", srow, srow)
            mk.mm(PSB[:, 512:768], onesF[32:33, :], srow, start=True, stop=True)
            mk.op("v", "tensor_copy", tmpg[:, 0:256], PSB[:, 512:768])
            for ec in range(DC):
                bank, off = divmod(ec, 2)
                mk.op("v", "tensor_mul", tmpf[:, 0:256],
                      PSA[:, bank * 512 + off * 256:bank * 512 + off * 256 + 256],
                      tmpg[:, 0:256])
                mk.op("v", "tensor_add", X(ec)[:, mb * 256:(mb + 1) * 256],
                      X(ec)[:, mb * 256:(mb + 1) * 256], tmpf[:, 0:256])

    if "l0a" in parts:
        part_l0a()
    if "l0fa" in parts:
        part_l0fa()
    if "l0fm" in parts:
        part_l0fm("l0fa" in parts)
    if "l1a" in parts:
        part_l1a()
    if "l1f" in parts:
        part_l1f()

    for dc in range(DC):
        mk.dma("y", out_x[dc * 128:(dc + 1) * 128, :], A_X[:, dc * S:(dc + 1) * S])
    mk.op("y", "nop")
    mk.flush(verbose=True)
    return nc


_CACHE = {}


def kernel(**inputs):
    S = 2048
    n_cores = 8
    if "nc" not in _CACHE:
        _CACHE["nc"] = build_program(S=S, parts=("l0a", "l0fa", "l0fm", "l1a", "l1f"),
                                     n_cores=n_cores)
    nc = _CACHE["nc"]
    prep = {}
    prep.update(prep_l0a(inputs))
    prep.update(prep_l0f(inputs))
    prep.update(prep_l1a(inputs))
    prep.update(prep_l1f(inputs))
    prep.update(host_constants(S))
    x = np.ascontiguousarray(np.asarray(inputs["L0_attn_x"], dtype=np.float32))
    maps = []
    for c in range(n_cores):
        im = dict(prep)
        im["xT"] = np.ascontiguousarray(x[c].T)
        maps.append(im)
    res = run_bass_kernel_spmd(nc, maps, core_ids=list(range(n_cores)))
    out = np.stack([np.ascontiguousarray(res.results[c]["outX"].T) for c in range(n_cores)], 0)
    return out


# version 13
# speedup vs baseline: 1.3182x; 1.0370x over previous
import numpy as np
import ml_dtypes
import concourse.bass as bass
import concourse.mybir as mybir
from concourse.bass_utils import run_bass_kernel_spmd
from contextlib import ExitStack

F32 = mybir.dt.float32
BF16 = mybir.dt.bfloat16
AF = mybir.ActivationFunctionType
ALU = mybir.AluOpType
AX = mybir.AxisListType

D = 1024
DC = 8
P129 = 129


DTSZ = {F32: 4, BF16: 2, mybir.dt.float16: 2, mybir.dt.int32: 4, mybir.dt.uint8: 1}


class _Rec:
    __slots__ = ("idx", "eng", "method", "args", "kw", "mwaits", "reads", "writes",
                 "is_dma", "ticked", "sem", "val", "edges", "clock", "eidx",
                 "_needed", "_waits", "_fence")

    def __init__(self, idx, eng, method, args, kw, mwaits, is_dma):
        self.idx = idx
        self.eng = eng
        self.method = method
        self.args = args
        self.kw = kw
        self.mwaits = mwaits
        self.is_dma = is_dma
        self.ticked = False
        self.sem = None
        self.val = None
        self.edges = []
        self.clock = None
        self.eidx = 0


class _SemPool:
    def __init__(self, nc, es, name, k, step):
        self.sems = [es.enter_context(nc.semaphore(f"{name}{i}")) for i in range(k)]
        self.count = [0] * k
        self.i = 0
        self.step = step

    def take(self):
        i = self.i
        self.i = (self.i + 1) % len(self.sems)
        prev = self.count[i]
        self.count[i] += self.step
        return self.sems[i], self.count[i], prev


class MK:
    ENGS = ("t", "v", "s", "g", "y")

    def __init__(self, nc):
        self.nc = nc
        self.es = ExitStack()
        self.ops = []
        self.aliases = {}
        self.pools = {
            "t": _SemPool(nc, self.es, "pt_", 12, 1),
            "v": _SemPool(nc, self.es, "pv_", 16, 1),
            "s": _SemPool(nc, self.es, "ps_", 12, 1),
            "g": _SemPool(nc, self.es, "pg_", 6, 1),
            "y": _SemPool(nc, self.es, "py_", 4, 1),
            "gd": _SemPool(nc, self.es, "pgd_", 16, 16),
            "yd": _SemPool(nc, self.es, "pyd_", 12, 16),
        }

    def alias(self, name, shape, dtype, backing):
        v = self.nc.virtual_sbuf_tensor(name, list(shape), dtype)
        self.nc.back_virtual_tensor_with(v, backing)
        self.aliases[name] = backing.name
        return v

    def op(self, eng, method, *args, wait=(), tick=False, **kw):
        r = _Rec(len(self.ops), eng, method, args, kw,
                 tuple(t for t in wait if t is not None), False)
        self.ops.append(r)
        return r

    def dma(self, eng, out, in_, wait=()):
        r = _Rec(len(self.ops), eng, "dma_start", (), {"out": out, "in_": in_},
                 tuple(t for t in wait if t is not None), True)
        self.ops.append(r)
        return r

    def tick(self, eng, wait=()):
        return self.op(eng, "nop", wait=wait)

    def mm(self, out, lhsT, rhs, start, stop, wait=(), tick=False, **kw):
        return self.op("t", "matmul", out, lhsT, rhs, start=start, stop=stop, wait=wait, **kw)

    PSUM_NAMES = ("PSA", "PSB", "PST")

    def _rect(self, ap):
        t = ap.tensor
        name = self.aliases.get(t.name, t.name)
        sz = DTSZ[t.dtype]
        dims = list(ap.ap)
        off = ap.offset
        if len(t.shape) >= 2 and t.shape[0] <= 128 and not name.startswith("DR_"):
            pcols = t.shape[1] if len(t.shape) == 2 else int(np.prod(t.shape[1:]))
            p0 = off // pcols
            c0 = off % pcols
            p1 = p0 + (dims[0][1] if dims else 1)
            cspan = sum((cnt - 1) * st for st, cnt in dims[1:]) + 1 if len(dims) > 1 else 1
            b0, b1 = c0 * sz, (c0 + cspan) * sz
            if name in self.PSUM_NAMES:
                p0, p1 = 0, 128
                b0 = (b0 // 2048) * 2048
                b1 = ((b1 + 2047) // 2048) * 2048
            return (name, p0, p1, b0, b1)
        span = sum((cnt - 1) * st for st, cnt in dims) + 1
        return (name, 0, 1, off * sz, (off + span) * sz)

    def _is_ap(self, x):
        return hasattr(x, "tensor") and hasattr(x, "offset")

    def _io(self, r):
        writes, reads = [], []
        outs = []
        if "out" in r.kw:
            outs.append(r.kw["out"])
        if "accum_out" in r.kw and self._is_ap(r.kw.get("accum_out")):
            outs.append(r.kw["accum_out"])
        args = list(r.args)
        if args and self._is_ap(args[0]):
            outs.append(args[0])
            args = args[1:]
        for a in args:
            if self._is_ap(a):
                reads.append(a)
        for k, vv in r.kw.items():
            if k in ("out", "accum_out"):
                continue
            if k == "outs" and isinstance(vv, list):
                outs.extend(x for x in vv if self._is_ap(x))
                continue
            if isinstance(vv, list):
                reads.extend(x for x in vv if self._is_ap(x))
                continue
            if self._is_ap(vv):
                reads.append(vv)
        return [self._rect(a) for a in reads], [self._rect(a) for a in outs]

    @staticmethod
    def _buckets(rect):
        name, p0, p1, b0, b1 = rect
        for pb in range(p0 // 32, (p1 + 31) // 32):
            for cb in range(b0 // 512, (b1 + 511) // 512):
                yield (name, pb, cb)

    @staticmethod
    def _overlap(a, b):
        return a[0] == b[0] and a[1] < b[2] and b[1] < a[2] and a[3] < b[4] and b[3] < a[4]

    def flush(self, verbose=False):
        last_w = {}
        readers = {}
        eidx = {e: 0 for e in self.ENGS}
        for r in self.ops:
            r.eidx = eidx[r.eng]
            eidx[r.eng] += 1
            reads, writes = self._io(r)
            r.reads, r.writes = reads, writes
            edges = {}
            for rect in reads:
                short = (rect[1], rect[3], rect[2], rect[4])
                for bk in self._buckets(rect):
                    w = last_w.get(bk)
                    if w is not None and w[1] is not r:
                        edges[w[1].idx] = w[1]
            for rect in writes:
                for bk in self._buckets(rect):
                    w = last_w.get(bk)
                    if w is not None and w[1] is not r:
                        edges[w[1].idx] = w[1]
                    for rd in readers.get(bk, ()):
                        if rd is not r:
                            edges[rd.idx] = rd
            for rect in reads:
                for bk in self._buckets(rect):
                    readers.setdefault(bk, []).append(r)
            for rect in writes:
                for bk in self._buckets(rect):
                    last_w[bk] = (rect, r)
                    readers[bk] = []
            r.edges = [self.ops[i] for i in sorted(edges)]

        prev_on_engine = {e: None for e in self.ENGS}
        for r in self.ops:
            ck = {}
            pred = prev_on_engine[r.eng]
            if pred is not None:
                ck.update(pred.clock)
                if r.eng == "t":
                    ck["T"] = max(ck.get("T", -1), pred.eidx)
            needed = []
            for p in r.edges + [w if isinstance(w, _Rec) else None for w in r.mwaits]:
                if p is None:
                    continue
                if p.eng == "t" and r.eng == "t" and not p.is_dma:
                    if ck.get("T", -1) < p.eidx:
                        ck["T"] = max(ck.get("T", -1), p.eidx)
                        for k, vv in p.clock.items():
                            if ck.get(k, -1 if k == "T" else 0) < vv:
                                ck[k] = vv
                    continue
                needed.append(p)
            for p in needed:
                p.ticked = True
            r.clock = ck
            r._needed = needed
            prev_on_engine[r.eng] = r

        for r in self.ops:
            if r.ticked or r.is_dma:
                pool = self.pools[r.eng + "d"] if r.is_dma else self.pools[r.eng]
                r.sem, r.val, r._fence = pool.take()
        prev_on_engine = {e: None for e in self.ENGS}
        nwaits = 0
        for r in self.ops:
            ck = {}
            pred = prev_on_engine[r.eng]
            if pred is not None:
                ck.update(pred.clock)
                if r.eng == "t":
                    ck["T"] = max(ck.get("T", -1), pred.eidx)
            waits = []
            for p in r._needed:
                if p.eng == "t" and r.eng == "t" and not p.is_dma:
                    continue
                if p.sem is None:
                    raise RuntimeError(f"producer without sem: {p.method}")
                key = id(p.sem)
                if ck.get(key, 0) >= p.val:
                    continue
                waits.append((p.sem, p.val))
                ck[key] = p.val
                for k, vv in p.clock.items():
                    if k == "T":
                        ck["T"] = max(ck.get("T", -1), vv)
                    elif ck.get(k, 0) < vv:
                        ck[k] = vv
                if p.eng == "t" and not p.is_dma:
                    ck["T"] = max(ck.get("T", -1), p.eidx)
            r._waits = waits
            nwaits += len(waits)
            r.clock = ck
            prev_on_engine[r.eng] = r
        if verbose:
            nt = sum(1 for r in self.ops if r.ticked)
            print(f"[mk] ops={len(self.ops)} ticked={nt} waits={nwaits}")

        eng_ops = {e: [r for r in self.ops if r.eng == e] for e in self.ENGS}

        with self.nc.Block() as block:
            def mk_body(name):
                def body(eng):
                    seen = {}
                    for r in eng_ops[name]:
                        if r.sem is not None and r._fence > 0 and seen.get(id(r.sem), 0) < r._fence:
                            eng.wait_ge(r.sem, r._fence)
                            seen[id(r.sem)] = r._fence
                        for (sem, val) in r._waits:
                            if seen.get(id(sem), 0) < val:
                                eng.wait_ge(sem, val)
                                seen[id(sem)] = val
                        ins = getattr(eng, r.method)(*r.args, **r.kw)
                        if r.sem is not None:
                            ins.then_inc(r.sem, 16 if r.is_dma else 1)
                return body
            block.tensor(mk_body("t"))
            block.vector(mk_body("v"))
            block.scalar(mk_body("s"))
            block.gpsimd(mk_body("g"))
            block.sync(mk_body("y"))


def host_constants(S):
    c = {"c_ident": np.eye(128, dtype=np.float32).astype(ml_dtypes.bfloat16),
         "c_identF": np.eye(128, dtype=np.float32)}
    inv_freq = 1.0 / 10000.0 ** (np.arange(0, 64, 2, dtype=np.float32) / 64.0)
    t = np.arange(S, dtype=np.float32)
    freqs = inv_freq[:, None] * t[None, :]
    emb = np.concatenate([freqs, freqs], 0)
    cos = np.cos(emb)
    sin = np.sin(emb)
    c["c_cos"] = np.concatenate([cos, cos], 0).astype(ml_dtypes.bfloat16)
    c["c_sin"] = np.concatenate([sin, sin], 0).astype(ml_dtypes.bfloat16)
    perm = np.zeros((128, 128), dtype=np.float32)
    for blk in (0, 64):
        for d_ in range(32):
            perm[blk + d_, blk + d_ + 32] = -1.0
            perm[blk + 32 + d_, blk + d_] = 1.0
    c["c_permT"] = perm.T.copy().astype(ml_dtypes.bfloat16)
    msk = np.zeros((128, 8 * 512), dtype=ml_dtypes.bfloat16)
    for ci in range(8):
        kc = ci if ci < 4 else ci + 4
        for p in range(128):
            kj = kc * 128 + p
            qi = np.arange(512)
            msk[p, ci * 512:(ci + 1) * 512] = ((kj >= qi) & (kj <= qi + 1024)).astype(np.float32)
    c["c_l1mask"] = msk
    return c


def wimg(wT):
    Din, O = wT.shape
    return np.ascontiguousarray(wT.reshape(Din // 128, 128, O).transpose(1, 0, 2)
                                .reshape(128, (Din // 128) * O))


def prep_l0a(inputs):
    d = {}
    d["l0a_wq"] = wimg(np.ascontiguousarray(inputs["L0_attn_w_q"].T))
    d["l0a_wk"] = wimg(np.ascontiguousarray(inputs["L0_attn_w_k"].T))
    d["l0a_wv"] = wimg(np.ascontiguousarray(inputs["L0_attn_w_v"].T))
    r32 = np.zeros((1, 1360), dtype=np.float32)
    r32[0, 0:129] = inputs["L0_attn_b_q"]
    r32[0, 160:289] = inputs["L0_attn_b_k"]
    r32[0, 320:1344] = inputs["L0_attn_b_v"]
    d["l0a_rows32"] = r32
    d["l0a_k0x"] = np.ascontiguousarray(inputs["L0_attn_k_0"][128:129][None, :])
    d["l0a_rows64"] = np.ascontiguousarray(inputs["L0_attn_v_0"][None, :])
    d["l0a_k0m"] = np.ascontiguousarray(inputs["L0_attn_k_0"][0:128, None])
    d["l0a_bcols"] = np.stack([inputs["L0_attn_b_q"][0:128], inputs["L0_attn_b_k"][0:128]], 1)
    return d




def prep_l0f(inputs):
    d = {}
    d["l0f_wqkv"] = qkv_group_img(np.ascontiguousarray(inputs["L0_ffn_qkv_w"].T))
    d["l0f_wout"] = wimg(np.ascontiguousarray(inputs["L0_ffn_out_w"].T))
    rows = np.zeros((1, 4096), dtype=np.float32)
    rows[0, 0:3072] = inputs["L0_ffn_qkv_b"]
    rows[0, 3072:4096] = inputs["L0_ffn_out_b"]
    d["l0f_rows"] = rows
    d["l0f_n1"] = np.ascontiguousarray(inputs["L0_ffn_norm1_w"].reshape(8, 128).T)
    d["l0f_n2"] = np.ascontiguousarray(inputs["L0_ffn_norm2_w"].reshape(8, 128).T)
    d["l0f_gw"] = wimg(np.ascontiguousarray(inputs["L0_ffn_gate_w"].T))
    w1 = inputs["L0_ffn_w1"]; w2 = inputs["L0_ffn_w2"]; w3 = inputs["L0_ffn_w3"]
    w1r = w1.reshape(8, 16, 128, 8, 128).transpose(0, 4, 1, 3, 2)
    w1r = w1.reshape(8, 16, 128, 8, 128).transpose(0, 4, 1, 3, 2)
    w2r = w2.reshape(8, 16, 128, 8, 128).transpose(0, 4, 1, 3, 2)
    img = np.stack([w1r, w2r], axis=4)
    d["l0f_w12"] = np.ascontiguousarray(img.reshape(8, 128, 16 * 2048))
    d["l0f_w3"] = np.ascontiguousarray(w3.reshape(8, 1024, 16, 128).transpose(0, 3, 2, 1))
    d["l0f_b1"] = np.ascontiguousarray(inputs["L0_ffn_b1"].reshape(8, 16, 128).transpose(0, 2, 1))
    d["l0f_b2"] = np.ascontiguousarray(inputs["L0_ffn_b2"])
    d["l0f_b3"] = np.ascontiguousarray(inputs["L0_ffn_b3"])
    d["l0f_b2c"] = np.ascontiguousarray(inputs["L0_ffn_b2"].reshape(8, 16, 128).transpose(0, 2, 1))
    qb = inputs["L0_ffn_qkv_b"]
    d["l0f_bqk"] = np.ascontiguousarray(np.stack(
        [qb[g * 128:(g + 1) * 128] for g in range(8)] +
        [qb[1024 + g * 128:1024 + (g + 1) * 128] for g in range(8)], 1))
    d["l0f_boc"] = np.ascontiguousarray(inputs["L0_ffn_out_b"].reshape(8, 128).T)
    return d

def qkv_group_img(qkvT):
    img = np.empty((128, 8 * 3072), dtype=np.float32)
    for g in range(8):
        for dc in range(8):
            blk = img[:, g * 3072 + dc * 384:g * 3072 + (dc + 1) * 384]
            rows = slice(dc * 128, (dc + 1) * 128)
            for which in range(3):
                blk[:, which * 128:(which + 1) * 128] = \
                    qkvT[rows, which * 1024 + 128 * g:which * 1024 + 128 * (g + 1)]
    return img


def prep_l1a(inputs):
    d = {}
    qkvT = np.concatenate([np.ascontiguousarray(inputs["L1_attn_wq"].T),
                           np.ascontiguousarray(inputs["L1_attn_wk"].T),
                           np.ascontiguousarray(inputs["L1_attn_wv"].T)], axis=1)
    d["l1a_wqkv"] = qkv_group_img(qkvT)
    d["l1a_wout"] = wimg(np.ascontiguousarray(inputs["L1_attn_wo"].T))
    rows = np.zeros((1, 4096), dtype=np.float32)
    rows[0, 0:1024] = inputs["L1_attn_bq"]
    rows[0, 1024:2048] = inputs["L1_attn_bk"]
    rows[0, 2048:3072] = inputs["L1_attn_bv"]
    rows[0, 3072:4096] = inputs["L1_attn_bo"]
    d["l1a_rows"] = rows
    d["l1a_bqk"] = np.ascontiguousarray(np.stack(
        [inputs["L1_attn_bq"][g * 128:(g + 1) * 128] for g in range(8)] +
        [inputs["L1_attn_bk"][g * 128:(g + 1) * 128] for g in range(8)], 1))
    d["l1a_boc"] = np.ascontiguousarray(inputs["L1_attn_bo"].reshape(8, 128).T)
    return d


def prep_l1f(inputs):
    d = {}
    phi = inputs["L1_ffn_phi"].reshape(1024, 2048)
    d["l1f_phi"] = wimg(np.ascontiguousarray(phi))
    W = inputs["L1_ffn_W"]
    d["l1f_W"] = np.ascontiguousarray(
        W.reshape(64, 8, 128, 1024).transpose(0, 2, 1, 3).reshape(64, 128, 8192))
    d["l1f_brep"] = np.ascontiguousarray(np.repeat(inputs["L1_ffn_b"], 32, axis=0))
    return d


def build_program(S=2048, parts=("l0a",), n_cores=8):
    nc = bass.Bass("TRN2", target_bir_lowering=False, debug=False)
    mk = MK(nc)
    NT = S // 128
    NQ = S // 512

    ap = {}

    def inp(name, shape, dt=F32):
        ap[name] = nc.dram_tensor(name, list(shape), dt, kind="ExternalInput").ap()
        return ap[name]

    xT = inp("xT", [D, S])
    inp("c_ident", [128, 128], BF16)
    inp("c_identF", [128, 128])
    if "l0a" in parts:
        inp("l0a_wq", [128, 8 * P129]); inp("l0a_wk", [128, 8 * P129])
        inp("l0a_wv", [128, 8 * 1024])
        inp("l0a_rows32", [1, 1360]); inp("l0a_rows64", [1, 1024]); inp("l0a_k0x", [1, 1])
        inp("l0a_k0m", [128, 1]); inp("l0a_bcols", [128, 2])
    if "l0fa" in parts:
        inp("l0f_wqkv", [128, 8 * 3072]); inp("l0f_wout", [128, 8 * 1024])
        inp("l0f_rows", [1, 4096]); inp("l0f_n1", [128, 8]); inp("l0f_n2", [128, 8])
        inp("l0f_bqk", [128, 16]); inp("l0f_boc", [128, 8])
        inp("c_cos", [128, S], BF16); inp("c_sin", [128, S], BF16)
        inp("c_permT", [128, 128], BF16)
    if "l0fm" in parts:
        inp("l0f_gw", [128, 64])
        inp("l0f_w12", [8, 128, 16 * 2048])
        inp("l0f_w3", [8, 128, 16, 1024])
        inp("l0f_b1", [8, 128, 16]); inp("l0f_b2c", [8, 128, 16]); inp("l0f_b3", [8, 1024])
        if "l0fa" not in parts:
            inp("tx1", [D, S])
        if "l0fa" not in parts or True:
            pass
    if "l0fm" in parts and "l0fa" not in parts:
        inp("l0f_n2", [128, 8])
    if "l1a" in parts:
        inp("l1a_wqkv", [128, 8 * 3072]); inp("l1a_wout", [128, 8 * 1024])
        inp("l1a_rows", [1, 4096]); inp("c_l1mask", [128, 8 * 512], BF16)
        inp("l1a_bqk", [128, 16]); inp("l1a_boc", [128, 8])
    if "l1f" in parts:
        inp("l1f_phi", [128, 8 * 2048])
        inp("l1f_W", [64, 128, 8192])
        inp("l1f_brep", [2048, 1024])
    out_x = nc.dram_tensor("outX", [D, S], F32, kind="ExternalOutput").ap()
    if "l1f" in parts:
        hE = nc.dram_tensor("hE", [S, 2048], F32, kind="Internal")
        hS = nc.dram_tensor("hS", [S, 2048], F32, kind="Internal")
        hWbf = nc.dram_tensor("hWbf", [64, 128, 8192], BF16, kind="Internal")
    hx1 = nc.dram_tensor("hx1", [D, S], F32, kind="Internal").ap()
    hcomb = nc.dram_tensor("hcomb", [8, S], F32, kind="Internal").ap()
    hattn = nc.dram_tensor("hattn", [D, S], BF16, kind="Internal").ap()

    A_X = nc.alloc_sbuf_tensor("A_X", [128, DC * S], F32)
    A_NB = nc.alloc_sbuf_tensor("A_NB", [128, max(DC * S, DC * 2048)], BF16)
    A_VT = nc.alloc_sbuf_tensor("A_VT", [128, max(DC * S, 2080 + 4 * S, 16384)], BF16)
    PTCOLS = max((S // 128 + 1) * 512 + (S + 128) + 2 * S, DC * S, DC * (S // 2) + 8192, 16384)
    A_PT = nc.alloc_sbuf_tensor("A_PT", [128, PTCOLS], BF16)
    A_WS = nc.alloc_sbuf_tensor("A_WS", [128, 8192], BF16)
    A_SC = nc.alloc_sbuf_tensor("A_SC", [128, 4608], F32)
    rowsB = nc.alloc_sbuf_tensor("rowsB", [128, 4100], BF16)
    onesB = nc.alloc_sbuf_tensor("onesB", [128, 512], BF16)
    onesF = nc.alloc_sbuf_tensor("onesF", [128, 128], F32)
    ident = nc.alloc_sbuf_tensor("ident", [128, 128], BF16)
    identG = nc.alloc_sbuf_tensor("identG", [128, 128], F32)
    ap["__bqkL__"] = nc.alloc_sbuf_tensor("bqkL", [128, 16], F32)
    maxcG = nc.alloc_sbuf_tensor("maxcG", [128, 1], F32)
    k0m = nc.alloc_sbuf_tensor("k0m", [128, 1], BF16)

    tmpf = A_SC[:, 4096:4608]
    negm = maxcG
    tmpg = nc.alloc_sbuf_tensor("tmpg", [128, 512], F32)
    PSA = nc.alloc_psum_tensor("PSA", [128, 2048], F32)
    PSB = nc.alloc_psum_tensor("PSB", [128, 1536], F32)
    PST = nc.alloc_psum_tensor("PST", [128, 1024], BF16)

    mk.op("v", "memset", onesB[:], 1.0)
    mk.op("v", "memset", onesF[:], 1.0)
    mk.dma("y", identG[:], ap["c_identF"])
    mk.dma("y", ident[:], ap["c_ident"])
    for dc in range(DC):
        mk.dma("y", A_X[:, dc * S:(dc + 1) * S], xT[dc * 128:(dc + 1) * 128, :])

    _precast = []
    if "l1f" in parts:
        _precast = [n for n in range(64)]

    def drain_precast(k):
        for _ in range(k):
            if _precast:
                n = _precast.pop(0)
                mk.dma("g", hWbf.ap()[n], ap["l1f_W"][n])

    def rowpack(idx):
        r, b = divmod(idx, 3)
        return PSB[32 * b:32 * b + 1, r * 512:(r + 1) * 512]

    def X(dc, sl=None):
        base = dc * S
        return A_X[:, base:base + S] if sl is None else A_X[:, base + sl.start:base + sl.stop]

    def ln_sum_stats(src_chunks, want_mean=True):
        sq_tile = A_PT[:, 0:S]
        for dc in range(DC):
            mk.op("s", "activation", sq_tile, src_chunks[dc], AF.Square)
            for ts in range(NQ):
                sl = slice(ts * 512, (ts + 1) * 512)
                if want_mean:
                    mk.mm(rowpack(ts), onesB[:, 0:1], src_chunks[dc][:, sl],
                          start=(dc == 0), stop=(dc == DC - 1))
                mk.mm(rowpack(4 + ts), onesB[:, 0:1], sq_tile[:, sl],
                      start=(dc == 0), stop=(dc == DC - 1))

    def part_l0a():
        wq = A_WS[:, 0:8 * P129]
        wk = A_WS[:, 8 * P129:16 * P129]
        wvh = A_WS[:, 16 * P129:16 * P129 + 8 * 512]
        mk.dma("g", wq, ap["l0a_wq"])
        mk.dma("g", wk, ap["l0a_wk"])
        rows32 = rowsB[32:33, 0:1360]
        mk.dma("g", rows32, ap["l0a_rows32"])
        rows64 = rowsB[64:65, 0:1024]
        mk.dma("g", rows64, ap["l0a_rows64"])
        mk.dma("g", k0m[:], ap["l0a_k0m"])
        bq = rowsB[32:33, 0:129]; bk = rowsB[32:33, 160:289]
        bv = rowsB[32:33, 320:1344]
        k0x = rowsB[0:1, 4096:4097]
        mk.dma("g", k0x, ap["l0a_k0x"])
        bcolA = nc.alloc_sbuf_tensor("bcolA", [128, 2], F32)
        mk.dma("g", bcolA[:], ap["l0a_bcols"])
        v0r = rowsB[64:65, 0:1024]
        qxr = rowsB[0:1, 0:S]
        kxr = rowsB[0:1, 2048:2048 + S]
        ptx = rowsB[64:65, 1024:1024 + 512]

        pp0 = 0
        preP0 = pp0 + (NT + 1) * 512
        qf0 = preP0 + S + 128
        kf0 = qf0 + S
        assert kf0 + S <= PTCOLS
        preP = A_PT[:, preP0:preP0 + S + 128]
        qf = A_PT[:, qf0:qf0 + S]
        kf = A_PT[:, kf0:kf0 + S]

        def PTt(kc, a, b):
            return A_PT[:, pp0 + kc * 512 + a:pp0 + kc * 512 + b]

        xb = [A_NB[:, dc * S:(dc + 1) * S] for dc in range(DC)]
        for dc in range(DC):
            mk.op("v", "tensor_copy", xb[dc], X(dc))

        sq_tile = A_PT[:, pp0:pp0 + S]
        for dc in range(DC):
            mk.op("s", "activation", sq_tile, xb[dc], AF.Square)
            for ts in range(NQ):
                sl = slice(ts * 512, (ts + 1) * 512)
                mk.mm(rowpack(ts), onesB[:, 0:1], xb[dc][:, sl],
                      start=(dc == 0), stop=(dc == DC - 1))
                mk.mm(rowpack(4 + ts), onesB[:, 0:1], sq_tile[:, sl],
                      start=(dc == 0), stop=(dc == DC - 1))
        mu = A_SC[0:1, 0:S]; rstd = A_SC[0:1, 2048:2048 + S]; rtmp = A_SC[0:1, 4096:4608]
        for ts in range(NQ):
            sl = slice(ts * 512, (ts + 1) * 512)
            mk.op("v", "tensor_scalar_mul", mu[:, sl], rowpack(ts), 1.0 / D)
            mk.op("v", "tensor_scalar_mul", rstd[:, sl], rowpack(4 + ts), 1.0 / D)
        for ts in range(NQ):
            sl = slice(ts * 512, (ts + 1) * 512)
            mk.op("v", "tensor_mul", rtmp, mu[:, sl], mu[:, sl])
            mk.op("v", "tensor_sub", rstd[:, sl], rstd[:, sl], rtmp)
            mk.op("v", "tensor_scalar_add", rstd[:, sl], rstd[:, sl], 1e-5)
            mk.op("s", "activation", rstd[:, sl], rstd[:, sl], AF.Ln)
            mk.op("s", "activation", rstd[:, sl], rstd[:, sl], AF.Exp, scale=-0.5)
            mk.op("v", "tensor_mul", mu[:, sl], mu[:, sl], rstd[:, sl])

        for ts in range(NQ):
            sl = slice(ts * 512, (ts + 1) * 512)
            mk.mm(PSB[:, 0:512], onesF[0:1, :], rstd[:, sl], start=True, stop=True)
            mk.mm(PSB[:, 512:1024], onesF[0:1, :], mu[:, sl], start=True, stop=True)
            for dc in range(DC):
                mk.op("v", "tensor_mul", tmpf[:], X(dc, sl), PSB[:, 0:512])
                mk.op("v", "tensor_sub", xb[dc][:, sl], tmpf[:], PSB[:, 512:1024])

        for wi, (wT, brow_x, dstf, dstx) in enumerate((
                (wq, rowsB[32:33, 128:129], qf, qxr),
                (wk, rowsB[32:33, 288:289], kf, kxr))):
            for ts in range(NQ):
                sl = slice(ts * 512, (ts + 1) * 512)
                for dc in range(DC):
                    mk.mm(PSA[:, sl], wT[:, dc * P129:dc * P129 + 128], xb[dc][:, sl],
                          start=(dc == 0), stop=(dc == DC - 1))
                for dc in range(DC):
                    mk.mm(rowpack(ts), wT[:, dc * P129 + 128:dc * P129 + 129],
                          xb[dc][:, sl], start=(dc == 0), stop=False)
                mk.mm(rowpack(ts), brow_x, onesB[32:33, 0:512], start=False, stop=True)
            mk.op("v", "tensor_scalar", dstf, PSA[:, 0:S], bcolA[:, wi:wi + 1], None,
                  op0=ALU.add)
            for ts in range(NQ):
                mk.op("v", "tensor_copy", dstx[:, ts * 512:(ts + 1) * 512], rowpack(ts))

        wv_img = ap["l0a_wv"]
        for oh in range(2):
            mk.dma("g", wvh, wv_img.rearrange("p (dc o) -> p dc o", o=1024)[:, :, oh * 512:(oh + 1) * 512])
            for tc in range(NT):
                tsl = slice(tc * 128, (tc + 1) * 128)
                osl = slice(oh * 512, (oh + 1) * 512)
                for dc in range(DC):
                    mk.mm(PSB[:, osl], xb[dc][:, tsl],
                          wvh[:, dc * 512:(dc + 1) * 512],
                          start=(dc == 0), stop=False)
                mk.mm(PSB[:, osl], onesB[32:33, 0:128], bv[:, oh * 512:(oh + 1) * 512],
                      start=False, stop=True)
                mk.op("v", "tensor_copy", A_VT[:, tc * 1024 + oh * 512:tc * 1024 + (oh + 1) * 512],
                      PSB[:, osl])

        for qb in range(NQ):
            qsl = slice(qb * 512, (qb + 1) * 512)
            for qt in range(4):
                qtile = slice(qb * 512 + qt * 128, qb * 512 + (qt + 1) * 128)
                for kt in range(NQ):
                    ksl = slice(kt * 512, (kt + 1) * 512)
                    mk.mm(PSA[:, ksl], qf[:, qtile], kf[:, ksl], start=True, stop=False)
                    mk.mm(PSA[:, ksl], qxr[:, qtile], kxr[:, ksl], start=False, stop=True)
                mk.mm(PSB[:, 1024:1025], qf[:, qtile], k0m[:, 0:1], start=True, stop=False)
                mk.mm(PSB[:, 1024:1025], qxr[:, qtile], k0x[:, 0:1], start=False, stop=True)
                mk.op("v", "reduce_max", negm[:, 0:1], PSA[:, 0:S], axis=AX.X)
                mk.op("v", "tensor_max", negm[:, 0:1], negm[:, 0:1], PSB[:, 1024:1025])
                mk.op("v", "tensor_scalar_mul", negm[:, 0:1], negm[:, 0:1], -1.0)
                mk.op("s", "activation", preP[:, 0:S], PSA[:, 0:S], AF.Exp,
                      bias=negm[:, 0:1])
                mk.op("s", "activation", preP[:, S:S + 1], PSB[:, 1024:1025], AF.Exp,
                      bias=negm[:, 0:1])
                for kc in range(NT):
                    slot = (kc % 2) * 128
                    mk.op("t", "transpose", PST[:, slot:slot + 128],
                          preP[:, kc * 128:(kc + 1) * 128], ident[:])
                    mk.op("v", "tensor_copy", PTt(kc, qt * 128, (qt + 1) * 128),
                          PST[:, slot:slot + 128])
                mk.op("t", "transpose", PST[0:1, 256:384], preP[:, S:S + 1], ident[:])
                mk.op("v", "tensor_copy", ptx[:, qt * 128:(qt + 1) * 128],
                      PST[0:1, 256:384])
            import os
            if os.environ.get("MK_STOP") == "evac":
                return
            for kc in range(NT):
                mk.mm(PSB[0:1, 0:512], onesB[:, 0:1], PTt(kc, 0, 512),
                      start=(kc == 0), stop=False)
            mk.mm(PSB[0:1, 0:512], onesB[64:65, 0:1], ptx[:, 0:512], start=False, stop=True)
            sums_r = A_SC[32:33, 0:512]
            mk.op("s", "activation", sums_r, PSB[0:1, 0:512], AF.Ln)
            mk.op("s", "activation", sums_r, sums_r, AF.Exp, scale=-1.0)
            mk.mm(PSB[:, 0:512], onesF[32:33, :], sums_r, start=True, stop=True)
            mk.op("v", "tensor_copy", tmpg[:], PSB[:, 0:512])
            mk.mm(PSB[:, 512:1024], onesF[0:1, :], rstd[:, qsl], start=True, stop=True)
            mk.mm(PSB[:, 1024:1536], onesF[0:1, :], mu[:, qsl], start=True, stop=True)
            if os.environ.get("MK_STOP") == "sums":
                return
            for half in range(2):
                for dci in range(4):
                    dc = half * 4 + dci
                    csl = slice(dci * 512, (dci + 1) * 512)
                    for kc in range(NT):
                        mk.mm(PSA[:, csl],
                              A_VT[:, kc * 1024 + dc * 128:kc * 1024 + (dc + 1) * 128],
                              PTt(kc, 0, 512),
                              start=(kc == 0), stop=False)
                    mk.mm(PSA[:, csl], v0r[:, dc * 128:(dc + 1) * 128], ptx[:, 0:512],
                          start=False, stop=True)
                for dci in range(4):
                    dc = half * 4 + dci
                    csl = slice(dci * 512, (dci + 1) * 512)
                    mk.op("v", "tensor_mul", tmpf[:], X(dc, qsl), PSB[:, 512:1024])
                    mk.op("v", "tensor_sub", tmpf[:], tmpf[:], PSB[:, 1024:1536])
                    mk.op("v", "tensor_add", X(dc, qsl), X(dc, qsl), tmpf[:])
                    mk.op("v", "tensor_mul", tmpf[:], PSA[:, csl], tmpg[:])
                    mk.op("v", "tensor_add", X(dc, qsl), X(dc, qsl), tmpf[:])


    def part_l0fa():
        NG = 8
        mk.dma("g", rowsB[32:33, 0:4096], ap["l0f_rows"])
        qkvb = rowsB[32:33, 0:3072]
        outb = rowsB[32:33, 3072:4096]
        n1sb = nc.alloc_sbuf_tensor("n1sb", [128, 8], BF16)
        n2sb = nc.alloc_sbuf_tensor("n2sb", [128, 8], BF16)
        mk.dma("g", n1sb[:], ap["l0f_n1"])
        mk.dma("g", n2sb[:], ap["l0f_n2"])
        ap["__n2sb__"] = n2sb
        bqk = nc.alloc_sbuf_tensor("bqk", [128, 16], F32)
        boc = nc.alloc_sbuf_tensor("boc", [128, 8], F32)
        ap["__bqk__"] = bqk
        mk.dma("g", bqk[:], ap["l0f_bqk"])
        mk.dma("g", boc[:], ap["l0f_boc"])
        vt_q = 0
        vt_k = 2 * S
        vt_v0 = 4 * S
        vt_cos = 4 * S + 2080
        vt_sin = 5 * S + 2080
        cosT = A_VT[:, vt_cos:vt_cos + S]
        sinT = A_VT[:, vt_sin:vt_sin + S]
        mk.dma("y", cosT, ap["c_cos"])
        mk.dma("y", sinT, ap["c_sin"])
        perm = nc.alloc_sbuf_tensor("perm", [128, 128], BF16)
        mk.dma("y", perm[:], ap["c_permT"])
        identF2 = identG

        def qh(h):
            return A_VT[0:65, vt_q + h * S:vt_q + (h + 1) * S]

        def kh(h):
            return A_VT[0:65, vt_k + h * S:vt_k + (h + 1) * S]
        bbuf = A_PT[:, 0:S]
        proll0 = S
        attst0 = S + 3072
        attn_st = A_PT[:, attst0:attst0 + S]

        for dc in range(DC):
            mk.dma("y", hx1[dc * 128:(dc + 1) * 128, :], X(dc))

        x1b = [A_NB[:, dc * S:(dc + 1) * S] for dc in range(DC)]
        for dc in range(DC):
            mk.op("v", "tensor_copy", x1b[dc], X(dc))
        ln_sum_stats(x1b)
        mu = A_SC[0:1, 0:S]
        rstd = A_SC[0:1, 2048:2048 + S]
        rtmp = A_SC[0:1, 4096:4608]
        rmss = A_SC[32:33, 512:512 + S]
        for ts in range(NQ):
            sl = slice(ts * 512, (ts + 1) * 512)
            mk.op("v", "tensor_scalar_mul", mu[:, sl], rowpack(ts), 1.0 / D)
            mk.op("v", "tensor_scalar_mul", rstd[:, sl], rowpack(4 + ts), 1.0 / D)
            mk.op("v", "tensor_mul", rtmp, mu[:, sl], mu[:, sl])
            mk.op("v", "tensor_sub", rstd[:, sl], rstd[:, sl], rtmp)
            mk.op("v", "tensor_scalar_add", rstd[:, sl], rstd[:, sl], 1e-5)
            mk.op("s", "activation", rstd[:, sl], rstd[:, sl], AF.Ln)
            mk.op("s", "activation", rstd[:, sl], rstd[:, sl], AF.Exp, scale=-0.5)
            mk.op("v", "tensor_mul", mu[:, sl], mu[:, sl], rstd[:, sl])
            mk.op("v", "tensor_mul", rtmp, rstd[:, sl], rstd[:, sl])
            mk.op("v", "tensor_scalar", rtmp, rtmp, -1e-5, 1.0 + 1e-6,
                  op0=ALU.mult, op1=ALU.add)
            mk.op("s", "activation", rtmp, rtmp, AF.Ln)
            mk.op("s", "activation", rtmp, rtmp, AF.Exp, scale=-0.5)
            mk.op("v", "tensor_copy", rmss[:, sl], rtmp)

        for ts in range(NQ):
            sl = slice(ts * 512, (ts + 1) * 512)
            mk.mm(PSB[:, 0:512], onesF[0:1, :], rstd[:, sl], start=True, stop=True)
            mk.mm(PSB[:, 512:1024], onesF[0:1, :], mu[:, sl], start=True, stop=True)
            mk.mm(PSB[:, 1024:1536], onesF[32:33, :], rmss[:, sl], start=True, stop=True)
            for dc in range(DC):
                mk.op("v", "tensor_mul", tmpf, X(dc, sl), PSB[:, 0:512])
                mk.op("v", "tensor_sub", X(dc, sl), tmpf, PSB[:, 512:1024])
                mk.op("v", "scalar_tensor_tensor", x1b[dc][:, sl], X(dc, sl),
                      n1sb[:, dc:dc + 1], PSB[:, 1024:1536],
                      op0=ALU.mult, op1=ALU.mult)
        a = x1b

        for g in range(NG):
            ws0 = (g % 2) * 3072
            wg = A_WS[:, ws0:ws0 + 3072]
            mk.dma("g", wg, ap["l0f_wqkv"][:, g * 3072:(g + 1) * 3072])
            for which in range(2):
                bcol = bqk[:, which * 8 + g:which * 8 + g + 1]
                for ts in range(NQ):
                    sl = slice(ts * 512, (ts + 1) * 512)
                    for dc in range(DC):
                        mk.mm(PSA[:, sl], wg[:, dc * 384 + which * 128:dc * 384 + which * 128 + 128],
                              a[dc][:, sl], start=(dc == 0), stop=(dc == DC - 1))
                for ts in range(NQ):
                    sl = slice(ts * 512, (ts + 1) * 512)
                    mk.op("v", "scalar_tensor_tensor", bbuf[:, sl], PSA[:, sl], bcol,
                          sinT[:, sl], op0=ALU.add, op1=ALU.mult)
                    mk.op("v", "scalar_tensor_tensor", tmpf, PSA[:, sl], bcol,
                          cosT[:, sl], op0=ALU.add, op1=ALU.mult)
                    mk.mm(PSB[:, 0:512], perm[:], bbuf[:, sl], start=True, stop=True)
                    for h in range(2):
                        dst = qh(h) if which == 0 else kh(h)
                        mk.op("v", "tensor_add", dst[0:64, sl.start:sl.stop],
                              tmpf[h * 64:(h + 1) * 64, :],
                              PSB[h * 64:(h + 1) * 64, 0:512])
            for h in range(2):
                mk.op("v", "memset", kh(h)[64:65, 0:S], -1.0)
            mk.op("v", "memset", A_VT[:, vt_v0:vt_v0 + 2080], 1.0)
            for tc in range(NT):
                for dc in range(DC):
                    mk.mm(PSB[:, 512:640], a[dc][:, tc * 128:(tc + 1) * 128],
                          wg[:, dc * 384 + 256:dc * 384 + 384],
                          start=(dc == 0), stop=False)
                mk.mm(PSB[:, 512:640], onesB[32:33, 0:128],
                      qkvb[:, 2048 + g * 128:2048 + (g + 1) * 128],
                      start=False, stop=True)
                mk.op("v", "tensor_copy", A_VT[:, vt_v0 + tc * 130:vt_v0 + tc * 130 + 64],
                      PSB[:, 512:576])
                mk.op("v", "tensor_copy", A_VT[:, vt_v0 + tc * 130 + 65:vt_v0 + tc * 130 + 129],
                      PSB[:, 576:640])
            maxc = maxcG
            for h in range(2):
                for qt in range(NT):
                    qtl = slice(qt * 128, (qt + 1) * 128)
                    for kt in range(NQ):
                        mk.mm(PSA[:, kt * 512:(kt + 1) * 512], qh(h)[0:64, qtl],
                              kh(h)[0:64, kt * 512:(kt + 1) * 512], start=True, stop=True)
                    mk.op("v", "reduce_max", maxc[:, 0:1], PSA[:, 0:S], axis=AX.X)
                    mk.op("t", "transpose", PSB[0:1, 512:640], maxc[:, 0:1], identF2[:])
                    mk.op("v", "tensor_copy", qh(h)[64:65, qt * 128:(qt + 1) * 128],
                          PSB[0:1, 512:640])
            for h in range(2):
                for qt in range(NQ):
                    qsl = slice(qt * 512, (qt + 1) * 512)
                    pvslot = PSB[0:65, 1024:1536]
                    npair = NT // 2
                    slots = (PSA[:, 0:1024], PSA[:, 1024:2048], PSB[:, 0:1024])

                    def sco(j, h=h, qsl=qsl):
                        pa = slots[j % 3]
                        for jj in range(2):
                            kc = 2 * j + jj
                            mk.mm(pa[:, jj * 512:(jj + 1) * 512],
                                  kh(h)[:, kc * 128:(kc + 1) * 128],
                                  qh(h)[:, qsl], start=True, stop=True)
                        roll = A_PT[:, proll0 + (j % 3) * 1024:proll0 + (j % 3) * 1024 + 1024]
                        mk.op("s", "activation", roll, pa, AF.Exp, scale=0.125)
                        return roll

                    def pv(j, roll, h=h):
                        for jj in range(2):
                            kc = 2 * j + jj
                            mk.mm(pvslot,
                                  A_VT[:, vt_v0 + kc * 130 + h * 65:vt_v0 + kc * 130 + (h + 1) * 65],
                                  roll[:, jj * 512:(jj + 1) * 512],
                                  start=(kc == 0), stop=(kc == NT - 1))

                    rolls = {0: sco(0), 1: sco(1)}
                    for j in range(2, npair):
                        rolls[j] = sco(j)
                        pv(j - 2, rolls[j - 2])
                    pv(npair - 2, rolls[npair - 2])
                    pv(npair - 1, rolls[npair - 1])
                    mk.op("v", "tensor_copy", attn_st[h * 64:(h + 1) * 64, qsl],
                          pvslot[0:64, :])
                    mk.op("v", "tensor_copy",
                          A_SC[32:33, (h * NQ + qt) * 512:(h * NQ + qt) * 512 + 512],
                          pvslot[64:65, :])
            mk.op("s", "activation", A_SC[32:33, 0:2 * NQ * 512],
                  A_SC[32:33, 0:2 * NQ * 512], AF.Ln)
            mk.op("s", "activation", A_SC[32:33, 0:2 * NQ * 512],
                  A_SC[32:33, 0:2 * NQ * 512], AF.Exp, scale=-1.0)
            for h in range(2):
                for qt in range(NQ):
                    qsl = slice(qt * 512, (qt + 1) * 512)
                    mk.mm(PSB[:, 1024:1536], onesF[32:33, :],
                          A_SC[32:33, (h * NQ + qt) * 512:(h * NQ + qt) * 512 + 512],
                          start=True, stop=True)
                    mk.op("v", "tensor_mul", attn_st[h * 64:(h + 1) * 64, qsl],
                          attn_st[h * 64:(h + 1) * 64, qsl], PSB[0:64, 1024:1536])
            mk.dma("y", hattn[g * 128:(g + 1) * 128, :], attn_st)
            drain_precast(4)

        wo = A_WS[:, 0:8192]
        mk.dma("g", wo, ap["l0f_wout"])
        for dc in range(DC):
            mk.dma("y", A_PT[:, dc * S:(dc + 1) * S], hattn[dc * 128:(dc + 1) * 128, :])
        for oc in range(DC):
            for ts in range(NQ):
                sl = slice(ts * 512, (ts + 1) * 512)
                for dc in range(DC):
                    mk.mm(PSA[:, sl], wo[:, dc * 1024 + oc * 128:dc * 1024 + (oc + 1) * 128],
                          A_PT[:, dc * S + ts * 512:dc * S + (ts + 1) * 512],
                          start=(dc == 0), stop=(dc == DC - 1))
                mk.op("v", "scalar_tensor_tensor", X(oc, sl), PSA[:, sl],
                      boc[:, oc:oc + 1], X(oc, sl), op0=ALU.add, op1=ALU.add)


    def part_l0fm(have_l0fa):
        NE, NH = 8, 16
        if not have_l0fa:
            n2sb = nc.alloc_sbuf_tensor("n2sb_m", [128, 8], BF16)
            mk.dma("g", n2sb[:], ap["l0f_n2"])
            for dc in range(DC):
                mk.dma("g", hx1[dc * 128:(dc + 1) * 128, :], ap["tx1"][dc * 128:(dc + 1) * 128, :])
        else:
            n2sb = ap["__n2sb__"]
        gwsb = nc.alloc_sbuf_tensor("gwsb", [128, 64], F32)
        mk.dma("g", gwsb[:], ap["l0f_gw"])
        n2f = nc.alloc_sbuf_tensor("n2f", [128, 8], F32)
        mk.dma("g", n2f[:], ap["l0f_n2"])
        rmsscol = nc.alloc_sbuf_tensor("rmsscol", [128, 16], F32)
        glt = nc.alloc_sbuf_tensor("glt", [128, 8], F32)
        v1 = nc.alloc_sbuf_tensor("v1", [128, 1], F32)
        v2 = nc.alloc_sbuf_tensor("v2", [128, 1], F32)
        sg = nc.alloc_sbuf_tensor("sg", [128, 1], F32)
        og = nc.alloc_sbuf_tensor("og", [128, 1], F32)
        tmp8 = nc.alloc_sbuf_tensor("tmp8", [128, 8], F32)
        mask1 = nc.alloc_sbuf_tensor("mask1", [128, 8], F32)
        identF = identG
        stage8 = tmpg[0:8, 0:128]
        if "__bqk__" not in ap:
            ap["__bqk__"] = nc.alloc_sbuf_tensor("bqk", [128, 16], F32)
        b1sb = ap["__bqk__"]
        b2sb = ap["__bqkL__"]

        m = [A_NB[:, dc * S:(dc + 1) * S] for dc in range(DC)]

        for dc in range(DC):
            mk.op("v", "tensor_copy", m[dc], X(dc))
        ln_sum_stats(m, want_mean=False)
        rmss = A_SC[0:1, 0:S]
        rtmp = A_SC[0:1, 4096:4608]
        for ts in range(NQ):
            sl = slice(ts * 512, (ts + 1) * 512)
            mk.op("v", "tensor_scalar", rtmp, rowpack(4 + ts), 1.0 / D, 1e-6,
                  op0=ALU.mult, op1=ALU.add)
            mk.op("s", "activation", rtmp, rtmp, AF.Ln)
            mk.op("s", "activation", rtmp, rtmp, AF.Exp, scale=-0.5)
            mk.op("v", "tensor_copy", rmss[:, sl], rtmp)
        for ts in range(NQ):
            sl = slice(ts * 512, (ts + 1) * 512)
            mk.mm(PSB[:, 0:512], onesF[0:1, :], rmss[:, sl], start=True, stop=True)
            for dc in range(DC):
                mk.op("v", "scalar_tensor_tensor", m[dc][:, sl], X(dc, sl),
                      n2sb[:, dc:dc + 1], PSB[:, 0:512], op0=ALU.mult, op1=ALU.mult)

        for dc in range(DC):
            mk.op("v", "tensor_scalar", gwsb[:, dc * 8:(dc + 1) * 8],
                  gwsb[:, dc * 8:(dc + 1) * 8], n2f[:, dc:dc + 1], None, op0=ALU.mult)
        for tt in range(NT):
            mk.op("t", "transpose", PSB[:, 1032:1033],
                  rmss[:, tt * 128:(tt + 1) * 128], identF[0:1, 0:1])
            mk.op("v", "tensor_copy", rmsscol[:, tt % 16:tt % 16 + 1], PSB[:, 1032:1033])
            tsl = slice(tt * 128, (tt + 1) * 128)
            for dc in range(DC):
                mk.mm(PSB[:, 1024:1032], X(dc)[:, tsl], gwsb[:, dc * 8:(dc + 1) * 8],
                      start=(dc == 0), stop=(dc == DC - 1))
            mk.op("v", "tensor_scalar", glt[:], PSB[:, 1024:1032],
                  rmsscol[:, tt % 16:tt % 16 + 1], None, op0=ALU.mult)
            mk.op("v", "reduce_max", v1[:, 0:1], glt[:], axis=AX.X)
            mk.op("v", "tensor_scalar", mask1[:], glt[:], v1[:, 0:1], None, op0=ALU.is_equal)
            mk.op("v", "scalar_tensor_tensor", tmp8[:], mask1[:], -1e30, glt[:],
                  op0=ALU.mult, op1=ALU.add)
            mk.op("v", "reduce_max", v2[:, 0:1], tmp8[:], axis=AX.X)
            mk.op("v", "tensor_sub", sg[:, 0:1], v2[:, 0:1], v1[:, 0:1])
            mk.op("s", "activation", sg[:, 0:1], sg[:, 0:1], AF.Exp)
            mk.op("v", "tensor_scalar_add", sg[:, 0:1], sg[:, 0:1], 1.0)
            mk.op("v", "reciprocal", sg[:, 0:1], sg[:, 0:1])
            mk.op("v", "tensor_scalar", og[:, 0:1], sg[:, 0:1], -1.0, 1.0,
                  op0=ALU.mult, op1=ALU.add)
            mk.op("v", "tensor_scalar", mask1[:], mask1[:], sg[:, 0:1], None, op0=ALU.mult)
            mk.op("v", "tensor_scalar", tmp8[:], tmp8[:], v2[:, 0:1], None, op0=ALU.is_equal)
            mk.op("v", "tensor_scalar", tmp8[:], tmp8[:], og[:, 0:1], None, op0=ALU.mult)
            combtile = glt
            mk.op("v", "tensor_add", combtile[:], mask1[:], tmp8[:])
            mk.op("t", "transpose", PSB[0:8, 0:128], combtile[:], identF[:])
            mk.op("v", "tensor_copy", stage8[:], PSB[0:8, 0:128])
            mk.dma("y", hcomb[:, tt * 128:(tt + 1) * 128], stage8[:])

        for dc in range(DC):
            for ts in range(NQ):
                sl = slice(ts * 512, (ts + 1) * 512)
                mk.dma("y", tmpg[:], hx1[dc * 128:(dc + 1) * 128, ts * 512:(ts + 1) * 512])
                mk.op("v", "tensor_add", X(dc, sl), X(dc, sl), tmpg[:])

        combrow = A_SC[0:1, 2048:2048 + S]
        combrow32 = A_SC[32:33, 0:S]
        brow = rowsB[0:1, 2048:4096]
        TH = S // 2
        NTH = max(1, TH // 512)
        THW = min(512, TH)

        me0 = 0
        w12s = (DC * TH, DC * TH + 2048)
        w3s = (DC * TH + 4096, DC * TH)
        for e in range(NE):
            mk.dma("g", combrow, hcomb[e:e + 1, :])
            mk.dma("g", combrow32, hcomb[e:e + 1, :])
            mk.dma("g", b1sb[:], ap["l0f_b1"][e])
            mk.dma("g", b2sb[:], ap["l0f_b2c"][e])
            drain_precast(4)
            for tbh in range(2):
                for th in range(NTH):
                    sl = slice(tbh * TH + th * THW, tbh * TH + (th + 1) * THW)
                    mk.mm(PSB[:, 1024:1024 + THW], onesF[0:1, :],
                          combrow[:, sl.start:sl.stop], start=True, stop=True)
                    mk.op("v", "tensor_copy",
                          rowsB[:, 2048 + th * THW:2048 + (th + 1) * THW],
                          PSB[:, 1024:1024 + THW])
                for hc in range(NH):
                    wsl = w12s[hc % 2]
                    w12 = A_WS[:, 0:1] if False else A_PT[:, wsl:wsl + 2048]
                    mk.dma("g", w12, ap["l0f_w12"][e, :, hc * 2048:(hc + 1) * 2048])
                    pa = PSA[:, (hc % 2) * TH:(hc % 2) * TH + TH]
                    pb = PSB[:, 0:TH]
                    for th in range(NTH):
                        sl = slice(tbh * TH + th * THW, tbh * TH + (th + 1) * THW)
                        psl = slice(th * THW, (th + 1) * THW)
                        for dc in range(DC):
                            mk.mm(pa[:, psl], w12[:, dc * 256:dc * 256 + 128],
                                  m[dc][:, sl], start=(dc == 0), stop=(dc == DC - 1))
                        for dc in range(DC):
                            mk.mm(pb[:, psl], w12[:, dc * 256 + 128:dc * 256 + 256],
                                  m[dc][:, sl], start=(dc == 0), stop=(dc == DC - 1))
                    hrow = A_VT[:, hc * TH:(hc + 1) * TH]
                    mk.op("s", "activation", hrow, pa, AF.Silu, bias=b1sb[:, hc:hc + 1])
                    mk.op("v", "scalar_tensor_tensor", hrow, pb, b2sb[:, hc:hc + 1],
                          hrow, op0=ALU.add, op1=ALU.mult)
                    for th in range(NTH):
                        psl = slice(th * THW, (th + 1) * THW)
                        mk.op("v", "tensor_mul", hrow[:, psl], hrow[:, psl],
                              rowsB[:, 2048 + th * THW:2048 + (th + 1) * THW])
                b3row = A_SC[32:33, 2048:2048 + 1024]
                mk.dma("g", b3row, ap["l0f_b3"][e:e + 1, 0:1024])
                for p2 in range(4):
                    wsl = w3s[p2 % 2]
                    w3c = A_PT[:, wsl:wsl + 4096]
                    mk.dma("g", w3c, ap["l0f_w3"][e, :, :, p2 * 256:(p2 + 1) * 256])
                    for dc2 in range(2):
                        dc = p2 * 2 + dc2
                        py = PSA[:, dc2 * TH:(dc2 + 1) * TH]
                        for th in range(NTH):
                            sl = slice(tbh * TH + th * THW, tbh * TH + (th + 1) * THW)
                            psl = slice(th * THW, (th + 1) * THW)
                            for hc in range(NH):
                                mk.mm(py[:, psl], w3c[:, hc * 256 + dc2 * 128:hc * 256 + (dc2 + 1) * 128],
                                      A_VT[:, hc * TH + th * THW:hc * TH + (th + 1) * THW],
                                      start=(hc == 0), stop=False)
                            mk.mm(py[:, psl], b3row[:, dc * 128:(dc + 1) * 128],
                                  combrow32[:, sl.start:sl.stop], start=False, stop=True)
                            mk.op("v", "tensor_add", X(dc, sl), X(dc, sl), py[:, psl])


    def part_l1a():
        NG = 8
        NW = S // 512
        mk.dma("g", rowsB[32:33, 0:4096], ap["l1a_rows"])
        qkvb = rowsB[32:33, 0:3072]
        outb = rowsB[32:33, 3072:4096]
        bqkL = ap["__bqkL__"]
        bocL = nc.alloc_sbuf_tensor("bocL", [128, 8], F32)
        mk.dma("g", bqkL[:], ap["l1a_bqk"])
        mk.dma("g", bocL[:], ap["l1a_boc"])
        padc = rowsB[64:65, 2048:2113]
        mk.op("v", "memset", padc[0:1, 0:64], -512.0)
        mk.op("v", "memset", padc[0:1, 64:65], 512.0)
        epad = rowsB[64:65, 2176:2688]

        vt_v0 = 0
        vt_q = 2080
        vt_k = 2080 + 2 * S
        attst0 = S + 3072
        attn_st = A_PT[:, attst0:attst0 + S]
        proll0 = S
        msk = A_PT[:, attst0 + S:attst0 + S + 4096]
        mk.dma("g", msk, ap["c_l1mask"])
        identF3 = identG
        maxc1 = maxcG

        def qh(h):
            return A_VT[0:65, vt_q + h * S:vt_q + (h + 1) * S]

        def kh(h):
            return A_VT[0:65, vt_k + h * S:vt_k + (h + 1) * S]

        zb = [A_NB[:, dc * S:(dc + 1) * S] for dc in range(DC)]
        for dc in range(DC):
            mk.op("v", "tensor_copy", zb[dc], X(dc))
        ln_sum_stats(zb)
        mu = A_SC[0:1, 0:S]
        rstd = A_SC[0:1, 2048:2048 + S]
        rtmp = A_SC[0:1, 4096:4608]
        for ts in range(NQ):
            sl = slice(ts * 512, (ts + 1) * 512)
            mk.op("v", "tensor_scalar_mul", mu[:, sl], rowpack(ts), 1.0 / D)
            mk.op("v", "tensor_scalar_mul", rstd[:, sl], rowpack(4 + ts), 1.0 / D)
            mk.op("v", "tensor_mul", rtmp, mu[:, sl], mu[:, sl])
            mk.op("v", "tensor_sub", rstd[:, sl], rstd[:, sl], rtmp)
            mk.op("v", "tensor_scalar_add", rstd[:, sl], rstd[:, sl], 1e-5)
            mk.op("s", "activation", rstd[:, sl], rstd[:, sl], AF.Ln)
            mk.op("s", "activation", rstd[:, sl], rstd[:, sl], AF.Exp, scale=-0.5)
            mk.op("v", "tensor_mul", mu[:, sl], mu[:, sl], rstd[:, sl])
        for ts in range(NQ):
            sl = slice(ts * 512, (ts + 1) * 512)
            mk.mm(PSB[:, 0:512], onesF[0:1, :], rstd[:, sl], start=True, stop=True)
            mk.mm(PSB[:, 512:1024], onesF[0:1, :], mu[:, sl], start=True, stop=True)
            for dc in range(DC):
                mk.op("v", "tensor_mul", tmpf, X(dc, sl), PSB[:, 0:512])
                mk.op("v", "tensor_sub", zb[dc][:, sl], tmpf, PSB[:, 512:1024])

        for g in range(NG):
            ws0 = (g % 2) * 3072
            wg = A_WS[:, ws0:ws0 + 3072]
            mk.dma("g", wg, ap["l1a_wqkv"][:, g * 3072:(g + 1) * 3072])
            for which in range(2):
                bcol = bqkL[:, which * 8 + g:which * 8 + g + 1]
                for ts in range(NQ):
                    sl = slice(ts * 512, (ts + 1) * 512)
                    for dc in range(DC):
                        mk.mm(PSA[:, sl], wg[:, dc * 384 + which * 128:dc * 384 + which * 128 + 128],
                              zb[dc][:, sl], start=(dc == 0), stop=(dc == DC - 1))
                for ts in range(NQ):
                    sl = slice(ts * 512, (ts + 1) * 512)
                    for h in range(2):
                        dst = qh(h) if which == 0 else kh(h)
                        mk.op("v", "tensor_scalar", dst[0:64, sl.start:sl.stop],
                              PSA[h * 64:(h + 1) * 64, sl],
                              bcol[h * 64:(h + 1) * 64, 0:1], None, op0=ALU.add)
            for h in range(2):
                mk.op("v", "memset", kh(h)[64:65, 0:S], -1.0)
            mk.op("v", "memset", A_VT[:, vt_v0:vt_v0 + 2080], 1.0)
            for tc in range(NT):
                for dc in range(DC):
                    mk.mm(PSB[:, 512:640], zb[dc][:, tc * 128:(tc + 1) * 128],
                          wg[:, dc * 384 + 256:dc * 384 + 384],
                          start=(dc == 0), stop=False)
                mk.mm(PSB[:, 512:640], onesB[32:33, 0:128],
                      qkvb[:, 2048 + g * 128:2048 + (g + 1) * 128],
                      start=False, stop=True)
                mk.op("v", "tensor_copy", A_VT[:, vt_v0 + tc * 130:vt_v0 + tc * 130 + 64],
                      PSB[:, 512:576])
                mk.op("v", "tensor_copy", A_VT[:, vt_v0 + tc * 130 + 65:vt_v0 + tc * 130 + 129],
                      PSB[:, 576:640])
            for h in range(2):
                for w in range(NW):
                    qsl = slice(w * 512, (w + 1) * 512)
                    kcs = [kc for kc in range(12) if 0 <= (w - 1) * 512 + kc * 128 < S]
                    has_lpad = (w == 0)
                    has_rpad = (w == NW - 1)
                    npad = (1 if has_lpad else 0) + (1 if has_rpad else 0)
                    kt0 = (w - 1) * 512 + kcs[0] * 128
                    kt1 = (w - 1) * 512 + (kcs[-1] + 1) * 128
                    if npad:
                        mk.mm(rowpack(6), onesB[0:64, 0:1], qh(h)[0:64, qsl],
                              start=True, stop=True)
                    for qt in range(4):
                        qtl = slice(w * 512 + qt * 128, w * 512 + (qt + 1) * 128)
                        nk = kt1 - kt0
                        for i, c0 in enumerate(range(kt0, kt1, 512)):
                            cw = min(512, kt1 - c0)
                            mk.mm(PSA[:, i * 512:i * 512 + cw], qh(h)[0:64, qtl],
                                  kh(h)[0:64, c0:c0 + cw], start=True, stop=True)
                        mk.op("v", "reduce_max", maxc1[:, 0:1], PSA[:, 0:nk], axis=AX.X)
                        mk.op("t", "transpose", PSB[0:1, 512:640], maxc1[:, 0:1], identF3[:])
                        mk.op("v", "tensor_copy", qh(h)[64:65, qtl.start:qtl.stop],
                              PSB[0:1, 512:640])
                    if npad:
                        mrow = A_SC[0:1, 4096:4608]
                        spadrow = A_SC[0:1, 0:512]
                        mk.op("v", "tensor_scalar_mul", spadrow, rowpack(6), -1.0)
                        mk.op("v", "tensor_copy", mrow[:, 0:512], qh(h)[64:65, qsl])
                        mk.op("v", "tensor_max", mrow[:, 0:512], mrow[:, 0:512], spadrow)
                        mk.op("v", "tensor_copy", qh(h)[64:65, qsl.start:qsl.stop], mrow[:, 0:512])
                        mk.op("v", "tensor_sub", spadrow, spadrow, mrow[:, 0:512])
                        mk.op("s", "activation", epad, spadrow, AF.Exp)
                    pvslot = PSB[0:65, 1024:1536]
                    slots6 = (PSA[:, 0:512], PSA[:, 512:1024],
                              PSA[:, 1024:1536], PSA[:, 1536:2048],
                              PSB[:, 0:512], PSB[:, 512:1024])

                    def sco(j, h=h, w=w, qsl=qsl):
                        kc = kcs[j]
                        kt = (w - 1) * 512 + kc * 128
                        pa = slots6[j % 6]
                        mk.mm(pa, kh(h)[:, kt:kt + 128], qh(h)[:, qsl],
                              start=True, stop=True)
                        roll = A_PT[:, proll0 + (j % 6) * 512:proll0 + (j % 6) * 512 + 512]
                        mk.op("s", "activation", roll, pa, AF.Exp)
                        ci = kc if kc < 4 else (kc - 4 if kc >= 8 else None)
                        if ci is not None:
                            mk.op("v", "tensor_mul", roll, roll,
                                  msk[:, ci * 512:(ci + 1) * 512])
                        return roll

                    def pv(j, roll, h=h, w=w):
                        kc = kcs[j]
                        ktc = ((w - 1) * 512 + kc * 128) // 128
                        mk.mm(pvslot,
                              A_VT[:, vt_v0 + ktc * 130 + h * 65:vt_v0 + ktc * 130 + (h + 1) * 65],
                              roll, start=(j == 0),
                              stop=(npad == 0 and j == len(kcs) - 1))

                    DEP = 5
                    rolls = {}
                    for j in range(len(kcs)):
                        rolls[j] = sco(j)
                        if j >= DEP:
                            pv(j - DEP, rolls[j - DEP])
                    for j in range(max(0, len(kcs) - DEP), len(kcs)):
                        pv(j, rolls[j])
                    for t in range(npad):
                        mk.mm(pvslot, padc[0:1, :], epad,
                              start=False, stop=(t == npad - 1))
                    mk.op("v", "tensor_copy", attn_st[h * 64:(h + 1) * 64, qsl],
                          pvslot[0:64, :])
                    mk.op("v", "tensor_copy",
                          A_SC[32:33, (h * NW + w) * 512:(h * NW + w) * 512 + 512],
                          pvslot[64:65, :])
            mk.op("s", "activation", A_SC[32:33, 0:2 * NW * 512],
                  A_SC[32:33, 0:2 * NW * 512], AF.Ln)
            mk.op("s", "activation", A_SC[32:33, 0:2 * NW * 512],
                  A_SC[32:33, 0:2 * NW * 512], AF.Exp, scale=-1.0)
            for h in range(2):
                for w in range(NW):
                    qsl = slice(w * 512, (w + 1) * 512)
                    mk.mm(PSB[:, 1024:1536], onesF[32:33, :],
                          A_SC[32:33, (h * NW + w) * 512:(h * NW + w) * 512 + 512],
                          start=True, stop=True)
                    mk.op("v", "tensor_mul", attn_st[h * 64:(h + 1) * 64, qsl],
                          attn_st[h * 64:(h + 1) * 64, qsl], PSB[0:64, 1024:1536])
            mk.dma("y", hattn[g * 128:(g + 1) * 128, :], attn_st)

        wo = A_WS[:, 0:8192]
        mk.dma("g", wo, ap["l1a_wout"])
        for dc in range(DC):
            mk.dma("y", A_PT[:, dc * S:(dc + 1) * S], hattn[dc * 128:(dc + 1) * 128, :])
        for oc in range(DC):
            for ts in range(NQ):
                sl = slice(ts * 512, (ts + 1) * 512)
                for dc in range(DC):
                    mk.mm(PSA[:, sl], wo[:, dc * 1024 + oc * 128:dc * 1024 + (oc + 1) * 128],
                          A_PT[:, dc * S + ts * 512:dc * S + (ts + 1) * 512],
                          start=(dc == 0), stop=(dc == DC - 1))
                mk.op("v", "scalar_tensor_tensor", X(oc, sl), PSA[:, sl],
                      bocL[:, oc:oc + 1], X(oc, sl), op0=ALU.add, op1=ALU.add)


    def part_l1f():
        drain_precast(64)
        NP = 2048
        MT = S // 128
        rg = [list(range(n_cores))]
        ub = [A_NB[:, dc * S:(dc + 1) * S] for dc in range(DC)]
        for dc in range(DC):
            mk.op("v", "tensor_copy", ub[dc], X(dc))
        ln_sum_stats(ub)
        mu = A_SC[0:1, 0:S]
        rstd = A_SC[0:1, 2048:2048 + S]
        rtmp = A_SC[0:1, 4096:4608]
        for ts in range(NQ):
            sl = slice(ts * 512, (ts + 1) * 512)
            mk.op("v", "tensor_scalar_mul", mu[:, sl], rowpack(ts), 1.0 / D)
            mk.op("v", "tensor_scalar_mul", rstd[:, sl], rowpack(4 + ts), 1.0 / D)
            mk.op("v", "tensor_mul", rtmp, mu[:, sl], mu[:, sl])
            mk.op("v", "tensor_sub", rstd[:, sl], rstd[:, sl], rtmp)
            mk.op("v", "tensor_scalar_add", rstd[:, sl], rstd[:, sl], 1e-5)
            mk.op("s", "activation", rstd[:, sl], rstd[:, sl], AF.Ln)
            mk.op("s", "activation", rstd[:, sl], rstd[:, sl], AF.Exp, scale=-0.5)
            mk.op("v", "tensor_mul", mu[:, sl], mu[:, sl], rstd[:, sl])
        for ts in range(NQ):
            sl = slice(ts * 512, (ts + 1) * 512)
            mk.mm(PSB[:, 0:512], onesF[0:1, :], rstd[:, sl], start=True, stop=True)
            mk.mm(PSB[:, 512:1024], onesF[0:1, :], mu[:, sl], start=True, stop=True)
            for dc in range(DC):
                mk.op("v", "tensor_mul", tmpf, X(dc, sl), PSB[:, 0:512])
                mk.op("v", "tensor_sub", ub[dc][:, sl], tmpf, PSB[:, 512:1024])

        for mt in range(MT):
            for dc in range(DC):
                slot = (dc % 2) * 128
                mk.op("t", "transpose", PST[:, slot:slot + 128],
                      ub[dc][:, mt * 128:(mt + 1) * 128], ident[:])
                mk.op("v", "tensor_copy",
                      A_VT[:, mt * 1024 + dc * 128:mt * 1024 + (dc + 1) * 128],
                      PST[:, slot:slot + 128])

        phih = A_WS[:, 0:8192]
        ar_chunk = max(512, S // 4)
        n_chunks = S // ar_chunk
        done_rows = 0
        for nph in range(2):
            mk.dma("g", phih, ap["l1f_phi"].rearrange("p (dc np) -> p dc np", np=2048)[:, :, nph * 1024:(nph + 1) * 1024])
            for mt in range(MT):
                for s5 in range(2):
                    nsl = slice(s5 * 512, (s5 + 1) * 512)
                    for dc in range(DC):
                        mk.mm(PSA[:, nsl], ub[dc][:, mt * 128:(mt + 1) * 128],
                              phih[:, dc * 1024 + s5 * 512:dc * 1024 + (s5 + 1) * 512],
                              start=(dc == 0), stop=(dc == DC - 1))
                    mk.op("s", "activation", tmpg[:], PSA[:, nsl], AF.Exp)
                    mk.dma("y", hE.ap()[mt * 128:(mt + 1) * 128,
                                        nph * 1024 + s5 * 512:nph * 1024 + (s5 + 1) * 512],
                           tmpg[:])
            if nph == 1:
                for c in range(n_chunks):
                    mk.op("g", "collective_compute", "AllReduce", ALU.add,
                          replica_groups=rg,
                          ins=[hE.ap()[c * ar_chunk:(c + 1) * ar_chunk, :].opt()],
                          outs=[hS.ap()[c * ar_chunk:(c + 1) * ar_chunk, :].opt()])

        dcache0 = 0
        for npb in range(4):
            nsl = slice(npb * 512, (npb + 1) * 512)
            for mt in range(MT):
                mk.dma("y", tmpg[:], hE.ap()[mt * 128:(mt + 1) * 128, nsl])
                mk.dma("y", tmpf[:, 0:512] if False else tmpf, hS.ap()[mt * 128:(mt + 1) * 128, nsl])
                mk.op("s", "activation", tmpf, tmpf, AF.Ln)
                mk.op("s", "activation", tmpf, tmpf, AF.Exp, scale=-1.0)
                mk.op("v", "tensor_mul", A_PT[:, dcache0 + mt * 512:dcache0 + (mt + 1) * 512],
                      tmpg[:], tmpf)
            for half in range(2):
                for dci in range(4):
                    dc = half * 4 + dci
                    for mt in range(MT):
                        mk.mm(PSA[:, dci * 512:(dci + 1) * 512],
                              A_VT[:, mt * 1024 + dc * 128:mt * 1024 + (dc + 1) * 128],
                              A_PT[:, dcache0 + mt * 512:dcache0 + (mt + 1) * 512],
                              start=(mt == 0), stop=(mt == MT - 1))
                for dci in range(4):
                    dc = half * 4 + dci
                    mk.op("v", "tensor_copy",
                          A_NB[:, dc * 2048 + npb * 512:dc * 2048 + (npb + 1) * 512],
                          PSA[:, dci * 512:(dci + 1) * 512])

        wslots = (A_WS[:, 0:8192], A_PT[:, 0:8192], A_PT[:, 8192:16384])
        for npq in range(16):
            pbase = 0 if npq % 2 == 0 else 1024
            for lane in range(4):
                n = npq * 4 + lane
                wslot = wslots[n % 3]
                mk.dma("y" if n % 2 == 0 else "g", wslot, hWbf.ap()[n])
                for eh in range(2):
                    pslot = PSA[:, pbase + eh * 512:pbase + (eh + 1) * 512]
                    for dc in range(DC):
                        mk.mm(pslot[lane * 32:lane * 32 + 32, :],
                              A_NB[:, dc * 2048 + npq * 128 + lane * 32:dc * 2048 + npq * 128 + (lane + 1) * 32],
                              wslot[:, dc * 1024 + eh * 512:dc * 1024 + (eh + 1) * 512],
                              start=(dc == 0), stop=(dc == DC - 1),
                              tile_position=(0, lane * 32), skip_group_check=True)
            for lane in range(4):
                n = npq * 4 + lane
                brep = rowsB[0:32, (lane % 2) * 1024:(lane % 2) * 1024 + 1024]
                mk.dma("g", brep, ap["l1f_brep"][npq * 128 + lane * 32:npq * 128 + (lane + 1) * 32, :])
                for eh in range(2):
                    pslot = PSA[:, pbase + eh * 512:pbase + (eh + 1) * 512]
                    mk.op("v", "tensor_add",
                          A_VT[:, npq * 1024 + eh * 512:npq * 1024 + (eh + 1) * 512][lane * 32:lane * 32 + 32, :],
                          pslot[lane * 32:lane * 32 + 32, :], brep[:, eh * 512:(eh + 1) * 512])

        est0 = 8192
        for mb in range(S // 256):
            for i in range(2):
                mrow = mb * 256 + i * 128
                mk.dma("g", A_PT[:, est0 + i * 2048:est0 + (i + 1) * 2048],
                       hE.ap()[mrow:mrow + 128, :])
                for npc in range(16):
                    slot = (npc % 2) * 128
                    mk.op("t", "transpose", PST[:, slot:slot + 128],
                          A_PT[:, est0 + i * 2048 + npc * 128:est0 + i * 2048 + (npc + 1) * 128],
                          ident[:])
                    mk.op("v", "tensor_copy",
                          A_PT[:, est0 + 4096 + npc * 256 + i * 128:est0 + 4096 + npc * 256 + (i + 1) * 128],
                          PST[:, slot:slot + 128])
            for npc in range(16):
                enp = A_PT[:, est0 + 4096 + npc * 256:est0 + 4096 + (npc + 1) * 256]
                for ec in range(DC):
                    bank, off = divmod(ec, 2)
                    mk.mm(PSA[:, bank * 512 + off * 256:bank * 512 + off * 256 + 256],
                          A_VT[:, npc * 1024 + ec * 128:npc * 1024 + (ec + 1) * 128],
                          enp, start=(npc == 0 and off == 0), stop=(npc == 15 and off == 1),
                          skip_group_check=True)
                mk.mm(rowpack(6)[:, 0:256], onesB[:, 0:1], enp,
                      start=(npc == 0), stop=(npc == 15))
            srow = A_SC[32:33, 0:256]
            mk.op("s", "activation", srow, rowpack(6)[:, 0:256], AF.Ln)
            mk.op("s", "activation", srow, srow, AF.Exp, scale=-1.0)
            mk.mm(PSB[:, 512:768], onesF[32:33, :], srow, start=True, stop=True)
            mk.op("v", "tensor_copy", tmpg[:, 0:256], PSB[:, 512:768])
            for ec in range(DC):
                bank, off = divmod(ec, 2)
                mk.op("v", "tensor_mul", tmpf[:, 0:256],
                      PSA[:, bank * 512 + off * 256:bank * 512 + off * 256 + 256],
                      tmpg[:, 0:256])
                mk.op("v", "tensor_add", X(ec)[:, mb * 256:(mb + 1) * 256],
                      X(ec)[:, mb * 256:(mb + 1) * 256], tmpf[:, 0:256])

    if "l0a" in parts:
        part_l0a()
    if "l0fa" in parts:
        part_l0fa()
    if "l0fm" in parts:
        part_l0fm("l0fa" in parts)
    if "l1a" in parts:
        part_l1a()
    if "l1f" in parts:
        part_l1f()

    for dc in range(DC):
        mk.dma("y", out_x[dc * 128:(dc + 1) * 128, :], A_X[:, dc * S:(dc + 1) * S])
    mk.op("y", "nop")
    mk.flush(verbose=True)
    return nc


_CACHE = {}


def kernel(**inputs):
    S = 2048
    n_cores = 8
    if "nc" not in _CACHE:
        _CACHE["nc"] = build_program(S=S, parts=("l0a", "l0fa", "l0fm", "l1a", "l1f"),
                                     n_cores=n_cores)
    nc = _CACHE["nc"]
    prep = {}
    prep.update(prep_l0a(inputs))
    prep.update(prep_l0f(inputs))
    prep.update(prep_l1a(inputs))
    prep.update(prep_l1f(inputs))
    prep.update(host_constants(S))
    x = np.ascontiguousarray(np.asarray(inputs["L0_attn_x"], dtype=np.float32))
    maps = []
    for c in range(n_cores):
        im = dict(prep)
        im["xT"] = np.ascontiguousarray(x[c].T)
        maps.append(im)
    res = run_bass_kernel_spmd(nc, maps, core_ids=list(range(n_cores)))
    out = np.stack([np.ascontiguousarray(res.results[c]["outX"].T) for c in range(n_cores)], 0)
    return out


# version 14
# speedup vs baseline: 1.3188x; 1.0004x over previous
import numpy as np
import ml_dtypes
import concourse.bass as bass
import concourse.mybir as mybir
from concourse.bass_utils import run_bass_kernel_spmd
from contextlib import ExitStack

F32 = mybir.dt.float32
BF16 = mybir.dt.bfloat16
AF = mybir.ActivationFunctionType
ALU = mybir.AluOpType
AX = mybir.AxisListType

D = 1024
DC = 8
P129 = 129


DTSZ = {F32: 4, BF16: 2, mybir.dt.float16: 2, mybir.dt.int32: 4, mybir.dt.uint8: 1}


class _Rec:
    __slots__ = ("idx", "eng", "method", "args", "kw", "mwaits", "reads", "writes",
                 "is_dma", "ticked", "sem", "val", "edges", "clock", "eidx",
                 "_needed", "_waits", "_fence")

    def __init__(self, idx, eng, method, args, kw, mwaits, is_dma):
        self.idx = idx
        self.eng = eng
        self.method = method
        self.args = args
        self.kw = kw
        self.mwaits = mwaits
        self.is_dma = is_dma
        self.ticked = False
        self.sem = None
        self.val = None
        self.edges = []
        self.clock = None
        self.eidx = 0


class _SemPool:
    def __init__(self, nc, es, name, k, step):
        self.sems = [es.enter_context(nc.semaphore(f"{name}{i}")) for i in range(k)]
        self.count = [0] * k
        self.i = 0
        self.step = step

    def take(self):
        i = self.i
        self.i = (self.i + 1) % len(self.sems)
        prev = self.count[i]
        self.count[i] += self.step
        return self.sems[i], self.count[i], prev


class MK:
    ENGS = ("t", "v", "s", "g", "y")

    def __init__(self, nc):
        self.nc = nc
        self.es = ExitStack()
        self.ops = []
        self.aliases = {}
        self.pools = {
            "t": _SemPool(nc, self.es, "pt_", 12, 1),
            "v": _SemPool(nc, self.es, "pv_", 16, 1),
            "s": _SemPool(nc, self.es, "ps_", 12, 1),
            "g": _SemPool(nc, self.es, "pg_", 6, 1),
            "y": _SemPool(nc, self.es, "py_", 4, 1),
            "gd": _SemPool(nc, self.es, "pgd_", 16, 16),
            "yd": _SemPool(nc, self.es, "pyd_", 12, 16),
        }

    def alias(self, name, shape, dtype, backing):
        v = self.nc.virtual_sbuf_tensor(name, list(shape), dtype)
        self.nc.back_virtual_tensor_with(v, backing)
        self.aliases[name] = backing.name
        return v

    def op(self, eng, method, *args, wait=(), tick=False, **kw):
        r = _Rec(len(self.ops), eng, method, args, kw,
                 tuple(t for t in wait if t is not None), False)
        self.ops.append(r)
        return r

    def dma(self, eng, out, in_, wait=()):
        r = _Rec(len(self.ops), eng, "dma_start", (), {"out": out, "in_": in_},
                 tuple(t for t in wait if t is not None), True)
        self.ops.append(r)
        return r

    def tick(self, eng, wait=()):
        return self.op(eng, "nop", wait=wait)

    def mm(self, out, lhsT, rhs, start, stop, wait=(), tick=False, **kw):
        return self.op("t", "matmul", out, lhsT, rhs, start=start, stop=stop, wait=wait, **kw)

    PSUM_NAMES = ("PSA", "PSB", "PST")

    def _rect(self, ap):
        t = ap.tensor
        name = self.aliases.get(t.name, t.name)
        sz = DTSZ[t.dtype]
        dims = list(ap.ap)
        off = ap.offset
        if len(t.shape) >= 2 and t.shape[0] <= 128 and not name.startswith("DR_"):
            pcols = t.shape[1] if len(t.shape) == 2 else int(np.prod(t.shape[1:]))
            p0 = off // pcols
            c0 = off % pcols
            p1 = p0 + (dims[0][1] if dims else 1)
            cspan = sum((cnt - 1) * st for st, cnt in dims[1:]) + 1 if len(dims) > 1 else 1
            b0, b1 = c0 * sz, (c0 + cspan) * sz
            if name in self.PSUM_NAMES:
                p0, p1 = 0, 128
                b0 = (b0 // 2048) * 2048
                b1 = ((b1 + 2047) // 2048) * 2048
            return (name, p0, p1, b0, b1)
        span = sum((cnt - 1) * st for st, cnt in dims) + 1
        return (name, 0, 1, off * sz, (off + span) * sz)

    def _is_ap(self, x):
        return hasattr(x, "tensor") and hasattr(x, "offset")

    def _io(self, r):
        writes, reads = [], []
        outs = []
        if "out" in r.kw:
            outs.append(r.kw["out"])
        if "accum_out" in r.kw and self._is_ap(r.kw.get("accum_out")):
            outs.append(r.kw["accum_out"])
        args = list(r.args)
        if args and self._is_ap(args[0]):
            outs.append(args[0])
            args = args[1:]
        for a in args:
            if self._is_ap(a):
                reads.append(a)
        for k, vv in r.kw.items():
            if k in ("out", "accum_out"):
                continue
            if k == "outs" and isinstance(vv, list):
                outs.extend(x for x in vv if self._is_ap(x))
                continue
            if isinstance(vv, list):
                reads.extend(x for x in vv if self._is_ap(x))
                continue
            if self._is_ap(vv):
                reads.append(vv)
        return [self._rect(a) for a in reads], [self._rect(a) for a in outs]

    @staticmethod
    def _buckets(rect):
        name, p0, p1, b0, b1 = rect
        for pb in range(p0 // 32, (p1 + 31) // 32):
            for cb in range(b0 // 512, (b1 + 511) // 512):
                yield (name, pb, cb)

    @staticmethod
    def _overlap(a, b):
        return a[0] == b[0] and a[1] < b[2] and b[1] < a[2] and a[3] < b[4] and b[3] < a[4]

    def flush(self, verbose=False):
        last_w = {}
        readers = {}
        eidx = {e: 0 for e in self.ENGS}
        for r in self.ops:
            r.eidx = eidx[r.eng]
            eidx[r.eng] += 1
            reads, writes = self._io(r)
            r.reads, r.writes = reads, writes
            edges = {}
            for rect in reads:
                short = (rect[1], rect[3], rect[2], rect[4])
                for bk in self._buckets(rect):
                    w = last_w.get(bk)
                    if w is not None and w[1] is not r:
                        edges[w[1].idx] = w[1]
            for rect in writes:
                for bk in self._buckets(rect):
                    w = last_w.get(bk)
                    if w is not None and w[1] is not r:
                        edges[w[1].idx] = w[1]
                    for rd in readers.get(bk, ()):
                        if rd is not r:
                            edges[rd.idx] = rd
            for rect in reads:
                for bk in self._buckets(rect):
                    readers.setdefault(bk, []).append(r)
            for rect in writes:
                for bk in self._buckets(rect):
                    last_w[bk] = (rect, r)
                    readers[bk] = []
            r.edges = [self.ops[i] for i in sorted(edges)]

        prev_on_engine = {e: None for e in self.ENGS}
        for r in self.ops:
            ck = {}
            pred = prev_on_engine[r.eng]
            if pred is not None:
                ck.update(pred.clock)
                if r.eng == "t":
                    ck["T"] = max(ck.get("T", -1), pred.eidx)
            needed = []
            for p in r.edges + [w if isinstance(w, _Rec) else None for w in r.mwaits]:
                if p is None:
                    continue
                if p.eng == "t" and r.eng == "t" and not p.is_dma:
                    if ck.get("T", -1) < p.eidx:
                        ck["T"] = max(ck.get("T", -1), p.eidx)
                        for k, vv in p.clock.items():
                            if ck.get(k, -1 if k == "T" else 0) < vv:
                                ck[k] = vv
                    continue
                needed.append(p)
            for p in needed:
                p.ticked = True
            r.clock = ck
            r._needed = needed
            prev_on_engine[r.eng] = r

        for r in self.ops:
            if r.ticked or r.is_dma:
                pool = self.pools[r.eng + "d"] if r.is_dma else self.pools[r.eng]
                r.sem, r.val, r._fence = pool.take()
        prev_on_engine = {e: None for e in self.ENGS}
        nwaits = 0
        for r in self.ops:
            ck = {}
            pred = prev_on_engine[r.eng]
            if pred is not None:
                ck.update(pred.clock)
                if r.eng == "t":
                    ck["T"] = max(ck.get("T", -1), pred.eidx)
            waits = []
            for p in r._needed:
                if p.eng == "t" and r.eng == "t" and not p.is_dma:
                    continue
                if p.sem is None:
                    raise RuntimeError(f"producer without sem: {p.method}")
                key = id(p.sem)
                if ck.get(key, 0) >= p.val:
                    continue
                waits.append((p.sem, p.val))
                ck[key] = p.val
                for k, vv in p.clock.items():
                    if k == "T":
                        ck["T"] = max(ck.get("T", -1), vv)
                    elif ck.get(k, 0) < vv:
                        ck[k] = vv
                if p.eng == "t" and not p.is_dma:
                    ck["T"] = max(ck.get("T", -1), p.eidx)
            r._waits = waits
            nwaits += len(waits)
            r.clock = ck
            prev_on_engine[r.eng] = r
        if verbose:
            nt = sum(1 for r in self.ops if r.ticked)
            print(f"[mk] ops={len(self.ops)} ticked={nt} waits={nwaits}")

        eng_ops = {e: [r for r in self.ops if r.eng == e] for e in self.ENGS}

        with self.nc.Block() as block:
            def mk_body(name):
                def body(eng):
                    seen = {}
                    for r in eng_ops[name]:
                        if r.sem is not None and r._fence > 0 and seen.get(id(r.sem), 0) < r._fence:
                            eng.wait_ge(r.sem, r._fence)
                            seen[id(r.sem)] = r._fence
                        for (sem, val) in r._waits:
                            if seen.get(id(sem), 0) < val:
                                eng.wait_ge(sem, val)
                                seen[id(sem)] = val
                        ins = getattr(eng, r.method)(*r.args, **r.kw)
                        if r.sem is not None:
                            ins.then_inc(r.sem, 16 if r.is_dma else 1)
                return body
            block.tensor(mk_body("t"))
            block.vector(mk_body("v"))
            block.scalar(mk_body("s"))
            block.gpsimd(mk_body("g"))
            block.sync(mk_body("y"))


def host_constants(S):
    c = {"c_ident": np.eye(128, dtype=np.float32).astype(ml_dtypes.bfloat16),
         "c_identF": np.eye(128, dtype=np.float32)}
    inv_freq = 1.0 / 10000.0 ** (np.arange(0, 64, 2, dtype=np.float32) / 64.0)
    t = np.arange(S, dtype=np.float32)
    freqs = inv_freq[:, None] * t[None, :]
    emb = np.concatenate([freqs, freqs], 0)
    cos = np.cos(emb)
    sin = np.sin(emb)
    c["c_cos"] = np.concatenate([cos, cos], 0).astype(ml_dtypes.bfloat16)
    c["c_sin"] = np.concatenate([sin, sin], 0).astype(ml_dtypes.bfloat16)
    perm = np.zeros((128, 128), dtype=np.float32)
    for blk in (0, 64):
        for d_ in range(32):
            perm[blk + d_, blk + d_ + 32] = -1.0
            perm[blk + 32 + d_, blk + d_] = 1.0
    c["c_permT"] = perm.T.copy().astype(ml_dtypes.bfloat16)
    msk = np.zeros((128, 8 * 512), dtype=ml_dtypes.bfloat16)
    for ci in range(8):
        kc = ci if ci < 4 else ci + 4
        for p in range(128):
            kj = kc * 128 + p
            qi = np.arange(512)
            msk[p, ci * 512:(ci + 1) * 512] = ((kj >= qi) & (kj <= qi + 1024)).astype(np.float32)
    c["c_l1mask"] = msk
    return c


def wimg(wT):
    Din, O = wT.shape
    return np.ascontiguousarray(wT.reshape(Din // 128, 128, O).transpose(1, 0, 2)
                                .reshape(128, (Din // 128) * O))


def prep_l0a(inputs):
    d = {}
    d["l0a_wq"] = wimg(np.ascontiguousarray(inputs["L0_attn_w_q"].T))
    d["l0a_wk"] = wimg(np.ascontiguousarray(inputs["L0_attn_w_k"].T))
    d["l0a_wv"] = wimg(np.ascontiguousarray(inputs["L0_attn_w_v"].T))
    r32 = np.zeros((1, 1360), dtype=np.float32)
    r32[0, 0:129] = inputs["L0_attn_b_q"]
    r32[0, 160:289] = inputs["L0_attn_b_k"]
    r32[0, 320:1344] = inputs["L0_attn_b_v"]
    d["l0a_rows32"] = r32
    d["l0a_k0x"] = np.ascontiguousarray(inputs["L0_attn_k_0"][128:129][None, :])
    d["l0a_rows64"] = np.ascontiguousarray(inputs["L0_attn_v_0"][None, :])
    d["l0a_k0m"] = np.ascontiguousarray(inputs["L0_attn_k_0"][0:128, None])
    d["l0a_bcols"] = np.stack([inputs["L0_attn_b_q"][0:128], inputs["L0_attn_b_k"][0:128]], 1)
    return d




def prep_l0f(inputs):
    d = {}
    d["l0f_wqkv"] = qkv_group_img(np.ascontiguousarray(inputs["L0_ffn_qkv_w"].T))
    d["l0f_wout"] = wimg(np.ascontiguousarray(inputs["L0_ffn_out_w"].T))
    rows = np.zeros((1, 4096), dtype=np.float32)
    rows[0, 0:3072] = inputs["L0_ffn_qkv_b"]
    rows[0, 3072:4096] = inputs["L0_ffn_out_b"]
    d["l0f_rows"] = rows
    d["l0f_n1"] = np.ascontiguousarray(inputs["L0_ffn_norm1_w"].reshape(8, 128).T)
    d["l0f_n2"] = np.ascontiguousarray(inputs["L0_ffn_norm2_w"].reshape(8, 128).T)
    d["l0f_gw"] = wimg(np.ascontiguousarray(inputs["L0_ffn_gate_w"].T))
    w1 = inputs["L0_ffn_w1"]; w2 = inputs["L0_ffn_w2"]; w3 = inputs["L0_ffn_w3"]
    w1r = w1.reshape(8, 16, 128, 8, 128).transpose(0, 4, 1, 3, 2)
    w1r = w1.reshape(8, 16, 128, 8, 128).transpose(0, 4, 1, 3, 2)
    w2r = w2.reshape(8, 16, 128, 8, 128).transpose(0, 4, 1, 3, 2)
    img = np.stack([w1r, w2r], axis=4)
    d["l0f_w12"] = np.ascontiguousarray(img.reshape(8, 128, 16 * 2048))
    d["l0f_w3"] = np.ascontiguousarray(w3.reshape(8, 1024, 16, 128).transpose(0, 3, 2, 1))
    d["l0f_b1"] = np.ascontiguousarray(inputs["L0_ffn_b1"].reshape(8, 16, 128).transpose(0, 2, 1))
    d["l0f_b2"] = np.ascontiguousarray(inputs["L0_ffn_b2"])
    d["l0f_b3"] = np.ascontiguousarray(inputs["L0_ffn_b3"])
    d["l0f_b2c"] = np.ascontiguousarray(inputs["L0_ffn_b2"].reshape(8, 16, 128).transpose(0, 2, 1))
    qb = inputs["L0_ffn_qkv_b"]
    d["l0f_bqk"] = np.ascontiguousarray(np.stack(
        [qb[g * 128:(g + 1) * 128] for g in range(8)] +
        [qb[1024 + g * 128:1024 + (g + 1) * 128] for g in range(8)], 1))
    d["l0f_boc"] = np.ascontiguousarray(inputs["L0_ffn_out_b"].reshape(8, 128).T)
    return d

def qkv_group_img(qkvT):
    img = np.empty((128, 8 * 3072), dtype=np.float32)
    for g in range(8):
        for dc in range(8):
            blk = img[:, g * 3072 + dc * 384:g * 3072 + (dc + 1) * 384]
            rows = slice(dc * 128, (dc + 1) * 128)
            for which in range(3):
                blk[:, which * 128:(which + 1) * 128] = \
                    qkvT[rows, which * 1024 + 128 * g:which * 1024 + 128 * (g + 1)]
    return img


def prep_l1a(inputs):
    d = {}
    qkvT = np.concatenate([np.ascontiguousarray(inputs["L1_attn_wq"].T),
                           np.ascontiguousarray(inputs["L1_attn_wk"].T),
                           np.ascontiguousarray(inputs["L1_attn_wv"].T)], axis=1)
    d["l1a_wqkv"] = qkv_group_img(qkvT)
    d["l1a_wout"] = wimg(np.ascontiguousarray(inputs["L1_attn_wo"].T))
    rows = np.zeros((1, 4096), dtype=np.float32)
    rows[0, 0:1024] = inputs["L1_attn_bq"]
    rows[0, 1024:2048] = inputs["L1_attn_bk"]
    rows[0, 2048:3072] = inputs["L1_attn_bv"]
    rows[0, 3072:4096] = inputs["L1_attn_bo"]
    d["l1a_rows"] = rows
    d["l1a_bqk"] = np.ascontiguousarray(np.stack(
        [inputs["L1_attn_bq"][g * 128:(g + 1) * 128] for g in range(8)] +
        [inputs["L1_attn_bk"][g * 128:(g + 1) * 128] for g in range(8)], 1))
    d["l1a_boc"] = np.ascontiguousarray(inputs["L1_attn_bo"].reshape(8, 128).T)
    return d


def prep_l1f(inputs):
    d = {}
    phi = inputs["L1_ffn_phi"].reshape(1024, 2048)
    d["l1f_phi"] = wimg(np.ascontiguousarray(phi))
    W = inputs["L1_ffn_W"]
    d["l1f_W"] = np.ascontiguousarray(
        W.reshape(64, 8, 128, 1024).transpose(0, 2, 1, 3).reshape(64, 128, 8192))
    d["l1f_brep"] = np.ascontiguousarray(np.repeat(inputs["L1_ffn_b"], 32, axis=0))
    return d


def build_program(S=2048, parts=("l0a",), n_cores=8):
    nc = bass.Bass("TRN2", target_bir_lowering=False, debug=False)
    mk = MK(nc)
    NT = S // 128
    NQ = S // 512

    ap = {}

    def inp(name, shape, dt=F32):
        ap[name] = nc.dram_tensor(name, list(shape), dt, kind="ExternalInput").ap()
        return ap[name]

    xT = inp("xT", [D, S])
    inp("c_ident", [128, 128], BF16)
    inp("c_identF", [128, 128])
    if "l0a" in parts:
        inp("l0a_wq", [128, 8 * P129]); inp("l0a_wk", [128, 8 * P129])
        inp("l0a_wv", [128, 8 * 1024])
        inp("l0a_rows32", [1, 1360]); inp("l0a_rows64", [1, 1024]); inp("l0a_k0x", [1, 1])
        inp("l0a_k0m", [128, 1]); inp("l0a_bcols", [128, 2])
    if "l0fa" in parts:
        inp("l0f_wqkv", [128, 8 * 3072]); inp("l0f_wout", [128, 8 * 1024])
        inp("l0f_rows", [1, 4096]); inp("l0f_n1", [128, 8]); inp("l0f_n2", [128, 8])
        inp("l0f_bqk", [128, 16]); inp("l0f_boc", [128, 8])
        inp("c_cos", [128, S], BF16); inp("c_sin", [128, S], BF16)
        inp("c_permT", [128, 128], BF16)
    if "l0fm" in parts:
        inp("l0f_gw", [128, 64])
        inp("l0f_w12", [8, 128, 16 * 2048])
        inp("l0f_w3", [8, 128, 16, 1024])
        inp("l0f_b1", [8, 128, 16]); inp("l0f_b2c", [8, 128, 16]); inp("l0f_b3", [8, 1024])
        if "l0fa" not in parts:
            inp("tx1", [D, S])
        if "l0fa" not in parts or True:
            pass
    if "l0fm" in parts and "l0fa" not in parts:
        inp("l0f_n2", [128, 8])
    if "l1a" in parts:
        inp("l1a_wqkv", [128, 8 * 3072]); inp("l1a_wout", [128, 8 * 1024])
        inp("l1a_rows", [1, 4096]); inp("c_l1mask", [128, 8 * 512], BF16)
        inp("l1a_bqk", [128, 16]); inp("l1a_boc", [128, 8])
    if "l1f" in parts:
        inp("l1f_phi", [128, 8 * 2048])
        inp("l1f_W", [64, 128, 8192])
        inp("l1f_brep", [2048, 1024])
    out_x = nc.dram_tensor("outX", [D, S], F32, kind="ExternalOutput").ap()
    if "l1f" in parts:
        hE = nc.dram_tensor("hE", [S, 2048], F32, kind="Internal")
        hS = nc.dram_tensor("hS", [S, 2048], F32, kind="Internal")
        hWbf = nc.dram_tensor("hWbf", [64, 128, 8192], BF16, kind="Internal")
    hx1 = nc.dram_tensor("hx1", [D, S], F32, kind="Internal").ap()
    hcomb = nc.dram_tensor("hcomb", [8, S], F32, kind="Internal").ap()
    hattn = nc.dram_tensor("hattn", [D, S], BF16, kind="Internal").ap()

    A_X = nc.alloc_sbuf_tensor("A_X", [128, DC * S], F32)
    A_NB = nc.alloc_sbuf_tensor("A_NB", [128, max(DC * S, DC * 2048)], BF16)
    A_VT = nc.alloc_sbuf_tensor("A_VT", [128, max(DC * S, 2080 + 4 * S, 16384)], BF16)
    PTCOLS = max((S // 128 + 1) * 512 + (S + 128) + 2 * S, DC * S, DC * (S // 2) + 8192, 16384)
    A_PT = nc.alloc_sbuf_tensor("A_PT", [128, PTCOLS], BF16)
    A_WS = nc.alloc_sbuf_tensor("A_WS", [128, 8192], BF16)
    A_SC = nc.alloc_sbuf_tensor("A_SC", [128, 4608], F32)
    rowsB = nc.alloc_sbuf_tensor("rowsB", [128, 4100], BF16)
    onesB = nc.alloc_sbuf_tensor("onesB", [128, 512], BF16)
    onesF = nc.alloc_sbuf_tensor("onesF", [128, 128], F32)
    ident = nc.alloc_sbuf_tensor("ident", [128, 128], BF16)
    identG = nc.alloc_sbuf_tensor("identG", [128, 128], F32)
    ap["__bqkL__"] = nc.alloc_sbuf_tensor("bqkL", [128, 16], F32)
    maxcG = nc.alloc_sbuf_tensor("maxcG", [128, 1], F32)
    k0m = nc.alloc_sbuf_tensor("k0m", [128, 1], BF16)

    tmpf = A_SC[:, 4096:4608]
    negm = maxcG
    tmpg = nc.alloc_sbuf_tensor("tmpg", [128, 512], F32)
    PSA = nc.alloc_psum_tensor("PSA", [128, 2048], F32)
    PSB = nc.alloc_psum_tensor("PSB", [128, 1536], F32)
    PST = nc.alloc_psum_tensor("PST", [128, 1024], BF16)

    mk.op("v", "memset", onesB[:], 1.0)
    mk.op("v", "memset", onesF[:], 1.0)
    mk.dma("y", identG[:], ap["c_identF"])
    mk.dma("y", ident[:], ap["c_ident"])
    for dc in range(DC):
        mk.dma("y", A_X[:, dc * S:(dc + 1) * S], xT[dc * 128:(dc + 1) * 128, :])

    _precast = []
    if "l1f" in parts:
        _precast = [n for n in range(64)]

    def drain_precast(k):
        for _ in range(k):
            if _precast:
                n = _precast.pop(0)
                mk.dma("g", hWbf.ap()[n], ap["l1f_W"][n])

    def rowpack(idx):
        r, b = divmod(idx, 3)
        return PSB[32 * b:32 * b + 1, r * 512:(r + 1) * 512]

    def X(dc, sl=None):
        base = dc * S
        return A_X[:, base:base + S] if sl is None else A_X[:, base + sl.start:base + sl.stop]

    def ln_sum_stats(src_chunks, want_mean=True):
        sq_tile = A_PT[:, 0:S]
        for dc in range(DC):
            mk.op("s", "activation", sq_tile, src_chunks[dc], AF.Square)
            for ts in range(NQ):
                sl = slice(ts * 512, (ts + 1) * 512)
                if want_mean:
                    mk.mm(rowpack(ts), onesB[:, 0:1], src_chunks[dc][:, sl],
                          start=(dc == 0), stop=(dc == DC - 1))
                mk.mm(rowpack(4 + ts), onesB[:, 0:1], sq_tile[:, sl],
                      start=(dc == 0), stop=(dc == DC - 1))

    def part_l0a():
        wq = A_WS[:, 0:8 * P129]
        wk = A_WS[:, 8 * P129:16 * P129]
        wvh = A_WS[:, 16 * P129:16 * P129 + 8 * 512]
        mk.dma("g", wq, ap["l0a_wq"])
        mk.dma("g", wk, ap["l0a_wk"])
        rows32 = rowsB[32:33, 0:1360]
        mk.dma("g", rows32, ap["l0a_rows32"])
        rows64 = rowsB[64:65, 0:1024]
        mk.dma("g", rows64, ap["l0a_rows64"])
        mk.dma("g", k0m[:], ap["l0a_k0m"])
        bq = rowsB[32:33, 0:129]; bk = rowsB[32:33, 160:289]
        bv = rowsB[32:33, 320:1344]
        k0x = rowsB[0:1, 4096:4097]
        mk.dma("g", k0x, ap["l0a_k0x"])
        bcolA = nc.alloc_sbuf_tensor("bcolA", [128, 2], F32)
        mk.dma("g", bcolA[:], ap["l0a_bcols"])
        v0r = rowsB[64:65, 0:1024]
        qxr = rowsB[0:1, 0:S]
        kxr = rowsB[0:1, 2048:2048 + S]
        ptx = rowsB[64:65, 1024:1024 + 512]

        pp0 = 0
        preP0 = pp0 + (NT + 1) * 512
        qf0 = preP0 + S + 128
        kf0 = qf0 + S
        assert kf0 + S <= PTCOLS
        preP = A_PT[:, preP0:preP0 + S + 128]
        qf = A_PT[:, qf0:qf0 + S]
        kf = A_PT[:, kf0:kf0 + S]

        def PTt(kc, a, b):
            return A_PT[:, pp0 + kc * 512 + a:pp0 + kc * 512 + b]

        xb = [A_NB[:, dc * S:(dc + 1) * S] for dc in range(DC)]
        for dc in range(DC):
            mk.op("v", "tensor_copy", xb[dc], X(dc))

        sq_tile = A_PT[:, pp0:pp0 + S]
        for dc in range(DC):
            mk.op("s", "activation", sq_tile, xb[dc], AF.Square)
            for ts in range(NQ):
                sl = slice(ts * 512, (ts + 1) * 512)
                mk.mm(rowpack(ts), onesB[:, 0:1], xb[dc][:, sl],
                      start=(dc == 0), stop=(dc == DC - 1))
                mk.mm(rowpack(4 + ts), onesB[:, 0:1], sq_tile[:, sl],
                      start=(dc == 0), stop=(dc == DC - 1))
        mu = A_SC[0:1, 0:S]; rstd = A_SC[0:1, 2048:2048 + S]; rtmp = A_SC[0:1, 4096:4608]
        for ts in range(NQ):
            sl = slice(ts * 512, (ts + 1) * 512)
            mk.op("v", "tensor_scalar_mul", mu[:, sl], rowpack(ts), 1.0 / D)
            mk.op("v", "tensor_scalar_mul", rstd[:, sl], rowpack(4 + ts), 1.0 / D)
        for ts in range(NQ):
            sl = slice(ts * 512, (ts + 1) * 512)
            mk.op("v", "tensor_mul", rtmp, mu[:, sl], mu[:, sl])
            mk.op("v", "tensor_sub", rstd[:, sl], rstd[:, sl], rtmp)
            mk.op("v", "tensor_scalar_add", rstd[:, sl], rstd[:, sl], 1e-5)
            mk.op("s", "activation", rstd[:, sl], rstd[:, sl], AF.Ln)
            mk.op("s", "activation", rstd[:, sl], rstd[:, sl], AF.Exp, scale=-0.5)
            mk.op("v", "tensor_mul", mu[:, sl], mu[:, sl], rstd[:, sl])

        for ts in range(NQ):
            sl = slice(ts * 512, (ts + 1) * 512)
            mk.mm(PSB[:, 0:512], onesF[0:1, :], rstd[:, sl], start=True, stop=True)
            mk.mm(PSB[:, 512:1024], onesF[0:1, :], mu[:, sl], start=True, stop=True)
            for dc in range(DC):
                mk.op("v", "tensor_mul", tmpf[:], X(dc, sl), PSB[:, 0:512])
                mk.op("v", "tensor_sub", xb[dc][:, sl], tmpf[:], PSB[:, 512:1024])

        for wi, (wT, brow_x, dstf, dstx) in enumerate((
                (wq, rowsB[32:33, 128:129], qf, qxr),
                (wk, rowsB[32:33, 288:289], kf, kxr))):
            for ts in range(NQ):
                sl = slice(ts * 512, (ts + 1) * 512)
                for dc in range(DC):
                    mk.mm(PSA[:, sl], wT[:, dc * P129:dc * P129 + 128], xb[dc][:, sl],
                          start=(dc == 0), stop=(dc == DC - 1))
                for dc in range(DC):
                    mk.mm(rowpack(ts), wT[:, dc * P129 + 128:dc * P129 + 129],
                          xb[dc][:, sl], start=(dc == 0), stop=False)
                mk.mm(rowpack(ts), brow_x, onesB[32:33, 0:512], start=False, stop=True)
            mk.op("v", "tensor_scalar", dstf, PSA[:, 0:S], bcolA[:, wi:wi + 1], None,
                  op0=ALU.add)
            for ts in range(NQ):
                mk.op("v", "tensor_copy", dstx[:, ts * 512:(ts + 1) * 512], rowpack(ts))

        wv_img = ap["l0a_wv"]
        for oh in range(2):
            mk.dma("g", wvh, wv_img.rearrange("p (dc o) -> p dc o", o=1024)[:, :, oh * 512:(oh + 1) * 512])
            for tc in range(NT):
                tsl = slice(tc * 128, (tc + 1) * 128)
                osl = slice(oh * 512, (oh + 1) * 512)
                for dc in range(DC):
                    mk.mm(PSB[:, osl], xb[dc][:, tsl],
                          wvh[:, dc * 512:(dc + 1) * 512],
                          start=(dc == 0), stop=False)
                mk.mm(PSB[:, osl], onesB[32:33, 0:128], bv[:, oh * 512:(oh + 1) * 512],
                      start=False, stop=True)
                mk.op("v", "tensor_copy", A_VT[:, tc * 1024 + oh * 512:tc * 1024 + (oh + 1) * 512],
                      PSB[:, osl])

        for qb in range(NQ):
            qsl = slice(qb * 512, (qb + 1) * 512)
            for qt in range(4):
                qtile = slice(qb * 512 + qt * 128, qb * 512 + (qt + 1) * 128)
                for kt in range(NQ):
                    ksl = slice(kt * 512, (kt + 1) * 512)
                    mk.mm(PSA[:, ksl], qf[:, qtile], kf[:, ksl], start=True, stop=False)
                    mk.mm(PSA[:, ksl], qxr[:, qtile], kxr[:, ksl], start=False, stop=True)
                mk.mm(PSB[:, 1024:1025], qf[:, qtile], k0m[:, 0:1], start=True, stop=False)
                mk.mm(PSB[:, 1024:1025], qxr[:, qtile], k0x[:, 0:1], start=False, stop=True)
                mk.op("v", "reduce_max", negm[:, 0:1], PSA[:, 0:S], axis=AX.X)
                mk.op("v", "tensor_max", negm[:, 0:1], negm[:, 0:1], PSB[:, 1024:1025])
                mk.op("v", "tensor_scalar_mul", negm[:, 0:1], negm[:, 0:1], -1.0)
                mk.op("s", "activation", preP[:, 0:S], PSA[:, 0:S], AF.Exp,
                      bias=negm[:, 0:1])
                mk.op("s", "activation", preP[:, S:S + 1], PSB[:, 1024:1025], AF.Exp,
                      bias=negm[:, 0:1])
                for kc in range(NT):
                    slot = (kc % 2) * 128
                    mk.op("t", "transpose", PST[:, slot:slot + 128],
                          preP[:, kc * 128:(kc + 1) * 128], ident[:])
                    mk.op("v", "tensor_copy", PTt(kc, qt * 128, (qt + 1) * 128),
                          PST[:, slot:slot + 128])
                mk.op("t", "transpose", PST[0:1, 256:384], preP[:, S:S + 1], ident[:])
                mk.op("v", "tensor_copy", ptx[:, qt * 128:(qt + 1) * 128],
                      PST[0:1, 256:384])
            import os
            if os.environ.get("MK_STOP") == "evac":
                return
            for kc in range(NT):
                mk.mm(PSB[0:1, 0:512], onesB[:, 0:1], PTt(kc, 0, 512),
                      start=(kc == 0), stop=False)
            mk.mm(PSB[0:1, 0:512], onesB[64:65, 0:1], ptx[:, 0:512], start=False, stop=True)
            sums_r = A_SC[32:33, 0:512]
            mk.op("s", "activation", sums_r, PSB[0:1, 0:512], AF.Ln)
            mk.op("s", "activation", sums_r, sums_r, AF.Exp, scale=-1.0)
            mk.mm(PSB[:, 0:512], onesF[32:33, :], sums_r, start=True, stop=True)
            mk.op("v", "tensor_copy", tmpg[:], PSB[:, 0:512])
            mk.mm(PSB[:, 512:1024], onesF[0:1, :], rstd[:, qsl], start=True, stop=True)
            mk.mm(PSB[:, 1024:1536], onesF[0:1, :], mu[:, qsl], start=True, stop=True)
            if os.environ.get("MK_STOP") == "sums":
                return
            for half in range(2):
                for dci in range(4):
                    dc = half * 4 + dci
                    csl = slice(dci * 512, (dci + 1) * 512)
                    for kc in range(NT):
                        mk.mm(PSA[:, csl],
                              A_VT[:, kc * 1024 + dc * 128:kc * 1024 + (dc + 1) * 128],
                              PTt(kc, 0, 512),
                              start=(kc == 0), stop=False)
                    mk.mm(PSA[:, csl], v0r[:, dc * 128:(dc + 1) * 128], ptx[:, 0:512],
                          start=False, stop=True)
                for dci in range(4):
                    dc = half * 4 + dci
                    csl = slice(dci * 512, (dci + 1) * 512)
                    mk.op("v", "tensor_mul", tmpf[:], X(dc, qsl), PSB[:, 512:1024])
                    mk.op("v", "tensor_sub", tmpf[:], tmpf[:], PSB[:, 1024:1536])
                    mk.op("v", "tensor_add", X(dc, qsl), X(dc, qsl), tmpf[:])
                    mk.op("v", "tensor_mul", tmpf[:], PSA[:, csl], tmpg[:])
                    mk.op("v", "tensor_add", X(dc, qsl), X(dc, qsl), tmpf[:])


    def part_l0fa():
        NG = 8
        mk.dma("g", rowsB[32:33, 0:4096], ap["l0f_rows"])
        qkvb = rowsB[32:33, 0:3072]
        outb = rowsB[32:33, 3072:4096]
        n1sb = nc.alloc_sbuf_tensor("n1sb", [128, 8], BF16)
        n2sb = nc.alloc_sbuf_tensor("n2sb", [128, 8], BF16)
        mk.dma("g", n1sb[:], ap["l0f_n1"])
        mk.dma("g", n2sb[:], ap["l0f_n2"])
        ap["__n2sb__"] = n2sb
        bqk = nc.alloc_sbuf_tensor("bqk", [128, 16], F32)
        boc = nc.alloc_sbuf_tensor("boc", [128, 8], F32)
        ap["__bqk__"] = bqk
        mk.dma("g", bqk[:], ap["l0f_bqk"])
        mk.dma("g", boc[:], ap["l0f_boc"])
        vt_q = 0
        vt_k = 2 * S
        vt_v0 = 4 * S
        vt_cos = 4 * S + 2080
        vt_sin = 5 * S + 2080
        cosT = A_VT[:, vt_cos:vt_cos + S]
        sinT = A_VT[:, vt_sin:vt_sin + S]
        mk.dma("y", cosT, ap["c_cos"])
        mk.dma("y", sinT, ap["c_sin"])
        perm = nc.alloc_sbuf_tensor("perm", [128, 128], BF16)
        mk.dma("y", perm[:], ap["c_permT"])
        identF2 = identG

        def qh(h):
            return A_VT[0:65, vt_q + h * S:vt_q + (h + 1) * S]

        def kh(h):
            return A_VT[0:65, vt_k + h * S:vt_k + (h + 1) * S]
        bbuf = A_PT[:, 0:S]
        proll0 = S
        attst0 = S + 3072
        attn_st = A_PT[:, attst0:attst0 + S]

        for dc in range(DC):
            mk.dma("y", hx1[dc * 128:(dc + 1) * 128, :], X(dc))

        x1b = [A_NB[:, dc * S:(dc + 1) * S] for dc in range(DC)]
        for dc in range(DC):
            mk.op("v", "tensor_copy", x1b[dc], X(dc))
        ln_sum_stats(x1b)
        mu = A_SC[0:1, 0:S]
        rstd = A_SC[0:1, 2048:2048 + S]
        rtmp = A_SC[0:1, 4096:4608]
        rmss = A_SC[32:33, 512:512 + S]
        for ts in range(NQ):
            sl = slice(ts * 512, (ts + 1) * 512)
            mk.op("v", "tensor_scalar_mul", mu[:, sl], rowpack(ts), 1.0 / D)
            mk.op("v", "tensor_scalar_mul", rstd[:, sl], rowpack(4 + ts), 1.0 / D)
            mk.op("v", "tensor_mul", rtmp, mu[:, sl], mu[:, sl])
            mk.op("v", "tensor_sub", rstd[:, sl], rstd[:, sl], rtmp)
            mk.op("v", "tensor_scalar_add", rstd[:, sl], rstd[:, sl], 1e-5)
            mk.op("s", "activation", rstd[:, sl], rstd[:, sl], AF.Ln)
            mk.op("s", "activation", rstd[:, sl], rstd[:, sl], AF.Exp, scale=-0.5)
            mk.op("v", "tensor_mul", mu[:, sl], mu[:, sl], rstd[:, sl])
            mk.op("v", "tensor_mul", rtmp, rstd[:, sl], rstd[:, sl])
            mk.op("v", "tensor_scalar", rtmp, rtmp, -1e-5, 1.0 + 1e-6,
                  op0=ALU.mult, op1=ALU.add)
            mk.op("s", "activation", rtmp, rtmp, AF.Ln)
            mk.op("s", "activation", rtmp, rtmp, AF.Exp, scale=-0.5)
            mk.op("v", "tensor_copy", rmss[:, sl], rtmp)

        for ts in range(NQ):
            sl = slice(ts * 512, (ts + 1) * 512)
            mk.mm(PSB[:, 0:512], onesF[0:1, :], rstd[:, sl], start=True, stop=True)
            mk.mm(PSB[:, 512:1024], onesF[0:1, :], mu[:, sl], start=True, stop=True)
            mk.mm(PSB[:, 1024:1536], onesF[32:33, :], rmss[:, sl], start=True, stop=True)
            for dc in range(DC):
                mk.op("v", "tensor_mul", tmpf, X(dc, sl), PSB[:, 0:512])
                mk.op("v", "tensor_sub", X(dc, sl), tmpf, PSB[:, 512:1024])
                mk.op("v", "scalar_tensor_tensor", x1b[dc][:, sl], X(dc, sl),
                      n1sb[:, dc:dc + 1], PSB[:, 1024:1536],
                      op0=ALU.mult, op1=ALU.mult)
        a = x1b

        for g in range(NG):
            ws0 = (g % 2) * 3072
            wg = A_WS[:, ws0:ws0 + 3072]
            mk.dma("g", wg, ap["l0f_wqkv"][:, g * 3072:(g + 1) * 3072])
            for which in range(2):
                bcol = bqk[:, which * 8 + g:which * 8 + g + 1]
                for ts in range(NQ):
                    sl = slice(ts * 512, (ts + 1) * 512)
                    for dc in range(DC):
                        mk.mm(PSA[:, sl], wg[:, dc * 384 + which * 128:dc * 384 + which * 128 + 128],
                              a[dc][:, sl], start=(dc == 0), stop=(dc == DC - 1))
                for ts in range(NQ):
                    sl = slice(ts * 512, (ts + 1) * 512)
                    mk.op("v", "scalar_tensor_tensor", bbuf[:, sl], PSA[:, sl], bcol,
                          sinT[:, sl], op0=ALU.add, op1=ALU.mult)
                    mk.op("v", "scalar_tensor_tensor", tmpf, PSA[:, sl], bcol,
                          cosT[:, sl], op0=ALU.add, op1=ALU.mult)
                    mk.mm(PSB[:, 0:512], perm[:], bbuf[:, sl], start=True, stop=True)
                    for h in range(2):
                        dst = qh(h) if which == 0 else kh(h)
                        mk.op("v", "tensor_add", dst[0:64, sl.start:sl.stop],
                              tmpf[h * 64:(h + 1) * 64, :],
                              PSB[h * 64:(h + 1) * 64, 0:512])
            for h in range(2):
                mk.op("v", "memset", kh(h)[64:65, 0:S], -1.0)
            mk.op("v", "memset", A_VT[:, vt_v0:vt_v0 + 2080], 1.0)
            for tc in range(NT):
                for dc in range(DC):
                    mk.mm(PSB[:, 512:640], a[dc][:, tc * 128:(tc + 1) * 128],
                          wg[:, dc * 384 + 256:dc * 384 + 384],
                          start=(dc == 0), stop=False)
                mk.mm(PSB[:, 512:640], onesB[32:33, 0:128],
                      qkvb[:, 2048 + g * 128:2048 + (g + 1) * 128],
                      start=False, stop=True)
                mk.op("v", "tensor_copy", A_VT[:, vt_v0 + tc * 130:vt_v0 + tc * 130 + 64],
                      PSB[:, 512:576])
                mk.op("v", "tensor_copy", A_VT[:, vt_v0 + tc * 130 + 65:vt_v0 + tc * 130 + 129],
                      PSB[:, 576:640])
            maxc = maxcG
            for h in range(2):
                for qt in range(NT):
                    qtl = slice(qt * 128, (qt + 1) * 128)
                    for kt in range(NQ):
                        mk.mm(PSA[:, kt * 512:(kt + 1) * 512], qh(h)[0:64, qtl],
                              kh(h)[0:64, kt * 512:(kt + 1) * 512], start=True, stop=True)
                    mk.op("v", "reduce_max", maxc[:, 0:1], PSA[:, 0:S], axis=AX.X)
                    mk.op("t", "transpose", PSB[0:1, 512:640], maxc[:, 0:1], identF2[:])
                    mk.op("v", "tensor_copy", qh(h)[64:65, qt * 128:(qt + 1) * 128],
                          PSB[0:1, 512:640])
            for h in range(2):
                for qt in range(NQ):
                    qsl = slice(qt * 512, (qt + 1) * 512)
                    pvslot = PSB[0:65, 1024:1536]
                    npair = NT // 2
                    slots = (PSA[:, 0:1024], PSA[:, 1024:2048], PSB[:, 0:1024])

                    def sco(j, h=h, qsl=qsl):
                        pa = slots[j % 3]
                        for jj in range(2):
                            kc = 2 * j + jj
                            mk.mm(pa[:, jj * 512:(jj + 1) * 512],
                                  kh(h)[:, kc * 128:(kc + 1) * 128],
                                  qh(h)[:, qsl], start=True, stop=True)
                        roll = A_PT[:, proll0 + (j % 3) * 1024:proll0 + (j % 3) * 1024 + 1024]
                        mk.op("s", "activation", roll, pa, AF.Exp, scale=0.125)
                        return roll

                    def pv(j, roll, h=h):
                        for jj in range(2):
                            kc = 2 * j + jj
                            mk.mm(pvslot,
                                  A_VT[:, vt_v0 + kc * 130 + h * 65:vt_v0 + kc * 130 + (h + 1) * 65],
                                  roll[:, jj * 512:(jj + 1) * 512],
                                  start=(kc == 0), stop=(kc == NT - 1))

                    rolls = {0: sco(0), 1: sco(1)}
                    for j in range(2, npair):
                        rolls[j] = sco(j)
                        pv(j - 2, rolls[j - 2])
                    pv(npair - 2, rolls[npair - 2])
                    pv(npair - 1, rolls[npair - 1])
                    mk.op("v", "tensor_copy", attn_st[h * 64:(h + 1) * 64, qsl],
                          pvslot[0:64, :])
                    mk.op("v", "tensor_copy",
                          A_SC[32:33, (h * NQ + qt) * 512:(h * NQ + qt) * 512 + 512],
                          pvslot[64:65, :])
            mk.op("s", "activation", A_SC[32:33, 0:2 * NQ * 512],
                  A_SC[32:33, 0:2 * NQ * 512], AF.Ln)
            mk.op("s", "activation", A_SC[32:33, 0:2 * NQ * 512],
                  A_SC[32:33, 0:2 * NQ * 512], AF.Exp, scale=-1.0)
            for h in range(2):
                for qt in range(NQ):
                    qsl = slice(qt * 512, (qt + 1) * 512)
                    mk.mm(PSB[:, 1024:1536], onesF[32:33, :],
                          A_SC[32:33, (h * NQ + qt) * 512:(h * NQ + qt) * 512 + 512],
                          start=True, stop=True)
                    mk.op("v", "tensor_mul", attn_st[h * 64:(h + 1) * 64, qsl],
                          attn_st[h * 64:(h + 1) * 64, qsl], PSB[0:64, 1024:1536])
            mk.dma("y", hattn[g * 128:(g + 1) * 128, :], attn_st)
            drain_precast(4)

        wo = A_WS[:, 0:8192]
        mk.dma("g", wo, ap["l0f_wout"])
        for dc in range(DC):
            mk.dma("y", A_PT[:, dc * S:(dc + 1) * S], hattn[dc * 128:(dc + 1) * 128, :])
        for oc in range(DC):
            for ts in range(NQ):
                sl = slice(ts * 512, (ts + 1) * 512)
                for dc in range(DC):
                    mk.mm(PSA[:, sl], wo[:, dc * 1024 + oc * 128:dc * 1024 + (oc + 1) * 128],
                          A_PT[:, dc * S + ts * 512:dc * S + (ts + 1) * 512],
                          start=(dc == 0), stop=(dc == DC - 1))
                mk.op("v", "scalar_tensor_tensor", X(oc, sl), PSA[:, sl],
                      boc[:, oc:oc + 1], X(oc, sl), op0=ALU.add, op1=ALU.add)


    def part_l0fm(have_l0fa):
        NE, NH = 8, 16
        if not have_l0fa:
            n2sb = nc.alloc_sbuf_tensor("n2sb_m", [128, 8], BF16)
            mk.dma("g", n2sb[:], ap["l0f_n2"])
            for dc in range(DC):
                mk.dma("g", hx1[dc * 128:(dc + 1) * 128, :], ap["tx1"][dc * 128:(dc + 1) * 128, :])
        else:
            n2sb = ap["__n2sb__"]
        gwsb = nc.alloc_sbuf_tensor("gwsb", [128, 64], F32)
        mk.dma("g", gwsb[:], ap["l0f_gw"])
        n2f = nc.alloc_sbuf_tensor("n2f", [128, 8], F32)
        mk.dma("g", n2f[:], ap["l0f_n2"])
        rmsscol = nc.alloc_sbuf_tensor("rmsscol", [128, 16], F32)
        glt = nc.alloc_sbuf_tensor("glt", [128, 8], F32)
        v1 = nc.alloc_sbuf_tensor("v1", [128, 1], F32)
        v2 = nc.alloc_sbuf_tensor("v2", [128, 1], F32)
        sg = nc.alloc_sbuf_tensor("sg", [128, 1], F32)
        og = nc.alloc_sbuf_tensor("og", [128, 1], F32)
        tmp8 = nc.alloc_sbuf_tensor("tmp8", [128, 8], F32)
        mask1 = nc.alloc_sbuf_tensor("mask1", [128, 8], F32)
        identF = identG
        stage8 = tmpg[0:8, 0:128]
        if "__bqk__" not in ap:
            ap["__bqk__"] = nc.alloc_sbuf_tensor("bqk", [128, 16], F32)
        b1sb = ap["__bqk__"]
        b2sb = ap["__bqkL__"]

        m = [A_NB[:, dc * S:(dc + 1) * S] for dc in range(DC)]

        for dc in range(DC):
            mk.op("v", "tensor_copy", m[dc], X(dc))
        ln_sum_stats(m, want_mean=False)
        rmss = A_SC[0:1, 0:S]
        rtmp = A_SC[0:1, 4096:4608]
        for ts in range(NQ):
            sl = slice(ts * 512, (ts + 1) * 512)
            mk.op("v", "tensor_scalar", rtmp, rowpack(4 + ts), 1.0 / D, 1e-6,
                  op0=ALU.mult, op1=ALU.add)
            mk.op("s", "activation", rtmp, rtmp, AF.Ln)
            mk.op("s", "activation", rtmp, rtmp, AF.Exp, scale=-0.5)
            mk.op("v", "tensor_copy", rmss[:, sl], rtmp)
        for ts in range(NQ):
            sl = slice(ts * 512, (ts + 1) * 512)
            mk.mm(PSB[:, 0:512], onesF[0:1, :], rmss[:, sl], start=True, stop=True)
            for dc in range(DC):
                mk.op("v", "scalar_tensor_tensor", m[dc][:, sl], X(dc, sl),
                      n2sb[:, dc:dc + 1], PSB[:, 0:512], op0=ALU.mult, op1=ALU.mult)

        for dc in range(DC):
            mk.op("v", "tensor_scalar", gwsb[:, dc * 8:(dc + 1) * 8],
                  gwsb[:, dc * 8:(dc + 1) * 8], n2f[:, dc:dc + 1], None, op0=ALU.mult)
        for tt in range(NT):
            mk.op("t", "transpose", PSB[:, 1032:1033],
                  rmss[:, tt * 128:(tt + 1) * 128], identF[0:1, 0:1])
            mk.op("v", "tensor_copy", rmsscol[:, tt % 16:tt % 16 + 1], PSB[:, 1032:1033])
            tsl = slice(tt * 128, (tt + 1) * 128)
            for dc in range(DC):
                mk.mm(PSB[:, 1024:1032], X(dc)[:, tsl], gwsb[:, dc * 8:(dc + 1) * 8],
                      start=(dc == 0), stop=(dc == DC - 1))
            mk.op("v", "tensor_scalar", glt[:], PSB[:, 1024:1032],
                  rmsscol[:, tt % 16:tt % 16 + 1], None, op0=ALU.mult)
            mk.op("v", "reduce_max", v1[:, 0:1], glt[:], axis=AX.X)
            mk.op("v", "tensor_scalar", mask1[:], glt[:], v1[:, 0:1], None, op0=ALU.is_equal)
            mk.op("v", "scalar_tensor_tensor", tmp8[:], mask1[:], -1e30, glt[:],
                  op0=ALU.mult, op1=ALU.add)
            mk.op("v", "reduce_max", v2[:, 0:1], tmp8[:], axis=AX.X)
            mk.op("v", "tensor_sub", sg[:, 0:1], v2[:, 0:1], v1[:, 0:1])
            mk.op("s", "activation", sg[:, 0:1], sg[:, 0:1], AF.Exp)
            mk.op("v", "tensor_scalar_add", sg[:, 0:1], sg[:, 0:1], 1.0)
            mk.op("v", "reciprocal", sg[:, 0:1], sg[:, 0:1])
            mk.op("v", "tensor_scalar", og[:, 0:1], sg[:, 0:1], -1.0, 1.0,
                  op0=ALU.mult, op1=ALU.add)
            mk.op("v", "tensor_scalar", mask1[:], mask1[:], sg[:, 0:1], None, op0=ALU.mult)
            mk.op("v", "tensor_scalar", tmp8[:], tmp8[:], v2[:, 0:1], None, op0=ALU.is_equal)
            mk.op("v", "tensor_scalar", tmp8[:], tmp8[:], og[:, 0:1], None, op0=ALU.mult)
            combtile = glt
            mk.op("v", "tensor_add", combtile[:], mask1[:], tmp8[:])
            mk.op("t", "transpose", PSB[0:8, 0:128], combtile[:], identF[:])
            mk.op("v", "tensor_copy", stage8[:], PSB[0:8, 0:128])
            mk.dma("y", hcomb[:, tt * 128:(tt + 1) * 128], stage8[:])

        for dc in range(DC):
            for ts in range(NQ):
                sl = slice(ts * 512, (ts + 1) * 512)
                mk.dma("y", tmpg[:], hx1[dc * 128:(dc + 1) * 128, ts * 512:(ts + 1) * 512])
                mk.op("v", "tensor_add", X(dc, sl), X(dc, sl), tmpg[:])

        combrow = A_SC[0:1, 2048:2048 + S]
        combrow32 = A_SC[32:33, 0:S]
        brow = rowsB[0:1, 2048:4096]
        TH = S // 2
        NTH = max(1, TH // 512)
        THW = min(512, TH)

        me0 = 0
        w12s = (DC * TH, DC * TH + 2048)
        w3s = (DC * TH + 4096, DC * TH)
        for e in range(NE):
            mk.dma("g", combrow, hcomb[e:e + 1, :])
            mk.dma("g", combrow32, hcomb[e:e + 1, :])
            mk.dma("g", b1sb[:], ap["l0f_b1"][e])
            mk.dma("g", b2sb[:], ap["l0f_b2c"][e])
            drain_precast(4)
            for tbh in range(2):
                for th in range(NTH):
                    sl = slice(tbh * TH + th * THW, tbh * TH + (th + 1) * THW)
                    mk.mm(PSB[:, 1024:1024 + THW], onesF[0:1, :],
                          combrow[:, sl.start:sl.stop], start=True, stop=True)
                    mk.op("v", "tensor_copy",
                          rowsB[:, 2048 + th * THW:2048 + (th + 1) * THW],
                          PSB[:, 1024:1024 + THW])
                for hc in range(NH):
                    wsl = w12s[hc % 2]
                    w12 = A_WS[:, 0:1] if False else A_PT[:, wsl:wsl + 2048]
                    mk.dma("g", w12, ap["l0f_w12"][e, :, hc * 2048:(hc + 1) * 2048])
                    pa = PSA[:, (hc % 2) * TH:(hc % 2) * TH + TH]
                    pb = PSB[:, 0:TH]
                    for th in range(NTH):
                        sl = slice(tbh * TH + th * THW, tbh * TH + (th + 1) * THW)
                        psl = slice(th * THW, (th + 1) * THW)
                        for dc in range(DC):
                            mk.mm(pa[:, psl], w12[:, dc * 256:dc * 256 + 128],
                                  m[dc][:, sl], start=(dc == 0), stop=(dc == DC - 1))
                        for dc in range(DC):
                            mk.mm(pb[:, psl], w12[:, dc * 256 + 128:dc * 256 + 256],
                                  m[dc][:, sl], start=(dc == 0), stop=(dc == DC - 1))
                    hrow = A_VT[:, hc * TH:(hc + 1) * TH]
                    mk.op("s", "activation", hrow, pa, AF.Silu, bias=b1sb[:, hc:hc + 1])
                    mk.op("v", "scalar_tensor_tensor", hrow, pb, b2sb[:, hc:hc + 1],
                          hrow, op0=ALU.add, op1=ALU.mult)
                    for th in range(NTH):
                        psl = slice(th * THW, (th + 1) * THW)
                        mk.op("v", "tensor_mul", hrow[:, psl], hrow[:, psl],
                              rowsB[:, 2048 + th * THW:2048 + (th + 1) * THW])
                b3row = A_SC[32:33, 2048:2048 + 1024]
                mk.dma("g", b3row, ap["l0f_b3"][e:e + 1, 0:1024])
                for p2 in range(4):
                    wsl = w3s[p2 % 2]
                    w3c = A_PT[:, wsl:wsl + 4096]
                    mk.dma("g", w3c, ap["l0f_w3"][e, :, :, p2 * 256:(p2 + 1) * 256])
                    for dc2 in range(2):
                        dc = p2 * 2 + dc2
                        py = PSA[:, dc2 * TH:(dc2 + 1) * TH]
                        for th in range(NTH):
                            sl = slice(tbh * TH + th * THW, tbh * TH + (th + 1) * THW)
                            psl = slice(th * THW, (th + 1) * THW)
                            for hc in range(NH):
                                mk.mm(py[:, psl], w3c[:, hc * 256 + dc2 * 128:hc * 256 + (dc2 + 1) * 128],
                                      A_VT[:, hc * TH + th * THW:hc * TH + (th + 1) * THW],
                                      start=(hc == 0), stop=False)
                            mk.mm(py[:, psl], b3row[:, dc * 128:(dc + 1) * 128],
                                  combrow32[:, sl.start:sl.stop], start=False, stop=True)
                            mk.op("v", "tensor_add", X(dc, sl), X(dc, sl), py[:, psl])


    def part_l1a():
        NG = 8
        NW = S // 512
        mk.dma("g", rowsB[32:33, 0:4096], ap["l1a_rows"])
        qkvb = rowsB[32:33, 0:3072]
        outb = rowsB[32:33, 3072:4096]
        bqkL = ap["__bqkL__"]
        bocL = nc.alloc_sbuf_tensor("bocL", [128, 8], F32)
        mk.dma("g", bqkL[:], ap["l1a_bqk"])
        mk.dma("g", bocL[:], ap["l1a_boc"])
        padc = rowsB[64:65, 2048:2113]
        mk.op("v", "memset", padc[0:1, 0:64], -512.0)
        mk.op("v", "memset", padc[0:1, 64:65], 512.0)
        epad = rowsB[64:65, 2176:2688]

        vt_v0 = 0
        vt_q = 2080
        vt_k = 2080 + 2 * S
        attst0 = S + 3072
        attn_st = A_PT[:, attst0:attst0 + S]
        proll0 = S
        msk = A_PT[:, attst0 + S:attst0 + S + 4096]
        mk.dma("g", msk, ap["c_l1mask"])
        identF3 = identG
        maxc1 = maxcG

        def qh(h):
            return A_VT[0:65, vt_q + h * S:vt_q + (h + 1) * S]

        def kh(h):
            return A_VT[0:65, vt_k + h * S:vt_k + (h + 1) * S]

        zb = [A_NB[:, dc * S:(dc + 1) * S] for dc in range(DC)]
        for dc in range(DC):
            mk.op("v", "tensor_copy", zb[dc], X(dc))
        ln_sum_stats(zb)
        mu = A_SC[0:1, 0:S]
        rstd = A_SC[0:1, 2048:2048 + S]
        rtmp = A_SC[0:1, 4096:4608]
        for ts in range(NQ):
            sl = slice(ts * 512, (ts + 1) * 512)
            mk.op("v", "tensor_scalar_mul", mu[:, sl], rowpack(ts), 1.0 / D)
            mk.op("v", "tensor_scalar_mul", rstd[:, sl], rowpack(4 + ts), 1.0 / D)
            mk.op("v", "tensor_mul", rtmp, mu[:, sl], mu[:, sl])
            mk.op("v", "tensor_sub", rstd[:, sl], rstd[:, sl], rtmp)
            mk.op("v", "tensor_scalar_add", rstd[:, sl], rstd[:, sl], 1e-5)
            mk.op("s", "activation", rstd[:, sl], rstd[:, sl], AF.Ln)
            mk.op("s", "activation", rstd[:, sl], rstd[:, sl], AF.Exp, scale=-0.5)
            mk.op("v", "tensor_mul", mu[:, sl], mu[:, sl], rstd[:, sl])
        for ts in range(NQ):
            sl = slice(ts * 512, (ts + 1) * 512)
            mk.mm(PSB[:, 0:512], onesF[0:1, :], rstd[:, sl], start=True, stop=True)
            mk.mm(PSB[:, 512:1024], onesF[0:1, :], mu[:, sl], start=True, stop=True)
            for dc in range(DC):
                mk.op("v", "tensor_mul", tmpf, X(dc, sl), PSB[:, 0:512])
                mk.op("v", "tensor_sub", zb[dc][:, sl], tmpf, PSB[:, 512:1024])

        for g in range(NG):
            ws0 = (g % 2) * 3072
            wg = A_WS[:, ws0:ws0 + 3072]
            mk.dma("g", wg, ap["l1a_wqkv"][:, g * 3072:(g + 1) * 3072])
            for which in range(2):
                bcol = bqkL[:, which * 8 + g:which * 8 + g + 1]
                for ts in range(NQ):
                    sl = slice(ts * 512, (ts + 1) * 512)
                    for dc in range(DC):
                        mk.mm(PSA[:, sl], wg[:, dc * 384 + which * 128:dc * 384 + which * 128 + 128],
                              zb[dc][:, sl], start=(dc == 0), stop=(dc == DC - 1))
                for ts in range(NQ):
                    sl = slice(ts * 512, (ts + 1) * 512)
                    for h in range(2):
                        dst = qh(h) if which == 0 else kh(h)
                        mk.op("v", "tensor_scalar", dst[0:64, sl.start:sl.stop],
                              PSA[h * 64:(h + 1) * 64, sl],
                              bcol[h * 64:(h + 1) * 64, 0:1], None, op0=ALU.add)
            for h in range(2):
                mk.op("v", "memset", kh(h)[64:65, 0:S], -1.0)
            mk.op("v", "memset", A_VT[:, vt_v0:vt_v0 + 2080], 1.0)
            for tc in range(NT):
                for dc in range(DC):
                    mk.mm(PSB[:, 512:640], zb[dc][:, tc * 128:(tc + 1) * 128],
                          wg[:, dc * 384 + 256:dc * 384 + 384],
                          start=(dc == 0), stop=False)
                mk.mm(PSB[:, 512:640], onesB[32:33, 0:128],
                      qkvb[:, 2048 + g * 128:2048 + (g + 1) * 128],
                      start=False, stop=True)
                mk.op("v", "tensor_copy", A_VT[:, vt_v0 + tc * 130:vt_v0 + tc * 130 + 64],
                      PSB[:, 512:576])
                mk.op("v", "tensor_copy", A_VT[:, vt_v0 + tc * 130 + 65:vt_v0 + tc * 130 + 129],
                      PSB[:, 576:640])
            for h in range(2):
                for w in range(NW):
                    qsl = slice(w * 512, (w + 1) * 512)
                    kcs = [kc for kc in range(12) if 0 <= (w - 1) * 512 + kc * 128 < S]
                    has_lpad = (w == 0)
                    has_rpad = (w == NW - 1)
                    npad = (1 if has_lpad else 0) + (1 if has_rpad else 0)
                    kt0 = (w - 1) * 512 + kcs[0] * 128
                    kt1 = (w - 1) * 512 + (kcs[-1] + 1) * 128
                    if npad:
                        mk.mm(rowpack(6), onesB[0:64, 0:1], qh(h)[0:64, qsl],
                              start=True, stop=True)
                    for qt in range(4):
                        qtl = slice(w * 512 + qt * 128, w * 512 + (qt + 1) * 128)
                        nk = kt1 - kt0
                        for i, c0 in enumerate(range(kt0, kt1, 512)):
                            cw = min(512, kt1 - c0)
                            mk.mm(PSA[:, i * 512:i * 512 + cw], qh(h)[0:64, qtl],
                                  kh(h)[0:64, c0:c0 + cw], start=True, stop=True)
                        mk.op("v", "reduce_max", maxc1[:, 0:1], PSA[:, 0:nk], axis=AX.X)
                        mk.op("t", "transpose", PSB[0:1, 512:640], maxc1[:, 0:1], identF3[:])
                        mk.op("v", "tensor_copy", qh(h)[64:65, qtl.start:qtl.stop],
                              PSB[0:1, 512:640])
                    if npad:
                        mrow = A_SC[0:1, 4096:4608]
                        spadrow = A_SC[0:1, 0:512]
                        mk.op("v", "tensor_scalar_mul", spadrow, rowpack(6), -1.0)
                        mk.op("v", "tensor_copy", mrow[:, 0:512], qh(h)[64:65, qsl])
                        mk.op("v", "tensor_max", mrow[:, 0:512], mrow[:, 0:512], spadrow)
                        mk.op("v", "tensor_copy", qh(h)[64:65, qsl.start:qsl.stop], mrow[:, 0:512])
                        mk.op("v", "tensor_sub", spadrow, spadrow, mrow[:, 0:512])
                        mk.op("s", "activation", epad, spadrow, AF.Exp)
                    pvslot = PSB[0:65, 1024:1536]
                    slots6 = (PSA[:, 0:512], PSA[:, 512:1024],
                              PSA[:, 1024:1536], PSA[:, 1536:2048],
                              PSB[:, 0:512], PSB[:, 512:1024])

                    def sco(j, h=h, w=w, qsl=qsl):
                        kc = kcs[j]
                        kt = (w - 1) * 512 + kc * 128
                        pa = slots6[j % 6]
                        mk.mm(pa, kh(h)[:, kt:kt + 128], qh(h)[:, qsl],
                              start=True, stop=True)
                        roll = A_PT[:, proll0 + (j % 6) * 512:proll0 + (j % 6) * 512 + 512]
                        mk.op("s", "activation", roll, pa, AF.Exp)
                        ci = kc if kc < 4 else (kc - 4 if kc >= 8 else None)
                        if ci is not None:
                            mk.op("v", "tensor_mul", roll, roll,
                                  msk[:, ci * 512:(ci + 1) * 512])
                        return roll

                    def pv(j, roll, h=h, w=w):
                        kc = kcs[j]
                        ktc = ((w - 1) * 512 + kc * 128) // 128
                        mk.mm(pvslot,
                              A_VT[:, vt_v0 + ktc * 130 + h * 65:vt_v0 + ktc * 130 + (h + 1) * 65],
                              roll, start=(j == 0),
                              stop=(npad == 0 and j == len(kcs) - 1))

                    DEP = 5
                    rolls = {}
                    for j in range(len(kcs)):
                        rolls[j] = sco(j)
                        if j >= DEP:
                            pv(j - DEP, rolls[j - DEP])
                    for j in range(max(0, len(kcs) - DEP), len(kcs)):
                        pv(j, rolls[j])
                    for t in range(npad):
                        mk.mm(pvslot, padc[0:1, :], epad,
                              start=False, stop=(t == npad - 1))
                    mk.op("v", "tensor_copy", attn_st[h * 64:(h + 1) * 64, qsl],
                          pvslot[0:64, :])
                    mk.op("v", "tensor_copy",
                          A_SC[32:33, (h * NW + w) * 512:(h * NW + w) * 512 + 512],
                          pvslot[64:65, :])
            mk.op("s", "activation", A_SC[32:33, 0:2 * NW * 512],
                  A_SC[32:33, 0:2 * NW * 512], AF.Ln)
            mk.op("s", "activation", A_SC[32:33, 0:2 * NW * 512],
                  A_SC[32:33, 0:2 * NW * 512], AF.Exp, scale=-1.0)
            for h in range(2):
                for w in range(NW):
                    qsl = slice(w * 512, (w + 1) * 512)
                    mk.mm(PSB[:, 1024:1536], onesF[32:33, :],
                          A_SC[32:33, (h * NW + w) * 512:(h * NW + w) * 512 + 512],
                          start=True, stop=True)
                    mk.op("v", "tensor_mul", attn_st[h * 64:(h + 1) * 64, qsl],
                          attn_st[h * 64:(h + 1) * 64, qsl], PSB[0:64, 1024:1536])
            mk.dma("y", hattn[g * 128:(g + 1) * 128, :], attn_st)

        wo = A_WS[:, 0:8192]
        mk.dma("g", wo, ap["l1a_wout"])
        for dc in range(DC):
            mk.dma("y", A_PT[:, dc * S:(dc + 1) * S], hattn[dc * 128:(dc + 1) * 128, :])
        for oc in range(DC):
            for ts in range(NQ):
                sl = slice(ts * 512, (ts + 1) * 512)
                for dc in range(DC):
                    mk.mm(PSA[:, sl], wo[:, dc * 1024 + oc * 128:dc * 1024 + (oc + 1) * 128],
                          A_PT[:, dc * S + ts * 512:dc * S + (ts + 1) * 512],
                          start=(dc == 0), stop=(dc == DC - 1))
                mk.op("v", "scalar_tensor_tensor", X(oc, sl), PSA[:, sl],
                      bocL[:, oc:oc + 1], X(oc, sl), op0=ALU.add, op1=ALU.add)


    def part_l1f():
        drain_precast(64)
        NP = 2048
        MT = S // 128
        rg = [list(range(n_cores))]
        ub = [A_NB[:, dc * S:(dc + 1) * S] for dc in range(DC)]
        for dc in range(DC):
            mk.op("v", "tensor_copy", ub[dc], X(dc))
        ln_sum_stats(ub)
        mu = A_SC[0:1, 0:S]
        rstd = A_SC[0:1, 2048:2048 + S]
        rtmp = A_SC[0:1, 4096:4608]
        for ts in range(NQ):
            sl = slice(ts * 512, (ts + 1) * 512)
            mk.op("v", "tensor_scalar_mul", mu[:, sl], rowpack(ts), 1.0 / D)
            mk.op("v", "tensor_scalar_mul", rstd[:, sl], rowpack(4 + ts), 1.0 / D)
            mk.op("v", "tensor_mul", rtmp, mu[:, sl], mu[:, sl])
            mk.op("v", "tensor_sub", rstd[:, sl], rstd[:, sl], rtmp)
            mk.op("v", "tensor_scalar_add", rstd[:, sl], rstd[:, sl], 1e-5)
            mk.op("s", "activation", rstd[:, sl], rstd[:, sl], AF.Ln)
            mk.op("s", "activation", rstd[:, sl], rstd[:, sl], AF.Exp, scale=-0.5)
            mk.op("v", "tensor_mul", mu[:, sl], mu[:, sl], rstd[:, sl])
        for ts in range(NQ):
            sl = slice(ts * 512, (ts + 1) * 512)
            mk.mm(PSB[:, 0:512], onesF[0:1, :], rstd[:, sl], start=True, stop=True)
            mk.mm(PSB[:, 512:1024], onesF[0:1, :], mu[:, sl], start=True, stop=True)
            for dc in range(DC):
                mk.op("v", "tensor_mul", tmpf, X(dc, sl), PSB[:, 0:512])
                mk.op("v", "tensor_sub", ub[dc][:, sl], tmpf, PSB[:, 512:1024])

        for mt in range(MT):
            for dc in range(DC):
                slot = (dc % 2) * 128
                mk.op("t", "transpose", PST[:, slot:slot + 128],
                      ub[dc][:, mt * 128:(mt + 1) * 128], ident[:])
                mk.op("v", "tensor_copy",
                      A_VT[:, mt * 1024 + dc * 128:mt * 1024 + (dc + 1) * 128],
                      PST[:, slot:slot + 128])

        phih = A_WS[:, 0:8192]
        ar_chunk = max(512, S // 4)
        n_chunks = S // ar_chunk
        done_rows = 0
        for nph in range(2):
            mk.dma("g", phih, ap["l1f_phi"].rearrange("p (dc np) -> p dc np", np=2048)[:, :, nph * 1024:(nph + 1) * 1024])
            for mt in range(MT):
                for s5 in range(2):
                    nsl = slice(s5 * 512, (s5 + 1) * 512)
                    for dc in range(DC):
                        mk.mm(PSA[:, nsl], ub[dc][:, mt * 128:(mt + 1) * 128],
                              phih[:, dc * 1024 + s5 * 512:dc * 1024 + (s5 + 1) * 512],
                              start=(dc == 0), stop=(dc == DC - 1))
                    mk.op("s", "activation", tmpg[:], PSA[:, nsl], AF.Exp)
                    mk.dma("y", hE.ap()[mt * 128:(mt + 1) * 128,
                                        nph * 1024 + s5 * 512:nph * 1024 + (s5 + 1) * 512],
                           tmpg[:])
            if nph == 1:
                for c in range(n_chunks):
                    mk.op("g", "collective_compute", "AllReduce", ALU.add,
                          replica_groups=rg,
                          ins=[hE.ap()[c * ar_chunk:(c + 1) * ar_chunk, :].opt()],
                          outs=[hS.ap()[c * ar_chunk:(c + 1) * ar_chunk, :].opt()])

        dcache0 = 0
        for npb in range(4):
            nsl = slice(npb * 512, (npb + 1) * 512)
            for mt in range(MT):
                mk.dma("y", tmpg[:], hE.ap()[mt * 128:(mt + 1) * 128, nsl])
                mk.dma("y", tmpf[:, 0:512] if False else tmpf, hS.ap()[mt * 128:(mt + 1) * 128, nsl])
                mk.op("s", "activation", tmpf, tmpf, AF.Ln)
                mk.op("s", "activation", tmpf, tmpf, AF.Exp, scale=-1.0)
                mk.op("v", "tensor_mul", A_PT[:, dcache0 + mt * 512:dcache0 + (mt + 1) * 512],
                      tmpg[:], tmpf)
            for half in range(2):
                for dci in range(4):
                    dc = half * 4 + dci
                    for mt in range(MT):
                        mk.mm(PSA[:, dci * 512:(dci + 1) * 512],
                              A_VT[:, mt * 1024 + dc * 128:mt * 1024 + (dc + 1) * 128],
                              A_PT[:, dcache0 + mt * 512:dcache0 + (mt + 1) * 512],
                              start=(mt == 0), stop=(mt == MT - 1))
                for dci in range(4):
                    dc = half * 4 + dci
                    mk.op("v", "tensor_copy",
                          A_NB[:, dc * 2048 + npb * 512:dc * 2048 + (npb + 1) * 512],
                          PSA[:, dci * 512:(dci + 1) * 512])

        wslots = (A_WS[:, 0:8192], A_PT[:, 0:8192], A_PT[:, 8192:16384])
        for npq in range(16):
            pbase = 0 if npq % 2 == 0 else 1024
            for lane in range(4):
                n = npq * 4 + lane
                wslot = wslots[n % 3]
                mk.dma("y" if n % 2 == 0 else "g", wslot, hWbf.ap()[n])
                for eh in range(2):
                    pslot = PSA[:, pbase + eh * 512:pbase + (eh + 1) * 512]
                    for dc in range(DC):
                        mk.mm(pslot[lane * 32:lane * 32 + 32, :],
                              A_NB[:, dc * 2048 + npq * 128 + lane * 32:dc * 2048 + npq * 128 + (lane + 1) * 32],
                              wslot[:, dc * 1024 + eh * 512:dc * 1024 + (eh + 1) * 512],
                              start=(dc == 0), stop=(dc == DC - 1),
                              tile_position=(0, lane * 32), skip_group_check=True)
            for lane in range(4):
                n = npq * 4 + lane
                brep = rowsB[0:32, (lane % 2) * 1024:(lane % 2) * 1024 + 1024]
                mk.dma("g", brep, ap["l1f_brep"][npq * 128 + lane * 32:npq * 128 + (lane + 1) * 32, :])
                for eh in range(2):
                    pslot = PSA[:, pbase + eh * 512:pbase + (eh + 1) * 512]
                    mk.op("v", "tensor_add",
                          A_VT[:, npq * 1024 + eh * 512:npq * 1024 + (eh + 1) * 512][lane * 32:lane * 32 + 32, :],
                          pslot[lane * 32:lane * 32 + 32, :], brep[:, eh * 512:(eh + 1) * 512])

        est0 = 8192
        for mb in range(S // 256):
            for i in range(2):
                mrow = mb * 256 + i * 128
                mk.dma("g", A_PT[:, est0 + i * 2048:est0 + (i + 1) * 2048],
                       hE.ap()[mrow:mrow + 128, :])
                for npc in range(16):
                    slot = (npc % 2) * 128
                    mk.op("t", "transpose", PST[:, slot:slot + 128],
                          A_PT[:, est0 + i * 2048 + npc * 128:est0 + i * 2048 + (npc + 1) * 128],
                          ident[:])
                    mk.op("v", "tensor_copy",
                          A_PT[:, est0 + 4096 + npc * 256 + i * 128:est0 + 4096 + npc * 256 + (i + 1) * 128],
                          PST[:, slot:slot + 128])
            for npc in range(16):
                enp = A_PT[:, est0 + 4096 + npc * 256:est0 + 4096 + (npc + 1) * 256]
                for ec in range(DC):
                    bank, off = divmod(ec, 2)
                    mk.mm(PSA[:, bank * 512 + off * 256:bank * 512 + off * 256 + 256],
                          A_VT[:, npc * 1024 + ec * 128:npc * 1024 + (ec + 1) * 128],
                          enp, start=(npc == 0 and off == 0), stop=(npc == 15 and off == 1),
                          skip_group_check=True)
                mk.mm(rowpack(6)[:, 0:256], onesB[:, 0:1], enp,
                      start=(npc == 0), stop=(npc == 15))
            srow = A_SC[32:33, 0:256]
            mk.op("s", "activation", srow, rowpack(6)[:, 0:256], AF.Ln)
            mk.op("s", "activation", srow, srow, AF.Exp, scale=-1.0)
            mk.mm(PSB[:, 512:768], onesF[32:33, :], srow, start=True, stop=True)
            mk.op("v", "tensor_copy", tmpg[:, 0:256], PSB[:, 512:768])
            for ec in range(DC):
                bank, off = divmod(ec, 2)
                mk.op("v", "tensor_mul", tmpf[:, 0:256],
                      PSA[:, bank * 512 + off * 256:bank * 512 + off * 256 + 256],
                      tmpg[:, 0:256])
                mk.op("v", "tensor_add", X(ec)[:, mb * 256:(mb + 1) * 256],
                      X(ec)[:, mb * 256:(mb + 1) * 256], tmpf[:, 0:256])

    if "l0a" in parts:
        part_l0a()
    if "l0fa" in parts:
        part_l0fa()
    if "l0fm" in parts:
        part_l0fm("l0fa" in parts)
    if "l1a" in parts:
        part_l1a()
    if "l1f" in parts:
        part_l1f()

    for dc in range(DC):
        mk.dma("y", out_x[dc * 128:(dc + 1) * 128, :], A_X[:, dc * S:(dc + 1) * S])
    mk.op("y", "nop")
    mk.flush(verbose=True)
    return nc


_CACHE = {}


def kernel(**inputs):
    S = 2048
    n_cores = 8
    if "nc" not in _CACHE:
        _CACHE["nc"] = build_program(S=S, parts=("l0a", "l0fa", "l0fm", "l1a", "l1f"),
                                     n_cores=n_cores)
    nc = _CACHE["nc"]
    prep = {}
    prep.update(prep_l0a(inputs))
    prep.update(prep_l0f(inputs))
    prep.update(prep_l1a(inputs))
    prep.update(prep_l1f(inputs))
    prep.update(host_constants(S))
    x = np.ascontiguousarray(np.asarray(inputs["L0_attn_x"], dtype=np.float32))
    declared = {a.memorylocations[0].name for a in nc.m.functions[0].allocations
                if isinstance(a, mybir.MemoryLocationSet) and a.kind == "ExternalInput"}
    maps = []
    for c in range(n_cores):
        im = {k: v for k, v in prep.items() if k in declared}
        im["xT"] = np.ascontiguousarray(x[c].T)
        maps.append(im)
    res = run_bass_kernel_spmd(nc, maps, core_ids=list(range(n_cores)))
    out = np.stack([np.ascontiguousarray(res.results[c]["outX"].T) for c in range(n_cores)], 0)
    return out


# version 15
# speedup vs baseline: 1.3728x; 1.0409x over previous
import numpy as np
import ml_dtypes
import concourse.bass as bass
import concourse.mybir as mybir
from concourse.bass_utils import run_bass_kernel_spmd
from contextlib import ExitStack

F32 = mybir.dt.float32
BF16 = mybir.dt.bfloat16
AF = mybir.ActivationFunctionType
ALU = mybir.AluOpType
AX = mybir.AxisListType

D = 1024
DC = 8
P129 = 129


DTSZ = {F32: 4, BF16: 2, mybir.dt.float16: 2, mybir.dt.int32: 4, mybir.dt.uint8: 1}


class _Rec:
    __slots__ = ("idx", "eng", "method", "args", "kw", "mwaits", "reads", "writes",
                 "is_dma", "ticked", "sem", "val", "edges", "clock", "eidx",
                 "_needed", "_waits", "_fence")

    def __init__(self, idx, eng, method, args, kw, mwaits, is_dma):
        self.idx = idx
        self.eng = eng
        self.method = method
        self.args = args
        self.kw = kw
        self.mwaits = mwaits
        self.is_dma = is_dma
        self.ticked = False
        self.sem = None
        self.val = None
        self.edges = []
        self.clock = None
        self.eidx = 0


class _SemPool:
    def __init__(self, nc, es, name, k, step):
        self.sems = [es.enter_context(nc.semaphore(f"{name}{i}")) for i in range(k)]
        self.count = [0] * k
        self.i = 0
        self.step = step

    def take(self):
        i = self.i
        self.i = (self.i + 1) % len(self.sems)
        prev = self.count[i]
        self.count[i] += self.step
        return self.sems[i], self.count[i], prev


class MK:
    ENGS = ("t", "v", "s", "g", "y")

    def __init__(self, nc):
        self.nc = nc
        self.es = ExitStack()
        self.ops = []
        self.aliases = {}
        self.pools = {
            "t": _SemPool(nc, self.es, "pt_", 12, 1),
            "v": _SemPool(nc, self.es, "pv_", 16, 1),
            "s": _SemPool(nc, self.es, "ps_", 12, 1),
            "g": _SemPool(nc, self.es, "pg_", 6, 1),
            "y": _SemPool(nc, self.es, "py_", 4, 1),
            "gd": _SemPool(nc, self.es, "pgd_", 16, 16),
            "yd": _SemPool(nc, self.es, "pyd_", 12, 16),
        }

    def alias(self, name, shape, dtype, backing):
        v = self.nc.virtual_sbuf_tensor(name, list(shape), dtype)
        self.nc.back_virtual_tensor_with(v, backing)
        self.aliases[name] = backing.name
        return v

    def op(self, eng, method, *args, wait=(), tick=False, **kw):
        r = _Rec(len(self.ops), eng, method, args, kw,
                 tuple(t for t in wait if t is not None), False)
        self.ops.append(r)
        return r

    def dma(self, eng, out, in_, wait=()):
        r = _Rec(len(self.ops), eng, "dma_start", (), {"out": out, "in_": in_},
                 tuple(t for t in wait if t is not None), True)
        self.ops.append(r)
        return r

    def tick(self, eng, wait=()):
        return self.op(eng, "nop", wait=wait)

    def mm(self, out, lhsT, rhs, start, stop, wait=(), tick=False, **kw):
        return self.op("t", "matmul", out, lhsT, rhs, start=start, stop=stop, wait=wait, **kw)

    PSUM_NAMES = ("PSA", "PSB", "PST")

    def _rect(self, ap):
        t = ap.tensor
        name = self.aliases.get(t.name, t.name)
        sz = DTSZ[t.dtype]
        dims = list(ap.ap)
        off = ap.offset
        if len(t.shape) >= 2 and t.shape[0] <= 128 and not name.startswith("DR_"):
            pcols = t.shape[1] if len(t.shape) == 2 else int(np.prod(t.shape[1:]))
            p0 = off // pcols
            c0 = off % pcols
            p1 = p0 + (dims[0][1] if dims else 1)
            cspan = sum((cnt - 1) * st for st, cnt in dims[1:]) + 1 if len(dims) > 1 else 1
            b0, b1 = c0 * sz, (c0 + cspan) * sz
            if name in self.PSUM_NAMES:
                p0, p1 = 0, 128
                b0 = (b0 // 2048) * 2048
                b1 = ((b1 + 2047) // 2048) * 2048
            return (name, p0, p1, b0, b1)
        span = sum((cnt - 1) * st for st, cnt in dims) + 1
        return (name, 0, 1, off * sz, (off + span) * sz)

    def _is_ap(self, x):
        return hasattr(x, "tensor") and hasattr(x, "offset")

    def _io(self, r):
        writes, reads = [], []
        outs = []
        if "out" in r.kw:
            outs.append(r.kw["out"])
        if "accum_out" in r.kw and self._is_ap(r.kw.get("accum_out")):
            outs.append(r.kw["accum_out"])
        args = list(r.args)
        if args and self._is_ap(args[0]):
            outs.append(args[0])
            args = args[1:]
        for a in args:
            if self._is_ap(a):
                reads.append(a)
        for k, vv in r.kw.items():
            if k in ("out", "accum_out"):
                continue
            if k == "outs" and isinstance(vv, list):
                outs.extend(x for x in vv if self._is_ap(x))
                continue
            if isinstance(vv, list):
                reads.extend(x for x in vv if self._is_ap(x))
                continue
            if self._is_ap(vv):
                reads.append(vv)
        return [self._rect(a) for a in reads], [self._rect(a) for a in outs]

    @staticmethod
    def _buckets(rect):
        name, p0, p1, b0, b1 = rect
        for pb in range(p0 // 32, (p1 + 31) // 32):
            for cb in range(b0 // 512, (b1 + 511) // 512):
                yield (name, pb, cb)

    @staticmethod
    def _overlap(a, b):
        return a[0] == b[0] and a[1] < b[2] and b[1] < a[2] and a[3] < b[4] and b[3] < a[4]

    def flush(self, verbose=False):
        last_w = {}
        readers = {}
        eidx = {e: 0 for e in self.ENGS}
        for r in self.ops:
            r.eidx = eidx[r.eng]
            eidx[r.eng] += 1
            reads, writes = self._io(r)
            r.reads, r.writes = reads, writes
            edges = {}
            for rect in reads:
                short = (rect[1], rect[3], rect[2], rect[4])
                for bk in self._buckets(rect):
                    w = last_w.get(bk)
                    if w is not None and w[1] is not r:
                        edges[w[1].idx] = w[1]
            for rect in writes:
                for bk in self._buckets(rect):
                    w = last_w.get(bk)
                    if w is not None and w[1] is not r:
                        edges[w[1].idx] = w[1]
                    for rd in readers.get(bk, ()):
                        if rd is not r:
                            edges[rd.idx] = rd
            for rect in reads:
                for bk in self._buckets(rect):
                    readers.setdefault(bk, []).append(r)
            for rect in writes:
                for bk in self._buckets(rect):
                    last_w[bk] = (rect, r)
                    readers[bk] = []
            r.edges = [self.ops[i] for i in sorted(edges)]

        prev_on_engine = {e: None for e in self.ENGS}
        for r in self.ops:
            ck = {}
            pred = prev_on_engine[r.eng]
            if pred is not None:
                ck.update(pred.clock)
                if r.eng == "t":
                    ck["T"] = max(ck.get("T", -1), pred.eidx)
            needed = []
            for p in r.edges + [w if isinstance(w, _Rec) else None for w in r.mwaits]:
                if p is None:
                    continue
                if p.eng == "t" and r.eng == "t" and not p.is_dma:
                    if ck.get("T", -1) < p.eidx:
                        ck["T"] = max(ck.get("T", -1), p.eidx)
                        for k, vv in p.clock.items():
                            if ck.get(k, -1 if k == "T" else 0) < vv:
                                ck[k] = vv
                    continue
                needed.append(p)
            for p in needed:
                p.ticked = True
            r.clock = ck
            r._needed = needed
            prev_on_engine[r.eng] = r

        for r in self.ops:
            if r.ticked or r.is_dma:
                pool = self.pools[r.eng + "d"] if r.is_dma else self.pools[r.eng]
                r.sem, r.val, r._fence = pool.take()
        prev_on_engine = {e: None for e in self.ENGS}
        nwaits = 0
        for r in self.ops:
            ck = {}
            pred = prev_on_engine[r.eng]
            if pred is not None:
                ck.update(pred.clock)
                if r.eng == "t":
                    ck["T"] = max(ck.get("T", -1), pred.eidx)
            waits = []
            for p in r._needed:
                if p.eng == "t" and r.eng == "t" and not p.is_dma:
                    continue
                if p.sem is None:
                    raise RuntimeError(f"producer without sem: {p.method}")
                key = id(p.sem)
                if ck.get(key, 0) >= p.val:
                    continue
                waits.append((p.sem, p.val))
                ck[key] = p.val
                for k, vv in p.clock.items():
                    if k == "T":
                        ck["T"] = max(ck.get("T", -1), vv)
                    elif ck.get(k, 0) < vv:
                        ck[k] = vv
                if p.eng == "t" and not p.is_dma:
                    ck["T"] = max(ck.get("T", -1), p.eidx)
            r._waits = waits
            nwaits += len(waits)
            r.clock = ck
            prev_on_engine[r.eng] = r
        if verbose:
            nt = sum(1 for r in self.ops if r.ticked)
            print(f"[mk] ops={len(self.ops)} ticked={nt} waits={nwaits}")

        eng_ops = {e: [r for r in self.ops if r.eng == e] for e in self.ENGS}

        with self.nc.Block() as block:
            def mk_body(name):
                def body(eng):
                    seen = {}
                    for r in eng_ops[name]:
                        if r.sem is not None and r._fence > 0 and seen.get(id(r.sem), 0) < r._fence:
                            eng.wait_ge(r.sem, r._fence)
                            seen[id(r.sem)] = r._fence
                        for (sem, val) in r._waits:
                            if seen.get(id(sem), 0) < val:
                                eng.wait_ge(sem, val)
                                seen[id(sem)] = val
                        ins = getattr(eng, r.method)(*r.args, **r.kw)
                        if r.sem is not None:
                            ins.then_inc(r.sem, 16 if r.is_dma else 1)
                return body
            block.tensor(mk_body("t"))
            block.vector(mk_body("v"))
            block.scalar(mk_body("s"))
            block.gpsimd(mk_body("g"))
            block.sync(mk_body("y"))


def host_constants(S):
    c = {"c_ident": np.eye(128, dtype=np.float32).astype(ml_dtypes.bfloat16),
         "c_identF": np.eye(128, dtype=np.float32)}
    inv_freq = 1.0 / 10000.0 ** (np.arange(0, 64, 2, dtype=np.float32) / 64.0)
    t = np.arange(S, dtype=np.float32)
    freqs = inv_freq[:, None] * t[None, :]
    emb = np.concatenate([freqs, freqs], 0)
    cos = np.cos(emb)
    sin = np.sin(emb)
    c["c_cos"] = np.concatenate([cos, cos], 0).astype(ml_dtypes.bfloat16)
    c["c_sin"] = np.concatenate([sin, sin], 0).astype(ml_dtypes.bfloat16)
    perm = np.zeros((128, 128), dtype=np.float32)
    for blk in (0, 64):
        for d_ in range(32):
            perm[blk + d_, blk + d_ + 32] = -1.0
            perm[blk + 32 + d_, blk + d_] = 1.0
    c["c_permT"] = perm.T.copy().astype(ml_dtypes.bfloat16)
    msk = np.zeros((128, 8 * 512), dtype=ml_dtypes.bfloat16)
    for ci in range(8):
        kc = ci if ci < 4 else ci + 4
        for p in range(128):
            kj = kc * 128 + p
            qi = np.arange(512)
            msk[p, ci * 512:(ci + 1) * 512] = ((kj >= qi) & (kj <= qi + 1024)).astype(np.float32)
    c["c_l1mask"] = msk
    return c


def wimg(wT):
    Din, O = wT.shape
    return np.ascontiguousarray(wT.reshape(Din // 128, 128, O).transpose(1, 0, 2)
                                .reshape(128, (Din // 128) * O))


def prep_l0a(inputs):
    d = {}
    d["l0a_wq"] = wimg(np.ascontiguousarray(inputs["L0_attn_w_q"].T))
    d["l0a_wk"] = wimg(np.ascontiguousarray(inputs["L0_attn_w_k"].T))
    d["l0a_wv"] = wimg(np.ascontiguousarray(inputs["L0_attn_w_v"].T))
    r32 = np.zeros((1, 1360), dtype=np.float32)
    r32[0, 0:129] = inputs["L0_attn_b_q"]
    r32[0, 160:289] = inputs["L0_attn_b_k"]
    r32[0, 320:1344] = inputs["L0_attn_b_v"]
    d["l0a_rows32"] = r32
    d["l0a_k0x"] = np.ascontiguousarray(inputs["L0_attn_k_0"][128:129][None, :])
    d["l0a_rows64"] = np.ascontiguousarray(inputs["L0_attn_v_0"][None, :])
    d["l0a_k0m"] = np.ascontiguousarray(inputs["L0_attn_k_0"][0:128, None])
    d["l0a_bcols"] = np.stack([inputs["L0_attn_b_q"][0:128], inputs["L0_attn_b_k"][0:128]], 1)
    return d




def prep_l0f(inputs):
    d = {}
    d["l0f_wqkv"] = qkv_group_img(np.ascontiguousarray(inputs["L0_ffn_qkv_w"].T))
    d["l0f_wout"] = wimg(np.ascontiguousarray(inputs["L0_ffn_out_w"].T))
    rows = np.zeros((1, 4096), dtype=np.float32)
    rows[0, 0:3072] = inputs["L0_ffn_qkv_b"]
    rows[0, 3072:4096] = inputs["L0_ffn_out_b"]
    d["l0f_rows"] = rows
    d["l0f_n1"] = np.ascontiguousarray(inputs["L0_ffn_norm1_w"].reshape(8, 128).T)
    d["l0f_n2"] = np.ascontiguousarray(inputs["L0_ffn_norm2_w"].reshape(8, 128).T)
    d["l0f_gw"] = wimg(np.ascontiguousarray(inputs["L0_ffn_gate_w"].T))
    w1 = inputs["L0_ffn_w1"]; w2 = inputs["L0_ffn_w2"]; w3 = inputs["L0_ffn_w3"]
    w1r = w1.reshape(8, 16, 128, 8, 128).transpose(0, 4, 1, 3, 2)
    w1r = w1.reshape(8, 16, 128, 8, 128).transpose(0, 4, 1, 3, 2)
    w2r = w2.reshape(8, 16, 128, 8, 128).transpose(0, 4, 1, 3, 2)
    img = np.stack([w1r, w2r], axis=4)
    d["l0f_w12"] = np.ascontiguousarray(img.reshape(8, 128, 16 * 2048))
    d["l0f_w3"] = np.ascontiguousarray(w3.reshape(8, 1024, 16, 128).transpose(0, 3, 2, 1))
    d["l0f_b1"] = np.ascontiguousarray(inputs["L0_ffn_b1"].reshape(8, 16, 128).transpose(0, 2, 1))
    d["l0f_b2"] = np.ascontiguousarray(inputs["L0_ffn_b2"])
    d["l0f_b3"] = np.ascontiguousarray(inputs["L0_ffn_b3"])
    d["l0f_b2c"] = np.ascontiguousarray(inputs["L0_ffn_b2"].reshape(8, 16, 128).transpose(0, 2, 1))
    qb = inputs["L0_ffn_qkv_b"]
    d["l0f_bqk"] = np.ascontiguousarray(np.stack(
        [qb[g * 128:(g + 1) * 128] for g in range(8)] +
        [qb[1024 + g * 128:1024 + (g + 1) * 128] for g in range(8)], 1))
    d["l0f_boc"] = np.ascontiguousarray(inputs["L0_ffn_out_b"].reshape(8, 128).T)
    return d

def qkv_group_img(qkvT):
    img = np.empty((128, 8 * 3072), dtype=np.float32)
    for g in range(8):
        for dc in range(8):
            blk = img[:, g * 3072 + dc * 384:g * 3072 + (dc + 1) * 384]
            rows = slice(dc * 128, (dc + 1) * 128)
            for which in range(3):
                blk[:, which * 128:(which + 1) * 128] = \
                    qkvT[rows, which * 1024 + 128 * g:which * 1024 + 128 * (g + 1)]
    return img


def prep_l1a(inputs):
    d = {}
    qkvT = np.concatenate([np.ascontiguousarray(inputs["L1_attn_wq"].T),
                           np.ascontiguousarray(inputs["L1_attn_wk"].T),
                           np.ascontiguousarray(inputs["L1_attn_wv"].T)], axis=1)
    d["l1a_wqkv"] = qkv_group_img(qkvT)
    d["l1a_wout"] = wimg(np.ascontiguousarray(inputs["L1_attn_wo"].T))
    rows = np.zeros((1, 4096), dtype=np.float32)
    rows[0, 0:1024] = inputs["L1_attn_bq"]
    rows[0, 1024:2048] = inputs["L1_attn_bk"]
    rows[0, 2048:3072] = inputs["L1_attn_bv"]
    rows[0, 3072:4096] = inputs["L1_attn_bo"]
    d["l1a_rows"] = rows
    d["l1a_bqk"] = np.ascontiguousarray(np.stack(
        [inputs["L1_attn_bq"][g * 128:(g + 1) * 128] for g in range(8)] +
        [inputs["L1_attn_bk"][g * 128:(g + 1) * 128] for g in range(8)], 1))
    d["l1a_boc"] = np.ascontiguousarray(inputs["L1_attn_bo"].reshape(8, 128).T)
    return d


def prep_l1f(inputs):
    d = {}
    phi = inputs["L1_ffn_phi"].reshape(1024, 2048)
    d["l1f_phi"] = wimg(np.ascontiguousarray(phi))
    W = inputs["L1_ffn_W"]
    d["l1f_W"] = np.ascontiguousarray(
        W.reshape(64, 8, 128, 1024).transpose(0, 2, 1, 3).reshape(64, 128, 8192))
    d["l1f_brep"] = np.ascontiguousarray(np.repeat(inputs["L1_ffn_b"], 32, axis=0))
    return d


def build_program(S=2048, parts=("l0a",), n_cores=8):
    nc = bass.Bass("TRN2", target_bir_lowering=False, debug=False)
    mk = MK(nc)
    NT = S // 128
    NQ = S // 512

    ap = {}

    def inp(name, shape, dt=F32):
        ap[name] = nc.dram_tensor(name, list(shape), dt, kind="ExternalInput").ap()
        return ap[name]

    xT = inp("xT", [D, S])
    inp("c_ident", [128, 128], BF16)
    inp("c_identF", [128, 128])
    if "l0a" in parts:
        inp("l0a_wq", [128, 8 * P129]); inp("l0a_wk", [128, 8 * P129])
        inp("l0a_wv", [128, 8 * 1024])
        inp("l0a_rows32", [1, 1360]); inp("l0a_rows64", [1, 1024]); inp("l0a_k0x", [1, 1])
        inp("l0a_k0m", [128, 1]); inp("l0a_bcols", [128, 2])
    if "l0fa" in parts:
        inp("l0f_wqkv", [128, 8 * 3072]); inp("l0f_wout", [128, 8 * 1024])
        inp("l0f_rows", [1, 4096]); inp("l0f_n1", [128, 8]); inp("l0f_n2", [128, 8])
        inp("l0f_bqk", [128, 16]); inp("l0f_boc", [128, 8])
        inp("c_cos", [128, S], BF16); inp("c_sin", [128, S], BF16)
        inp("c_permT", [128, 128], BF16)
    if "l0fm" in parts:
        inp("l0f_gw", [128, 64])
        inp("l0f_w12", [8, 128, 16 * 2048])
        inp("l0f_w3", [8, 128, 16, 1024])
        inp("l0f_b1", [8, 128, 16]); inp("l0f_b2c", [8, 128, 16]); inp("l0f_b3", [8, 1024])
        if "l0fa" not in parts:
            inp("tx1", [D, S])
        if "l0fa" not in parts or True:
            pass
    if "l0fm" in parts and "l0fa" not in parts:
        inp("l0f_n2", [128, 8])
    if "l1a" in parts:
        inp("l1a_wqkv", [128, 8 * 3072]); inp("l1a_wout", [128, 8 * 1024])
        inp("l1a_rows", [1, 4096]); inp("c_l1mask", [128, 8 * 512], BF16)
        inp("l1a_bqk", [128, 16]); inp("l1a_boc", [128, 8])
    if "l1f" in parts:
        inp("l1f_phi", [128, 8 * 2048])
        inp("l1f_W", [64, 128, 8192])
        inp("l1f_brep", [2048, 1024])
    out_x = nc.dram_tensor("outX", [D, S], F32, kind="ExternalOutput").ap()
    if "l1f" in parts:
        hE = nc.dram_tensor("hE", [S, 2048], F32, kind="Internal")
        hS = nc.dram_tensor("hS", [S, 2048], F32, kind="Internal")
        hWbf = nc.dram_tensor("hWbf", [64, 128, 8192], BF16, kind="Internal")
    hx1 = nc.dram_tensor("hx1", [D, S], F32, kind="Internal").ap()
    hcomb = nc.dram_tensor("hcomb", [8, S], F32, kind="Internal").ap()
    hattn = nc.dram_tensor("hattn", [D, S], BF16, kind="Internal").ap()

    A_X = nc.alloc_sbuf_tensor("A_X", [128, DC * S], F32)
    A_NB = nc.alloc_sbuf_tensor("A_NB", [128, max(DC * S, DC * 2048)], BF16)
    A_VT = nc.alloc_sbuf_tensor("A_VT", [128, max(DC * S, 2080 + 4 * S, 16384)], BF16)
    PTCOLS = max((S // 128 + 1) * 512 + (S + 128) + 2 * S, DC * S, DC * (S // 2) + 8192, 16384)
    A_PT = nc.alloc_sbuf_tensor("A_PT", [128, PTCOLS], BF16)
    A_WS = nc.alloc_sbuf_tensor("A_WS", [128, 8192], BF16)
    A_SC = nc.alloc_sbuf_tensor("A_SC", [128, 4608], F32)
    rowsB = nc.alloc_sbuf_tensor("rowsB", [128, 4100], BF16)
    onesB = nc.alloc_sbuf_tensor("onesB", [128, 512], BF16)
    onesF = nc.alloc_sbuf_tensor("onesF", [128, 128], F32)
    ident = nc.alloc_sbuf_tensor("ident", [128, 128], BF16)
    identG = nc.alloc_sbuf_tensor("identG", [128, 128], F32)
    ap["__bqkL__"] = nc.alloc_sbuf_tensor("bqkL", [128, 16], F32)
    maxcG = nc.alloc_sbuf_tensor("maxcG", [128, 1], F32)
    k0m = nc.alloc_sbuf_tensor("k0m", [128, 1], BF16)

    tmpf = A_SC[:, 4096:4608]
    negm = maxcG
    tmpg = nc.alloc_sbuf_tensor("tmpg", [128, 512], F32)
    PSA = nc.alloc_psum_tensor("PSA", [128, 2048], F32)
    PSB = nc.alloc_psum_tensor("PSB", [128, 1536], F32)
    PST = nc.alloc_psum_tensor("PST", [128, 1024], BF16)

    mk.op("v", "memset", onesB[:], 1.0)
    mk.op("v", "memset", onesF[:], 1.0)
    mk.dma("y", identG[:], ap["c_identF"])
    mk.dma("y", ident[:], ap["c_ident"])
    for dc in range(DC):
        mk.dma("y", A_X[:, dc * S:(dc + 1) * S], xT[dc * 128:(dc + 1) * 128, :])

    _precast = []
    if "l1f" in parts:
        _precast = [n for n in range(64)]

    def drain_precast(k):
        for _ in range(k):
            if _precast:
                n = _precast.pop(0)
                mk.dma("g", hWbf.ap()[n], ap["l1f_W"][n])

    def rowpack(idx):
        r, b = divmod(idx, 3)
        return PSB[32 * b:32 * b + 1, r * 512:(r + 1) * 512]

    def X(dc, sl=None):
        base = dc * S
        return A_X[:, base:base + S] if sl is None else A_X[:, base + sl.start:base + sl.stop]

    def ln_sum_stats(src_chunks, want_mean=True):
        sq_tile = A_PT[:, 0:S]
        for dc in range(DC):
            mk.op("s", "activation", sq_tile, src_chunks[dc], AF.Square)
            for ts in range(NQ):
                sl = slice(ts * 512, (ts + 1) * 512)
                if want_mean:
                    mk.mm(rowpack(ts), onesB[:, 0:1], src_chunks[dc][:, sl],
                          start=(dc == 0), stop=(dc == DC - 1))
                mk.mm(rowpack(4 + ts), onesB[:, 0:1], sq_tile[:, sl],
                      start=(dc == 0), stop=(dc == DC - 1))

    def part_l0a():
        wq = A_WS[:, 0:8 * P129]
        wk = A_WS[:, 8 * P129:16 * P129]
        wvh = A_WS[:, 16 * P129:16 * P129 + 8 * 512]
        mk.dma("g", wq, ap["l0a_wq"])
        mk.dma("g", wk, ap["l0a_wk"])
        rows32 = rowsB[32:33, 0:1360]
        mk.dma("g", rows32, ap["l0a_rows32"])
        rows64 = rowsB[64:65, 0:1024]
        mk.dma("g", rows64, ap["l0a_rows64"])
        mk.dma("g", k0m[:], ap["l0a_k0m"])
        bq = rowsB[32:33, 0:129]; bk = rowsB[32:33, 160:289]
        bv = rowsB[32:33, 320:1344]
        k0x = rowsB[0:1, 4096:4097]
        mk.dma("g", k0x, ap["l0a_k0x"])
        bcolA = nc.alloc_sbuf_tensor("bcolA", [128, 2], F32)
        mk.dma("g", bcolA[:], ap["l0a_bcols"])
        v0r = rowsB[64:65, 0:1024]
        qxr = rowsB[0:1, 0:S]
        kxr = rowsB[0:1, 2048:2048 + S]
        ptx = rowsB[64:65, 1024:1024 + 512]

        pp0 = 0
        preP0 = pp0 + (NT + 1) * 512
        qf0 = preP0 + S + 128
        kf0 = qf0 + S
        assert kf0 + S <= PTCOLS
        preP = A_PT[:, preP0:preP0 + S + 128]
        qf = A_PT[:, qf0:qf0 + S]
        kf = A_PT[:, kf0:kf0 + S]

        def PTt(kc, a, b):
            return A_PT[:, pp0 + kc * 512 + a:pp0 + kc * 512 + b]

        xb = [A_NB[:, dc * S:(dc + 1) * S] for dc in range(DC)]
        for dc in range(DC):
            mk.op("v", "tensor_copy", xb[dc], X(dc))

        sq_tile = A_PT[:, pp0:pp0 + S]
        for dc in range(DC):
            mk.op("s", "activation", sq_tile, xb[dc], AF.Square)
            for ts in range(NQ):
                sl = slice(ts * 512, (ts + 1) * 512)
                mk.mm(rowpack(ts), onesB[:, 0:1], xb[dc][:, sl],
                      start=(dc == 0), stop=(dc == DC - 1))
                mk.mm(rowpack(4 + ts), onesB[:, 0:1], sq_tile[:, sl],
                      start=(dc == 0), stop=(dc == DC - 1))
        mu = A_SC[0:1, 0:S]; rstd = A_SC[0:1, 2048:2048 + S]; rtmp = A_SC[0:1, 4096:4608]
        for ts in range(NQ):
            sl = slice(ts * 512, (ts + 1) * 512)
            mk.op("v", "tensor_scalar_mul", mu[:, sl], rowpack(ts), 1.0 / D)
            mk.op("v", "tensor_scalar_mul", rstd[:, sl], rowpack(4 + ts), 1.0 / D)
        for ts in range(NQ):
            sl = slice(ts * 512, (ts + 1) * 512)
            mk.op("v", "tensor_mul", rtmp, mu[:, sl], mu[:, sl])
            mk.op("v", "tensor_sub", rstd[:, sl], rstd[:, sl], rtmp)
            mk.op("v", "tensor_scalar_add", rstd[:, sl], rstd[:, sl], 1e-5)
            mk.op("s", "activation", rstd[:, sl], rstd[:, sl], AF.Ln)
            mk.op("s", "activation", rstd[:, sl], rstd[:, sl], AF.Exp, scale=-0.5)
            mk.op("v", "tensor_mul", mu[:, sl], mu[:, sl], rstd[:, sl])

        for ts in range(NQ):
            sl = slice(ts * 512, (ts + 1) * 512)
            mk.mm(PSB[:, 0:512], onesF[0:1, :], rstd[:, sl], start=True, stop=True)
            mk.mm(PSB[:, 512:1024], onesF[0:1, :], mu[:, sl], start=True, stop=True)
            for dc in range(DC):
                mk.op("v", "tensor_mul", tmpf[:], X(dc, sl), PSB[:, 0:512])
                mk.op("v", "tensor_sub", xb[dc][:, sl], tmpf[:], PSB[:, 512:1024])

        for wi, (wT, brow_x, dstf, dstx) in enumerate((
                (wq, rowsB[32:33, 128:129], qf, qxr),
                (wk, rowsB[32:33, 288:289], kf, kxr))):
            for ts in range(NQ):
                sl = slice(ts * 512, (ts + 1) * 512)
                for dc in range(DC):
                    mk.mm(PSA[:, sl], wT[:, dc * P129:dc * P129 + 128], xb[dc][:, sl],
                          start=(dc == 0), stop=(dc == DC - 1))
                for dc in range(DC):
                    mk.mm(rowpack(ts), wT[:, dc * P129 + 128:dc * P129 + 129],
                          xb[dc][:, sl], start=(dc == 0), stop=False)
                mk.mm(rowpack(ts), brow_x, onesB[32:33, 0:512], start=False, stop=True)
            mk.op("v", "tensor_scalar", dstf, PSA[:, 0:S], bcolA[:, wi:wi + 1], None,
                  op0=ALU.add)
            for ts in range(NQ):
                mk.op("v", "tensor_copy", dstx[:, ts * 512:(ts + 1) * 512], rowpack(ts))

        wv_img = ap["l0a_wv"]
        for oh in range(2):
            mk.dma("g", wvh, wv_img.rearrange("p (dc o) -> p dc o", o=1024)[:, :, oh * 512:(oh + 1) * 512])
            if oh == 1:
                drain_precast(24)
            for tc in range(NT):
                tsl = slice(tc * 128, (tc + 1) * 128)
                osl = slice(oh * 512, (oh + 1) * 512)
                for dc in range(DC):
                    mk.mm(PSB[:, osl], xb[dc][:, tsl],
                          wvh[:, dc * 512:(dc + 1) * 512],
                          start=(dc == 0), stop=False)
                mk.mm(PSB[:, osl], onesB[32:33, 0:128], bv[:, oh * 512:(oh + 1) * 512],
                      start=False, stop=True)
                mk.op("v", "tensor_copy", A_VT[:, tc * 1024 + oh * 512:tc * 1024 + (oh + 1) * 512],
                      PSB[:, osl])

        for qb in range(NQ):
            qsl = slice(qb * 512, (qb + 1) * 512)
            for qt in range(4):
                qtile = slice(qb * 512 + qt * 128, qb * 512 + (qt + 1) * 128)
                for kt in range(NQ):
                    ksl = slice(kt * 512, (kt + 1) * 512)
                    mk.mm(PSA[:, ksl], qf[:, qtile], kf[:, ksl], start=True, stop=False)
                    mk.mm(PSA[:, ksl], qxr[:, qtile], kxr[:, ksl], start=False, stop=True)
                mk.mm(PSB[:, 1024:1025], qf[:, qtile], k0m[:, 0:1], start=True, stop=False)
                mk.mm(PSB[:, 1024:1025], qxr[:, qtile], k0x[:, 0:1], start=False, stop=True)
                mk.op("v", "reduce_max", negm[:, 0:1], PSA[:, 0:S], axis=AX.X)
                mk.op("v", "tensor_max", negm[:, 0:1], negm[:, 0:1], PSB[:, 1024:1025])
                mk.op("v", "tensor_scalar_mul", negm[:, 0:1], negm[:, 0:1], -1.0)
                mk.op("s", "activation", preP[:, 0:S], PSA[:, 0:S], AF.Exp,
                      bias=negm[:, 0:1])
                mk.op("s", "activation", preP[:, S:S + 1], PSB[:, 1024:1025], AF.Exp,
                      bias=negm[:, 0:1])
                for kc in range(NT):
                    slot = (kc % 2) * 128
                    mk.op("t", "transpose", PST[:, slot:slot + 128],
                          preP[:, kc * 128:(kc + 1) * 128], ident[:])
                    mk.op("v", "tensor_copy", PTt(kc, qt * 128, (qt + 1) * 128),
                          PST[:, slot:slot + 128])
                mk.op("t", "transpose", PST[0:1, 256:384], preP[:, S:S + 1], ident[:])
                mk.op("v", "tensor_copy", ptx[:, qt * 128:(qt + 1) * 128],
                      PST[0:1, 256:384])
            import os
            if os.environ.get("MK_STOP") == "evac":
                return
            for kc in range(NT):
                mk.mm(PSB[0:1, 0:512], onesB[:, 0:1], PTt(kc, 0, 512),
                      start=(kc == 0), stop=False)
            mk.mm(PSB[0:1, 0:512], onesB[64:65, 0:1], ptx[:, 0:512], start=False, stop=True)
            sums_r = A_SC[32:33, 0:512]
            mk.op("s", "activation", sums_r, PSB[0:1, 0:512], AF.Ln)
            mk.op("s", "activation", sums_r, sums_r, AF.Exp, scale=-1.0)
            mk.mm(PSB[:, 0:512], onesF[32:33, :], sums_r, start=True, stop=True)
            mk.op("v", "tensor_copy", tmpg[:], PSB[:, 0:512])
            mk.mm(PSB[:, 512:1024], onesF[0:1, :], rstd[:, qsl], start=True, stop=True)
            mk.mm(PSB[:, 1024:1536], onesF[0:1, :], mu[:, qsl], start=True, stop=True)
            if os.environ.get("MK_STOP") == "sums":
                return
            for half in range(2):
                for dci in range(4):
                    dc = half * 4 + dci
                    csl = slice(dci * 512, (dci + 1) * 512)
                    for kc in range(NT):
                        mk.mm(PSA[:, csl],
                              A_VT[:, kc * 1024 + dc * 128:kc * 1024 + (dc + 1) * 128],
                              PTt(kc, 0, 512),
                              start=(kc == 0), stop=False)
                    mk.mm(PSA[:, csl], v0r[:, dc * 128:(dc + 1) * 128], ptx[:, 0:512],
                          start=False, stop=True)
                for dci in range(4):
                    dc = half * 4 + dci
                    csl = slice(dci * 512, (dci + 1) * 512)
                    mk.op("v", "tensor_mul", tmpf[:], X(dc, qsl), PSB[:, 512:1024])
                    mk.op("v", "tensor_sub", tmpf[:], tmpf[:], PSB[:, 1024:1536])
                    mk.op("v", "tensor_add", X(dc, qsl), X(dc, qsl), tmpf[:])
                    mk.op("v", "tensor_mul", tmpf[:], PSA[:, csl], tmpg[:])
                    mk.op("v", "tensor_add", X(dc, qsl), X(dc, qsl), tmpf[:])


    def part_l0fa():
        NG = 8
        mk.dma("g", rowsB[32:33, 0:4096], ap["l0f_rows"])
        qkvb = rowsB[32:33, 0:3072]
        outb = rowsB[32:33, 3072:4096]
        n1sb = nc.alloc_sbuf_tensor("n1sb", [128, 8], BF16)
        n2sb = nc.alloc_sbuf_tensor("n2sb", [128, 8], BF16)
        mk.dma("g", n1sb[:], ap["l0f_n1"])
        mk.dma("g", n2sb[:], ap["l0f_n2"])
        ap["__n2sb__"] = n2sb
        bqk = nc.alloc_sbuf_tensor("bqk", [128, 16], F32)
        boc = nc.alloc_sbuf_tensor("boc", [128, 8], F32)
        ap["__bqk__"] = bqk
        mk.dma("g", bqk[:], ap["l0f_bqk"])
        mk.dma("g", boc[:], ap["l0f_boc"])
        vt_q = 0
        vt_k = 2 * S
        vt_v0 = 4 * S
        vt_cos = 4 * S + 2080
        vt_sin = 5 * S + 2080
        cosT = A_VT[:, vt_cos:vt_cos + S]
        sinT = A_VT[:, vt_sin:vt_sin + S]
        mk.dma("y", cosT, ap["c_cos"])
        mk.dma("y", sinT, ap["c_sin"])
        perm = nc.alloc_sbuf_tensor("perm", [128, 128], BF16)
        mk.dma("y", perm[:], ap["c_permT"])
        identF2 = identG

        def qh(h):
            return A_VT[0:65, vt_q + h * S:vt_q + (h + 1) * S]

        def kh(h):
            return A_VT[0:65, vt_k + h * S:vt_k + (h + 1) * S]
        bbuf = A_PT[:, 0:S]
        proll0 = S
        attst0 = S + 3072
        attn_st = A_PT[:, attst0:attst0 + S]

        for dc in range(DC):
            mk.dma("y", hx1[dc * 128:(dc + 1) * 128, :], X(dc))

        x1b = [A_NB[:, dc * S:(dc + 1) * S] for dc in range(DC)]
        for dc in range(DC):
            mk.op("v", "tensor_copy", x1b[dc], X(dc))
        ln_sum_stats(x1b)
        mu = A_SC[0:1, 0:S]
        rstd = A_SC[0:1, 2048:2048 + S]
        rtmp = A_SC[0:1, 4096:4608]
        rmss = A_SC[32:33, 512:512 + S]
        for ts in range(NQ):
            sl = slice(ts * 512, (ts + 1) * 512)
            mk.op("v", "tensor_scalar_mul", mu[:, sl], rowpack(ts), 1.0 / D)
            mk.op("v", "tensor_scalar_mul", rstd[:, sl], rowpack(4 + ts), 1.0 / D)
            mk.op("v", "tensor_mul", rtmp, mu[:, sl], mu[:, sl])
            mk.op("v", "tensor_sub", rstd[:, sl], rstd[:, sl], rtmp)
            mk.op("v", "tensor_scalar_add", rstd[:, sl], rstd[:, sl], 1e-5)
            mk.op("s", "activation", rstd[:, sl], rstd[:, sl], AF.Ln)
            mk.op("s", "activation", rstd[:, sl], rstd[:, sl], AF.Exp, scale=-0.5)
            mk.op("v", "tensor_mul", mu[:, sl], mu[:, sl], rstd[:, sl])
            mk.op("v", "tensor_mul", rtmp, rstd[:, sl], rstd[:, sl])
            mk.op("v", "tensor_scalar", rtmp, rtmp, -1e-5, 1.0 + 1e-6,
                  op0=ALU.mult, op1=ALU.add)
            mk.op("s", "activation", rtmp, rtmp, AF.Ln)
            mk.op("s", "activation", rtmp, rtmp, AF.Exp, scale=-0.5)
            mk.op("v", "tensor_copy", rmss[:, sl], rtmp)

        for ts in range(NQ):
            sl = slice(ts * 512, (ts + 1) * 512)
            mk.mm(PSB[:, 0:512], onesF[0:1, :], rstd[:, sl], start=True, stop=True)
            mk.mm(PSB[:, 512:1024], onesF[0:1, :], mu[:, sl], start=True, stop=True)
            mk.mm(PSB[:, 1024:1536], onesF[32:33, :], rmss[:, sl], start=True, stop=True)
            for dc in range(DC):
                mk.op("v", "tensor_mul", tmpf, X(dc, sl), PSB[:, 0:512])
                mk.op("v", "tensor_sub", X(dc, sl), tmpf, PSB[:, 512:1024])
                mk.op("v", "scalar_tensor_tensor", x1b[dc][:, sl], X(dc, sl),
                      n1sb[:, dc:dc + 1], PSB[:, 1024:1536],
                      op0=ALU.mult, op1=ALU.mult)
        a = x1b

        for g in range(NG):
            ws0 = (g % 2) * 3072
            wg = A_WS[:, ws0:ws0 + 3072]
            if g == 0:
                mk.dma("g", wg, ap["l0f_wqkv"][:, 0:3072])
            if g + 1 < NG:
                mk.dma("g", A_WS[:, ((g + 1) % 2) * 3072:((g + 1) % 2) * 3072 + 3072],
                       ap["l0f_wqkv"][:, (g + 1) * 3072:(g + 2) * 3072])
            drain_precast(5)
            for which in range(2):
                bcol = bqk[:, which * 8 + g:which * 8 + g + 1]
                for ts in range(NQ):
                    sl = slice(ts * 512, (ts + 1) * 512)
                    for dc in range(DC):
                        mk.mm(PSA[:, sl], wg[:, dc * 384 + which * 128:dc * 384 + which * 128 + 128],
                              a[dc][:, sl], start=(dc == 0), stop=(dc == DC - 1))
                for ts in range(NQ):
                    sl = slice(ts * 512, (ts + 1) * 512)
                    mk.op("v", "scalar_tensor_tensor", bbuf[:, sl], PSA[:, sl], bcol,
                          sinT[:, sl], op0=ALU.add, op1=ALU.mult)
                    mk.op("v", "scalar_tensor_tensor", tmpf, PSA[:, sl], bcol,
                          cosT[:, sl], op0=ALU.add, op1=ALU.mult)
                    mk.mm(PSB[:, 0:512], perm[:], bbuf[:, sl], start=True, stop=True)
                    for h in range(2):
                        dst = qh(h) if which == 0 else kh(h)
                        mk.op("v", "tensor_add", dst[0:64, sl.start:sl.stop],
                              tmpf[h * 64:(h + 1) * 64, :],
                              PSB[h * 64:(h + 1) * 64, 0:512])
            for h in range(2):
                mk.op("v", "memset", kh(h)[64:65, 0:S], -1.0)
            mk.op("v", "memset", A_VT[:, vt_v0:vt_v0 + 2080], 1.0)
            for tc in range(NT):
                for dc in range(DC):
                    mk.mm(PSB[:, 512:640], a[dc][:, tc * 128:(tc + 1) * 128],
                          wg[:, dc * 384 + 256:dc * 384 + 384],
                          start=(dc == 0), stop=False)
                mk.mm(PSB[:, 512:640], onesB[32:33, 0:128],
                      qkvb[:, 2048 + g * 128:2048 + (g + 1) * 128],
                      start=False, stop=True)
                mk.op("v", "tensor_copy", A_VT[:, vt_v0 + tc * 130:vt_v0 + tc * 130 + 64],
                      PSB[:, 512:576])
                mk.op("v", "tensor_copy", A_VT[:, vt_v0 + tc * 130 + 65:vt_v0 + tc * 130 + 129],
                      PSB[:, 576:640])
            maxc = maxcG
            for h in range(2):
                for qt in range(NT):
                    qtl = slice(qt * 128, (qt + 1) * 128)
                    for kt in range(NQ):
                        mk.mm(PSA[:, kt * 512:(kt + 1) * 512], qh(h)[0:64, qtl],
                              kh(h)[0:64, kt * 512:(kt + 1) * 512], start=True, stop=True)
                    mk.op("v", "reduce_max", maxc[:, 0:1], PSA[:, 0:S], axis=AX.X)
                    mk.op("t", "transpose", PSB[0:1, 512:640], maxc[:, 0:1], identF2[:])
                    mk.op("v", "tensor_copy", qh(h)[64:65, qt * 128:(qt + 1) * 128],
                          PSB[0:1, 512:640])
            for h in range(2):
                for qt in range(NQ):
                    qsl = slice(qt * 512, (qt + 1) * 512)
                    pvslot = PSB[0:65, 1024:1536]
                    npair = NT // 2
                    slots = (PSA[:, 0:1024], PSA[:, 1024:2048], PSB[:, 0:1024])

                    def sco(j, h=h, qsl=qsl):
                        pa = slots[j % 3]
                        for jj in range(2):
                            kc = 2 * j + jj
                            mk.mm(pa[:, jj * 512:(jj + 1) * 512],
                                  kh(h)[:, kc * 128:(kc + 1) * 128],
                                  qh(h)[:, qsl], start=True, stop=True)
                        roll = A_PT[:, proll0 + (j % 3) * 1024:proll0 + (j % 3) * 1024 + 1024]
                        mk.op("s", "activation", roll, pa, AF.Exp, scale=0.125)
                        return roll

                    def pv(j, roll, h=h):
                        for jj in range(2):
                            kc = 2 * j + jj
                            mk.mm(pvslot,
                                  A_VT[:, vt_v0 + kc * 130 + h * 65:vt_v0 + kc * 130 + (h + 1) * 65],
                                  roll[:, jj * 512:(jj + 1) * 512],
                                  start=(kc == 0), stop=(kc == NT - 1))

                    rolls = {0: sco(0), 1: sco(1)}
                    for j in range(2, npair):
                        rolls[j] = sco(j)
                        pv(j - 2, rolls[j - 2])
                    pv(npair - 2, rolls[npair - 2])
                    pv(npair - 1, rolls[npair - 1])
                    mk.op("v", "tensor_copy", attn_st[h * 64:(h + 1) * 64, qsl],
                          pvslot[0:64, :])
                    mk.op("v", "tensor_copy",
                          A_SC[32:33, (h * NQ + qt) * 512:(h * NQ + qt) * 512 + 512],
                          pvslot[64:65, :])
            mk.op("s", "activation", A_SC[32:33, 0:2 * NQ * 512],
                  A_SC[32:33, 0:2 * NQ * 512], AF.Ln)
            mk.op("s", "activation", A_SC[32:33, 0:2 * NQ * 512],
                  A_SC[32:33, 0:2 * NQ * 512], AF.Exp, scale=-1.0)
            for h in range(2):
                for qt in range(NQ):
                    qsl = slice(qt * 512, (qt + 1) * 512)
                    mk.mm(PSB[:, 1024:1536], onesF[32:33, :],
                          A_SC[32:33, (h * NQ + qt) * 512:(h * NQ + qt) * 512 + 512],
                          start=True, stop=True)
                    mk.op("v", "tensor_mul", attn_st[h * 64:(h + 1) * 64, qsl],
                          attn_st[h * 64:(h + 1) * 64, qsl], PSB[0:64, 1024:1536])
            mk.dma("y", hattn[g * 128:(g + 1) * 128, :], attn_st)

        wo = A_WS[:, 0:8192]
        mk.dma("g", wo, ap["l0f_wout"])
        for dc in range(DC):
            mk.dma("y", A_PT[:, dc * S:(dc + 1) * S], hattn[dc * 128:(dc + 1) * 128, :])
        for oc in range(DC):
            for ts in range(NQ):
                sl = slice(ts * 512, (ts + 1) * 512)
                for dc in range(DC):
                    mk.mm(PSA[:, sl], wo[:, dc * 1024 + oc * 128:dc * 1024 + (oc + 1) * 128],
                          A_PT[:, dc * S + ts * 512:dc * S + (ts + 1) * 512],
                          start=(dc == 0), stop=(dc == DC - 1))
                mk.op("v", "scalar_tensor_tensor", X(oc, sl), PSA[:, sl],
                      boc[:, oc:oc + 1], X(oc, sl), op0=ALU.add, op1=ALU.add)


    def part_l0fm(have_l0fa):
        NE, NH = 8, 16
        if not have_l0fa:
            n2sb = nc.alloc_sbuf_tensor("n2sb_m", [128, 8], BF16)
            mk.dma("g", n2sb[:], ap["l0f_n2"])
            for dc in range(DC):
                mk.dma("g", hx1[dc * 128:(dc + 1) * 128, :], ap["tx1"][dc * 128:(dc + 1) * 128, :])
        else:
            n2sb = ap["__n2sb__"]
        gwsb = nc.alloc_sbuf_tensor("gwsb", [128, 64], F32)
        mk.dma("g", gwsb[:], ap["l0f_gw"])
        n2f = nc.alloc_sbuf_tensor("n2f", [128, 8], F32)
        mk.dma("g", n2f[:], ap["l0f_n2"])
        rmsscol = nc.alloc_sbuf_tensor("rmsscol", [128, 16], F32)
        glt = nc.alloc_sbuf_tensor("glt", [128, 8], F32)
        v1 = nc.alloc_sbuf_tensor("v1", [128, 1], F32)
        v2 = nc.alloc_sbuf_tensor("v2", [128, 1], F32)
        sg = nc.alloc_sbuf_tensor("sg", [128, 1], F32)
        og = nc.alloc_sbuf_tensor("og", [128, 1], F32)
        tmp8 = nc.alloc_sbuf_tensor("tmp8", [128, 8], F32)
        mask1 = nc.alloc_sbuf_tensor("mask1", [128, 8], F32)
        identF = identG
        stage8 = tmpg[0:8, 0:128]
        if "__bqk__" not in ap:
            ap["__bqk__"] = nc.alloc_sbuf_tensor("bqk", [128, 16], F32)
        b1sb = ap["__bqk__"]
        b2sb = ap["__bqkL__"]

        m = [A_NB[:, dc * S:(dc + 1) * S] for dc in range(DC)]

        for dc in range(DC):
            mk.op("v", "tensor_copy", m[dc], X(dc))
        ln_sum_stats(m, want_mean=False)
        rmss = A_SC[0:1, 0:S]
        rtmp = A_SC[0:1, 4096:4608]
        for ts in range(NQ):
            sl = slice(ts * 512, (ts + 1) * 512)
            mk.op("v", "tensor_scalar", rtmp, rowpack(4 + ts), 1.0 / D, 1e-6,
                  op0=ALU.mult, op1=ALU.add)
            mk.op("s", "activation", rtmp, rtmp, AF.Ln)
            mk.op("s", "activation", rtmp, rtmp, AF.Exp, scale=-0.5)
            mk.op("v", "tensor_copy", rmss[:, sl], rtmp)
        for ts in range(NQ):
            sl = slice(ts * 512, (ts + 1) * 512)
            mk.mm(PSB[:, 0:512], onesF[0:1, :], rmss[:, sl], start=True, stop=True)
            for dc in range(DC):
                mk.op("v", "scalar_tensor_tensor", m[dc][:, sl], X(dc, sl),
                      n2sb[:, dc:dc + 1], PSB[:, 0:512], op0=ALU.mult, op1=ALU.mult)

        for dc in range(DC):
            mk.op("v", "tensor_scalar", gwsb[:, dc * 8:(dc + 1) * 8],
                  gwsb[:, dc * 8:(dc + 1) * 8], n2f[:, dc:dc + 1], None, op0=ALU.mult)
        for tt in range(NT):
            mk.op("t", "transpose", PSB[:, 1032:1033],
                  rmss[:, tt * 128:(tt + 1) * 128], identF[0:1, 0:1])
            mk.op("v", "tensor_copy", rmsscol[:, tt % 16:tt % 16 + 1], PSB[:, 1032:1033])
            tsl = slice(tt * 128, (tt + 1) * 128)
            for dc in range(DC):
                mk.mm(PSB[:, 1024:1032], X(dc)[:, tsl], gwsb[:, dc * 8:(dc + 1) * 8],
                      start=(dc == 0), stop=(dc == DC - 1))
            mk.op("v", "tensor_scalar", glt[:], PSB[:, 1024:1032],
                  rmsscol[:, tt % 16:tt % 16 + 1], None, op0=ALU.mult)
            mk.op("v", "reduce_max", v1[:, 0:1], glt[:], axis=AX.X)
            mk.op("v", "tensor_scalar", mask1[:], glt[:], v1[:, 0:1], None, op0=ALU.is_equal)
            mk.op("v", "scalar_tensor_tensor", tmp8[:], mask1[:], -1e30, glt[:],
                  op0=ALU.mult, op1=ALU.add)
            mk.op("v", "reduce_max", v2[:, 0:1], tmp8[:], axis=AX.X)
            mk.op("v", "tensor_sub", sg[:, 0:1], v2[:, 0:1], v1[:, 0:1])
            mk.op("s", "activation", sg[:, 0:1], sg[:, 0:1], AF.Exp)
            mk.op("v", "tensor_scalar_add", sg[:, 0:1], sg[:, 0:1], 1.0)
            mk.op("v", "reciprocal", sg[:, 0:1], sg[:, 0:1])
            mk.op("v", "tensor_scalar", og[:, 0:1], sg[:, 0:1], -1.0, 1.0,
                  op0=ALU.mult, op1=ALU.add)
            mk.op("v", "tensor_scalar", mask1[:], mask1[:], sg[:, 0:1], None, op0=ALU.mult)
            mk.op("v", "tensor_scalar", tmp8[:], tmp8[:], v2[:, 0:1], None, op0=ALU.is_equal)
            mk.op("v", "tensor_scalar", tmp8[:], tmp8[:], og[:, 0:1], None, op0=ALU.mult)
            combtile = glt
            mk.op("v", "tensor_add", combtile[:], mask1[:], tmp8[:])
            mk.op("t", "transpose", PSB[0:8, 0:128], combtile[:], identF[:])
            mk.op("v", "tensor_copy", stage8[:], PSB[0:8, 0:128])
            mk.dma("y", hcomb[:, tt * 128:(tt + 1) * 128], stage8[:])

        for dc in range(DC):
            for ts in range(NQ):
                sl = slice(ts * 512, (ts + 1) * 512)
                mk.dma("y", tmpg[:], hx1[dc * 128:(dc + 1) * 128, ts * 512:(ts + 1) * 512])
                mk.op("v", "tensor_add", X(dc, sl), X(dc, sl), tmpg[:])

        combrow = A_SC[0:1, 2048:2048 + S]
        combrow32 = A_SC[32:33, 0:S]
        brow = rowsB[0:1, 2048:4096]
        TH = S // 2
        NTH = max(1, TH // 512)
        THW = min(512, TH)

        me0 = 0
        w12s = (DC * TH, DC * TH + 2048)
        w3s = (DC * TH + 4096, DC * TH)
        for e in range(NE):
            mk.dma("g", combrow, hcomb[e:e + 1, :])
            mk.dma("g", combrow32, hcomb[e:e + 1, :])
            mk.dma("g", b1sb[:], ap["l0f_b1"][e])
            mk.dma("g", b2sb[:], ap["l0f_b2c"][e])
            for tbh in range(2):
                for th in range(NTH):
                    sl = slice(tbh * TH + th * THW, tbh * TH + (th + 1) * THW)
                    mk.mm(PSB[:, 1024:1024 + THW], onesF[0:1, :],
                          combrow[:, sl.start:sl.stop], start=True, stop=True)
                    mk.op("v", "tensor_copy",
                          rowsB[:, 2048 + th * THW:2048 + (th + 1) * THW],
                          PSB[:, 1024:1024 + THW])
                for hc in range(NH):
                    wsl = w12s[hc % 2]
                    w12 = A_WS[:, 0:1] if False else A_PT[:, wsl:wsl + 2048]
                    mk.dma("g", w12, ap["l0f_w12"][e, :, hc * 2048:(hc + 1) * 2048])
                    pa = PSA[:, (hc % 2) * TH:(hc % 2) * TH + TH]
                    pb = PSB[:, 0:TH]
                    for th in range(NTH):
                        sl = slice(tbh * TH + th * THW, tbh * TH + (th + 1) * THW)
                        psl = slice(th * THW, (th + 1) * THW)
                        for dc in range(DC):
                            mk.mm(pa[:, psl], w12[:, dc * 256:dc * 256 + 128],
                                  m[dc][:, sl], start=(dc == 0), stop=(dc == DC - 1))
                        for dc in range(DC):
                            mk.mm(pb[:, psl], w12[:, dc * 256 + 128:dc * 256 + 256],
                                  m[dc][:, sl], start=(dc == 0), stop=(dc == DC - 1))
                    hrow = A_VT[:, hc * TH:(hc + 1) * TH]
                    mk.op("s", "activation", hrow, pa, AF.Silu, bias=b1sb[:, hc:hc + 1])
                    mk.op("v", "scalar_tensor_tensor", hrow, pb, b2sb[:, hc:hc + 1],
                          hrow, op0=ALU.add, op1=ALU.mult)
                    for th in range(NTH):
                        psl = slice(th * THW, (th + 1) * THW)
                        mk.op("v", "tensor_mul", hrow[:, psl], hrow[:, psl],
                              rowsB[:, 2048 + th * THW:2048 + (th + 1) * THW])
                b3row = A_SC[32:33, 2048:2048 + 1024]
                mk.dma("g", b3row, ap["l0f_b3"][e:e + 1, 0:1024])
                for p2 in range(4):
                    wsl = w3s[p2 % 2]
                    w3c = A_PT[:, wsl:wsl + 4096]
                    mk.dma("g", w3c, ap["l0f_w3"][e, :, :, p2 * 256:(p2 + 1) * 256])
                    for dc2 in range(2):
                        dc = p2 * 2 + dc2
                        py = PSA[:, dc2 * TH:(dc2 + 1) * TH]
                        for th in range(NTH):
                            sl = slice(tbh * TH + th * THW, tbh * TH + (th + 1) * THW)
                            psl = slice(th * THW, (th + 1) * THW)
                            for hc in range(NH):
                                mk.mm(py[:, psl], w3c[:, hc * 256 + dc2 * 128:hc * 256 + (dc2 + 1) * 128],
                                      A_VT[:, hc * TH + th * THW:hc * TH + (th + 1) * THW],
                                      start=(hc == 0), stop=False)
                            mk.mm(py[:, psl], b3row[:, dc * 128:(dc + 1) * 128],
                                  combrow32[:, sl.start:sl.stop], start=False, stop=True)
                            mk.op("v", "tensor_add", X(dc, sl), X(dc, sl), py[:, psl])


    def part_l1a():
        NG = 8
        NW = S // 512
        mk.dma("g", rowsB[32:33, 0:4096], ap["l1a_rows"])
        qkvb = rowsB[32:33, 0:3072]
        outb = rowsB[32:33, 3072:4096]
        bqkL = ap["__bqkL__"]
        bocL = nc.alloc_sbuf_tensor("bocL", [128, 8], F32)
        mk.dma("g", bqkL[:], ap["l1a_bqk"])
        mk.dma("g", bocL[:], ap["l1a_boc"])
        padc = rowsB[64:65, 2048:2113]
        mk.op("v", "memset", padc[0:1, 0:64], -512.0)
        mk.op("v", "memset", padc[0:1, 64:65], 512.0)
        epad = rowsB[64:65, 2176:2688]

        vt_v0 = 0
        vt_q = 2080
        vt_k = 2080 + 2 * S
        attst0 = S + 3072
        attn_st = A_PT[:, attst0:attst0 + S]
        proll0 = S
        msk = A_PT[:, attst0 + S:attst0 + S + 4096]
        mk.dma("g", msk, ap["c_l1mask"])
        identF3 = identG
        maxc1 = maxcG

        def qh(h):
            return A_VT[0:65, vt_q + h * S:vt_q + (h + 1) * S]

        def kh(h):
            return A_VT[0:65, vt_k + h * S:vt_k + (h + 1) * S]

        zb = [A_NB[:, dc * S:(dc + 1) * S] for dc in range(DC)]
        for dc in range(DC):
            mk.op("v", "tensor_copy", zb[dc], X(dc))
        ln_sum_stats(zb)
        mu = A_SC[0:1, 0:S]
        rstd = A_SC[0:1, 2048:2048 + S]
        rtmp = A_SC[0:1, 4096:4608]
        for ts in range(NQ):
            sl = slice(ts * 512, (ts + 1) * 512)
            mk.op("v", "tensor_scalar_mul", mu[:, sl], rowpack(ts), 1.0 / D)
            mk.op("v", "tensor_scalar_mul", rstd[:, sl], rowpack(4 + ts), 1.0 / D)
            mk.op("v", "tensor_mul", rtmp, mu[:, sl], mu[:, sl])
            mk.op("v", "tensor_sub", rstd[:, sl], rstd[:, sl], rtmp)
            mk.op("v", "tensor_scalar_add", rstd[:, sl], rstd[:, sl], 1e-5)
            mk.op("s", "activation", rstd[:, sl], rstd[:, sl], AF.Ln)
            mk.op("s", "activation", rstd[:, sl], rstd[:, sl], AF.Exp, scale=-0.5)
            mk.op("v", "tensor_mul", mu[:, sl], mu[:, sl], rstd[:, sl])
        for ts in range(NQ):
            sl = slice(ts * 512, (ts + 1) * 512)
            mk.mm(PSB[:, 0:512], onesF[0:1, :], rstd[:, sl], start=True, stop=True)
            mk.mm(PSB[:, 512:1024], onesF[0:1, :], mu[:, sl], start=True, stop=True)
            for dc in range(DC):
                mk.op("v", "tensor_mul", tmpf, X(dc, sl), PSB[:, 0:512])
                mk.op("v", "tensor_sub", zb[dc][:, sl], tmpf, PSB[:, 512:1024])

        for g in range(NG):
            ws0 = (g % 2) * 3072
            wg = A_WS[:, ws0:ws0 + 3072]
            mk.dma("g", wg, ap["l1a_wqkv"][:, g * 3072:(g + 1) * 3072])
            for which in range(2):
                bcol = bqkL[:, which * 8 + g:which * 8 + g + 1]
                for ts in range(NQ):
                    sl = slice(ts * 512, (ts + 1) * 512)
                    for dc in range(DC):
                        mk.mm(PSA[:, sl], wg[:, dc * 384 + which * 128:dc * 384 + which * 128 + 128],
                              zb[dc][:, sl], start=(dc == 0), stop=(dc == DC - 1))
                for ts in range(NQ):
                    sl = slice(ts * 512, (ts + 1) * 512)
                    for h in range(2):
                        dst = qh(h) if which == 0 else kh(h)
                        mk.op("v", "tensor_scalar", dst[0:64, sl.start:sl.stop],
                              PSA[h * 64:(h + 1) * 64, sl],
                              bcol[h * 64:(h + 1) * 64, 0:1], None, op0=ALU.add)
            for h in range(2):
                mk.op("v", "memset", kh(h)[64:65, 0:S], -1.0)
            mk.op("v", "memset", A_VT[:, vt_v0:vt_v0 + 2080], 1.0)
            for tc in range(NT):
                for dc in range(DC):
                    mk.mm(PSB[:, 512:640], zb[dc][:, tc * 128:(tc + 1) * 128],
                          wg[:, dc * 384 + 256:dc * 384 + 384],
                          start=(dc == 0), stop=False)
                mk.mm(PSB[:, 512:640], onesB[32:33, 0:128],
                      qkvb[:, 2048 + g * 128:2048 + (g + 1) * 128],
                      start=False, stop=True)
                mk.op("v", "tensor_copy", A_VT[:, vt_v0 + tc * 130:vt_v0 + tc * 130 + 64],
                      PSB[:, 512:576])
                mk.op("v", "tensor_copy", A_VT[:, vt_v0 + tc * 130 + 65:vt_v0 + tc * 130 + 129],
                      PSB[:, 576:640])
            for h in range(2):
                for w in range(NW):
                    qsl = slice(w * 512, (w + 1) * 512)
                    kcs = [kc for kc in range(12) if 0 <= (w - 1) * 512 + kc * 128 < S]
                    has_lpad = (w == 0)
                    has_rpad = (w == NW - 1)
                    npad = (1 if has_lpad else 0) + (1 if has_rpad else 0)
                    kt0 = (w - 1) * 512 + kcs[0] * 128
                    kt1 = (w - 1) * 512 + (kcs[-1] + 1) * 128
                    if npad:
                        mk.mm(rowpack(6), onesB[0:64, 0:1], qh(h)[0:64, qsl],
                              start=True, stop=True)
                    for qt in range(4):
                        qtl = slice(w * 512 + qt * 128, w * 512 + (qt + 1) * 128)
                        nk = kt1 - kt0
                        for i, c0 in enumerate(range(kt0, kt1, 512)):
                            cw = min(512, kt1 - c0)
                            mk.mm(PSA[:, i * 512:i * 512 + cw], qh(h)[0:64, qtl],
                                  kh(h)[0:64, c0:c0 + cw], start=True, stop=True)
                        mk.op("v", "reduce_max", maxc1[:, 0:1], PSA[:, 0:nk], axis=AX.X)
                        mk.op("t", "transpose", PSB[0:1, 512:640], maxc1[:, 0:1], identF3[:])
                        mk.op("v", "tensor_copy", qh(h)[64:65, qtl.start:qtl.stop],
                              PSB[0:1, 512:640])
                    if npad:
                        mrow = A_SC[0:1, 4096:4608]
                        spadrow = A_SC[0:1, 0:512]
                        mk.op("v", "tensor_scalar_mul", spadrow, rowpack(6), -1.0)
                        mk.op("v", "tensor_copy", mrow[:, 0:512], qh(h)[64:65, qsl])
                        mk.op("v", "tensor_max", mrow[:, 0:512], mrow[:, 0:512], spadrow)
                        mk.op("v", "tensor_copy", qh(h)[64:65, qsl.start:qsl.stop], mrow[:, 0:512])
                        mk.op("v", "tensor_sub", spadrow, spadrow, mrow[:, 0:512])
                        mk.op("s", "activation", epad, spadrow, AF.Exp)
                    pvslot = PSB[0:65, 1024:1536]
                    slots6 = (PSA[:, 0:512], PSA[:, 512:1024],
                              PSA[:, 1024:1536], PSA[:, 1536:2048],
                              PSB[:, 0:512], PSB[:, 512:1024])

                    def sco(j, h=h, w=w, qsl=qsl):
                        kc = kcs[j]
                        kt = (w - 1) * 512 + kc * 128
                        pa = slots6[j % 6]
                        mk.mm(pa, kh(h)[:, kt:kt + 128], qh(h)[:, qsl],
                              start=True, stop=True)
                        roll = A_PT[:, proll0 + (j % 6) * 512:proll0 + (j % 6) * 512 + 512]
                        mk.op("s", "activation", roll, pa, AF.Exp)
                        ci = kc if kc < 4 else (kc - 4 if kc >= 8 else None)
                        if ci is not None:
                            mk.op("v", "tensor_mul", roll, roll,
                                  msk[:, ci * 512:(ci + 1) * 512])
                        return roll

                    def pv(j, roll, h=h, w=w):
                        kc = kcs[j]
                        ktc = ((w - 1) * 512 + kc * 128) // 128
                        mk.mm(pvslot,
                              A_VT[:, vt_v0 + ktc * 130 + h * 65:vt_v0 + ktc * 130 + (h + 1) * 65],
                              roll, start=(j == 0),
                              stop=(npad == 0 and j == len(kcs) - 1))

                    DEP = 5
                    rolls = {}
                    for j in range(len(kcs)):
                        rolls[j] = sco(j)
                        if j >= DEP:
                            pv(j - DEP, rolls[j - DEP])
                    for j in range(max(0, len(kcs) - DEP), len(kcs)):
                        pv(j, rolls[j])
                    for t in range(npad):
                        mk.mm(pvslot, padc[0:1, :], epad,
                              start=False, stop=(t == npad - 1))
                    mk.op("v", "tensor_copy", attn_st[h * 64:(h + 1) * 64, qsl],
                          pvslot[0:64, :])
                    mk.op("v", "tensor_copy",
                          A_SC[32:33, (h * NW + w) * 512:(h * NW + w) * 512 + 512],
                          pvslot[64:65, :])
            mk.op("s", "activation", A_SC[32:33, 0:2 * NW * 512],
                  A_SC[32:33, 0:2 * NW * 512], AF.Ln)
            mk.op("s", "activation", A_SC[32:33, 0:2 * NW * 512],
                  A_SC[32:33, 0:2 * NW * 512], AF.Exp, scale=-1.0)
            for h in range(2):
                for w in range(NW):
                    qsl = slice(w * 512, (w + 1) * 512)
                    mk.mm(PSB[:, 1024:1536], onesF[32:33, :],
                          A_SC[32:33, (h * NW + w) * 512:(h * NW + w) * 512 + 512],
                          start=True, stop=True)
                    mk.op("v", "tensor_mul", attn_st[h * 64:(h + 1) * 64, qsl],
                          attn_st[h * 64:(h + 1) * 64, qsl], PSB[0:64, 1024:1536])
            mk.dma("y", hattn[g * 128:(g + 1) * 128, :], attn_st)

        wo = A_WS[:, 0:8192]
        mk.dma("g", wo, ap["l1a_wout"])
        for dc in range(DC):
            mk.dma("y", A_PT[:, dc * S:(dc + 1) * S], hattn[dc * 128:(dc + 1) * 128, :])
        for oc in range(DC):
            for ts in range(NQ):
                sl = slice(ts * 512, (ts + 1) * 512)
                for dc in range(DC):
                    mk.mm(PSA[:, sl], wo[:, dc * 1024 + oc * 128:dc * 1024 + (oc + 1) * 128],
                          A_PT[:, dc * S + ts * 512:dc * S + (ts + 1) * 512],
                          start=(dc == 0), stop=(dc == DC - 1))
                mk.op("v", "scalar_tensor_tensor", X(oc, sl), PSA[:, sl],
                      bocL[:, oc:oc + 1], X(oc, sl), op0=ALU.add, op1=ALU.add)


    def part_l1f():
        drain_precast(64)
        NP = 2048
        MT = S // 128
        rg = [list(range(n_cores))]
        ub = [A_NB[:, dc * S:(dc + 1) * S] for dc in range(DC)]
        for dc in range(DC):
            mk.op("v", "tensor_copy", ub[dc], X(dc))
        ln_sum_stats(ub)
        mu = A_SC[0:1, 0:S]
        rstd = A_SC[0:1, 2048:2048 + S]
        rtmp = A_SC[0:1, 4096:4608]
        for ts in range(NQ):
            sl = slice(ts * 512, (ts + 1) * 512)
            mk.op("v", "tensor_scalar_mul", mu[:, sl], rowpack(ts), 1.0 / D)
            mk.op("v", "tensor_scalar_mul", rstd[:, sl], rowpack(4 + ts), 1.0 / D)
            mk.op("v", "tensor_mul", rtmp, mu[:, sl], mu[:, sl])
            mk.op("v", "tensor_sub", rstd[:, sl], rstd[:, sl], rtmp)
            mk.op("v", "tensor_scalar_add", rstd[:, sl], rstd[:, sl], 1e-5)
            mk.op("s", "activation", rstd[:, sl], rstd[:, sl], AF.Ln)
            mk.op("s", "activation", rstd[:, sl], rstd[:, sl], AF.Exp, scale=-0.5)
            mk.op("v", "tensor_mul", mu[:, sl], mu[:, sl], rstd[:, sl])
        for ts in range(NQ):
            sl = slice(ts * 512, (ts + 1) * 512)
            mk.mm(PSB[:, 0:512], onesF[0:1, :], rstd[:, sl], start=True, stop=True)
            mk.mm(PSB[:, 512:1024], onesF[0:1, :], mu[:, sl], start=True, stop=True)
            for dc in range(DC):
                mk.op("v", "tensor_mul", tmpf, X(dc, sl), PSB[:, 0:512])
                mk.op("v", "tensor_sub", ub[dc][:, sl], tmpf, PSB[:, 512:1024])

        for mt in range(MT):
            for dc in range(DC):
                slot = (dc % 2) * 128
                mk.op("t", "transpose", PST[:, slot:slot + 128],
                      ub[dc][:, mt * 128:(mt + 1) * 128], ident[:])
                mk.op("v", "tensor_copy",
                      A_VT[:, mt * 1024 + dc * 128:mt * 1024 + (dc + 1) * 128],
                      PST[:, slot:slot + 128])

        phih = A_WS[:, 0:8192]
        ar_chunk = max(512, S // 4)
        n_chunks = S // ar_chunk
        done_rows = 0
        for nph in range(2):
            mk.dma("g", phih, ap["l1f_phi"].rearrange("p (dc np) -> p dc np", np=2048)[:, :, nph * 1024:(nph + 1) * 1024])
            for mt in range(MT):
                for s5 in range(2):
                    nsl = slice(s5 * 512, (s5 + 1) * 512)
                    for dc in range(DC):
                        mk.mm(PSA[:, nsl], ub[dc][:, mt * 128:(mt + 1) * 128],
                              phih[:, dc * 1024 + s5 * 512:dc * 1024 + (s5 + 1) * 512],
                              start=(dc == 0), stop=(dc == DC - 1))
                    mk.op("s", "activation", tmpg[:], PSA[:, nsl], AF.Exp)
                    mk.dma("y", hE.ap()[mt * 128:(mt + 1) * 128,
                                        nph * 1024 + s5 * 512:nph * 1024 + (s5 + 1) * 512],
                           tmpg[:])
            if nph == 1:
                for c in range(n_chunks):
                    mk.op("g", "collective_compute", "AllReduce", ALU.add,
                          replica_groups=rg,
                          ins=[hE.ap()[c * ar_chunk:(c + 1) * ar_chunk, :].opt()],
                          outs=[hS.ap()[c * ar_chunk:(c + 1) * ar_chunk, :].opt()])

        dcache0 = 0
        for npb in range(4):
            nsl = slice(npb * 512, (npb + 1) * 512)
            for mt in range(MT):
                mk.dma("y", tmpg[:], hE.ap()[mt * 128:(mt + 1) * 128, nsl])
                mk.dma("y", tmpf[:, 0:512] if False else tmpf, hS.ap()[mt * 128:(mt + 1) * 128, nsl])
                mk.op("s", "activation", tmpf, tmpf, AF.Ln)
                mk.op("s", "activation", tmpf, tmpf, AF.Exp, scale=-1.0)
                mk.op("v", "tensor_mul", A_PT[:, dcache0 + mt * 512:dcache0 + (mt + 1) * 512],
                      tmpg[:], tmpf)
            for half in range(2):
                for dci in range(4):
                    dc = half * 4 + dci
                    for mt in range(MT):
                        mk.mm(PSA[:, dci * 512:(dci + 1) * 512],
                              A_VT[:, mt * 1024 + dc * 128:mt * 1024 + (dc + 1) * 128],
                              A_PT[:, dcache0 + mt * 512:dcache0 + (mt + 1) * 512],
                              start=(mt == 0), stop=(mt == MT - 1))
                for dci in range(4):
                    dc = half * 4 + dci
                    mk.op("v", "tensor_copy",
                          A_NB[:, dc * 2048 + npb * 512:dc * 2048 + (npb + 1) * 512],
                          PSA[:, dci * 512:(dci + 1) * 512])

        wslots = (A_WS[:, 0:8192], A_PT[:, 0:8192], A_PT[:, 8192:16384])
        for npq in range(16):
            pbase = 0 if npq % 2 == 0 else 1024
            for lane in range(4):
                n = npq * 4 + lane
                wslot = wslots[n % 3]
                mk.dma("y" if n % 2 == 0 else "g", wslot, hWbf.ap()[n])
                for eh in range(2):
                    pslot = PSA[:, pbase + eh * 512:pbase + (eh + 1) * 512]
                    for dc in range(DC):
                        mk.mm(pslot[lane * 32:lane * 32 + 32, :],
                              A_NB[:, dc * 2048 + npq * 128 + lane * 32:dc * 2048 + npq * 128 + (lane + 1) * 32],
                              wslot[:, dc * 1024 + eh * 512:dc * 1024 + (eh + 1) * 512],
                              start=(dc == 0), stop=(dc == DC - 1),
                              tile_position=(0, lane * 32), skip_group_check=True)
            for lane in range(4):
                n = npq * 4 + lane
                brep = rowsB[0:32, (lane % 2) * 1024:(lane % 2) * 1024 + 1024]
                mk.dma("g", brep, ap["l1f_brep"][npq * 128 + lane * 32:npq * 128 + (lane + 1) * 32, :])
                for eh in range(2):
                    pslot = PSA[:, pbase + eh * 512:pbase + (eh + 1) * 512]
                    mk.op("v", "tensor_add",
                          A_VT[:, npq * 1024 + eh * 512:npq * 1024 + (eh + 1) * 512][lane * 32:lane * 32 + 32, :],
                          pslot[lane * 32:lane * 32 + 32, :], brep[:, eh * 512:(eh + 1) * 512])

        est0 = 8192
        for mb in range(S // 256):
            for i in range(2):
                mrow = mb * 256 + i * 128
                mk.dma("g", A_PT[:, est0 + i * 2048:est0 + (i + 1) * 2048],
                       hE.ap()[mrow:mrow + 128, :])
                for npc in range(16):
                    slot = (npc % 2) * 128
                    mk.op("t", "transpose", PST[:, slot:slot + 128],
                          A_PT[:, est0 + i * 2048 + npc * 128:est0 + i * 2048 + (npc + 1) * 128],
                          ident[:])
                    mk.op("v", "tensor_copy",
                          A_PT[:, est0 + 4096 + npc * 256 + i * 128:est0 + 4096 + npc * 256 + (i + 1) * 128],
                          PST[:, slot:slot + 128])
            for npc in range(16):
                enp = A_PT[:, est0 + 4096 + npc * 256:est0 + 4096 + (npc + 1) * 256]
                for ec in range(DC):
                    bank, off = divmod(ec, 2)
                    mk.mm(PSA[:, bank * 512 + off * 256:bank * 512 + off * 256 + 256],
                          A_VT[:, npc * 1024 + ec * 128:npc * 1024 + (ec + 1) * 128],
                          enp, start=(npc == 0 and off == 0), stop=(npc == 15 and off == 1),
                          skip_group_check=True)
                mk.mm(rowpack(6)[:, 0:256], onesB[:, 0:1], enp,
                      start=(npc == 0), stop=(npc == 15))
            srow = A_SC[32:33, 0:256]
            mk.op("s", "activation", srow, rowpack(6)[:, 0:256], AF.Ln)
            mk.op("s", "activation", srow, srow, AF.Exp, scale=-1.0)
            mk.mm(PSB[:, 512:768], onesF[32:33, :], srow, start=True, stop=True)
            mk.op("v", "tensor_copy", tmpg[:, 0:256], PSB[:, 512:768])
            for ec in range(DC):
                bank, off = divmod(ec, 2)
                mk.op("v", "tensor_mul", tmpf[:, 0:256],
                      PSA[:, bank * 512 + off * 256:bank * 512 + off * 256 + 256],
                      tmpg[:, 0:256])
                mk.op("v", "tensor_add", X(ec)[:, mb * 256:(mb + 1) * 256],
                      X(ec)[:, mb * 256:(mb + 1) * 256], tmpf[:, 0:256])

    if "l0a" in parts:
        part_l0a()
    if "l0fa" in parts:
        part_l0fa()
    if "l0fm" in parts:
        part_l0fm("l0fa" in parts)
    if "l1a" in parts:
        part_l1a()
    if "l1f" in parts:
        part_l1f()

    for dc in range(DC):
        mk.dma("y", out_x[dc * 128:(dc + 1) * 128, :], A_X[:, dc * S:(dc + 1) * S])
    mk.op("y", "nop")
    mk.flush(verbose=True)
    return nc


_CACHE = {}


def kernel(**inputs):
    S = 2048
    n_cores = 8
    if "nc" not in _CACHE:
        _CACHE["nc"] = build_program(S=S, parts=("l0a", "l0fa", "l0fm", "l1a", "l1f"),
                                     n_cores=n_cores)
    nc = _CACHE["nc"]
    prep = {}
    prep.update(prep_l0a(inputs))
    prep.update(prep_l0f(inputs))
    prep.update(prep_l1a(inputs))
    prep.update(prep_l1f(inputs))
    prep.update(host_constants(S))
    x = np.ascontiguousarray(np.asarray(inputs["L0_attn_x"], dtype=np.float32))
    declared = {a.memorylocations[0].name for a in nc.m.functions[0].allocations
                if isinstance(a, mybir.MemoryLocationSet) and a.kind == "ExternalInput"}
    maps = []
    for c in range(n_cores):
        im = {k: v for k, v in prep.items() if k in declared}
        im["xT"] = np.ascontiguousarray(x[c].T)
        maps.append(im)
    res = run_bass_kernel_spmd(nc, maps, core_ids=list(range(n_cores)))
    out = np.stack([np.ascontiguousarray(res.results[c]["outX"].T) for c in range(n_cores)], 0)
    return out
